# version 03
# baseline (speedup 1.0000x reference)
import math
import numpy as np
import ml_dtypes
from contextlib import ExitStack

import concourse.bass as bass
import concourse.mybir as mybir
from concourse.bass_utils import run_bass_kernel_spmd
from concourse.ap import AP

F32 = mybir.dt.float32
BF16 = mybir.dt.bfloat16
AF = mybir.ActivationFunctionType
ALU = mybir.AluOpType
AX = mybir.AxisListType
BFDT = ml_dtypes.bfloat16

NCORES = 8
SEQ = 2
T = 2048
TOK = SEQ * T
D = 1024
NCH = D // 128
NEG = -1e9
DBG = False
DBG_NAMES = None


class Sem:
    def __init__(self, nc, es, name):
        self.h = es.enter_context(nc.semaphore(name))
        self.n = 0
        self.name = name


class Eng:

    def __init__(self, name, eng):
        self.name = name
        self.e = eng
        self.last = None
        self.seen = {}
        self._marked_ins = None
        self._marked_val = 0

    def __getattr__(self, op):
        fn = getattr(self.e, op)

        def wrap(*a, **k):
            ins = fn(*a, **k)
            if op != "dma_start":
                self.last = ins
            return ins
        return wrap

    def wait(self, sem, val=None):
        if val is None:
            val = sem.n
        if val <= 0:
            return
        if self.seen.get(sem.name, -1) >= val:
            return
        self.e.wait_ge(sem.h, val)
        self.seen[sem.name] = val

    def mark(self, sem, by=1):
        assert self.last is not None, f"nothing to mark on {self.name}"
        if self._marked_ins is self.last:
            return self._marked_val
        sem.n += by
        self.last.then_inc(sem.h, by)
        self._marked_ins = self.last
        self._marked_val = sem.n
        return sem.n

    def fence(self):
        if self.last is None:
            return
        v = self.mark(self.own)
        self.wait(self.own, v)


class Ctx:
    def __init__(self, nc):
        self.nc = nc
        self.es = ExitStack()
        self.q = Eng("q", nc.sync)
        self.s = Eng("s", nc.scalar)
        self.v = Eng("v", nc.vector)
        self.t = Eng("t", nc.tensor)
        self.g = Eng("g", nc.gpsimd)
        self.engs = [self.q, self.s, self.v, self.t, self.g]
        self.lds = [Sem(nc, self.es, f"ld{i}") for i in range(4)]
        self.sts = [Sem(nc, self.es, f"st{i}") for i in range(4)]
        self.ld = self.lds[0]
        self.st = self.sts[0]
        self._ldi = 0
        self._sti = 0
        self.pe = Sem(nc, self.es, "pe")
        self.act = Sem(nc, self.es, "act")
        self.vec = Sem(nc, self.es, "vec")
        self.gp = Sem(nc, self.es, "gp")
        self._own = {"q": self.ld, "s": self.act, "v": self.vec,
                     "t": self.pe, "g": self.gp}
        for e in self.engs:
            e.own = self._own[e.name]
        self._consts = {}

    def dram(self, name, shape, dt=F32, kind="Internal"):
        return self.nc.dram_tensor(name, shape, dt, kind=kind).ap()

    def sbuf(self, name, shape, dt=F32, stack=None):
        return (stack or self.es).enter_context(self.nc.sbuf_tensor(name, shape, dt))

    def psum(self, name, shape, dt=F32, stack=None):
        return (stack or self.es).enter_context(self.nc.psum_tensor(name, shape, dt))

    def dma(self, eng, out, in_, **kw):
        e = {"q": self.q, "s": self.s}[eng]
        if eng == "q":
            sem = self.lds[self._ldi % len(self.lds)]
            self._ldi += 1
        else:
            sem = self.sts[self._sti % len(self.sts)]
            self._sti += 1
        e.wait(sem, sem.n)
        ins = e.dma_start(out=out, in_=in_, **kw)
        sem.n += 16
        ins.then_inc(sem.h, 16)
        return (sem, sem.n)

    def dbg(self, name, view):
        if not DBG or (DBG_NAMES and name not in DBG_NAMES):
            return
        self.barrier()
        shape = list(view.shape)
        ap = self.dram("dbg_" + name, shape,
                       view.dtype if view.dtype == F32 else view.dtype,
                       kind="ExternalOutput")
        self.dma("s", ap[tuple(slice(0, s) for s in shape)], view)
        self.barrier()

    def constf(self, val):
        key = float(val)
        assert key in self._consts, f"const {val} not pre-registered"
        return key

    def barrier(self):
        vals = {}
        for e in self.engs:
            sem = self._own[e.name]
            if e.last is not None:
                e.mark(sem)
                e.last = None
            vals[e.name] = sem.n
        for e in self.engs:
            for o in self.engs:
                if o is not e:
                    e.wait(self._own[o.name], vals[o.name])
            for sem in self.lds + self.sts:
                e.wait(sem)


CONST_FLOATS = [1e-5, 1e-8, 1.0, 0.5, -1.0]



def prep_host(I):
    C = {}
    f32 = np.float32
    C["ident"] = np.eye(128, dtype=BFDT)
    C["ident_f32"] = np.eye(128, dtype=f32)
    C["ones_col"] = np.ones((128, 1), dtype=BFDT)
    C["ones_row"] = np.ones((1, 128), dtype=BFDT)
    tri = np.triu(np.ones((128, 128), f32), k=1)
    C["mask_l0a"] = np.ascontiguousarray(tri * -100000.0)
    C["mask_neg"] = np.ascontiguousarray(tri * NEG)
    C["mask_negb"] = (tri * NEG).astype(BFDT)

    C["l0a_qw"] = np.ascontiguousarray(I["L0_attn_Q_w"].transpose(1, 0, 2).reshape(D, D))
    C["l0a_kw"] = np.ascontiguousarray(I["L0_attn_K_w"].transpose(1, 0, 2).reshape(D, D))
    C["l0a_vw"] = np.ascontiguousarray(I["L0_attn_V_w"].transpose(1, 0, 2).reshape(D, D))
    C["l0a_ow"] = np.ascontiguousarray(I["L0_attn_O_w"].reshape(D, D))
    C["l0a_qb"] = np.ascontiguousarray(I["L0_attn_Q_b"].reshape(NCH, 128).T)
    C["l0a_kb"] = np.ascontiguousarray(I["L0_attn_K_b"].reshape(NCH, 128).T)
    C["l0a_vb"] = np.ascontiguousarray(I["L0_attn_V_b"].reshape(1, D))
    C["l0a_ob"] = np.ascontiguousarray(I["L0_attn_O_b"].reshape(NCH, 128).T)

    C["l1a_w"] = I["L1_attn_w_attn"]
    C["l1a_wp"] = I["L1_attn_w_proj"]
    ba = I["L1_attn_b_attn"]
    C["l1a_qb"] = np.ascontiguousarray(ba[0:1024].reshape(NCH, 128).T)
    C["l1a_kb"] = np.ascontiguousarray(ba[1024:2048].reshape(NCH, 128).T)
    C["l1a_vb"] = np.ascontiguousarray(ba[2048:3072].reshape(1, D))
    C["l1a_pb"] = np.ascontiguousarray(I["L1_attn_b_proj"].reshape(NCH, 128).T)

    C["l3a_qw"] = np.ascontiguousarray(I["L3_attn_Wq"].T)
    C["l3a_kw"] = np.ascontiguousarray(I["L3_attn_Wk"].T)
    C["l3a_vw"] = np.ascontiguousarray(I["L3_attn_Wv"].T)
    C["l3a_ow"] = np.ascontiguousarray(I["L3_attn_Wo"].T)
    C["l3a_qb"] = np.ascontiguousarray(I["L3_attn_bq"].reshape(NCH, 128).T)
    C["l3a_kb"] = np.ascontiguousarray(I["L3_attn_bk"].reshape(NCH, 128).T)
    C["l3a_vb"] = np.ascontiguousarray(I["L3_attn_bv"].reshape(1, D))
    C["l3a_ob"] = np.ascontiguousarray(I["L3_attn_bo"].reshape(NCH, 128).T)
    tt = np.arange(4224)
    idx = np.clip(tt - 2174, -15, 15) + 15
    C["l3a_gext"] = np.ascontiguousarray(I["L3_attn_rel_bias"][idx, :].T)
    ad8 = np.zeros((128, 128), np.float32)
    ad8[np.arange(128), 127 - np.arange(128)] = 8.0
    C["antidiag8"] = ad8.astype(BFDT)

    for nm in ("Wq", "Wk", "Wv", "Wo"):
        C[f"l2a_{nm.lower()}"] = I[f"L2_attn_{nm}"]
    for nm in ("lambda_q1", "lambda_k1", "lambda_q2", "lambda_k2"):
        C[f"l2a_{nm}"] = np.ascontiguousarray(I[f"L2_attn_{nm}"].reshape(1, 64))
    C["l2a_lnw"] = np.ascontiguousarray(I["L2_attn_ln_w"].reshape(1, 128))
    C["l2a_lnb"] = np.ascontiguousarray(I["L2_attn_ln_b"].reshape(1, 128))
    dd = 64
    invf = 1.0 / 10000.0 ** (np.arange(0, dd, 2, dtype=f32) / dd)
    pos = np.arange(T, dtype=f32)
    fr = np.repeat(pos[:, None] * invf[None, :], 2, axis=-1)
    cosr = np.cos(fr)
    sinr = np.sin(fr)
    sA = np.zeros((T, dd), f32); sA[:, 0::2] = -sinr[:, 0::2]
    sB = np.zeros((T, dd), f32); sB[:, 1::2] = sinr[:, 1::2]
    def tile_tok(tab):
        return np.ascontiguousarray(
            tab.reshape(16, 128, dd).transpose(1, 0, 2).reshape(128, 16 * dd))
    C["l2a_cos"] = tile_tok(cosr)
    C["l2a_sa"] = tile_tok(sA)
    C["l2a_sb"] = tile_tok(sB)

    C["l0f_win"] = np.ascontiguousarray(I["L0_ffn_w_in"].T)
    C["l0f_wout"] = np.ascontiguousarray(I["L0_ffn_w_out"].T)
    half64 = np.arange(64, dtype=f32)
    invr = 1.0 / 10000.0 ** (half64 / 64.0)
    ang = np.arange(T, dtype=f32)[:, None] * invr[None, :]
    C["l0f_cos"] = np.ascontiguousarray(
        np.cos(ang).reshape(16, 128, 64).transpose(1, 0, 2).reshape(128, 1024))
    C["l0f_sin"] = np.ascontiguousarray(
        np.sin(ang).reshape(16, 128, 64).transpose(1, 0, 2).reshape(128, 1024))
    C["l0f_ke"] = np.ascontiguousarray(
        I["L0_ffn_k_ffwd"].transpose(1, 3, 0, 2).reshape(8 * 128, 16 * 256))
    vf = I["L0_ffn_v_ffwd"].reshape(16, 8, 2, 128, 128)
    C["l0f_ve"] = np.ascontiguousarray(
        vf.transpose(1, 3, 0, 2, 4).reshape(8 * 128, 16 * 2 * 128))

    C["l1f_wqkv"] = np.ascontiguousarray(I["L1_ffn_Wqkv"].reshape(6 * D, 3072))
    C["l1f_wo"] = np.ascontiguousarray(I["L1_ffn_Wo"].reshape(6 * D, D))
    w1p = np.zeros((6, D, 5632), f32)
    w1p[:, :, 0:2730] = I["L1_ffn_W1"][:, :, 0:2730]
    w1p[:, :, 2816:2816 + 2730] = I["L1_ffn_W1"][:, :, 2730:5460]
    C["l1f_w1"] = np.ascontiguousarray(w1p.reshape(6 * D, 5632))
    w2p = np.zeros((6, 2816, D), f32)
    w2p[:, 0:2730, :] = I["L1_ffn_W2"]
    C["l1f_w2"] = np.ascontiguousarray(w2p.reshape(6 * 2816, D))
    b1p = np.zeros((6, 2, 2816), f32)
    b1p[:, 0, 0:2730] = I["L1_ffn_b1"][:, 0:2730]
    b1p[:, 1, 0:2730] = I["L1_ffn_b1"][:, 2730:5460]
    C["l1f_b1a"] = np.ascontiguousarray(
        b1p[:, 0].reshape(6, 22, 128).transpose(0, 2, 1).reshape(6 * 128, 22))
    C["l1f_b1g"] = np.ascontiguousarray(
        b1p[:, 1].reshape(6, 22, 128).transpose(0, 2, 1).reshape(6 * 128, 22))
    C["l1f_b2"] = np.ascontiguousarray(
        I["L1_ffn_b2"].reshape(6, 8, 128).transpose(0, 2, 1).reshape(6 * 128, 8))
    C["l1f_lng"] = np.ascontiguousarray(
        I["L1_ffn_ln_g"].reshape(6, 8, 128).transpose(0, 2, 1).reshape(6 * 128, 8))
    C["l1f_lnb"] = np.ascontiguousarray(
        I["L1_ffn_ln_b"].reshape(6, 8, 128).transpose(0, 2, 1).reshape(6 * 128, 8))
    C["l1f_qs"] = I["L1_ffn_q_scale"]
    C["l1f_ks"] = I["L1_ffn_k_scale"]
    dh = 64
    half32 = np.arange(0, dh, 2, dtype=f32)
    invf2 = 1.0 / 10000.0 ** (half32 / dh)
    tpos = np.arange(T, dtype=f32)
    fr2 = tpos[:, None] * invf2[None, :]
    fr2 = np.concatenate([fr2, fr2], -1)
    base_sc = (half32 + 0.4 * dh) / (1.4 * dh)
    power = (tpos - T // 2) / (256 // 2)
    xsc = base_sc[None, :] ** power[:, None]
    xsc = np.concatenate([xsc, xsc], -1)
    cs_q = np.cos(fr2) * xsc
    sn_q = np.sin(fr2) * xsc
    cs_k = np.cos(fr2) / xsc
    sn_k = np.sin(fr2) / xsc
    def xtile(tab):
        return np.ascontiguousarray(
            tab.reshape(16, 128, dh).transpose(1, 0, 2).reshape(128, 16 * dh))
    ssn_q = np.concatenate([-sn_q[:, 0:32], sn_q[:, 32:64]], -1)
    ssn_k = np.concatenate([-sn_k[:, 0:32], sn_k[:, 32:64]], -1)
    C["l1f_cs"] = xtile(cs_q)
    C["l1f_ssn"] = xtile(ssn_q)
    C["l1f_csi"] = xtile(cs_k)
    C["l1f_ssi"] = xtile(ssn_k)
    C["mask_low"] = np.ascontiguousarray(
        np.tril(np.ones((128, 128), f32), k=-1) * NEG)

    C["l2f_wgw"] = I["L2_ffn_Wg"]
    C["l2f_ws1"] = I["L2_ffn_Ws1"]
    C["l2f_ws3"] = I["L2_ffn_Ws3"]
    C["l2f_ws2"] = I["L2_ffn_Ws2"]
    C["l2f_we1"] = np.ascontiguousarray(I["L2_ffn_We1"].reshape(8 * D, 2048))
    C["l2f_we3"] = np.ascontiguousarray(I["L2_ffn_We3"].reshape(8 * D, 2048))
    C["l2f_we2"] = np.ascontiguousarray(I["L2_ffn_We2"].reshape(8 * 2048, D))

    C["l3f_wu"] = np.ascontiguousarray(I["L3_ffn_w_up"].T)
    C["l3f_wd"] = np.ascontiguousarray(I["L3_ffn_w_down"].T)

    x = I["L0_attn_x"]
    xs = []
    for c in range(NCORES):
        xc = x[2 * c:2 * c + 2].transpose(2, 0, 1)
        xs.append(np.ascontiguousarray(xc).reshape(NCH, 128, TOK))
    return C, xs


def unprep_out(outs):
    res = np.empty((16, T, D), np.float32)
    for c, oc in enumerate(outs):
        res[2 * c:2 * c + 2] = oc.reshape(D, SEQ, T).transpose(1, 2, 0)
    return res



def x_span_ap(x_dram, t0, w=512, nch=NCH):
    return AP(tensor=x_dram.tensor,
              offset=x_dram.offset + t0,
              ap=[[TOK, 128], [128 * TOK, nch], [1, w]])


def x_chunk_span_ap(x_dram, c, t0, w=512):
    return AP(tensor=x_dram.tensor,
              offset=x_dram.offset + c * 128 * TOK + t0,
              ap=[[TOK, 128], [1, w]])


def col_block_ap(w_dram, dout_total, col0, w, nch=NCH):
    return AP(tensor=w_dram.tensor,
              offset=w_dram.offset + col0,
              ap=[[dout_total, 128], [128 * dout_total, nch], [1, w]])


class WStream:

    def __init__(self, ctx, st, tag, cols, stage_cols=None, stage=None, nch=NCH):
        self.ctx = ctx
        self.cols = cols
        self.nch = nch
        sc = stage_cols or cols
        self.stage = stage if stage is not None else \
            ctx.sbuf(f"{tag}_wst", [128, 2 * sc], F32, st)
        self.sc = sc
        self.bf = ctx.sbuf(f"{tag}_wbf", [128, nch * cols], BF16, st)

    def load(self, w_ap, row0=0):
        ctx = self.ctx
        if not hasattr(ctx, "_stage_guard"):
            ctx._stage_guard = {}
        guard = ctx._stage_guard.get(id(self.stage), 0)
        marks = {}
        for c in range(self.nch):
            half = (c % 2) * self.sc
            if c >= 2:
                ctx.q.wait(ctx.gp, marks[c - 2])
            elif guard:
                ctx.q.wait(ctx.gp, guard)
            lv = ctx.dma("q", self.stage[:, half:half + self.cols],
                         w_ap[row0 + c * 128: row0 + (c + 1) * 128, :])
            ctx.g.wait(*lv)
            ctx.g.tensor_copy(self.bf[:, c * self.cols:(c + 1) * self.cols],
                              self.stage[:, half:half + self.cols])
            marks[c] = ctx.g.mark(ctx.gp)
        ctx._stage_guard[id(self.stage)] = marks[self.nch - 1]
        return marks[self.nch - 1]

    def tile(self, c, j0, w=128):
        return self.bf[:, c * self.cols + j0: c * self.cols + j0 + w]


def emit_ln(ctx, cst, w, x_dram, xn_dst, ntok, tok0=0, eps=1e-5,
            fp32_cb=None, ln_width=D, banks=None, xn_ntok=None, scale_cb=None,
            shift=False, seq_start=None, affine=None, fp32_store=None):
    nsp = ntok // 512
    xn_ntok = xn_ntok or ntok
    ps_s = banks[0][0:1, :]
    ps_q = banks[1][0:1, :]
    ps_b1 = banks[2]
    ps_b2 = banks[3]
    r = w["rows"]
    mu = r[0:1, 0:512]
    mu2 = r[0:1, 512:1024]
    var = r[0:1, 1024:1536]
    lnv = r[0:1, 512:1024]
    rstd = r[0:1, 1536:2048]
    nm = r[0:1, 1024:1536]
    rb = w["rows_b"]
    rstd_b, nm_b = rb[0:1, 0:512], rb[0:1, 512:1024]
    rstdb, nmb = w["bc"][:, 0:512], w["bc"][:, 512:1024]
    xfw = w["xf"].shape[1]
    stw = 513 if shift else 512
    fst = {}
    for sp in range(nsp):
        t0 = tok0 + sp * 512
        if shift:
            at_start = (t0 == seq_start)
            if at_start:
                lv = ctx.dma("q", AP(tensor=w["xf"], offset=1,
                                     ap=[[xfw, 128], [stw, NCH], [1, 512]]),
                             x_span_ap(x_dram, t0, w=512))
                ctx.v.wait(*lv)
                ctx.v.memset(AP(tensor=w["xf"], offset=0,
                                ap=[[xfw, 128], [stw, NCH], [1, 1]]), 0.0)
                ctx.v.fence()
                zm = ctx.v.mark(ctx.vec)
                ctx.s.wait(ctx.vec, zm)
            else:
                lv = ctx.dma("q", AP(tensor=w["xf"], offset=0,
                                     ap=[[xfw, 128], [stw, NCH], [1, stw]]),
                             x_span_ap(x_dram, t0 - 1, w=stw))
                ctx.s.wait(*lv)
                ctx.v.wait(*lv)

            def csl(c):
                off = c * stw + (1 if c < 4 else 0)
                return slice(off, off + 512)
        else:
            lv = ctx.dma("q", w["xf"][:, 0:NCH * 512], x_span_ap(x_dram, t0))
            ctx.s.wait(*lv)
            ctx.v.wait(*lv)

            def csl(c):
                return slice(c * 512, (c + 1) * 512)
        cm = {}
        for c in range(NCH):
            hb = slice((c % 2) * 512, (c % 2) * 512 + 512)
            if c >= 2:
                ctx.s.wait(ctx.pe, cm[c - 2])
                ctx.v.wait(ctx.pe, cm[c - 2])
            ctx.s.copy(w["xb"][:, hb], w["xf"][:, csl(c)])
            am = ctx.s.mark(ctx.act)
            ctx.v.tensor_tensor(w["xqb"][:, hb], w["xf"][:, csl(c)],
                                w["xf"][:, csl(c)], ALU.mult)
            vm = ctx.v.mark(ctx.vec)
            ctx.t.wait(ctx.act, am)
            ctx.t.wait(ctx.vec, vm)
            ctx.t.matmul(ps_s, cst["ones_col"][:], w["xb"][:, hb],
                         start=(c == 0), stop=(c == NCH - 1))
            ctx.t.matmul(ps_q, cst["ones_col"][:], w["xqb"][:, hb],
                         start=(c == 0), stop=(c == NCH - 1))
            cm[c] = ctx.t.mark(ctx.pe)
        pm = cm[NCH - 1]
        ctx.s.wait(ctx.pe, pm)
        ctx.s.mul(mu, ps_s, 1.0 / ln_width)
        ctx.s.fence()
        ctx.s.activation(mu2, mu, AF.Square)
        am2 = ctx.s.mark(ctx.act)
        ctx.v.wait(ctx.act, am2)
        ctx.v.scalar_tensor_tensor(var, ps_q, 1.0 / ln_width, mu2,
                                   ALU.mult, ALU.subtract)
        vm2 = ctx.v.mark(ctx.vec)
        ctx.s.wait(ctx.vec, vm2)
        ctx.s.activation(lnv, var, AF.Ln, bias=float(eps))
        ctx.s.fence()
        ctx.s.activation(rstd, lnv, AF.Exp, scale=-0.5)
        am3 = ctx.s.mark(ctx.act)
        if scale_cb is not None:
            scale_cb(ctx, w, mu, var, rstd)
        ctx.v.wait(ctx.act, am3)
        ctx.v.tensor_tensor(nm, mu, rstd, ALU.mult)
        ctx.v.fence()
        vm3 = ctx.v.mark(ctx.vec)
        ctx.s.wait(ctx.vec, vm3)
        ctx.s.fence()
        ctx.s.copy(rstd_b, rstd)
        ctx.s.copy(nm_b, nm)
        am4 = ctx.s.mark(ctx.act)
        ctx.t.wait(ctx.act, am4)
        ctx.t.matmul(ps_b1[:, :], cst["ones_row"][:], rstd_b, start=True, stop=True)
        ctx.t.matmul(ps_b2[:, :], cst["ones_row"][:], nm_b, start=True, stop=True)
        pm2 = ctx.t.mark(ctx.pe)
        ctx.s.wait(ctx.pe, pm2)
        ctx.s.copy(rstdb, ps_b1[:, :])
        ctx.s.copy(nmb, ps_b2[:, :])
        am5 = ctx.s.mark(ctx.act)
        ctx.v.wait(ctx.act, am5)
        for c in range(NCH):
            dsl = slice(c * xn_ntok + sp * 512, c * xn_ntok + sp * 512 + 512)
            ctx.v.tensor_tensor(w["tmp"][:], w["xf"][:, csl(c)], rstdb, ALU.mult)
            ctx.v.fence()
            if affine is not None:
                gfn, bfn = affine
                ctx.v.tensor_tensor(w["tmp"][:], w["tmp"][:], nmb, ALU.subtract)
                ctx.v.fence()
                ctx.v.scalar_tensor_tensor(xn_dst[:, dsl], w["tmp"][:],
                                           gfn(c), bfn(c), ALU.mult, ALU.add)
            else:
                if xn_dst is not None:
                    ctx.v.tensor_tensor(xn_dst[:, dsl], w["tmp"][:], nmb,
                                        ALU.subtract)
                if fp32_cb is not None:
                    ctx.v.tensor_tensor(fp32_cb(c, sp), w["tmp"][:], nmb,
                                        ALU.subtract)
                if fp32_store is not None:
                    ydram, yscr = fp32_store
                    gi = sp * NCH + c
                    if gi >= 2:
                        ctx.v.wait(*fst[gi - 2])
                    ysl = yscr[:, (gi % 2) * 512:(gi % 2) * 512 + 512]
                    ctx.v.tensor_tensor(ysl, w["tmp"][:], nmb, ALU.subtract)
                    ctx.v.fence()
                    fm = ctx.v.mark(ctx.vec)
                    ctx.s.wait(ctx.vec, fm)
                    fst[gi] = ctx.dma("s", x_chunk_span_ap(ydram, c, t0), ysl)
            ctx.v.fence()
        vm4 = ctx.v.mark(ctx.vec)
        ctx.q.wait(ctx.vec, vm4)
        ctx.t.wait(ctx.vec, vm4)


def ln_work(ctx, st, tag):
    return {
        "xf": ctx.sbuf(f"{tag}_xf", [128, NCH * 513], F32, st),
        "xb": ctx.sbuf(f"{tag}_xb", [128, 1024], BF16, st),
        "xqb": ctx.sbuf(f"{tag}_xqb", [128, 1024], BF16, st),
        "rows": ctx.sbuf(f"{tag}_rows", [1, 512 * 4], F32, st),
        "rows_b": ctx.sbuf(f"{tag}_rowsb", [1, 1024], BF16, st),
        "bc": ctx.sbuf(f"{tag}_bc", [128, 1024], F32, st),
        "tmp": ctx.sbuf(f"{tag}_tmp", [128, 512], F32, st),
    }


def emit_residual_out(ctx, cst, lnw, xin, xout, tok0, ntok, proj_cb, bias_sb,
                      banks, extra_add=None, scale=None):
    nsp = ntok // 512
    xnew0 = lnw["bc"][:, 0:512]
    xnew1 = lnw["bc"][:, 512:1024]
    evs = {}
    stores = {}
    for spn in range(nsp):
        t0 = tok0 + spn * 512
        lv = ctx.dma("q", lnw["xf"][:, 0:NCH * 512], x_span_ap(xin, t0))
        for dc in range(NCH):
            i = spn * NCH + dc
            ps = banks[dc % 2]
            if i >= 2:
                ctx.t.wait(ctx.vec, evs[i - 2])
            proj_cb(dc, spn, ps)
            pm = ctx.t.mark(ctx.pe)
            ctx.s.wait(ctx.pe, pm)
            if bias_sb is not None:
                ctx.s.activation(lnw["tmp"][:], ps[:, :], AF.Identity,
                                 bias=bias_sb[:, dc:dc + 1],
                                 scale=(scale or 1.0))
            else:
                ctx.s.activation(lnw["tmp"][:], ps[:, :], AF.Identity,
                                 scale=(scale or 1.0))
            am = ctx.s.mark(ctx.act)
            xnew = xnew0 if i % 2 == 0 else xnew1
            ctx.v.wait(ctx.act, am)
            ctx.v.wait(*lv)
            if i >= 2:
                ctx.v.wait(*stores[i - 2])
            if extra_add is not None:
                ctx.v.tensor_tensor(lnw["tmp"][:], lnw["tmp"][:],
                                    extra_add(dc, spn), ALU.add)
                ctx.v.fence()
            ctx.v.tensor_tensor(xnew, lnw["tmp"][:],
                                lnw["xf"][:, dc * 512:(dc + 1) * 512], ALU.add)
            evs[i] = ctx.v.mark(ctx.vec)
            ctx.s.wait(ctx.vec, evs[i])
            stores[i] = ctx.dma("s", x_chunk_span_ap(xout, dc, t0), xnew)
        ctx.q.wait(ctx.vec, evs[spn * NCH + NCH - 1])


def emit_softmax_attn_head(ctx, cst, qf, kf, vtm, s_sb, p_sb, pt_sb, stat,
                           banks, ps_t, ps_pv, mask, r0, scale, dv, vcol,
                           acat_cb, causal=True, bias_mm=None, width_full=T,
                           normalize=True, dbg_tag=None):
    dh = dv if r0 is None else 64
    bank_ev = {0: 0, 1: 0}
    nslots = 2 if s_sb.shape[1] >= 2 * width_full else 1
    exp_am = {}
    for qs in range(4):
        qts = [qs * 4 + i for i in range(4)]
        pns = {}
        for i, qt in enumerate(qts):
            negmax = stat[:, 4 * i:4 * i + 1]
            negmax_s = stat[:, 4 * i + 1:4 * i + 2]
            sums = stat[:, 4 * i + 2:4 * i + 3]
            recip = stat[:, 4 * i + 3:4 * i + 4]
            slot = qt % nslots
            sofs = slot * width_full
            wk = (qt + 1) * 128 if causal else width_full
            nks = (wk + 511) // 512
            for ks in range(nks):
                kw = min(512, wk - ks * 512)
                b = ks % 2
                ps = banks[b]
                ctx.t.wait(ctx.act, bank_ev[b])
                ctx.t.matmul(ps[:, 0:kw],
                             qf[r0:r0 + dh, qt * 128:(qt + 1) * 128] if r0 is not None
                             else qf[:, qt * 128:(qt + 1) * 128],
                             kf[r0:r0 + dh, ks * 512:ks * 512 + kw] if r0 is not None
                             else kf[:, ks * 512:ks * 512 + kw],
                             start=True, stop=(bias_mm is None))
                if bias_mm is not None:
                    bias_mm(ctx, ps, ks, kw, qt)
                pm = ctx.t.mark(ctx.pe)
                if ks == 0 and slot in exp_am:
                    ctx.s.wait(ctx.act, exp_am[slot])
                ctx.s.wait(ctx.pe, pm)
                ctx.s.copy(s_sb[:, sofs + ks * 512:sofs + ks * 512 + kw],
                           ps[:, 0:kw])
                bank_ev[b] = ctx.s.mark(ctx.act)
            ctx.v.wait(ctx.act, bank_ev[(nks - 1) % 2])
            if causal and mask is not None:
                ctx.v.tensor_tensor(s_sb[:, sofs + qt * 128:sofs + (qt + 1) * 128],
                                    s_sb[:, sofs + qt * 128:sofs + (qt + 1) * 128],
                                    mask[:], ALU.add)
                ctx.v.fence()
            ctx.v.reduce_max(negmax, s_sb[:, sofs:sofs + wk], axis=AX.X,
                             negate=True)
            vm = ctx.v.mark(ctx.vec)
            ctx.s.wait(ctx.vec, vm)
            ctx.s.mul(negmax_s, negmax, float(scale))
            ctx.s.memzero(sums)
            ctx.s.fence()
            ctx.s.activation(p_sb[:, i * T:i * T + wk], s_sb[:, sofs:sofs + wk],
                             AF.Exp, scale=float(scale), bias=negmax_s,
                             accum_out=sums)
            am = ctx.s.mark(ctx.act)
            exp_am[slot] = am
            if dbg_tag and qt == 15:
                ctx.dbg(dbg_tag + "_s15", s_sb[:, sofs:sofs + width_full])
                ctx.dbg(dbg_tag + "_praw15", p_sb[:, i * T:i * T + wk])
                ctx.dbg(dbg_tag + "_stat15", stat[:])
            ctx.v.wait(ctx.act, am)
            if normalize:
                ctx.v.reciprocal(recip, sums)
                ctx.v.fence()
                ctx.v.tensor_scalar(p_sb[:, i * T:i * T + wk],
                                    p_sb[:, i * T:i * T + wk],
                                    recip, None, ALU.mult)
            pns[i] = ctx.v.mark(ctx.vec)
        nkt = qts[-1] + 1 if causal else width_full // 128
        first_pv = True
        ctx.t.wait(ctx.vec, pns[3])
        for kt in range(nkt):
            idx = [i for i, qt in enumerate(qts) if (not causal) or qt >= kt]
            for j, i in enumerate(idx):
                ctx.t.matmul(ps_t[:, i * 128:(i + 1) * 128],
                             p_sb[:, i * T + kt * 128:i * T + (kt + 1) * 128],
                             cst["ident_sb"][:], is_transpose=True,
                             start=(j == 0), stop=(j == len(idx) - 1))
            tm = ctx.t.mark(ctx.pe)
            ctx.v.wait(ctx.pe, tm)
            off = idx[0] * 128
            ctx.v.tensor_copy(pt_sb[:, off:512], ps_t[:, off:512])
            ptm = ctx.v.mark(ctx.vec)
            ctx.t.wait(ctx.vec, ptm)
            col0, w = vcol(kt)
            ctx.t.matmul(ps_pv[0:dv, off:512], vtm[:, col0:col0 + w],
                         pt_sb[:, off:512],
                         start=first_pv, stop=(kt == nkt - 1))
            first_pv = False
        pvm = ctx.t.mark(ctx.pe)
        acat_cb(qs, pvm)


def emit_sparsemax_attn_head(ctx, cst, qf, kf, vtm, sp_all, stat16, pt_sb,
                             banks, ps_t, ps_pv, mask, r0, scale, dv, vcol,
                             acat_cb, scr_bf, niter=8):
    OFF = [128 * qt * (qt + 1) // 2 for qt in range(17)]
    m_all = stat16[:, 0:16]
    negtau = stat16[:, 16:32]
    S_all = stat16[:, 32:48]
    cnt = stat16[:, 48:64]
    rc = stat16[:, 64:80]
    postau = stat16[:, 80:96]
    bank_ev = {0: 0, 1: 0}
    for qt in range(16):
        wk = (qt + 1) * 128
        nks = (wk + 511) // 512
        for ks in range(nks):
            kw = min(512, wk - ks * 512)
            b = ks % 2
            ctx.t.wait(ctx.act, bank_ev[b])
            ctx.t.matmul(banks[b][:, 0:kw],
                         qf[r0:r0 + 64, qt * 128:(qt + 1) * 128],
                         kf[r0:r0 + 64, ks * 512:ks * 512 + kw],
                         start=True, stop=True)
            pm = ctx.t.mark(ctx.pe)
            ctx.s.wait(ctx.pe, pm)
            ctx.s.activation(sp_all[:, OFF[qt] + ks * 512:OFF[qt] + ks * 512 + kw],
                             banks[b][:, 0:kw], AF.Identity, scale=float(scale))
            bank_ev[b] = ctx.s.mark(ctx.act)
        ctx.v.wait(ctx.act, bank_ev[(nks - 1) % 2])
        ctx.v.tensor_tensor(sp_all[:, OFF[qt] + qt * 128:OFF[qt] + (qt + 1) * 128],
                            sp_all[:, OFF[qt] + qt * 128:OFF[qt] + (qt + 1) * 128],
                            cst["maskb_sb"][:], ALU.add)
        ctx.v.fence()
        ctx.v.reduce_max(m_all[:, qt:qt + 1], sp_all[:, OFF[qt]:OFF[qt] + wk],
                         axis=AX.X)
    ctx.v.fence()
    ctx.v.tensor_scalar(negtau, m_all, -1.0, 1.0, ALU.mult, ALU.add)
    ctx.v.tensor_scalar_add(postau, m_all, -1.0)
    ctx.v.fence()
    vm = ctx.v.mark(ctx.vec)
    ctx.s.wait(ctx.vec, vm)
    for it in range(niter):
        ctx.s.memzero(S_all)
        ctx.s.fence()
        for qt in range(16):
            wk = (qt + 1) * 128
            ctx.s.activation(scr_bf[:, 0:wk], sp_all[:, OFF[qt]:OFF[qt] + wk],
                             AF.Relu, bias=negtau[:, qt:qt + 1],
                             accum_out=S_all[:, qt:qt + 1])
            ctx.s.fence()
            ctx.v.tensor_scalar(scr_bf[:, T:T + wk], sp_all[:, OFF[qt]:OFF[qt] + wk],
                                postau[:, qt:qt + 1], None, ALU.is_gt, ALU.add,
                                accum_out=cnt[:, qt:qt + 1])
            ctx.v.fence()
        am = ctx.s.mark(ctx.act)
        ctx.v.fence()
        ctx.v.tensor_scalar_max(cnt, cnt, 1.0)
        ctx.v.fence()
        ctx.v.reciprocal(rc, cnt)
        ctx.v.wait(ctx.act, am)
        ctx.v.tensor_scalar_add(S_all, S_all, -1.0)
        ctx.v.fence()
        ctx.v.tensor_tensor(S_all, S_all, rc, ALU.mult)
        ctx.v.fence()
        ctx.v.tensor_tensor(negtau, negtau, S_all, ALU.subtract)
        ctx.v.tensor_tensor(postau, postau, S_all, ALU.add)
        ctx.v.fence()
        vm = ctx.v.mark(ctx.vec)
        ctx.s.wait(ctx.vec, vm)
    for qs in range(4):
        qts = [qs * 4 + i for i in range(4)]
        for qt in qts:
            wk = (qt + 1) * 128
            ctx.s.activation(sp_all[:, OFF[qt]:OFF[qt] + wk],
                             sp_all[:, OFF[qt]:OFF[qt] + wk],
                             AF.Relu, bias=negtau[:, qt:qt + 1])
        am2 = ctx.s.mark(ctx.act)
        ctx.t.wait(ctx.act, am2)
        ctx.v.wait(ctx.act, am2)
        nkt = qts[-1] + 1
        first_pv = True
        for kt in range(nkt):
            idx = [i for i, qt in enumerate(qts) if qt >= kt]
            for j, i in enumerate(idx):
                qt = qts[i]
                ctx.t.matmul(ps_t[:, i * 128:(i + 1) * 128],
                             sp_all[:, OFF[qt] + kt * 128:OFF[qt] + (kt + 1) * 128],
                             cst["ident_sb"][:], is_transpose=True,
                             start=(j == 0), stop=(j == len(idx) - 1))
            tm = ctx.t.mark(ctx.pe)
            ctx.v.wait(ctx.pe, tm)
            off = idx[0] * 128
            ctx.v.tensor_copy(pt_sb[:, off:512], ps_t[:, off:512])
            ptm = ctx.v.mark(ctx.vec)
            ctx.t.wait(ctx.vec, ptm)
            col0, w = vcol(kt)
            ctx.t.matmul(ps_pv[0:dv, off:512], vtm[:, col0:col0 + w],
                         pt_sb[:, off:512],
                         start=first_pv, stop=(kt == nkt - 1))
            first_pv = False
        pvm = ctx.t.mark(ctx.pe)
        acat_cb(qs, pvm)



def emit_l0a(ctx, cst, xin, xout):
    st = ExitStack()
    xn = ctx.sbuf("l0a_xn", [128, NCH * T], BF16, st)
    lnw = ln_work(ctx, st, "l0a")
    wstage = ctx.sbuf("l0a_wstage", [128, 2 * D], F32, st)
    ws_qk = WStream(ctx, st, "l0a_qk", 128, stage=wstage)
    ws_v = WStream(ctx, st, "l0a_v", D, stage=wstage)
    ws_o = WStream(ctx, st, "l0a_o", D, stage=wstage)
    qb_sb = ctx.sbuf("l0a_qb_s", [128, NCH], F32, st)
    kb_sb = ctx.sbuf("l0a_kb_s", [128, NCH], F32, st)
    ob_sb = ctx.sbuf("l0a_ob_s", [128, NCH], F32, st)
    vb_bc = ctx.sbuf("l0a_vbbc", [128, D], F32, st)
    mask = ctx.sbuf("l0a_mask", [128, 128], F32, st)
    qf = ctx.sbuf("l0a_qf", [128, T], BF16, st)
    kf = ctx.sbuf("l0a_kf", [128, T], BF16, st)
    vtm = ctx.sbuf("l0a_vtm", [128, 16 * D], BF16, st)
    acat = ctx.sbuf("l0a_acat", [128, NCH * T], BF16, st)
    s_sb = lnw["xf"][:, 0:2 * T]
    p_sb = ctx.sbuf("l0a_p", [128, 4 * T], BF16, st)
    pt_sb = ctx.sbuf("l0a_pt", [128, 512], BF16, st)
    stat = ctx.sbuf("l0a_stat", [128, 16], F32, st)
    banks = [ctx.psum(f"l0a_ps{i}", [128, 512], F32, st) for i in range(4)]
    ps_t = ctx.psum("l0a_pst", [128, 1024], BF16, st)
    ps_pv = ctx.psum("l0a_pspv", [128, 512], F32, st)

    W = ctx.wins
    for dst, key in ((qb_sb, "l0a_qb"), (kb_sb, "l0a_kb"), (ob_sb, "l0a_ob"),
                     (mask, "mask_l0a")):
        ctx.dma("q", dst[:], W[key][:, :])
    ctx.dma("q", vb_bc[:], W["l0a_vb"][0:1, :].broadcast_to([128, D]))
    ws_v.load(W["l0a_vw"])
    ws_o.load(W["l0a_ow"])
    ctx.barrier()

    for sq in range(SEQ):
        tok0 = sq * T
        emit_ln(ctx, cst, lnw, xin, xn, T, tok0=tok0, banks=banks)
        ctx.barrier()
        if sq == 0:
            ctx.dbg("xn", xn[:])

        ev = {}
        for kt in range(16):
            for half in range(2):
                i = kt * 2 + half
                ps = banks[i % 4]
                if i >= 4:
                    ctx.t.wait(ctx.vec, ev[i - 4])
                for c in range(NCH):
                    ctx.t.matmul(ps[:, :],
                                 xn[:, c * T + kt * 128:c * T + (kt + 1) * 128],
                                 ws_v.bf[:, c * D + half * 512:c * D + half * 512 + 512],
                                 start=(c == 0), stop=(c == NCH - 1))
                pm = ctx.t.mark(ctx.pe)
                ctx.v.wait(ctx.pe, pm)
                ctx.v.tensor_tensor(
                    vtm[:, kt * D + half * 512:kt * D + half * 512 + 512],
                    ps[:, :], vb_bc[:, half * 512:half * 512 + 512], ALU.add)
                ev[i] = ctx.v.mark(ctx.vec)
        ctx.barrier()
        if sq == 0:
            ctx.dbg("vtm", vtm[:])

        for g in range(8):
            for wb_is_q in (True, False):
                wdram = W["l0a_qw"] if wb_is_q else W["l0a_kw"]
                dst = qf if wb_is_q else kf
                bias = qb_sb if wb_is_q else kb_sb
                lv = ctx.dma("q", ws_qk.stage[:, 0:NCH * 128],
                             col_block_ap(wdram, D, g * 128, 128))
                ctx.g.wait(*lv)
                for c in range(NCH):
                    ctx.g.tensor_copy(ws_qk.bf[:, c * 128:(c + 1) * 128],
                                      ws_qk.stage[:, c * 128:(c + 1) * 128])
                gm = ctx.g.mark(ctx.gp)
                ctx.t.wait(ctx.gp, gm)
                for spn in range(4):
                    ps = banks[spn]
                    for c in range(NCH):
                        ctx.t.matmul(ps[:, :], ws_qk.bf[:, c * 128:(c + 1) * 128],
                                     xn[:, c * T + spn * 512:c * T + spn * 512 + 512],
                                     start=(c == 0), stop=(c == NCH - 1))
                    pm = ctx.t.mark(ctx.pe)
                    ctx.s.wait(ctx.pe, pm)
                    ctx.s.activation(dst[:, spn * 512:spn * 512 + 512], ps[:, :],
                                     AF.Identity, bias=bias[:, g:g + 1])
                    ctx.s.mark(ctx.act)
                ctx.barrier()

            for hh in range(2):
                h = 2 * g + hh

                def acat_cb(qs, pvm, h=h):
                    ctx.s.wait(ctx.pe, pvm)
                    ctx.s.copy(acat[(h % 2) * 64:(h % 2) * 64 + 64,
                                    (h // 2) * T + qs * 512:(h // 2) * T + qs * 512 + 512],
                               ps_pv[0:64, :])
                    sm = ctx.s.mark(ctx.act)
                    ctx.t.wait(ctx.act, sm)

                emit_softmax_attn_head(
                    ctx, cst, qf, kf, vtm, s_sb, p_sb, pt_sb, stat,
                    banks, ps_t, ps_pv, mask, r0=hh * 64, scale=0.125, dv=64,
                    vcol=lambda kt, h=h: (kt * D + h * 64, 64),
                    acat_cb=acat_cb, causal=True,
                    dbg_tag=("h0" if (sq == 0 and g == 0 and hh == 0) else None))
                if sq == 0 and g == 0 and hh == 0:
                    ctx.dbg("p0", p_sb[:])
            ctx.barrier()
            if sq == 0 and g == 0:
                ctx.dbg("qf", qf[:])
                ctx.dbg("kf", kf[:])
                ctx.dbg("acat0", acat[:, 0:T])

        def oproj(dc, spn, ps):
            for c in range(NCH):
                ctx.t.matmul(ps[:, :], ws_o.tile(c, dc * 128),
                             acat[:, c * T + spn * 512:c * T + spn * 512 + 512],
                             start=(c == 0), stop=(c == NCH - 1))

        emit_residual_out(ctx, cst, lnw, xin, xout, tok0, T, oproj, ob_sb,
                          banks)
        ctx.barrier()
    st.close()



def emit_l1a(ctx, cst, xin, xout):
    st = ExitStack()
    xn = ctx.sbuf("l1a_xn", [128, NCH * T], BF16, st)
    lnw = ln_work(ctx, st, "l1a")
    wstage = ctx.sbuf("l1a_wstage", [128, 2 * D], F32, st)
    ws_qk = WStream(ctx, st, "l1a_qk", 128, stage=wstage)
    ws_v = WStream(ctx, st, "l1a_vo", D, stage=wstage)
    ws_o = ws_v
    qb_sb = ctx.sbuf("l1a_qb_s", [128, NCH], F32, st)
    kb_sb = ctx.sbuf("l1a_kb_s", [128, NCH], F32, st)
    pb_sb = ctx.sbuf("l1a_pb_s", [128, NCH], F32, st)
    vb_bc = ctx.sbuf("l1a_vbbc", [128, D], F32, st)
    qf = ctx.sbuf("l1a_qf", [128, T], BF16, st)
    kf = ctx.sbuf("l1a_kf", [128, T], BF16, st)
    vtm = ctx.sbuf("l1a_vtm", [128, 16 * D], BF16, st)
    acat = ctx.sbuf("l1a_acat", [128, NCH * T], BF16, st)
    sp_all = ctx.sbuf("l1a_sp", [128, 17408], BF16, st)
    scr_bf = lnw["xf"][:].bitcast(BF16)
    stat16 = ctx.sbuf("l1a_stat", [128, 96], F32, st)
    pt_sb = ctx.sbuf("l1a_pt", [128, 512], BF16, st)
    banks = [ctx.psum(f"l1a_ps{i}", [128, 512], F32, st) for i in range(4)]
    ps_t = ctx.psum("l1a_pst", [128, 1024], BF16, st)
    ps_pv = ctx.psum("l1a_pspv", [128, 512], F32, st)
    W = ctx.wins

    for dst, key in ((qb_sb, "l1a_qb"), (kb_sb, "l1a_kb"), (pb_sb, "l1a_pb")):
        ctx.dma("q", dst[:], W[key][:, :])
    ctx.dma("q", vb_bc[:], W["l1a_vb"][0:1, :].broadcast_to([128, D]))
    ctx.barrier()

    for sq in range(SEQ):
        tok0 = sq * T
        ws_v.load(W["l1a_w"][:, 2048:3072])
        emit_ln(ctx, cst, lnw, xin, xn, T, tok0=tok0, banks=banks)
        ctx.barrier()

        ev = {}
        for kt in range(16):
            for half in range(2):
                i = kt * 2 + half
                ps = banks[i % 4]
                if i >= 4:
                    ctx.t.wait(ctx.vec, ev[i - 4])
                for c in range(NCH):
                    ctx.t.matmul(ps[:, :],
                                 xn[:, c * T + kt * 128:c * T + (kt + 1) * 128],
                                 ws_v.bf[:, c * D + half * 512:c * D + half * 512 + 512],
                                 start=(c == 0), stop=(c == NCH - 1))
                pm = ctx.t.mark(ctx.pe)
                ctx.v.wait(ctx.pe, pm)
                ctx.v.tensor_tensor(
                    vtm[:, kt * D + half * 512:kt * D + half * 512 + 512],
                    ps[:, :], vb_bc[:, half * 512:half * 512 + 512], ALU.add)
                ev[i] = ctx.v.mark(ctx.vec)
        ctx.barrier()

        for g in range(8):
            for wb_is_q in (True, False):
                col0 = g * 128 if wb_is_q else 1024 + g * 128
                dst = qf if wb_is_q else kf
                bias = qb_sb if wb_is_q else kb_sb
                lv = ctx.dma("q", ws_qk.stage[:, 0:NCH * 128],
                             col_block_ap(W["l1a_w"], 3072, col0, 128))
                ctx.g.wait(*lv)
                for c in range(NCH):
                    ctx.g.tensor_copy(ws_qk.bf[:, c * 128:(c + 1) * 128],
                                      ws_qk.stage[:, c * 128:(c + 1) * 128])
                gm = ctx.g.mark(ctx.gp)
                ctx.t.wait(ctx.gp, gm)
                for spn in range(4):
                    ps = banks[spn]
                    for c in range(NCH):
                        ctx.t.matmul(ps[:, :], ws_qk.bf[:, c * 128:(c + 1) * 128],
                                     xn[:, c * T + spn * 512:c * T + spn * 512 + 512],
                                     start=(c == 0), stop=(c == NCH - 1))
                    pm = ctx.t.mark(ctx.pe)
                    ctx.s.wait(ctx.pe, pm)
                    ctx.s.activation(dst[:, spn * 512:spn * 512 + 512], ps[:, :],
                                     AF.Identity, bias=bias[:, g:g + 1])
                    ctx.s.mark(ctx.act)
                ctx.barrier()

            for hh in range(2):
                h = 2 * g + hh

                def acat_cb(qs, pvm, h=h):
                    ctx.s.wait(ctx.pe, pvm)
                    ctx.s.copy(acat[(h % 2) * 64:(h % 2) * 64 + 64,
                                    (h // 2) * T + qs * 512:(h // 2) * T + qs * 512 + 512],
                               ps_pv[0:64, :])
                    sm = ctx.s.mark(ctx.act)
                    ctx.t.wait(ctx.act, sm)

                emit_sparsemax_attn_head(
                    ctx, cst, qf, kf, vtm, sp_all, stat16, pt_sb,
                    banks, ps_t, ps_pv, None, r0=hh * 64, scale=0.125, dv=64,
                    vcol=lambda kt, h=h: (kt * D + h * 64, 64),
                    acat_cb=acat_cb, scr_bf=scr_bf)
            ctx.barrier()

        ws_o.load(W["l1a_wp"])
        ctx.barrier()

        def oproj(dc, spn, ps):
            for c in range(NCH):
                ctx.t.matmul(ps[:, :], ws_o.tile(c, dc * 128),
                             acat[:, c * T + spn * 512:c * T + spn * 512 + 512],
                             start=(c == 0), stop=(c == NCH - 1))

        emit_residual_out(ctx, cst, lnw, xin, xout, tok0, T, oproj, pb_sb,
                          banks)
        ctx.barrier()
    st.close()



def emit_l3a(ctx, cst, xin, xout):
    st = ExitStack()
    xn = ctx.sbuf("l3a_xn", [128, NCH * T], BF16, st)
    lnw = ln_work(ctx, st, "l3a")
    wstage = ctx.sbuf("l3a_wstage", [128, 2 * D], F32, st)
    ws_qk = WStream(ctx, st, "l3a_qk", 128, stage=wstage)
    ws_v = WStream(ctx, st, "l3a_vo", D, stage=wstage)
    ws_o = ws_v
    qb_sb = ctx.sbuf("l3a_qb_s", [128, NCH], F32, st)
    kb_sb = ctx.sbuf("l3a_kb_s", [128, NCH], F32, st)
    ob_sb = ctx.sbuf("l3a_ob_s", [128, NCH], F32, st)
    vb_bc = ctx.sbuf("l3a_vbbc", [128, D], F32, st)
    qf = ctx.sbuf("l3a_qf", [128, T], BF16, st)
    kf = ctx.sbuf("l3a_kf", [128, T], BF16, st)
    vtm = ctx.sbuf("l3a_vtm", [128, 16 * D], BF16, st)
    acat = ctx.sbuf("l3a_acat", [128, NCH * T], BF16, st)
    s_sb = lnw["xf"][:, 0:2 * T]
    p_sb = ctx.sbuf("l3a_p", [128, 4 * T], BF16, st)
    pt_sb = ctx.sbuf("l3a_pt", [128, 512], BF16, st)
    stat = ctx.sbuf("l3a_stat", [128, 16], F32, st)
    grepb = ctx.sbuf("l3a_grepb", [128, 4096], BF16, st)
    adiag = ctx.sbuf("l3a_adiag", [128, 128], BF16, st)
    banks = [ctx.psum(f"l3a_ps{i}", [128, 512], F32, st) for i in range(4)]
    ps_t = ctx.psum("l3a_pst", [128, 1024], BF16, st)
    ps_pv = ctx.psum("l3a_pspv", [128, 512], F32, st)
    W = ctx.wins

    for dst, key in ((qb_sb, "l3a_qb"), (kb_sb, "l3a_kb"), (ob_sb, "l3a_ob"),
                     (adiag, "antidiag8")):
        ctx.dma("q", dst[:], W[key][:, :])
    ctx.dma("q", vb_bc[:], W["l3a_vb"][0:1, :].broadcast_to([128, D]))
    ctx.barrier()

    for sq in range(SEQ):
        tok0 = sq * T
        ws_v.load(W["l3a_vw"])
        emit_ln(ctx, cst, lnw, xin, xn, T, tok0=tok0, banks=banks)
        ctx.barrier()

        ev = {}
        for kt in range(16):
            for half in range(2):
                i = kt * 2 + half
                ps = banks[i % 4]
                if i >= 4:
                    ctx.t.wait(ctx.vec, ev[i - 4])
                for c in range(NCH):
                    ctx.t.matmul(ps[:, :],
                                 xn[:, c * T + kt * 128:c * T + (kt + 1) * 128],
                                 ws_v.bf[:, c * D + half * 512:c * D + half * 512 + 512],
                                 start=(c == 0), stop=(c == NCH - 1))
                pm = ctx.t.mark(ctx.pe)
                ctx.v.wait(ctx.pe, pm)
                ctx.v.tensor_tensor(
                    vtm[:, kt * D + half * 512:kt * D + half * 512 + 512],
                    ps[:, :], vb_bc[:, half * 512:half * 512 + 512], ALU.add)
                ev[i] = ctx.v.mark(ctx.vec)
        ctx.barrier()

        for g in range(8):
            for wb_is_q in (True, False):
                wdram = W["l3a_qw"] if wb_is_q else W["l3a_kw"]
                dst = qf if wb_is_q else kf
                bias = qb_sb if wb_is_q else kb_sb
                lv = ctx.dma("q", ws_qk.stage[:, 0:NCH * 128],
                             col_block_ap(wdram, D, g * 128, 128))
                ctx.g.wait(*lv)
                for c in range(NCH):
                    ctx.g.tensor_copy(ws_qk.bf[:, c * 128:(c + 1) * 128],
                                      ws_qk.stage[:, c * 128:(c + 1) * 128])
                gm = ctx.g.mark(ctx.gp)
                ctx.t.wait(ctx.gp, gm)
                for spn in range(4):
                    ps = banks[spn]
                    for c in range(NCH):
                        ctx.t.matmul(ps[:, :], ws_qk.bf[:, c * 128:(c + 1) * 128],
                                     xn[:, c * T + spn * 512:c * T + spn * 512 + 512],
                                     start=(c == 0), stop=(c == NCH - 1))
                    pm = ctx.t.mark(ctx.pe)
                    ctx.s.wait(ctx.pe, pm)
                    ctx.s.activation(dst[:, spn * 512:spn * 512 + 512], ps[:, :],
                                     AF.Identity, bias=bias[:, g:g + 1])
                    ctx.s.mark(ctx.act)
                ctx.barrier()

            for hh in range(2):
                h = 2 * g + hh
                for piece in range(3):
                    p0 = piece * 1536
                    pw = min(1536, 4096 - p0)
                    gext_src = AP(tensor=W["l3a_gext"].tensor,
                                  offset=W["l3a_gext"].offset + h * 4224 + p0,
                                  ap=[[1, 128], [1, pw]])
                    lv = ctx.dma("q", wstage[:, 0:pw], gext_src)
                    ctx.g.wait(*lv)
                    ctx.g.tensor_copy(grepb[:, p0:p0 + pw], wstage[:, 0:pw])
                    gm2 = ctx.g.mark(ctx.gp)
                    ctx.q.wait(ctx.gp, gm2)
                ctx.t.wait(ctx.gp, gm2)

                def bias_mm(ctx, ps, ks, kw, qt):
                    col0 = ks * 512 - qt * 128 + 2047
                    ctx.t.matmul(ps[:, 0:kw], adiag[:], grepb[:, col0:col0 + kw],
                                 start=False, stop=True)

                def acat_cb(qs, pvm, h=h):
                    ctx.s.wait(ctx.pe, pvm)
                    ctx.s.copy(acat[(h % 2) * 64:(h % 2) * 64 + 64,
                                    (h // 2) * T + qs * 512:(h // 2) * T + qs * 512 + 512],
                               ps_pv[0:64, :])
                    sm = ctx.s.mark(ctx.act)
                    ctx.t.wait(ctx.act, sm)

                emit_softmax_attn_head(
                    ctx, cst, qf, kf, vtm, s_sb, p_sb, pt_sb, stat,
                    banks, ps_t, ps_pv, None, r0=hh * 64, scale=0.125, dv=64,
                    vcol=lambda kt, h=h: (kt * D + h * 64, 64),
                    acat_cb=acat_cb, causal=False, bias_mm=bias_mm)
                ctx.barrier()

        ws_o.load(W["l3a_ow"])
        ctx.barrier()

        def oproj(dc, spn, ps):
            for c in range(NCH):
                ctx.t.matmul(ps[:, :], ws_o.tile(c, dc * 128),
                             acat[:, c * T + spn * 512:c * T + spn * 512 + 512],
                             start=(c == 0), stop=(c == NCH - 1))

        emit_residual_out(ctx, cst, lnw, xin, xout, tok0, T, oproj, ob_sb,
                          banks)
        ctx.barrier()
    st.close()



def emit_l3f(ctx, cst, xin, xout):
    st = ExitStack()
    QTOK = 1024
    wtup = ctx.sbuf("l3f_wtup", [128, NCH * 4096], BF16, st)
    wtdn = ctx.sbuf("l3f_wtdn", [128, 32 * D], BF16, st)
    h_bf = ctx.sbuf("l3f_h", [128, 32 * 256], BF16, st)
    xn = ctx.sbuf("l3f_xn", [128, NCH * QTOK], BF16, st)
    wstage = h_bf[:].bitcast(F32)
    sgf = xn[:, 0:4096].bitcast(F32)
    msf = xn[:, 4096:8192].bitcast(F32)
    apart = ctx.sbuf("l3f_apart", [128, 40], F32, st)
    arow = ctx.sbuf("l3f_arow", [1, 16], F32, st)
    thr = ctx.sbuf("l3f_thr", [128, 1], F32, st)
    thr_d = ctx.dram("l3f_thr_d", [1, 1], F32)
    hq = ctx.sbuf("l3f_hq", [128, 512], BF16, st)
    r2rows = ctx.sbuf("l3f_r2", [1, 3 * 256], F32, st)
    r2b = ctx.sbuf("l3f_r2b", [1, 256], BF16, st)
    r2bc = ctx.sbuf("l3f_r2bc", [128, 256], BF16, st)
    otmp = ctx.sbuf("l3f_otmp", [128, 2 * 256], F32, st)
    banks = [ctx.psum(f"l3f_ps{i}", [128, 512], F32, st) for i in range(6)]
    W = ctx.wins

    for wi, (wdram, wdst, nch, cols) in enumerate(
            ((W["l3f_wu"], wtup, NCH, 4096), (W["l3f_wd"], wtdn, 32, D))):
        pw = min(cols, 2048)
        npiece = nch * cols // pw
        per_chunk = cols // pw
        for p in range(npiece):
            c, sub = divmod(p, per_chunk)
            half = (p % 2) * 2048
            if p >= 2:
                ctx.q.wait(ctx.vec, vparts[p - 2])
            lv = ctx.dma("q", wstage[:, half:half + pw],
                         wdram[c * 128:(c + 1) * 128, sub * pw:sub * pw + pw])
            ctx.v.wait(*lv)
            ctx.v.tensor_reduce(apart[:, p:p + 1],
                                wstage[:, half:half + pw], AX.X, ALU.add,
                                apply_absolute_value=True)
            if p == 0:
                vparts = {}
            ctx.v.fence()
            vparts[p] = ctx.v.mark(ctx.vec)
        ctx.v.fence()
        ctx.v.reduce_sum(apart[:, 39:40], apart[:, 0:npiece], axis=AX.X)
        ctx.v.fence()
        vm = ctx.v.mark(ctx.vec)
        ctx.t.wait(ctx.vec, vm)
        ctx.t.matmul(banks[0][0:1, 0:128], apart[:, 39:40], cst["ident_f32"][:],
                     is_transpose=True, start=True, stop=True)
        pm = ctx.t.mark(ctx.pe)
        ctx.v.wait(ctx.pe, pm)
        ctx.v.reduce_sum(arow[0:1, 1:2], banks[0][0:1, 0:128], axis=AX.X)
        ctx.v.fence()
        ctx.v.tensor_scalar_mul(arow[0:1, 0:1], arow[0:1, 1:2],
                                0.5 / float(nch * 128 * cols))
        ctx.v.fence()
        vm2 = ctx.v.mark(ctx.vec)
        ctx.s.wait(ctx.vec, vm2)
        sv = ctx.dma("s", thr_d[0:1, :], arow[0:1, 0:1])
        ctx.q.wait(*sv)
        lv = ctx.dma("q", thr[:], thr_d[0:1, :].broadcast_to([128, 1]))
        ctx.v.wait(*lv)
        ctx.s.wait(*lv)
        gmarks = {}
        for p in range(npiece):
            c, sub = divmod(p, per_chunk)
            half = (p % 2) * 2048
            if p >= 2:
                ctx.q.wait(ctx.gp, gmarks[p - 2])
            lv = ctx.dma("q", wstage[:, half:half + pw],
                         wdram[c * 128:(c + 1) * 128, sub * pw:sub * pw + pw])
            ctx.s.wait(*lv)
            ctx.s.sign(sgf[:, 0:pw], wstage[:, half:half + pw])
            ctx.s.activation(msf[:, 0:pw], wstage[:, half:half + pw], AF.Abs)
            am = ctx.s.mark(ctx.act)
            ctx.v.wait(ctx.act, am)
            ctx.v.tensor_scalar(msf[:, 0:pw], msf[:, 0:pw], thr[:], None,
                                ALU.is_gt)
            vm3 = ctx.v.mark(ctx.vec)
            ctx.g.wait(ctx.act, am)
            ctx.g.wait(ctx.vec, vm3)
            ctx.g.tensor_tensor(wdst[:, c * cols + sub * pw:
                                     c * cols + sub * pw + pw],
                                sgf[:, 0:pw], msf[:, 0:pw], ALU.mult)
            gmarks[p] = ctx.g.mark(ctx.gp)
            ctx.s.wait(ctx.gp, gmarks[p])
            ctx.v.wait(ctx.gp, gmarks[p])
        ctx.barrier()

    def l3f_scale(ctx, w, mu, var, rstd):
        r = w["rows"]
        b_row = r[0:1, 512:1024]
        ctx.s.fence()
        ctx.s.activation(b_row, var, AF.Ln)
        ctx.s.fence()
        ctx.s.activation(b_row, b_row, AF.Exp, scale=0.5)
        am = ctx.s.mark(ctx.act)
        ctx.v.wait(ctx.act, am)
        ctx.v.tensor_tensor(b_row, b_row, rstd, ALU.mult)
        ctx.v.fence()
        ctx.v.tensor_scalar_add(b_row, b_row, 1e-8)
        ctx.v.fence()
        ctx.v.reciprocal(b_row, b_row)
        ctx.v.fence()
        ctx.v.tensor_tensor(rstd, rstd, b_row, ALU.mult)
        ctx.v.fence()
        vm = ctx.v.mark(ctx.vec)
        ctx.s.wait(ctx.vec, vm)

    lnw = ln_work(ctx, st, "l3f_lnw")
    for qq in range(4):
        tq0 = qq * QTOK
        emit_ln(ctx, cst, lnw, xin, xn, QTOK, tok0=tq0, banks=banks[0:4],
                scale_cb=l3f_scale)
        ctx.barrier()

        for sp in range(QTOK // 256):
            t0 = tq0 + sp * 256
            hev = {}
            for j in range(32):
                b = j % 2
                if j >= 2:
                    ctx.t.wait(ctx.act, hev[j - 2])
                for c in range(NCH):
                    ctx.t.matmul(banks[b][:, 0:256],
                                 wtup[:, c * 4096 + j * 128:c * 4096 + (j + 1) * 128],
                                 xn[:, c * QTOK + sp * 256:c * QTOK + sp * 256 + 256],
                                 start=(c == 0), stop=(c == NCH - 1))
                pm = ctx.t.mark(ctx.pe)
                ctx.s.wait(ctx.pe, pm)
                ctx.s.activation(h_bf[:, j * 256:(j + 1) * 256], banks[b][:, 0:256],
                                 AF.Gelu)
                hev[j] = ctx.s.mark(ctx.act)
            s2 = r2rows[0:1, 0:256]
            l2r = r2rows[0:1, 256:512]
            rr2 = r2rows[0:1, 512:768]
            ctx.v.wait(ctx.act, hev[31])
            for j in range(32):
                hb2 = slice((j % 2) * 256, (j % 2) * 256 + 256)
                if j >= 2:
                    ctx.v.wait(ctx.pe, c2[j - 2])
                ctx.v.tensor_tensor(hq[:, hb2], h_bf[:, j * 256:(j + 1) * 256],
                                    h_bf[:, j * 256:(j + 1) * 256], ALU.mult)
                vm = ctx.v.mark(ctx.vec)
                ctx.t.wait(ctx.vec, vm)
                ctx.t.matmul(banks[2][0:1, 0:256], cst["ones_col"][:], hq[:, hb2],
                             start=(j == 0), stop=(j == 31))
                if j == 0:
                    c2 = {}
                c2[j] = ctx.t.mark(ctx.pe)
            ctx.s.wait(ctx.pe, c2[31])
            ctx.s.activation(l2r, banks[2][0:1, 0:256], AF.Ln, scale=1.0 / 4096.0)
            ctx.s.fence()
            ctx.s.activation(s2, l2r, AF.Exp, scale=0.5)
            ctx.s.fence()
            ctx.s.add(rr2, s2, 1e-8)
            am = ctx.s.mark(ctx.act)
            ctx.v.wait(ctx.act, am)
            ctx.v.reciprocal(l2r, rr2)
            ctx.v.fence()
            ctx.v.tensor_copy(r2b[0:1, :], l2r)
            vm2 = ctx.v.mark(ctx.vec)
            ctx.t.wait(ctx.vec, vm2)
            ctx.t.matmul(banks[3][:, 0:256], cst["ones_row"][:], r2b[0:1, :],
                         start=True, stop=True)
            pm2 = ctx.t.mark(ctx.pe)
            ctx.v.wait(ctx.pe, pm2)
            ctx.v.tensor_copy(r2bc[:], banks[3][:, 0:256])
            ctx.v.fence()
            for j in range(32):
                ctx.v.tensor_tensor(h_bf[:, j * 256:(j + 1) * 256],
                                    h_bf[:, j * 256:(j + 1) * 256], r2bc[:], ALU.mult)
            vm3 = ctx.v.mark(ctx.vec)
            ctx.t.wait(ctx.vec, vm3)
            lvx = ctx.dma("q", lnw["xf"][:, 0:NCH * 256], x_span_ap(xin, t0, w=256))
            for dc in range(NCH):
                gi = sp * NCH + dc
                bb = banks[4 + dc % 2]
                if gi >= 2:
                    ctx.t.wait(ctx.vec, l3f_oev[gi - 2])
                for j in range(32):
                    ctx.t.matmul(bb[:, 0:256],
                                 wtdn[:, j * D + dc * 128:j * D + dc * 128 + 128],
                                 h_bf[:, j * 256:(j + 1) * 256],
                                 start=(j == 0), stop=(j == 31))
                pm3 = ctx.t.mark(ctx.pe)
                ctx.v.wait(ctx.pe, pm3)
                ctx.v.wait(*lvx)
                if gi >= 2:
                    ctx.v.wait(*l3f_sts[gi - 2])
                ot = otmp[:, (gi % 2) * 256:(gi % 2) * 256 + 256]
                ctx.v.tensor_tensor(ot, bb[:, 0:256],
                                    lnw["xf"][:, dc * 256:dc * 256 + 256], ALU.add)
                if gi == 0:
                    l3f_oev = {}
                    l3f_sts = {}
                l3f_oev[gi] = ctx.v.mark(ctx.vec)
                ctx.s.wait(ctx.vec, l3f_oev[gi])
                l3f_sts[gi] = ctx.dma("s", x_chunk_span_ap(xout, dc, t0, w=256), ot)
            ctx.q.wait(ctx.vec, l3f_oev[sp * NCH + NCH - 1])
        ctx.barrier()
    st.close()



LAMBDA_INIT = 0.8 - 0.6 * math.exp(-0.3 * 12)

def emit_l2a(ctx, cst, xin, xout):
    st = ExitStack()
    xn = ctx.sbuf("l2a_xn", [128, NCH * T], BF16, st)
    lnw = ln_work(ctx, st, "l2a")
    wstage = ctx.sbuf("l2a_wstage", [128, 2 * D], F32, st)
    ws_q = WStream(ctx, st, "l2a_wq", D, stage=wstage)
    ws_k = WStream(ctx, st, "l2a_wk", D, stage=wstage)
    ws_v = WStream(ctx, st, "l2a_wv", D, stage=wstage)
    ws_o = ws_q
    cosb = ctx.sbuf("l2a_cos_s", [128, 1024], F32, st)
    sab = ctx.sbuf("l2a_sa_s", [128, 1024], F32, st)
    sbb = ctx.sbuf("l2a_sb_s", [128, 1024], F32, st)
    lamrow = ctx.sbuf("l2a_lam", [1, 64 * 4 + 16], F32, st)
    lam_bc = ctx.sbuf("l2a_lambc", [128, 1], F32, st)
    lam_d = ctx.dram("l2a_lam_d", [1, 1], F32)
    lnwp = ctx.sbuf("l2a_lnwp", [128, 128], F32, st)
    lnbp = ctx.sbuf("l2a_lnbp", [128, 128], F32, st)
    qtf = ctx.sbuf("l2a_qtf", [128, 1024], F32, st)
    ktf = ctx.sbuf("l2a_ktf", [128, 1024], F32, st)
    vtf = ctx.sbuf("l2a_vtf", [128, 1024], F32, st)
    rsc = ctx.sbuf("l2a_rsc", [128, 1024], F32, st)
    dots = ctx.sbuf("l2a_dots", [128, 128], F32, st)
    dscr = ctx.sbuf("l2a_dscr", [128, 4 * 64], F32, st)
    probs = ctx.sbuf("l2a_probs", [128, 128], F32, st)
    stat8 = ctx.sbuf("l2a_stat8", [128, 64], F32, st)
    mix = ctx.sbuf("l2a_mix", [128, 2 * 128], F32, st)
    a_f = ctx.sbuf("l2a_af", [128, 1024], F32, st)
    a_bf = ctx.sbuf("l2a_abf", [128, 1024], BF16, st)
    acat = ctx.sbuf("l2a_acat", [128, NCH * T], BF16, st)
    banks = [ctx.psum(f"l2a_ps{i}", [128, 512], F32, st) for i in range(4)]
    ps_t = ctx.psum("l2a_pst", [128, 1024], BF16, st)
    W = ctx.wins

    for dst, key in ((cosb, "l2a_cos"), (sab, "l2a_sa"), (sbb, "l2a_sb")):
        ctx.dma("q", dst[:], W[key][:, :])
    for i, key in enumerate(("l2a_lambda_q1", "l2a_lambda_k1",
                             "l2a_lambda_q2", "l2a_lambda_k2")):
        ctx.dma("q", lamrow[0:1, i * 64:(i + 1) * 64], W[key][0:1, :])
    ctx.dma("q", lnwp[:], W["l2a_lnw"][0:1, :].broadcast_to([128, 128]))
    ctx.dma("q", lnbp[:], W["l2a_lnb"][0:1, :].broadcast_to([128, 128]))
    ctx.barrier()
    c0 = lamrow[0:1, 256:257]
    c1 = lamrow[0:1, 257:258]
    ctx.v.tensor_tensor(lamrow[0:1, 0:64], lamrow[0:1, 0:64],
                        lamrow[0:1, 64:128], ALU.mult)
    ctx.v.tensor_tensor(lamrow[0:1, 128:192], lamrow[0:1, 128:192],
                        lamrow[0:1, 192:256], ALU.mult)
    ctx.v.fence()
    ctx.v.reduce_sum(c0, lamrow[0:1, 0:64], axis=AX.X)
    ctx.v.reduce_sum(c1, lamrow[0:1, 128:192], axis=AX.X)
    ctx.v.fence()
    vm = ctx.v.mark(ctx.vec)
    ctx.s.wait(ctx.vec, vm)
    ctx.s.activation(c0, c0, AF.Exp)
    ctx.s.fence()
    ctx.s.activation(c1, c1, AF.Exp)
    am = ctx.s.mark(ctx.act)
    ctx.v.wait(ctx.act, am)
    ctx.v.tensor_tensor(c1, c1, c0, ALU.subtract)
    ctx.v.fence()
    ctx.v.tensor_scalar_add(c1, c1, -LAMBDA_INIT)
    ctx.v.fence()
    vm2 = ctx.v.mark(ctx.vec)
    ctx.s.wait(ctx.vec, vm2)
    sv = ctx.dma("s", lam_d[0:1, :], c1)
    ctx.q.wait(*sv)
    ctx.dma("q", lam_bc[:], lam_d[0:1, :].broadcast_to([128, 1]))
    ctx.v.tensor_scalar_mul(lnwp[:], lnwp[:], 1.0 - LAMBDA_INIT)
    ctx.v.tensor_scalar_mul(lnbp[:], lnbp[:], 1.0 - LAMBDA_INIT)
    ctx.barrier()

    for sq in range(SEQ):
        tok0 = sq * T
        ws_q.load(W["l2a_wq"])
        ws_k.load(W["l2a_wk"])
        ws_v.load(W["l2a_wv"])
        emit_ln(ctx, cst, lnw, xin, xn, T, tok0=tok0, banks=banks)
        ctx.barrier()

        for kt in range(16):
            for which, ws, dstf in ((0, ws_q, qtf), (1, ws_k, ktf), (2, ws_v, vtf)):
                for half in range(2):
                    ps = banks[half]
                    for c in range(NCH):
                        ctx.t.matmul(ps[:, :],
                                     xn[:, c * T + kt * 128:c * T + (kt + 1) * 128],
                                     ws.bf[:, c * D + half * 512:c * D + half * 512 + 512],
                                     start=(c == 0), stop=(c == NCH - 1))
                    pm = ctx.t.mark(ctx.pe)
                    ctx.s.wait(ctx.pe, pm)
                    ctx.s.copy(dstf[:, half * 512:half * 512 + 512], ps[:, :])
                    am = ctx.s.mark(ctx.act)
                    ctx.t.wait(ctx.act, am)
            am_qkv = ctx.s.mark(ctx.act)
            tsl = slice(kt * 64, (kt + 1) * 64)
            hrep = lambda tab: AP(tensor=tab, offset=kt * 64,
                                  ap=[[1024, 128], [0, 16], [1, 64]])
            ctx.v.wait(ctx.act, am_qkv)
            for tf in (qtf, ktf):
                ctx.v.tensor_tensor(rsc[:], tf[:], hrep(cosb), ALU.mult)
                even = AP(tensor=tf, offset=1,
                          ap=[[1024, 128], [64, 16], [2, 32]])
                odd0 = AP(tensor=tf, offset=0,
                          ap=[[1024, 128], [64, 16], [2, 32]])
                sae = AP(tensor=sab, offset=kt * 64,
                         ap=[[1024, 128], [0, 16], [2, 32]])
                sbo = AP(tensor=sbb, offset=kt * 64 + 1,
                         ap=[[1024, 128], [0, 16], [2, 32]])
                dste = AP(tensor=rsc, offset=1024 * 0,
                          ap=[[1024, 128], [64, 16], [2, 32]])
                scr2e = AP(tensor=a_f, offset=0,
                           ap=[[1024, 128], [64, 16], [2, 32]])
                scr2o = AP(tensor=a_f, offset=1,
                           ap=[[1024, 128], [64, 16], [2, 32]])
                ctx.v.tensor_tensor(scr2e, even, sae, ALU.mult)
                ctx.v.tensor_tensor(scr2o, odd0, sbo, ALU.mult)
                ctx.v.fence()
                ctx.v.tensor_tensor(tf[:], rsc[:], a_f[:], ALU.add)
                ctx.v.fence()
            vrot = ctx.v.mark(ctx.vec)
            ctx.v.wait(ctx.vec, vrot)
            nops = 0
            for g2 in range(2):
                for n in range(8):
                    for m in range(8):
                        db = dscr[:, (nops % 4) * 64:(nops % 4) * 64 + 64]
                        if nops >= 4 and nops % 4 == 0:
                            ctx.v.fence()
                        ctx.v.scalar_tensor_tensor(
                            db, qtf[:, n * 128 + g2 * 64:n * 128 + g2 * 64 + 64],
                            0.35355339059327373,
                            ktf[:, m * 128 + g2 * 64:m * 128 + g2 * 64 + 64],
                            ALU.mult, ALU.mult,
                            accum_out=dots[:, g2 * 64 + n * 8 + m:g2 * 64 + n * 8 + m + 1])
                        nops += 1
            ctx.v.fence()
            ctx.v.tensor_reduce(stat8[:, 0:16], dots[:].rearrange("p (s m) -> p s m", m=8),
                                AX.X, ALU.max, negate=True)
            ctx.v.fence()
            seg = AP(tensor=stat8, offset=0,
                     ap=[[64, 128], [1, 16], [0, 8]])
            ctx.v.tensor_tensor(probs[:], dots[:], seg, ALU.add)
            vmx = ctx.v.mark(ctx.vec)
            ctx.s.wait(ctx.vec, vmx)
            ctx.s.activation(probs[:], probs[:], AF.Exp)
            am2 = ctx.s.mark(ctx.act)
            ctx.v.wait(ctx.act, am2)
            ctx.v.tensor_reduce(stat8[:, 16:32], probs[:].rearrange("p (s m) -> p s m", m=8),
                                AX.X, ALU.add)
            ctx.v.fence()
            ctx.v.reciprocal(stat8[:, 32:48], stat8[:, 16:32])
            ctx.v.fence()
            seg2 = AP(tensor=stat8, offset=32,
                      ap=[[64, 128], [1, 16], [0, 8]])
            ctx.v.tensor_tensor(probs[:], probs[:], seg2, ALU.mult)
            ctx.v.fence()
            for g2 in range(2):
                for n in range(8):
                    for m in range(8):
                        pcell = probs[:, g2 * 64 + n * 8 + m:g2 * 64 + n * 8 + m + 1]
                        if m == 0:
                            ctx.v.tensor_scalar(mix[:, 0:128], vtf[:, m * 128:(m + 1) * 128],
                                                pcell, None, ALU.mult)
                        else:
                            ctx.v.scalar_tensor_tensor(
                                mix[:, (m % 2) * 128:(m % 2) * 128 + 128],
                                vtf[:, m * 128:(m + 1) * 128], pcell,
                                mix[:, ((m - 1) % 2) * 128:((m - 1) % 2) * 128 + 128],
                                ALU.mult, ALU.add)
                        ctx.v.fence()
                    mres = mix[:, ((7) % 2) * 128:((7) % 2) * 128 + 128]
                    if g2 == 0:
                        ctx.v.tensor_copy(a_f[:, n * 128:(n + 1) * 128], mres)
                    else:
                        ctx.v.scalar_tensor_tensor(
                            a_f[:, n * 128:(n + 1) * 128], mres, lam_bc[:],
                            a_f[:, n * 128:(n + 1) * 128], ALU.mult, ALU.add)
                    ctx.v.fence()
            ctx.v.tensor_tensor(rsc[:], a_f[:], a_f[:], ALU.mult)
            ctx.v.fence()
            ctx.v.tensor_reduce(stat8[:, 48:56], rsc[:].rearrange("p (s m) -> p s m", m=128),
                                AX.X, ALU.add)
            ctx.v.fence()
            vms = ctx.v.mark(ctx.vec)
            ctx.s.wait(ctx.vec, vms)
            ctx.s.activation(stat8[:, 56:64], stat8[:, 48:56], AF.Ln,
                             scale=1.0 / 128.0, bias=1e-8)
            ctx.s.fence()
            ctx.s.activation(stat8[:, 48:56], stat8[:, 56:64], AF.Exp, scale=-0.5)
            am3 = ctx.s.mark(ctx.act)
            ctx.v.wait(ctx.act, am3)
            segr = AP(tensor=stat8, offset=48,
                      ap=[[64, 128], [1, 8], [0, 128]])
            lnw_rep = AP(tensor=lnwp, offset=0,
                         ap=[[128, 128], [0, 8], [1, 128]])
            lnb_rep = AP(tensor=lnbp, offset=0,
                         ap=[[128, 128], [0, 8], [1, 128]])
            ctx.v.tensor_tensor(a_f[:], a_f[:], segr, ALU.mult)
            ctx.v.fence()
            ctx.v.tensor_tensor(a_f[:], a_f[:], lnw_rep, ALU.mult)
            ctx.v.fence()
            ctx.v.tensor_tensor(a_bf[:], a_f[:], lnb_rep, ALU.add)
            ctx.v.fence()
            vfin = ctx.v.mark(ctx.vec)
            ctx.t.wait(ctx.vec, vfin)
            for c in range(NCH):
                ctx.t.matmul(ps_t[:, c * 128:(c + 1) * 128],
                             a_bf[:, c * 128:(c + 1) * 128], cst["ident_sb"][:],
                             is_transpose=True, start=(c == 0), stop=(c == NCH - 1))
            tm = ctx.t.mark(ctx.pe)
            ctx.v.wait(ctx.pe, tm)
            out_ap = AP(tensor=acat, offset=kt * 128,
                        ap=[[NCH * T, 128], [T, NCH], [1, 128]])
            ctx.v.tensor_copy(out_ap, ps_t[:])
            vtr = ctx.v.mark(ctx.vec)
            ctx.t.wait(ctx.vec, vtr)
            ctx.s.wait(ctx.vec, vtr)
        ctx.barrier()

        ws_o.load(W["l2a_wo"])
        ctx.barrier()

        def oproj(dc, spn, ps):
            for c in range(NCH):
                ctx.t.matmul(ps[:, :], ws_o.tile(c, dc * 128),
                             acat[:, c * T + spn * 512:c * T + spn * 512 + 512],
                             start=(c == 0), stop=(c == NCH - 1))

        emit_residual_out(ctx, cst, lnw, xin, xout, tok0, T, oproj, None, banks)
        ctx.barrier()
    st.close()



def emit_l2f(ctx, cst, xin, xout):
    st = ExitStack()
    QTOK = 1024
    lnw = ln_work(ctx, st, "l2f")
    xn = ctx.sbuf("l2f_xn", [128, NCH * QTOK], BF16, st)
    we1b = ctx.sbuf("l2f_we1b", [128, NCH * 2048], BF16, st)
    we3b = ctx.sbuf("l2f_we3b", [128, NCH * 2048], BF16, st)
    we2b = ctx.sbuf("l2f_we2b", [128, 2 * D], BF16, st)
    wstage = ctx.sbuf("l2f_wst", [128, 2 * 2048], F32, st)
    wgb = ctx.sbuf("l2f_wgb", [128, NCH * 8], BF16, st)
    out_acc = ctx.sbuf("l2f_oacc", [128, NCH * QTOK], F32, st)
    h_bf = ctx.sbuf("l2f_h", [128, 16 * 256], BF16, st)
    s_bf = ctx.sbuf("l2f_sbf", [128, 2 * 256], BF16, st)
    t_bf = ctx.sbuf("l2f_tbf", [128, 2 * 256], BF16, st)
    wb_bc = ctx.sbuf("l2f_wbbc", [128, 256], BF16, st)
    gate = ctx.sbuf("l2f_gate", [128, 64], F32, st)
    wq_tm = ctx.sbuf("l2f_wqtm", [128, 8 * 8], F32, st)
    wsT = ctx.sbuf("l2f_wsT", [8, 1024], F32, st)
    werow = ctx.sbuf("l2f_werow", [1, QTOK], F32, st)
    werow_b = ctx.sbuf("l2f_werowb", [1, QTOK], BF16, st)
    w_dram = ctx.dram("l2f_w_d", [8, QTOK], F32)
    banks = [ctx.psum(f"l2f_ps{i}", [128, 512], F32, st) for i in range(8)]
    W = ctx.wins

    ws_g = WStream(ctx, st, "l2f_wg", 8, stage=wstage)
    ws_g.bf = wgb
    ws_g.load(W["l2f_wgw"])
    ctx.barrier()

    for qq in range(4):
        tq0 = qq * QTOK
        emit_ln(ctx, cst, lnw, xin, xn, QTOK, tok0=tq0, banks=banks[0:4])
        ctx.barrier()

        m1 = gate[:, 8:9]; m2 = gate[:, 9:10]; ssum = gate[:, 10:11]
        l2c = gate[:, 16:24]; sel = gate[:, 24:32]; lg = gate[:, 32:40]
        sig = gate[:, 40:48]
        for kt in range(QTOK // 128):
            ps = banks[0]
            for c in range(NCH):
                ctx.t.matmul(ps[:, 0:8],
                             xn[:, c * QTOK + kt * 128:c * QTOK + (kt + 1) * 128],
                             wgb[:, c * 8:(c + 1) * 8],
                             start=(c == 0), stop=(c == NCH - 1))
            pm = ctx.t.mark(ctx.pe)
            ctx.v.wait(ctx.pe, pm)
            ctx.v.tensor_copy(lg, ps[:, 0:8])
            ctx.v.fence()
            ctx.v.reduce_max(m1, lg, axis=AX.X)
            ctx.v.fence()
            ctx.v.tensor_scalar(l2c, lg, m1, None, ALU.is_ge)
            ctx.v.fence()
            ctx.v.scalar_tensor_tensor(l2c, l2c, NEG, lg, ALU.mult, ALU.add)
            ctx.v.fence()
            ctx.v.reduce_max(m2, l2c, axis=AX.X)
            ctx.v.fence()
            ctx.v.tensor_scalar(sel, lg, m2, None, ALU.is_ge)
            vm = ctx.v.mark(ctx.vec)
            ctx.s.wait(ctx.vec, vm)
            ctx.s.activation(sig, lg, AF.Sigmoid)
            am = ctx.s.mark(ctx.act)
            ctx.v.wait(ctx.act, am)
            ctx.v.tensor_tensor(sel, sel, sig, ALU.mult)
            ctx.v.fence()
            ctx.v.reduce_sum(ssum, sel, axis=AX.X)
            ctx.v.fence()
            ctx.v.reciprocal(m1, ssum)
            ctx.v.fence()
            ctx.v.tensor_scalar(wq_tm[:, kt * 8:(kt + 1) * 8], sel, m1, None,
                                ALU.mult)
            ctx.v.fence()
            vm2 = ctx.v.mark(ctx.vec)
            ctx.t.wait(ctx.vec, vm2)
            ctx.t.matmul(banks[1][0:8, kt * 128 % 512:kt * 128 % 512 + 128],
                         wq_tm[:, kt * 8:(kt + 1) * 8], cst["ident_f32"][:],
                         is_transpose=True, start=True, stop=True)
            tm = ctx.t.mark(ctx.pe)
            ctx.v.wait(ctx.pe, tm)
            ctx.v.tensor_copy(wsT[0:8, kt * 128:(kt + 1) * 128],
                              banks[1][0:8, kt * 128 % 512:kt * 128 % 512 + 128])
            vtr = ctx.v.mark(ctx.vec)
            ctx.t.wait(ctx.vec, vtr)
        vfin = ctx.v.mark(ctx.vec)
        ctx.s.wait(ctx.vec, vfin)
        sv = ctx.dma("s", w_dram[:, :], wsT[0:8, :])
        ctx.q.wait(*sv)
        ctx.barrier()

        l2f_oev = 0
        wsg = 0
        l2f_w2last = 0
        for e in range(9):
            shared = (e == 8)
            if not shared:
                lv = ctx.dma("q", werow[0:1, :], w_dram[e:e + 1, :])
                ctx.v.wait(*lv)
                ctx.v.tensor_copy(werow_b[0:1, :], werow[0:1, :])
                ctx.v.fence()
                wem = ctx.v.mark(ctx.vec)
                ctx.t.wait(ctx.vec, wem)
            if shared:
                w1d, w3d, w2d = W["l2f_ws1"], W["l2f_ws3"], W["l2f_ws2"]
            else:
                w1d = W["l2f_we1"][e * D:(e + 1) * D, :]
                w3d = W["l2f_we3"][e * D:(e + 1) * D, :]
                w2d = W["l2f_we2"][e * 2048:(e + 1) * 2048, :]
            gm1 = {}
            for c in range(NCH):
                half = (c % 2) * 2048
                if c >= 2:
                    ctx.q.wait(ctx.gp, gm1[c - 2])
                else:
                    ctx.q.wait(ctx.gp, wsg)
                lv = ctx.dma("q", wstage[:, half:half + 2048],
                             w1d[c * 128:(c + 1) * 128, :])
                ctx.g.wait(*lv)
                ctx.g.tensor_copy(we1b[:, c * 2048:(c + 1) * 2048],
                                  wstage[:, half:half + 2048])
                gm1[c] = ctx.g.mark(ctx.gp)
            for c in range(NCH):
                half = (c % 2) * 2048
                ctx.q.wait(ctx.gp, gm1[c + 6] if c >= 2 else gm1[NCH - 1])
                lv = ctx.dma("q", wstage[:, half:half + 2048],
                             w3d[c * 128:(c + 1) * 128, :])
                ctx.g.wait(*lv)
                ctx.g.tensor_copy(we3b[:, c * 2048:(c + 1) * 2048],
                                  wstage[:, half:half + 2048])
                gm1[c + 8] = ctx.g.mark(ctx.gp)
            gw = gm1[15]
            wsg = gw
            ctx.t.wait(ctx.gp, gw)

            for sp in range(QTOK // 256):
                t0l = sp * 256
                if not shared:
                    ctx.t.wait(ctx.vec, l2f_oev)
                    ctx.t.matmul(banks[4][:, 0:256], cst["ones_row"][:],
                                 werow_b[0:1, t0l:t0l + 256],
                                 start=True, stop=True)
                    bm = ctx.t.mark(ctx.pe)
                    ctx.s.wait(ctx.pe, bm)
                    ctx.s.copy(wb_bc[:], banks[4][:, 0:256])
                    wbev = ctx.s.mark(ctx.act)
                    ctx.t.wait(ctx.act, wbev)
                hev = {}
                for j in range(16):
                    b1 = banks[j % 2]
                    b3 = banks[2 + j % 2]
                    if j >= 2:
                        ctx.t.wait(ctx.vec, hev[j - 2])
                    for c in range(NCH):
                        ctx.t.matmul(b1[:, 0:256],
                                     we1b[:, c * 2048 + j * 128:c * 2048 + (j + 1) * 128],
                                     xn[:, c * QTOK + t0l:c * QTOK + t0l + 256],
                                     start=(c == 0), stop=(c == NCH - 1))
                    for c in range(NCH):
                        ctx.t.matmul(b3[:, 0:256],
                                     we3b[:, c * 2048 + j * 128:c * 2048 + (j + 1) * 128],
                                     xn[:, c * QTOK + t0l:c * QTOK + t0l + 256],
                                     start=(c == 0), stop=(c == NCH - 1))
                    pm = ctx.t.mark(ctx.pe)
                    ctx.s.wait(ctx.pe, pm)
                    ctx.s.activation(s_bf[:, (j % 2) * 256:(j % 2) * 256 + 256],
                                     b1[:, 0:256], AF.Silu)
                    am = ctx.s.mark(ctx.act)
                    ctx.v.wait(ctx.pe, pm)
                    ctx.v.tensor_copy(t_bf[:, (j % 2) * 256:(j % 2) * 256 + 256],
                                      b3[:, 0:256])
                    ctx.v.wait(ctx.act, am)
                    if shared:
                        ctx.v.fence()
                        ctx.v.tensor_tensor(h_bf[:, j * 256:(j + 1) * 256],
                                            s_bf[:, (j % 2) * 256:(j % 2) * 256 + 256],
                                            t_bf[:, (j % 2) * 256:(j % 2) * 256 + 256],
                                            ALU.mult)
                    else:
                        ctx.v.fence()
                        ctx.v.tensor_tensor(h_bf[:, j * 256:(j + 1) * 256],
                                            s_bf[:, (j % 2) * 256:(j % 2) * 256 + 256],
                                            t_bf[:, (j % 2) * 256:(j % 2) * 256 + 256],
                                            ALU.mult)
                        ctx.v.wait(ctx.act, wbev)
                        ctx.v.fence()
                        ctx.v.tensor_tensor(h_bf[:, j * 256:(j + 1) * 256],
                                            h_bf[:, j * 256:(j + 1) * 256],
                                            wb_bc[:], ALU.mult)
                    hev[j] = ctx.v.mark(ctx.vec)
                ctx.t.wait(ctx.vec, hev[15])
                w2m = {}
                for j in range(16):
                    wb2 = we2b[:, (j % 2) * D:(j % 2) * D + D]
                    if j >= 2:
                        ctx.q.wait(ctx.pe, w2m[j - 2])
                    else:
                        ctx.q.wait(ctx.gp, wsg)
                    lv = ctx.dma("q", wstage[:, 0:D] if j % 2 == 0 else wstage[:, 2048:2048 + D],
                                 w2d[j * 128:(j + 1) * 128, :])
                    ctx.g.wait(*lv)
                    ctx.g.wait(ctx.pe, w2m[j - 2] if j >= 2 else l2f_w2last)
                    ctx.g.tensor_copy(wb2, wstage[:, 0:D] if j % 2 == 0 else wstage[:, 2048:2048 + D])
                    gm2 = ctx.g.mark(ctx.gp)
                    wsg = gm2
                    ctx.t.wait(ctx.gp, gm2)
                    if j == 0:
                        ctx.t.wait(ctx.vec, l2f_oev)
                    for dc in range(NCH):
                        ctx.t.matmul(banks[4 + dc // 2][:, (dc % 2) * 256:(dc % 2) * 256 + 256],
                                     wb2[:, dc * 128:(dc + 1) * 128],
                                     h_bf[:, j * 256:(j + 1) * 256],
                                     start=(j == 0 and dc % 2 == 0),
                                     stop=(j == 15 and dc % 2 == 1))
                    w2m[j] = ctx.t.mark(ctx.pe)
                l2f_w2last = w2m[15]
                pmf = w2m[15]
                ctx.v.wait(ctx.pe, pmf)
                for dc in range(NCH):
                    src_ps = banks[4 + dc // 2][:, (dc % 2) * 256:(dc % 2) * 256 + 256]
                    dsl = slice(dc * QTOK + t0l, dc * QTOK + t0l + 256)
                    if e == 0:
                        ctx.v.tensor_copy(out_acc[:, dsl], src_ps)
                    else:
                        ctx.v.tensor_tensor(out_acc[:, dsl], out_acc[:, dsl],
                                            src_ps, ALU.add)
                    ctx.v.fence()
                l2f_oev = ctx.v.mark(ctx.vec)
        ctx.barrier()

        sts = {}
        for sp in range(QTOK // 512):
            t0 = tq0 + sp * 512
            lv = ctx.dma("q", lnw["xf"][:, 0:NCH * 512], x_span_ap(xin, t0))
            for dc in range(NCH):
                gi = sp * NCH + dc
                ctx.v.wait(*lv)
                if gi >= 2:
                    ctx.v.wait(*sts[gi - 2])
                ot = lnw["bc"][:, (gi % 2) * 512:(gi % 2) * 512 + 512]
                ctx.v.scalar_tensor_tensor(
                    ot, out_acc[:, dc * QTOK + sp * 512:dc * QTOK + sp * 512 + 512],
                    1.0 / 3.0, lnw["xf"][:, dc * 512:(dc + 1) * 512],
                    ALU.mult, ALU.add)
                ctx.v.fence()
                ov = ctx.v.mark(ctx.vec)
                ctx.s.wait(ctx.vec, ov)
                sts[gi] = ctx.dma("s", x_chunk_span_ap(xout, dc, t0), ot)
            ctx.q.wait(ctx.vec, ov)
        ctx.barrier()
    st.close()



def emit_l0f(ctx, cst, xin, xout):
    st = ExitStack()
    xn = ctx.sbuf("l0f_xn", [128, NCH * T], BF16, st)
    lnw = ln_work(ctx, st, "l0f")
    acat_d = ctx.dram("l0f_acat_d", [2, NCH, 128, T], BF16)
    ac_scr = ctx.sbuf("l0f_acs", [128, 2 * 512], BF16, st)
    p_sb = lnw["xf"][:].bitcast(BF16)
    pt_sb = ctx.sbuf("l0f_pt", [128, 512], BF16, st)
    stat = ctx.sbuf("l0f_stat", [128, 16], F32, st)
    wstage = ctx.sbuf("l0f_wstage", [128, 2048], F32, st)
    maskf = ctx.sbuf("l0f_mask", [128, 128], F32, st)
    banks = [ctx.psum(f"l0f_ps{i}", [128, 512], F32, st) for i in range(4)]
    ps_t = ctx.psum("l0f_pst", [128, 1024], BF16, st)
    ps_pv = ctx.psum("l0f_pspv", [128, 512], F32, st)
    W = ctx.wins
    win = W["l0f_win"]
    SC = 1.0 / math.sqrt(128.0)
    acq = [0]
    ctx.dma("q", maskf[:], W["mask_neg"][:, :])

    def acat_store(part, h, qs, pvm, rows=128):
        i = acq[0]
        acq[0] += 1
        sl = (i % 2) * 512
        ctx.s.wait(ctx.pe, pvm)
        if i >= 2:
            ctx.s.wait(*ac_sts[i - 2])
        ctx.s.copy(ac_scr[:rows, sl:sl + 512], ps_pv[0:rows, :])
        sm = ctx.s.mark(ctx.act)
        ctx.t.wait(ctx.act, sm)
        ctx.s.wait(ctx.act, sm)
        dst = AP(tensor=acat_d.tensor,
                 offset=acat_d.offset + (part * NCH + h) * 128 * T + qs * 512,
                 ap=[[T, rows], [1, 512]])
        if i == 0:
            ac_sts.clear()
        ac_sts[i] = ctx.dma("s", dst, ac_scr[:rows, sl:sl + 512])
    ac_sts = {}

    for sq in range(SEQ):
        tok0 = sq * T
        emit_ln(ctx, cst, lnw, xin, xn, T, tok0=tok0, banks=banks)
        ctx.barrier()

        stA = ExitStack()
        vtm = ctx.sbuf(f"l0f_vtm{sq}", [128, 16 * D], BF16, stA)
        stA1 = ExitStack()
        wv_b = ctx.sbuf(f"l0f_wvb{sq}", [128, NCH * D], BF16, stA1)
        gmk = {}
        for c in range(NCH):
            half = (c % 2) * 1024
            if c >= 2:
                ctx.q.wait(ctx.gp, gmk[c - 2])
            lv = ctx.dma("q", wstage[:, half:half + 1024],
                         win[c * 128:(c + 1) * 128, 3072:4096])
            ctx.g.wait(*lv)
            ctx.g.tensor_copy(wv_b[:, c * D:(c + 1) * D], wstage[:, half:half + 1024])
            gmk[c] = ctx.g.mark(ctx.gp)
        ctx.t.wait(ctx.gp, gmk[NCH - 1])
        ev = {}
        for kt in range(16):
            for half in range(2):
                i = kt * 2 + half
                ps = banks[i % 4]
                if i >= 4:
                    ctx.t.wait(ctx.act, ev[i - 4])
                for c in range(NCH):
                    ctx.t.matmul(ps[:, :],
                                 xn[:, c * T + kt * 128:c * T + (kt + 1) * 128],
                                 wv_b[:, c * D + half * 512:c * D + half * 512 + 512],
                                 start=(c == 0), stop=(c == NCH - 1))
                pm = ctx.t.mark(ctx.pe)
                ctx.s.wait(ctx.pe, pm)
                ctx.s.copy(vtm[:, kt * D + half * 512:kt * D + half * 512 + 512],
                           ps[:, :])
                ev[i] = ctx.s.mark(ctx.act)
        ctx.barrier()
        stA1.close()

        wq_h = ctx.sbuf(f"l0f_wqh{sq}", [128, NCH * 128], BF16, stA)
        wk_h = ctx.sbuf(f"l0f_wkh{sq}", [128, NCH * 128], BF16, stA)
        wk_h2 = None
        qa_f = ctx.sbuf(f"l0f_qaf{sq}", [128, 2 * 128], F32, stA)
        m_f = ctx.sbuf(f"l0f_mf{sq}", [128, 2 * 128], F32, stA)
        rp_b = ctx.sbuf(f"l0f_rpb{sq}", [128, 2 * 128], BF16, stA)
        cosb = ctx.sbuf(f"l0f_cosb{sq}", [128, 1024], F32, stA)
        sinb = ctx.sbuf(f"l0f_sinb{sq}", [128, 1024], F32, stA)
        qf_h = ctx.sbuf(f"l0f_qfh{sq}", [128, T], BF16, stA)
        kf_h = ctx.sbuf(f"l0f_kfh{sq}", [128, T], BF16, stA)
        s_sb = ctx.sbuf(f"l0f_ssb{sq}", [128, T], F32, stA)
        ctx.dma("q", cosb[:], W["l0f_cos"][:, :])
        ctx.dma("q", sinb[:], W["l0f_sin"][:, :])
        ctx.barrier()
        for h in range(8):
            for is_q in (True, False):
                col0 = (1024 if is_q else 2048) + h * 128
                wb = wq_h if is_q else wk_h
                dstf = qf_h if is_q else kf_h
                ctx.q.wait(ctx.gp)
                lv = ctx.dma("q", wstage[:, 0:NCH * 128],
                             col_block_ap(win, 4112, col0, 128))
                ctx.g.wait(*lv)
                for c in range(NCH):
                    ctx.g.tensor_copy(wb[:, c * 128:(c + 1) * 128],
                                      wstage[:, c * 128:(c + 1) * 128])
                gm = ctx.g.mark(ctx.gp)
                ctx.t.wait(ctx.gp, gm)
                for kt in range(16):
                    sl = (kt % 2) * 128
                    bb = banks[kt % 2]
                    if kt >= 2:
                        ctx.t.wait(ctx.act, qev[kt - 2])
                    for c in range(NCH):
                        ctx.t.matmul(bb[:, 0:128],
                                     xn[:, c * T + kt * 128:c * T + (kt + 1) * 128],
                                     wb[:, c * 128:(c + 1) * 128],
                                     start=(c == 0), stop=(c == NCH - 1))
                    pm = ctx.t.mark(ctx.pe)
                    ctx.s.wait(ctx.pe, pm)
                    if kt >= 2:
                        ctx.s.wait(ctx.vec, rev[kt - 2])
                    ctx.s.copy(qa_f[:, sl:sl + 128], bb[:, 0:128])
                    if kt == 0:
                        qev = {}
                        rev = {}
                    qev[kt] = ctx.s.mark(ctx.act)
                    t1 = qa_f[:, sl:sl + 64]
                    t2 = qa_f[:, sl + 64:sl + 128]
                    o1 = rp_b[:, sl:sl + 64]
                    o2 = rp_b[:, sl + 64:sl + 128]
                    u1 = m_f[:, sl:sl + 64]
                    u2 = m_f[:, sl + 64:sl + 128]
                    csl = cosb[:, kt * 64:(kt + 1) * 64]
                    ssl = sinb[:, kt * 64:(kt + 1) * 64]
                    ctx.v.wait(ctx.act, qev[kt])
                    ctx.v.tensor_tensor(u1, t1, csl, ALU.mult)
                    ctx.v.tensor_tensor(u2, t2, csl, ALU.mult)
                    ctx.v.fence()
                    ctx.v.tensor_tensor(t1, t1, ssl, ALU.mult)
                    ctx.v.tensor_tensor(t2, t2, ssl, ALU.mult)
                    ctx.v.fence()
                    ctx.v.tensor_tensor(o1, u1, t2, ALU.subtract)
                    ctx.v.tensor_tensor(o2, u2, t1, ALU.add)
                    ctx.v.fence()
                    rev[kt] = ctx.v.mark(ctx.vec)
                    ctx.t.wait(ctx.vec, rev[kt])
                    ctx.t.matmul(ps_t[:, sl + (kt % 8 // 2) * 256:
                                      sl + (kt % 8 // 2) * 256 + 128],
                                 rp_b[:, sl:sl + 128], cst["ident_sb"][:],
                                 is_transpose=True,
                                 start=(kt % 8 == 0), stop=(kt % 8 == 7))
                    tm = ctx.t.mark(ctx.pe)
                    if kt % 8 == 7:
                        ctx.v.wait(ctx.pe, tm)
                        ctx.v.tensor_copy(dstf[:, (kt - 7) * 128:(kt + 1) * 128],
                                          ps_t[:])
                        trm = ctx.v.mark(ctx.vec)
                        ctx.t.wait(ctx.vec, trm)
            ctx.barrier()

            def acat_cb(qs, pvm, h=h):
                acat_store(0, h, qs, pvm)

            emit_softmax_attn_head(
                ctx, cst, qf_h, kf_h, vtm, s_sb, p_sb, pt_sb, stat,
                banks, ps_t, ps_pv, maskf, r0=None, scale=SC, dv=128,
                vcol=lambda kt, h=h: (kt * D + h * 128, 128),
                acat_cb=acat_cb, causal=True)
            ctx.barrier()
        stA.close()
        ctx.barrier()

        stB = ExitStack()
        qf = ctx.sbuf(f"l0f_qf{sq}", [128, NCH * T], BF16, stB)
        ke_b = ctx.sbuf(f"l0f_keb{sq}", [128, 4096], BF16, stB)
        ve_b = ctx.sbuf(f"l0f_veb{sq}", [128, 4096], BF16, stB)
        wbc = lnw["xf"][:].bitcast(BF16)
        aT = ctx.sbuf(f"l0f_aT{sq}", [128, 2 * 512], BF16, stB)
        wrt = ctx.sbuf(f"l0f_wrt{sq}", [128, NCH * 16], BF16, stB)
        gcell = ctx.sbuf(f"l0f_gcell{sq}", [128, 64], F32, stB)
        wsT16 = ctx.sbuf(f"l0f_wsT{sq}", [16, T], F32, stB)
        wsT16b = ctx.sbuf(f"l0f_wsTb{sq}", [16, T], BF16, stB)
        w_dram = ctx.dram(f"l0f_wd{sq}", [16, T], BF16)
        for dc in range(NCH):
            lv = ctx.dma("q", wstage[:, 0:NCH * 128],
                         col_block_ap(win, 4112, dc * 128, 128))
            ctx.g.wait(*lv)
            for c in range(NCH):
                ctx.g.tensor_copy(ke_b[:, c * 128:(c + 1) * 128],
                                  wstage[:, c * 128:(c + 1) * 128])
            gm = ctx.g.mark(ctx.gp)
            ctx.t.wait(ctx.gp, gm)
            for spn in range(4):
                ps = banks[spn]
                for c in range(NCH):
                    ctx.t.matmul(ps[:, :], ke_b[:, c * 128:(c + 1) * 128],
                                 xn[:, c * T + spn * 512:c * T + spn * 512 + 512],
                                 start=(c == 0), stop=(c == NCH - 1))
                pm = ctx.t.mark(ctx.pe)
                ctx.s.wait(ctx.pe, pm)
                ctx.s.copy(qf[:, dc * T + spn * 512:dc * T + spn * 512 + 512],
                           ps[:, :])
                ctx.s.mark(ctx.act)
            ctx.barrier()
        lv = ctx.dma("q", wstage[:, 0:NCH * 16], col_block_ap(win, 4112, 4096, 16))
        ctx.g.wait(*lv)
        for c in range(NCH):
            ctx.g.tensor_copy(wrt[:, c * 16:(c + 1) * 16],
                              wstage[:, c * 16:(c + 1) * 16])
        gm = ctx.g.mark(ctx.gp)
        ctx.t.wait(ctx.gp, gm)
        m1 = gcell[:, 16:17]; m2 = gcell[:, 17:18]
        lg = gcell[:, 32:48]; l2c = gcell[:, 48:64]
        for kt in range(16):
            ps = banks[0]
            for c in range(NCH):
                ctx.t.matmul(ps[:, 0:16],
                             xn[:, c * T + kt * 128:c * T + (kt + 1) * 128],
                             wrt[:, c * 16:(c + 1) * 16],
                             start=(c == 0), stop=(c == NCH - 1))
            pm = ctx.t.mark(ctx.pe)
            ctx.v.wait(ctx.pe, pm)
            ctx.v.tensor_copy(lg, ps[:, 0:16])
            ctx.v.fence()
            ctx.v.reduce_max(m1, lg, axis=AX.X)
            ctx.v.fence()
            ctx.v.tensor_scalar(l2c, lg, m1, None, ALU.is_ge)
            ctx.v.fence()
            ctx.v.scalar_tensor_tensor(l2c, l2c, NEG, lg, ALU.mult, ALU.add)
            ctx.v.fence()
            ctx.v.reduce_max(m2, l2c, axis=AX.X)
            ctx.v.fence()
            ctx.v.tensor_scalar(l2c, lg, m2, None, ALU.is_ge)
            vm = ctx.v.mark(ctx.vec)
            ctx.s.wait(ctx.vec, vm)
            ctx.s.activation(gcell[:, 0:16], lg, AF.Sigmoid)
            am = ctx.s.mark(ctx.act)
            ctx.v.wait(ctx.act, am)
            ctx.v.tensor_tensor(gcell[:, 0:16], gcell[:, 0:16], l2c, ALU.mult)
            ctx.v.fence()
            vm2 = ctx.v.mark(ctx.vec)
            ctx.t.wait(ctx.vec, vm2)
            ctx.t.matmul(banks[1][0:16, (kt % 4) * 128:(kt % 4) * 128 + 128],
                         gcell[:, 0:16], cst["ident_f32"][:],
                         is_transpose=True, start=(kt % 4 == 0), stop=(kt % 4 == 3))
            tm = ctx.t.mark(ctx.pe)
            if kt % 4 == 3:
                ctx.v.wait(ctx.pe, tm)
                ctx.v.tensor_copy(wsT16[0:16, (kt - 3) * 128:(kt + 1) * 128],
                                  banks[1][0:16, :])
                wtm = ctx.v.mark(ctx.vec)
                ctx.t.wait(ctx.vec, wtm)
        ctx.v.fence()
        ctx.v.tensor_copy(wsT16b[0:16, :], wsT16[0:16, :])
        vmb = ctx.v.mark(ctx.vec)
        ctx.s.wait(ctx.vec, vmb)
        sv = ctx.dma("s", w_dram[:, :], wsT16b[0:16, :])
        ctx.q.wait(*sv)
        ctx.barrier()

        last_gml = 0
        last_pvm = 0
        for h in range(8):
            ctx.q.wait(ctx.gp)
            lvk = ctx.dma("q", wstage[:, 0:2048],
                          W["l0f_ke"][h * 128:(h + 1) * 128, 0:2048])
            ctx.g.wait(*lvk)
            ctx.g.wait(ctx.pe, last_pvm)
            ctx.g.tensor_copy(ke_b[:, 0:2048], wstage[:, 0:2048])
            g1 = ctx.g.mark(ctx.gp)
            ctx.q.wait(ctx.gp, g1)
            lvk = ctx.dma("q", wstage[:, 0:2048],
                          W["l0f_ke"][h * 128:(h + 1) * 128, 2048:4096])
            ctx.g.wait(*lvk)
            ctx.g.tensor_copy(ke_b[:, 2048:4096], wstage[:, 0:2048])
            g2 = ctx.g.mark(ctx.gp)
            ctx.q.wait(ctx.gp, g2)
            lvk = ctx.dma("q", wstage[:, 0:2048],
                          W["l0f_ve"][h * 128:(h + 1) * 128, 0:2048])
            ctx.g.wait(*lvk)
            ctx.g.tensor_copy(ve_b[:, 0:2048], wstage[:, 0:2048])
            g3 = ctx.g.mark(ctx.gp)
            ctx.q.wait(ctx.gp, g3)
            lvk = ctx.dma("q", wstage[:, 0:2048],
                          W["l0f_ve"][h * 128:(h + 1) * 128, 2048:4096])
            ctx.g.wait(*lvk)
            ctx.g.tensor_copy(ve_b[:, 2048:4096], wstage[:, 0:2048])
            g4 = ctx.g.mark(ctx.gp)
            ctx.t.wait(ctx.gp, g4)
            for spn in range(4):
                t0l = spn * 512
                ctx.q.wait(ctx.vec, last_gml)
                lvb = ctx.dma("q", wbc[:, 0:8192],
                              AP(tensor=w_dram.tensor, offset=w_dram.offset + t0l,
                                 ap=[[0, 128], [T, 16], [1, 512]]))
                ctx.v.wait(*lvb)
                ctx.t.wait(*lvb)
                first = True
                for e in range(16):
                    for sk in range(2):
                        bb = banks[sk % 2]
                        i2 = e * 2 + sk
                        if i2 >= 2:
                            ctx.t.wait(ctx.act, gel[i2 - 2])
                        ctx.t.matmul(bb[:, :],
                                     ke_b[:, e * 256 + sk * 128:e * 256 + sk * 128 + 128],
                                     qf[:, h * T + t0l:h * T + t0l + 512],
                                     start=True, stop=True)
                        pm = ctx.t.mark(ctx.pe)
                        ctx.s.wait(ctx.pe, pm)
                        if i2 >= 2:
                            ctx.s.wait(ctx.vec, gml[i2 - 2])
                        ctx.s.activation(aT[:, sk * 512:sk * 512 + 512], bb[:, :],
                                         AF.Gelu)
                        if i2 == 0:
                            gel = {}
                            gml = {}
                        gel[i2] = ctx.s.mark(ctx.act)
                        ctx.v.wait(ctx.act, gel[i2])
                        ctx.v.tensor_tensor(aT[:, sk * 512:sk * 512 + 512],
                                            aT[:, sk * 512:sk * 512 + 512],
                                            wbc[:, e * 512:(e + 1) * 512], ALU.mult)
                        ctx.v.fence()
                        gml[i2] = ctx.v.mark(ctx.vec)
                        ctx.t.wait(ctx.vec, gml[i2])
                        ctx.t.matmul(ps_pv[:, :],
                                     ve_b[:, e * 256 + sk * 128:e * 256 + sk * 128 + 128],
                                     aT[:, sk * 512:sk * 512 + 512],
                                     start=first, stop=(e == 15 and sk == 1))
                        first = False
                pvm = ctx.t.mark(ctx.pe)
                last_gml = gml[31]
                last_pvm = pvm
                acat_store(1, h, spn, pvm)
        ctx.barrier()
        stB.close()

        stC = ExitStack()
        ws_out = WStream(ctx, stC, f"l0f_wo{sq}", D, nch=16, stage=wstage)
        ac_in = ctx.sbuf(f"l0f_acin{sq}", [128, 2 * NCH * 512], BF16, stC)
        ws_out.load(W["l0f_wout"])
        ctx.barrier()
        cur_spn = [-1]

        def oproj(dc, spn, ps):
            if spn != cur_spn[0]:
                src_ap = AP(tensor=acat_d.tensor,
                            offset=acat_d.offset + spn * 512,
                            ap=[[T, 128], [128 * T, 2 * NCH], [1, 512]])
                lv = ctx.dma("q", ac_in[:], src_ap)
                ctx.t.wait(*lv)
                cur_spn[0] = spn
            for c in range(NCH):
                ctx.t.matmul(ps[:, :], ws_out.tile(c, dc * 128),
                             ac_in[:, c * 512:(c + 1) * 512],
                             start=(c == 0), stop=False)
            for c in range(NCH):
                ctx.t.matmul(ps[:, :], ws_out.tile(8 + c, dc * 128),
                             ac_in[:, (NCH + c) * 512:(NCH + c + 1) * 512],
                             start=False, stop=(c == NCH - 1))

        emit_residual_out(ctx, cst, lnw, xin, xout, tok0, T, oproj, None, banks)
        ctx.barrier()
        stC.close()
    st.close()



def emit_l1f(ctx, cst, xin, xout):
    st = ExitStack()
    lnw = ln_work(ctx, st, "l1f")
    y_res = ctx.dram("l1f_y", [NCH, 128, TOK], F32)
    acat_d = ctx.dram("l1f_acat", [NCH, 128, T], BF16)
    qr_d = ctx.dram("l1f_qr", [NCH, 128, T], BF16)
    kr_d = ctx.dram("l1f_kr", [NCH, 128, T], BF16)
    wstage = ctx.sbuf("l1f_wst", [128, 2 * 1024], F32, st)
    maskup = ctx.sbuf("l1f_mup", [128, 128], F32, st)
    masklo = ctx.sbuf("l1f_mlo", [128, 128], F32, st)
    banks = [ctx.psum(f"l1f_ps{i}", [128, 512], F32, st) for i in range(6)]
    ps_t = ctx.psum("l1f_pst", [128, 1024], BF16, st)
    ps_pv = ctx.psum("l1f_pspv", [128, 512], F32, st)
    W = ctx.wins
    ctx.dma("q", maskup[:], W["mask_neg"][:, :])
    ctx.dma("q", masklo[:], W["mask_low"][:, :])

    st0 = ExitStack()
    yscr = ctx.sbuf("l1f_yscr", [128, 1024], F32, st0)
    for sq in range(SEQ):
        emit_ln(ctx, cst, lnw, xin, None, T, tok0=sq * T, banks=banks[0:4],
                fp32_store=(y_res, yscr))
        ctx.barrier()
    st0.close()

    for dep in range(6):
        row0 = dep * D
        for sq in range(SEQ):
            tok0 = sq * T
            stA = ExitStack()
            ybs = ctx.sbuf(f"l1f_ybs{dep}{sq}", [128, NCH * T], BF16, stA)
            vtm = ctx.sbuf(f"l1f_vtm{dep}{sq}", [128, 16 * D], BF16, stA)
            wsec = ctx.sbuf(f"l1f_wsec{dep}{sq}", [128, NCH * 1024], BF16, stA)
            qkv_f = ctx.sbuf(f"l1f_qkvf{dep}{sq}", [128, 1024], F32, stA)
            rp_f = ctx.sbuf(f"l1f_rpf{dep}{sq}", [128, 1024], F32, stA)
            rp_b = ctx.sbuf(f"l1f_rpb{dep}{sq}", [128, 1024], BF16, stA)
            seg16 = ctx.sbuf(f"l1f_seg{dep}{sq}", [128, 64], F32, stA)
            csq = ctx.sbuf(f"l1f_csq{dep}{sq}", [128, 1024], F32, stA)
            ssq = ctx.sbuf(f"l1f_ssq{dep}{sq}", [128, 1024], F32, stA)
            csk = ctx.sbuf(f"l1f_csk{dep}{sq}", [128, 1024], F32, stA)
            ssk = ctx.sbuf(f"l1f_ssk{dep}{sq}", [128, 1024], F32, stA)
            sc_bc = ctx.sbuf(f"l1f_scbc{dep}{sq}", [128, 128], F32, stA)
            trs = ctx.sbuf(f"l1f_trs{dep}{sq}", [128, 1024], BF16, stA)
            qkh = ctx.sbuf(f"l1f_qkh{dep}{sq}", [64, 2 * T], BF16, stA)
            s_sb = ctx.sbuf(f"l1f_ssb{dep}{sq}", [128, 768], F32, stA)
            p_b = ctx.sbuf(f"l1f_pb{dep}{sq}", [128, 768], BF16, stA)
            pt_s = ctx.sbuf(f"l1f_pts{dep}{sq}", [128, 384], BF16, stA)
            aout = ctx.sbuf(f"l1f_aout{dep}{sq}", [64, T], BF16, stA)
            stat = ctx.sbuf(f"l1f_stat{dep}{sq}", [128, 8], F32, stA)

            emit_shift_cast(ctx, y_res, ybs, tok0, lnw)
            lvq = ctx.dma("q", sc_bc[:, 0:64],
                          W["l1f_qs"][dep:dep + 1, :].broadcast_to([128, 64]))
            lvk = ctx.dma("q", sc_bc[:, 64:128],
                          W["l1f_ks"][dep:dep + 1, :].broadcast_to([128, 64]))
            ctx.dma("q", csq[:], W["l1f_cs"][:, :])
            ctx.dma("q", ssq[:], W["l1f_ssn"][:, :])
            ctx.dma("q", csk[:], W["l1f_csi"][:, :])
            ctx.dma("q", ssk[:], W["l1f_ssi"][:, :])
            ctx.barrier()

            for sec in range(3):
                gm = {}
                for c in range(NCH):
                    half = (c % 2) * 1024
                    if c >= 2:
                        ctx.q.wait(ctx.gp, gm[c - 2])
                    else:
                        ctx.q.wait(ctx.gp)
                    lv = ctx.dma("q", wstage[:, half:half + 1024],
                                 W["l1f_wqkv"][row0 + c * 128:row0 + (c + 1) * 128,
                                               sec * 1024:(sec + 1) * 1024])
                    ctx.g.wait(*lv)
                    ctx.g.tensor_copy(wsec[:, c * 1024:(c + 1) * 1024],
                                      wstage[:, half:half + 1024])
                    gm[c] = ctx.g.mark(ctx.gp)
                ctx.t.wait(ctx.gp, gm[NCH - 1])
                for kt in range(16):
                    for half in range(2):
                        i = kt * 2 + half
                        ps = banks[i % 4]
                        if i >= 4:
                            ctx.t.wait(ctx.act, qkev[i - 4])
                        for c in range(NCH):
                            ctx.t.matmul(ps[:, :],
                                         ybs[:, c * T + kt * 128:c * T + (kt + 1) * 128],
                                         wsec[:, c * 1024 + half * 512:
                                              c * 1024 + half * 512 + 512],
                                         start=(c == 0), stop=(c == NCH - 1))
                        pm = ctx.t.mark(ctx.pe)
                        ctx.s.wait(ctx.pe, pm)
                        if sec == 2:
                            ctx.s.copy(vtm[:, kt * D + half * 512:
                                           kt * D + half * 512 + 512], ps[:, :])
                        else:
                            if half == 0 and kt >= 1:
                                ctx.s.wait(ctx.vec, rpev[kt - 1])
                            ctx.s.copy(qkv_f[:, half * 512:half * 512 + 512],
                                       ps[:, :])
                        if i == 0:
                            qkev = {}
                            rpev = {}
                        qkev[i] = ctx.s.mark(ctx.act)
                    if sec == 2:
                        continue
                    ctx.v.wait(ctx.act, qkev[kt * 2 + 1])
                    ctx.v.tensor_tensor(rp_f[:], qkv_f[:], qkv_f[:], ALU.mult)
                    ctx.v.fence()
                    ctx.v.tensor_reduce(seg16[:, 0:16],
                                        rp_f[:].rearrange("p (s k) -> p s k", k=64),
                                        AX.X, ALU.add)
                    ctx.v.fence()
                    ctx.v.tensor_scalar_max(seg16[:, 0:16], seg16[:, 0:16], 1e-24)
                    ctx.v.fence()
                    vm = ctx.v.mark(ctx.vec)
                    ctx.s.wait(ctx.vec, vm)
                    ctx.s.activation(seg16[:, 16:32], seg16[:, 0:16], AF.Ln)
                    ctx.s.fence()
                    ctx.s.activation(seg16[:, 0:16], seg16[:, 16:32], AF.Exp,
                                     scale=0.5)
                    am = ctx.s.mark(ctx.act)
                    ctx.v.wait(ctx.act, am)
                    ctx.v.tensor_scalar_max(seg16[:, 0:16], seg16[:, 0:16], 1e-12)
                    ctx.v.fence()
                    ctx.v.reciprocal(seg16[:, 16:32], seg16[:, 0:16])
                    ctx.v.fence()
                    rnrep = AP(tensor=seg16, offset=16,
                               ap=[[64, 128], [1, 16], [0, 64]])
                    ctx.v.tensor_tensor(qkv_f[:], qkv_f[:], rnrep, ALU.mult)
                    ctx.v.fence()
                    screp = AP(tensor=sc_bc, offset=(0 if sec == 0 else 64),
                               ap=[[128, 128], [0, 16], [1, 64]])
                    ctx.v.tensor_tensor(qkv_f[:], qkv_f[:], screp, ALU.mult)
                    ctx.v.fence()
                    cstab = csq if sec == 0 else csk
                    sstab = ssq if sec == 0 else ssk
                    tabsl = slice(kt * 64, (kt + 1) * 64)
                    cs_rep = AP(tensor=cstab, offset=kt * 64,
                                ap=[[1024, 128], [0, 16], [1, 64]])
                    ctx.v.tensor_tensor(rp_f[:], qkv_f[:], cs_rep, ALU.mult)
                    for hswap in range(2):
                        qsw = AP(tensor=qkv_f, offset=(32 if hswap == 0 else 0),
                                 ap=[[1024, 128], [64, 16], [1, 32]])
                        ssl = AP(tensor=sstab, offset=kt * 64 + hswap * 32,
                                 ap=[[1024, 128], [0, 16], [1, 32]])
                        dsw = AP(tensor=rp_b, offset=hswap * 32,
                                 ap=[[1024, 128], [64, 16], [1, 32]])
                        ctx.v.tensor_tensor(dsw, qsw, ssl, ALU.mult)
                    ctx.v.fence()
                    ctx.v.tensor_tensor(rp_b[:], rp_b[:], rp_f[:], ALU.add)
                    ctx.v.fence()
                    rpev[kt] = ctx.v.mark(ctx.vec)
                    ctx.t.wait(ctx.vec, rpev[kt])
                    for hp in range(8):
                        ctx.t.matmul(ps_t[:, hp * 128:(hp + 1) * 128],
                                     rp_b[:, hp * 128:(hp + 1) * 128],
                                     cst["ident_sb"][:], is_transpose=True,
                                     start=(hp == 0), stop=(hp == 7))
                    tm = ctx.t.mark(ctx.pe)
                    ctx.v.wait(ctx.pe, tm)
                    if kt >= 1:
                        ctx.v.wait(*trst[kt - 1])
                    ctx.v.tensor_copy(trs[:], ps_t[:])
                    ctx.v.fence()
                    trm = ctx.v.mark(ctx.vec)
                    ctx.t.wait(ctx.vec, trm)
                    ctx.s.wait(ctx.vec, trm)
                    if kt == 0:
                        trst = {}
                    qdst = qr_d if sec == 0 else kr_d
                    dst = AP(tensor=qdst.tensor,
                             offset=qdst.offset + kt * 128,
                             ap=[[T, 128], [128 * T, NCH], [1, 128]])
                    trst[kt] = ctx.dma("s", dst, trs[:])
                ctx.barrier()

            prev_pvm = 0
            sev = {}
            pnv = {}
            ams = {}
            aev = {}
            for h in range(16):
                hp, r0 = h // 2, (h % 2) * 64
                ctx.q.wait(ctx.pe, prev_pvm)
                lv1 = ctx.dma("q", qkh[0:64, 0:T],
                              AP(tensor=qr_d.tensor,
                                 offset=qr_d.offset + (hp * 128 + r0) * T,
                                 ap=[[T, 64], [1, T]]))
                lv2 = ctx.dma("q", qkh[0:64, T:2 * T],
                              AP(tensor=kr_d.tensor,
                                 offset=kr_d.offset + (hp * 128 + r0) * T,
                                 ap=[[T, 64], [1, T]]))
                ctx.t.wait(*lv1)
                ctx.t.wait(*lv2)
                for qt in range(16):
                    gq = h * 16 + qt
                    kt0 = max(0, qt - 2)
                    nkt = qt - kt0 + 1
                    wk = nkt * 128
                    ofs = (qt % 2) * 384
                    st0 = (qt % 2) * 4
                    bb = banks[qt % 2]
                    if gq >= 2:
                        ctx.t.wait(ctx.act, sev[gq - 2])
                    ctx.t.matmul(bb[:, 0:wk], qkh[0:64, qt * 128:(qt + 1) * 128],
                                 qkh[0:64, T + kt0 * 128:T + kt0 * 128 + wk],
                                 start=True, stop=True)
                    pm = ctx.t.mark(ctx.pe)
                    ctx.s.wait(ctx.pe, pm)
                    if gq >= 2:
                        ctx.s.wait(ctx.act, ams[gq - 2])
                    ctx.s.copy(s_sb[:, ofs:ofs + wk], bb[:, 0:wk])
                    sev[gq] = ctx.s.mark(ctx.act)
                    ctx.v.wait(ctx.act, sev[gq])
                    ctx.v.tensor_tensor(s_sb[:, ofs + wk - 128:ofs + wk],
                                        s_sb[:, ofs + wk - 128:ofs + wk],
                                        maskup[:], ALU.add)
                    if nkt == 3:
                        ctx.v.tensor_tensor(s_sb[:, ofs:ofs + 128],
                                            s_sb[:, ofs:ofs + 128],
                                            masklo[:], ALU.add)
                    ctx.v.fence()
                    ctx.v.reduce_max(stat[:, st0:st0 + 1], s_sb[:, ofs:ofs + wk],
                                     axis=AX.X, negate=True)
                    ctx.v.fence()
                    vm2 = ctx.v.mark(ctx.vec)
                    ctx.s.wait(ctx.vec, vm2)
                    ctx.s.mul(stat[:, st0 + 1:st0 + 2], stat[:, st0:st0 + 1], 8.0)
                    ctx.s.memzero(stat[:, st0 + 2:st0 + 3])
                    ctx.s.fence()
                    ctx.s.activation(p_b[:, ofs:ofs + wk], s_sb[:, ofs:ofs + wk],
                                     AF.Exp, scale=8.0,
                                     bias=stat[:, st0 + 1:st0 + 2],
                                     accum_out=stat[:, st0 + 2:st0 + 3])
                    am2 = ctx.s.mark(ctx.act)
                    ams[gq] = am2
                    ctx.v.wait(ctx.act, am2)
                    ctx.v.reciprocal(stat[:, st0 + 3:st0 + 4],
                                     stat[:, st0 + 2:st0 + 3])
                    ctx.v.fence()
                    ctx.v.tensor_scalar(p_b[:, ofs:ofs + wk], p_b[:, ofs:ofs + wk],
                                        stat[:, st0 + 3:st0 + 4], None, ALU.mult)
                    ctx.v.fence()
                    pnv[gq] = ctx.v.mark(ctx.vec)
                    ctx.t.wait(ctx.vec, pnv[gq])
                    for j in range(nkt):
                        ctx.t.matmul(ps_t[:, j * 128:(j + 1) * 128],
                                     p_b[:, ofs + j * 128:ofs + (j + 1) * 128],
                                     cst["ident_sb"][:], is_transpose=True,
                                     start=(j == 0), stop=(j == nkt - 1))
                    tm2 = ctx.t.mark(ctx.pe)
                    ctx.v.wait(ctx.pe, tm2)
                    ctx.v.tensor_copy(pt_s[:, 0:wk], ps_t[:, 0:wk])
                    ptm = ctx.v.mark(ctx.vec)
                    ctx.t.wait(ctx.vec, ptm)
                    if gq >= 1:
                        ctx.t.wait(ctx.act, aev[gq - 1])
                    for j in range(nkt):
                        ctx.t.matmul(ps_pv[0:64, 0:128],
                                     vtm[:, (kt0 + j) * D + h * 64:
                                         (kt0 + j) * D + h * 64 + 64],
                                     pt_s[:, j * 128:(j + 1) * 128],
                                     start=(j == 0), stop=(j == nkt - 1))
                    pvm = ctx.t.mark(ctx.pe)
                    prev_pvm = pvm
                    ctx.s.wait(ctx.pe, pvm)
                    ctx.s.copy(aout[0:64, qt * 128:(qt + 1) * 128], ps_pv[0:64, 0:128])
                    aev[gq] = ctx.s.mark(ctx.act)
                ctx.s.wait(ctx.act, aev[h * 16 + 15])
                dsta = AP(tensor=acat_d.tensor,
                          offset=acat_d.offset + (hp * 128 + r0) * T,
                          ap=[[T, 64], [1, T]])
                hst = ctx.dma("s", dsta, aout[0:64, :])
                ctx.q.wait(*hst)
            ctx.barrier()
            stA.close()

            stW = ExitStack()
            ws_o = WStream(ctx, stW, f"l1f_wo{dep}{sq}", D, stage=wstage)
            ac_in = ctx.sbuf(f"l1f_acin{dep}{sq}", [128, NCH * 512], BF16, stW)
            ws_o.load(W["l1f_wo"], row0=row0)
            ctx.barrier()
            cur = [-1]

            def oproj(dc, spn, ps):
                if spn != cur[0]:
                    srcp = AP(tensor=acat_d.tensor,
                              offset=acat_d.offset + spn * 512,
                              ap=[[T, 128], [128 * T, NCH], [1, 512]])
                    lv = ctx.dma("q", ac_in[:], srcp)
                    ctx.t.wait(*lv)
                    cur[0] = spn
                for c in range(NCH):
                    ctx.t.matmul(ps[:, :], ws_o.tile(c, dc * 128),
                                 ac_in[:, c * 512:(c + 1) * 512],
                                 start=(c == 0), stop=(c == NCH - 1))

            emit_residual_out(ctx, cst, lnw, y_res, y_res, tok0, T, oproj,
                              None, banks)
            ctx.barrier()
            stW.close()

            stF = ExitStack()
            w1b = ctx.sbuf(f"l1f_w1b{dep}{sq}", [128, NCH * 5632], BF16, stF)
            w2b = ctx.sbuf(f"l1f_w2b{dep}{sq}", [128, 22 * D], BF16, stF)
            xnf = ctx.sbuf(f"l1f_xnf{dep}{sq}", [128, NCH * 1024], BF16, stF)
            h_all = ctx.sbuf(f"l1f_hall{dep}{sq}", [128, 22 * 256], BF16, stF)
            afgl = ctx.sbuf(f"l1f_afgl{dep}{sq}", [128, 2 * 256], BF16, stF)
            cols = ctx.sbuf(f"l1f_cols{dep}{sq}", [128, 4 * 32], F32, stF)
            otmp = lnw["bc"]
            for dst2, key, wdt in ((cols[:, 0:22], "l1f_b1a", 22),
                                   (cols[:, 32:54], "l1f_b1g", 22),
                                   (cols[:, 64:72], "l1f_b2", 8),
                                   (cols[:, 96:104], "l1f_lng", 8),
                                   (cols[:, 104:112], "l1f_lnb", 8)):
                ctx.dma("q", dst2, W[key][dep * 128:(dep + 1) * 128, :])
            gmw = {}
            gi = 0
            for c in range(NCH):
                for piece in range(6):
                    p0 = piece * 1024
                    pw = min(1024, 5632 - p0)
                    half = (gi % 2) * 1024
                    if gi >= 2:
                        ctx.q.wait(ctx.gp, gmw[gi - 2])
                    else:
                        ctx.q.wait(ctx.gp)
                    lv = ctx.dma("q", wstage[:, half:half + pw],
                                 W["l1f_w1"][dep * D + c * 128:
                                             dep * D + (c + 1) * 128, p0:p0 + pw])
                    ctx.g.wait(*lv)
                    ctx.g.tensor_copy(w1b[:, c * 5632 + p0:c * 5632 + p0 + pw],
                                      wstage[:, half:half + pw])
                    gmw[gi] = ctx.g.mark(ctx.gp)
                    gi += 1
            for j in range(22):
                half = (gi % 2) * 1024
                ctx.q.wait(ctx.gp, gmw[gi - 2])
                lv = ctx.dma("q", wstage[:, half:half + 1024],
                             W["l1f_w2"][dep * 2816 + j * 128:
                                         dep * 2816 + (j + 1) * 128, :])
                ctx.g.wait(*lv)
                ctx.g.tensor_copy(w2b[:, j * D:(j + 1) * D],
                                  wstage[:, half:half + 1024])
                gmw[gi] = ctx.g.mark(ctx.gp)
                gi += 1
            ctx.barrier()

            for qq in range(2):
                tq0 = tok0 + qq * 1024
                emit_ln(ctx, cst, lnw, y_res, xnf, 1024, tok0=tq0,
                        banks=banks[0:4], xn_ntok=1024, shift=True,
                        seq_start=tok0,
                        affine=(lambda c: cols[:, 96 + c:97 + c],
                                lambda c: AP(tensor=cols, offset=104 + c,
                                             ap=[[128, 128], [0, 512]])))
                ctx.barrier()
                for sp in range(4):
                    t0l = sp * 256
                    hev = {}
                    for j in range(22):
                        ba = banks[4]
                        bg = banks[5]
                        if j >= 1:
                            ctx.t.wait(ctx.vec, hev[j - 1])
                        for c in range(NCH):
                            ctx.t.matmul(ba[:, 0:256],
                                         w1b[:, c * 5632 + j * 128:
                                             c * 5632 + (j + 1) * 128],
                                         xnf[:, c * 1024 + t0l:c * 1024 + t0l + 256],
                                         start=(c == 0), stop=(c == NCH - 1))
                        for c in range(NCH):
                            ctx.t.matmul(bg[:, 0:256],
                                         w1b[:, c * 5632 + 2816 + j * 128:
                                             c * 5632 + 2816 + (j + 1) * 128],
                                         xnf[:, c * 1024 + t0l:c * 1024 + t0l + 256],
                                         start=(c == 0), stop=(c == NCH - 1))
                        pm = ctx.t.mark(ctx.pe)
                        ctx.s.wait(ctx.pe, pm)
                        ctx.s.activation(afgl[:, 0:256], ba[:, 0:256], AF.Identity,
                                         bias=cols[:, j:j + 1])
                        ctx.s.fence()
                        ctx.s.activation(afgl[:, 256:512], bg[:, 0:256], AF.Gelu,
                                         bias=cols[:, 32 + j:33 + j])
                        am = ctx.s.mark(ctx.act)
                        ctx.v.wait(ctx.act, am)
                        ctx.v.tensor_tensor(h_all[:, j * 256:(j + 1) * 256],
                                            afgl[:, 0:256], afgl[:, 256:512],
                                            ALU.mult)
                        hev[j] = ctx.v.mark(ctx.vec)
                    ctx.t.wait(ctx.vec, hev[21])
                    cm2 = {}
                    for j in range(22):
                        for dc in range(NCH):
                            ctx.t.matmul(banks[dc // 2][:, (dc % 2) * 256:
                                                        (dc % 2) * 256 + 256],
                                         w2b[:, j * D + dc * 128:
                                             j * D + dc * 128 + 128],
                                         h_all[:, j * 256:(j + 1) * 256],
                                         start=(j == 0 and dc % 2 == 0),
                                         stop=(j == 21 and dc % 2 == 1))
                    pmf = ctx.t.mark(ctx.pe)
                    lvx = ctx.dma("q", lnw["xf"][:, 0:NCH * 256],
                                  x_span_ap(y_res, tq0 + t0l, w=256))
                    for dc in range(NCH):
                        gi2 = sp * NCH + dc
                        ctx.s.wait(ctx.pe, pmf)
                        if gi2 >= 2:
                            ctx.s.wait(*fsts[gi2 - 2])
                        ctx.s.activation(otmp[:, (gi2 % 2) * 256:(gi2 % 2) * 256 + 256],
                                         banks[dc // 2][:, (dc % 2) * 256:
                                                        (dc % 2) * 256 + 256],
                                         AF.Identity, bias=cols[:, 64 + dc:65 + dc])
                        am3 = ctx.s.mark(ctx.act)
                        ctx.v.wait(ctx.act, am3)
                        ctx.v.wait(*lvx)
                        if gi2 >= 2:
                            ctx.v.wait(*fsts[gi2 - 2])
                        ot = otmp[:, (gi2 % 2) * 256:(gi2 % 2) * 256 + 256]
                        ctx.v.tensor_tensor(ot, ot,
                                            lnw["xf"][:, dc * 256:(dc + 1) * 256],
                                            ALU.add)
                        ctx.v.fence()
                        if gi2 == 0:
                            fsts = {}
                        fsv = ctx.v.mark(ctx.vec)
                        ctx.s.wait(ctx.vec, fsv)
                        fsts[gi2] = ctx.dma("s",
                                            x_chunk_span_ap(y_res, dc, tq0 + t0l,
                                                            w=256), ot)
                    ctx.q.wait(ctx.vec, fsv)
                    ctx.t.wait(ctx.vec, fsv)
                ctx.barrier()
            stF.close()
    yscr = ctx.sbuf("l1f_yscr2", [128, 1024], F32, st)
    sts2 = {}
    for spn in range(TOK // 512):
        t0 = spn * 512
        lv1 = ctx.dma("q", lnw["xf"][:, 0:NCH * 512], x_span_ap(xin, t0))
        for dc in range(NCH):
            gi3 = spn * NCH + dc
            if gi3 >= 2:
                ctx.q.wait(ctx.vec, ovs2[gi3 - 2])
            lvy = ctx.dma("q", yscr[:, (gi3 % 2) * 512:(gi3 % 2) * 512 + 512],
                          x_chunk_span_ap(y_res, dc, t0))
            ctx.v.wait(*lv1)
            ctx.v.wait(*lvy)
            if gi3 >= 2:
                ctx.v.wait(*sts2[gi3 - 2])
            ot = lnw["bc"][:, (gi3 % 2) * 512:(gi3 % 2) * 512 + 512]
            ctx.v.tensor_tensor(ot, yscr[:, (gi3 % 2) * 512:(gi3 % 2) * 512 + 512],
                                lnw["xf"][:, dc * 512:(dc + 1) * 512], ALU.add)
            ctx.v.fence()
            if gi3 == 0:
                ovs2 = {}
            ovs2[gi3] = ctx.v.mark(ctx.vec)
            ctx.s.wait(ctx.vec, ovs2[gi3])
            sts2[gi3] = ctx.dma("s", x_chunk_span_ap(xout, dc, t0), ot)
        ctx.q.wait(ctx.vec, ovs2[spn * NCH + NCH - 1])
    ctx.barrier()
    st.close()


def emit_shift_cast(ctx, ydram, ybs, tok0, lnw):
    xf = lnw["xf"]
    xfw = xf.shape[1]
    stw = 513
    for sp in range(T // 512):
        t0 = tok0 + sp * 512
        at_start = (sp == 0)
        if at_start:
            lv = ctx.dma("q", AP(tensor=xf, offset=1,
                                 ap=[[xfw, 128], [stw, NCH], [1, 512]]),
                         x_span_ap(ydram, t0, w=512))
            ctx.v.wait(*lv)
            ctx.v.memset(AP(tensor=xf, offset=0,
                            ap=[[xfw, 128], [stw, NCH], [1, 1]]), 0.0)
            ctx.v.fence()
        else:
            lv = ctx.dma("q", AP(tensor=xf, offset=0,
                                 ap=[[xfw, 128], [stw, NCH], [1, stw]]),
                         x_span_ap(ydram, t0 - 1, w=stw))
            ctx.v.wait(*lv)
        for c in range(NCH):
            off = c * stw + (1 if c < 4 else 0)
            ctx.v.tensor_copy(ybs[:, c * T + sp * 512:c * T + sp * 512 + 512],
                              xf[:, off:off + 512])
        ctx.v.fence()
        vm = ctx.v.mark(ctx.vec)
        ctx.q.wait(ctx.vec, vm)



CONST_SPECS = [
    ("ident", (128, 128), BF16), ("ones_col", (128, 1), BF16),
    ("ones_row", (1, 128), BF16), ("mask_l0a", (128, 128), F32),
    ("mask_neg", (128, 128), F32), ("mask_negb", (128, 128), BF16),
    ("antidiag8", (128, 128), BF16), ("ident_f32", (128, 128), F32),
]

SUBLAYER_WEIGHTS = {
    0: [("l0a_qw", (D, D)), ("l0a_kw", (D, D)), ("l0a_vw", (D, D)),
        ("l0a_ow", (D, D)), ("l0a_qb", (128, NCH)), ("l0a_kb", (128, NCH)),
        ("l0a_vb", (1, D)), ("l0a_ob", (128, NCH))],
    7: [("l3f_wu", (D, 4096)), ("l3f_wd", (4096, D))],
    3: [("l1f_wqkv", (6 * D, 3072)), ("l1f_wo", (6 * D, D)),
        ("l1f_w1", (6 * D, 5632)), ("l1f_w2", (6 * 2816, D)),
        ("l1f_b1a", (6 * 128, 22)), ("l1f_b1g", (6 * 128, 22)),
        ("l1f_b2", (6 * 128, 8)), ("l1f_lng", (6 * 128, 8)),
        ("l1f_lnb", (6 * 128, 8)), ("l1f_qs", (6, 64)), ("l1f_ks", (6, 64)),
        ("l1f_cs", (128, 1024)), ("l1f_ssn", (128, 1024)),
        ("l1f_csi", (128, 1024)), ("l1f_ssi", (128, 1024)),
        ("mask_low", (128, 128))],
    1: [("l0f_win", (D, 4112)), ("l0f_wout", (2048, D)),
        ("l0f_cos", (128, 1024)), ("l0f_sin", (128, 1024)),
        ("l0f_ke", (8 * 128, 4096)), ("l0f_ve", (8 * 128, 4096))],
    5: [("l2f_wgw", (D, 8)), ("l2f_ws1", (D, 2048)), ("l2f_ws3", (D, 2048)),
        ("l2f_ws2", (2048, D)), ("l2f_we1", (8 * D, 2048)),
        ("l2f_we3", (8 * D, 2048)), ("l2f_we2", (8 * 2048, D))],
    4: [("l2a_wq", (D, D)), ("l2a_wk", (D, D)), ("l2a_wv", (D, D)),
        ("l2a_wo", (D, D)), ("l2a_lambda_q1", (1, 64)), ("l2a_lambda_k1", (1, 64)),
        ("l2a_lambda_q2", (1, 64)), ("l2a_lambda_k2", (1, 64)),
        ("l2a_lnw", (1, 128)), ("l2a_lnb", (1, 128)),
        ("l2a_cos", (128, 1024)), ("l2a_sa", (128, 1024)), ("l2a_sb", (128, 1024))],
    2: [("l1a_w", (D, 3072)), ("l1a_wp", (D, D)), ("l1a_qb", (128, NCH)),
        ("l1a_kb", (128, NCH)), ("l1a_vb", (1, D)), ("l1a_pb", (128, NCH))],
    6: [("l3a_qw", (D, D)), ("l3a_kw", (D, D)), ("l3a_vw", (D, D)),
        ("l3a_ow", (D, D)), ("l3a_qb", (128, NCH)), ("l3a_kb", (128, NCH)),
        ("l3a_vb", (1, D)), ("l3a_ob", (128, NCH)),
        ("l3a_gext", (16, 4224))],
}
SUBLAYER_EMIT = {}
SUBLAYER_EMIT[0] = emit_l0a
SUBLAYER_EMIT[2] = emit_l1a
SUBLAYER_EMIT[6] = emit_l3a
SUBLAYER_EMIT[7] = emit_l3f
SUBLAYER_EMIT[4] = emit_l2a
SUBLAYER_EMIT[5] = emit_l2f
SUBLAYER_EMIT[1] = emit_l0f
SUBLAYER_EMIT[3] = emit_l1f


def build_program(sublayers):
    nc = bass.Bass("TRN2", target_bir_lowering=False)
    ctx = Ctx(nc)
    ctx.wins = {}
    for name, shape, dt in CONST_SPECS:
        ctx.wins[name] = ctx.dram(name, list(shape), dt, kind="ExternalInput")
    for sl in sublayers:
        for name, shape in SUBLAYER_WEIGHTS.get(sl, []):
            ctx.wins[name] = ctx.dram(name, list(shape), F32, kind="ExternalInput")

    xa = ctx.dram("x_in", [NCH, 128, TOK], F32, kind="ExternalInput")
    xb = ctx.dram("x_out", [NCH, 128, TOK], F32, kind="ExternalOutput")
    xmid = ctx.dram("x_mid", [NCH, 128, TOK], F32, kind="Internal") \
        if len(sublayers) > 1 else None

    cst = {}
    cst["ones_col"] = ctx.sbuf("c_onescol", [128, 1], BF16)
    cst["ones_row"] = ctx.sbuf("c_onesrow", [1, 128], BF16)
    cst["ident_sb"] = ctx.sbuf("c_ident", [128, 128], BF16)
    cst["maskb_sb"] = ctx.sbuf("c_maskb", [128, 128], BF16)
    cst["ident_f32"] = ctx.sbuf("c_identf", [128, 128], F32)
    cst["cpool"] = ctx.sbuf("c_pool", [128, len(CONST_FLOATS)], F32)
    for sb_t, key in ((cst["ones_col"], "ones_col"), (cst["ones_row"], "ones_row"),
                      (cst["ident_sb"], "ident"), (cst["maskb_sb"], "mask_negb"),
                      (cst["ident_f32"], "ident_f32")):
        ctx.dma("q", sb_t[:], ctx.wins[key][:, :])
    for i, v in enumerate(CONST_FLOATS):
        ctx.g.memset(cst["cpool"][:, i:i + 1], float(v))
        nc.const_aps.register(F32, float(v), cst["cpool"][:, i:i + 1])
    ctx.g.mark(ctx.gp)
    ctx.barrier()

    n = len(sublayers)
    for i, sl in enumerate(sublayers):
        src = xa if i == 0 else xmid
        dst = xb if i == n - 1 else xmid
        SUBLAYER_EMIT[sl](ctx, cst, src, dst)
        ctx.barrier()
    for e in ctx.engs:
        for sem in ctx.lds + ctx.sts:
            e.wait(sem)
    ctx.es.close()
    return nc


def make_in_map(C, x_core, sublayers):
    m = {"x_in": x_core}
    for name, shape, dt in CONST_SPECS:
        m[name] = C[name]
    for sl in sublayers:
        for name, shape in SUBLAYER_WEIGHTS.get(sl, []):
            m[name] = C[name]
    return m


def kernel(**inputs):
    sublayers = sorted(SUBLAYER_EMIT.keys())
    C, xs = prep_host(inputs)
    nc = build_program(sublayers)
    in_maps = [make_in_map(C, xs[c], sublayers) for c in range(NCORES)]
    res = run_bass_kernel_spmd(nc, in_maps, core_ids=list(range(NCORES)))
    return unprep_out([r["x_out"] for r in res.results])


# version 04
# speedup vs baseline: 1.0061x; 1.0061x over previous
import math
import numpy as np
import ml_dtypes
from contextlib import ExitStack

import concourse.bass as bass
import concourse.mybir as mybir
from concourse.bass_utils import run_bass_kernel_spmd
from concourse.ap import AP

F32 = mybir.dt.float32
BF16 = mybir.dt.bfloat16
AF = mybir.ActivationFunctionType
ALU = mybir.AluOpType
AX = mybir.AxisListType
BFDT = ml_dtypes.bfloat16

NCORES = 8
SEQ = 2
T = 2048
TOK = SEQ * T
D = 1024
NCH = D // 128
NEG = -1e9
DBG = False
DBG_NAMES = None


class Sem:
    def __init__(self, nc, es, name):
        self.h = es.enter_context(nc.semaphore(name))
        self.n = 0
        self.name = name


class Eng:

    def __init__(self, name, eng):
        self.name = name
        self.e = eng
        self.last = None
        self.seen = {}
        self._marked_ins = None
        self._marked_val = 0

    def __getattr__(self, op):
        fn = getattr(self.e, op)

        def wrap(*a, **k):
            ins = fn(*a, **k)
            if op != "dma_start":
                self.last = ins
            return ins
        return wrap

    def wait(self, sem, val=None):
        if val is None:
            val = sem.n
        if val <= 0:
            return
        if self.seen.get(sem.name, -1) >= val:
            return
        self.e.wait_ge(sem.h, val)
        self.seen[sem.name] = val

    def mark(self, sem, by=1):
        assert self.last is not None, f"nothing to mark on {self.name}"
        if self._marked_ins is self.last:
            return self._marked_val
        sem.n += by
        self.last.then_inc(sem.h, by)
        self._marked_ins = self.last
        self._marked_val = sem.n
        return sem.n

    def fence(self):
        if self.last is None:
            return
        v = self.mark(self.own)
        self.wait(self.own, v)


class Ctx:
    def __init__(self, nc):
        self.nc = nc
        self.es = ExitStack()
        self.q = Eng("q", nc.sync)
        self.s = Eng("s", nc.scalar)
        self.v = Eng("v", nc.vector)
        self.t = Eng("t", nc.tensor)
        self.g = Eng("g", nc.gpsimd)
        self.engs = [self.q, self.s, self.v, self.t, self.g]
        self.lds = [Sem(nc, self.es, f"ld{i}") for i in range(4)]
        self.sts = [Sem(nc, self.es, f"st{i}") for i in range(4)]
        self.ld = self.lds[0]
        self.st = self.sts[0]
        self._ldi = 0
        self._sti = 0
        self.pe = Sem(nc, self.es, "pe")
        self.act = Sem(nc, self.es, "act")
        self.vec = Sem(nc, self.es, "vec")
        self.gp = Sem(nc, self.es, "gp")
        self._own = {"q": self.ld, "s": self.act, "v": self.vec,
                     "t": self.pe, "g": self.gp}
        for e in self.engs:
            e.own = self._own[e.name]
        self._consts = {}

    def dram(self, name, shape, dt=F32, kind="Internal"):
        return self.nc.dram_tensor(name, shape, dt, kind=kind).ap()

    def sbuf(self, name, shape, dt=F32, stack=None):
        return (stack or self.es).enter_context(self.nc.sbuf_tensor(name, shape, dt))

    def psum(self, name, shape, dt=F32, stack=None):
        return (stack or self.es).enter_context(self.nc.psum_tensor(name, shape, dt))

    def dma(self, eng, out, in_, **kw):
        e = {"q": self.q, "s": self.s}[eng]
        if eng == "q":
            sem = self.lds[self._ldi % len(self.lds)]
            self._ldi += 1
        else:
            sem = self.sts[self._sti % len(self.sts)]
            self._sti += 1
        e.wait(sem, sem.n)
        ins = e.dma_start(out=out, in_=in_, **kw)
        sem.n += 16
        ins.then_inc(sem.h, 16)
        return (sem, sem.n)

    def dbg(self, name, view):
        if not DBG or (DBG_NAMES and name not in DBG_NAMES):
            return
        self.barrier()
        shape = list(view.shape)
        ap = self.dram("dbg_" + name, shape,
                       view.dtype if view.dtype == F32 else view.dtype,
                       kind="ExternalOutput")
        self.dma("s", ap[tuple(slice(0, s) for s in shape)], view)
        self.barrier()

    def constf(self, val):
        key = float(val)
        assert key in self._consts, f"const {val} not pre-registered"
        return key

    def barrier(self):
        vals = {}
        for e in self.engs:
            sem = self._own[e.name]
            if e.last is not None:
                e.mark(sem)
                e.last = None
            vals[e.name] = sem.n
        for e in self.engs:
            for o in self.engs:
                if o is not e:
                    e.wait(self._own[o.name], vals[o.name])
            for sem in self.lds + self.sts:
                e.wait(sem)


CONST_FLOATS = [1e-5, 1e-8, 1.0, 0.5, -1.0]



def prep_host(I):
    C = {}
    f32 = np.float32
    C["ident"] = np.eye(128, dtype=BFDT)
    C["ident_f32"] = np.eye(128, dtype=f32)
    C["ones_col"] = np.ones((128, 1), dtype=BFDT)
    C["ones_row"] = np.ones((1, 128), dtype=BFDT)
    tri = np.triu(np.ones((128, 128), f32), k=1)
    C["mask_l0a"] = np.ascontiguousarray(tri * -100000.0)
    C["mask_neg"] = np.ascontiguousarray(tri * NEG)
    C["mask_negb"] = (tri * NEG).astype(BFDT)

    C["l0a_qw"] = np.ascontiguousarray(I["L0_attn_Q_w"].transpose(1, 0, 2).reshape(D, D))
    C["l0a_kw"] = np.ascontiguousarray(I["L0_attn_K_w"].transpose(1, 0, 2).reshape(D, D))
    C["l0a_vw"] = np.ascontiguousarray(I["L0_attn_V_w"].transpose(1, 0, 2).reshape(D, D))
    C["l0a_ow"] = np.ascontiguousarray(I["L0_attn_O_w"].reshape(D, D))
    C["l0a_qb"] = np.ascontiguousarray(I["L0_attn_Q_b"].reshape(NCH, 128).T)
    C["l0a_kb"] = np.ascontiguousarray(I["L0_attn_K_b"].reshape(NCH, 128).T)
    C["l0a_vb"] = np.ascontiguousarray(I["L0_attn_V_b"].reshape(1, D))
    C["l0a_ob"] = np.ascontiguousarray(I["L0_attn_O_b"].reshape(NCH, 128).T)

    C["l1a_w"] = I["L1_attn_w_attn"]
    C["l1a_wp"] = I["L1_attn_w_proj"]
    ba = I["L1_attn_b_attn"]
    C["l1a_qb"] = np.ascontiguousarray(ba[0:1024].reshape(NCH, 128).T)
    C["l1a_kb"] = np.ascontiguousarray(ba[1024:2048].reshape(NCH, 128).T)
    C["l1a_vb"] = np.ascontiguousarray(ba[2048:3072].reshape(1, D))
    C["l1a_pb"] = np.ascontiguousarray(I["L1_attn_b_proj"].reshape(NCH, 128).T)

    C["l3a_qw"] = np.ascontiguousarray(I["L3_attn_Wq"].T)
    C["l3a_kw"] = np.ascontiguousarray(I["L3_attn_Wk"].T)
    C["l3a_vw"] = np.ascontiguousarray(I["L3_attn_Wv"].T)
    C["l3a_ow"] = np.ascontiguousarray(I["L3_attn_Wo"].T)
    C["l3a_qb"] = np.ascontiguousarray(I["L3_attn_bq"].reshape(NCH, 128).T)
    C["l3a_kb"] = np.ascontiguousarray(I["L3_attn_bk"].reshape(NCH, 128).T)
    C["l3a_vb"] = np.ascontiguousarray(I["L3_attn_bv"].reshape(1, D))
    C["l3a_ob"] = np.ascontiguousarray(I["L3_attn_bo"].reshape(NCH, 128).T)
    tt = np.arange(4224)
    idx = np.clip(tt - 2174, -15, 15) + 15
    C["l3a_gext"] = np.ascontiguousarray(I["L3_attn_rel_bias"][idx, :].T)
    ad8 = np.zeros((128, 128), np.float32)
    ad8[np.arange(128), 127 - np.arange(128)] = 8.0
    C["antidiag8"] = ad8.astype(BFDT)

    for nm in ("Wq", "Wk", "Wv", "Wo"):
        C[f"l2a_{nm.lower()}"] = I[f"L2_attn_{nm}"]
    for nm in ("lambda_q1", "lambda_k1", "lambda_q2", "lambda_k2"):
        C[f"l2a_{nm}"] = np.ascontiguousarray(I[f"L2_attn_{nm}"].reshape(1, 64))
    C["l2a_lnw"] = np.ascontiguousarray(I["L2_attn_ln_w"].reshape(1, 128))
    C["l2a_lnb"] = np.ascontiguousarray(I["L2_attn_ln_b"].reshape(1, 128))
    dd = 64
    invf = 1.0 / 10000.0 ** (np.arange(0, dd, 2, dtype=f32) / dd)
    pos = np.arange(T, dtype=f32)
    fr = np.repeat(pos[:, None] * invf[None, :], 2, axis=-1)
    cosr = np.cos(fr)
    sinr = np.sin(fr)
    sA = np.zeros((T, dd), f32); sA[:, 0::2] = -sinr[:, 0::2]
    sB = np.zeros((T, dd), f32); sB[:, 1::2] = sinr[:, 1::2]
    def tile_tok(tab):
        return np.ascontiguousarray(
            tab.reshape(16, 128, dd).transpose(1, 0, 2).reshape(128, 16 * dd))
    C["l2a_cos"] = tile_tok(cosr)
    C["l2a_sa"] = tile_tok(sA)
    C["l2a_sb"] = tile_tok(sB)

    C["l0f_win"] = np.ascontiguousarray(I["L0_ffn_w_in"].T)
    C["l0f_wout"] = np.ascontiguousarray(I["L0_ffn_w_out"].T)
    half64 = np.arange(64, dtype=f32)
    invr = 1.0 / 10000.0 ** (half64 / 64.0)
    ang = np.arange(T, dtype=f32)[:, None] * invr[None, :]
    C["l0f_cos"] = np.ascontiguousarray(
        np.cos(ang).reshape(16, 128, 64).transpose(1, 0, 2).reshape(128, 1024))
    C["l0f_sin"] = np.ascontiguousarray(
        np.sin(ang).reshape(16, 128, 64).transpose(1, 0, 2).reshape(128, 1024))
    C["l0f_ke"] = np.ascontiguousarray(
        I["L0_ffn_k_ffwd"].transpose(1, 3, 0, 2).reshape(8 * 128, 16 * 256))
    vf = I["L0_ffn_v_ffwd"].reshape(16, 8, 2, 128, 128)
    C["l0f_ve"] = np.ascontiguousarray(
        vf.transpose(1, 3, 0, 2, 4).reshape(8 * 128, 16 * 2 * 128))

    C["l1f_wqkv"] = np.ascontiguousarray(I["L1_ffn_Wqkv"].reshape(6 * D, 3072))
    C["l1f_wo"] = np.ascontiguousarray(I["L1_ffn_Wo"].reshape(6 * D, D))
    w1p = np.zeros((6, D, 5632), f32)
    w1p[:, :, 0:2730] = I["L1_ffn_W1"][:, :, 0:2730]
    w1p[:, :, 2816:2816 + 2730] = I["L1_ffn_W1"][:, :, 2730:5460]
    C["l1f_w1"] = np.ascontiguousarray(w1p.reshape(6 * D, 5632))
    w2p = np.zeros((6, 2816, D), f32)
    w2p[:, 0:2730, :] = I["L1_ffn_W2"]
    C["l1f_w2"] = np.ascontiguousarray(w2p.reshape(6 * 2816, D))
    b1p = np.zeros((6, 2, 2816), f32)
    b1p[:, 0, 0:2730] = I["L1_ffn_b1"][:, 0:2730]
    b1p[:, 1, 0:2730] = I["L1_ffn_b1"][:, 2730:5460]
    C["l1f_b1a"] = np.ascontiguousarray(
        b1p[:, 0].reshape(6, 22, 128).transpose(0, 2, 1).reshape(6 * 128, 22))
    C["l1f_b1g"] = np.ascontiguousarray(
        b1p[:, 1].reshape(6, 22, 128).transpose(0, 2, 1).reshape(6 * 128, 22))
    C["l1f_b2"] = np.ascontiguousarray(
        I["L1_ffn_b2"].reshape(6, 8, 128).transpose(0, 2, 1).reshape(6 * 128, 8))
    C["l1f_lng"] = np.ascontiguousarray(
        I["L1_ffn_ln_g"].reshape(6, 8, 128).transpose(0, 2, 1).reshape(6 * 128, 8))
    C["l1f_lnb"] = np.ascontiguousarray(
        I["L1_ffn_ln_b"].reshape(6, 8, 128).transpose(0, 2, 1).reshape(6 * 128, 8))
    C["l1f_qs"] = I["L1_ffn_q_scale"]
    C["l1f_ks"] = I["L1_ffn_k_scale"]
    dh = 64
    half32 = np.arange(0, dh, 2, dtype=f32)
    invf2 = 1.0 / 10000.0 ** (half32 / dh)
    tpos = np.arange(T, dtype=f32)
    fr2 = tpos[:, None] * invf2[None, :]
    fr2 = np.concatenate([fr2, fr2], -1)
    base_sc = (half32 + 0.4 * dh) / (1.4 * dh)
    power = (tpos - T // 2) / (256 // 2)
    xsc = base_sc[None, :] ** power[:, None]
    xsc = np.concatenate([xsc, xsc], -1)
    cs_q = np.cos(fr2) * xsc
    sn_q = np.sin(fr2) * xsc
    cs_k = np.cos(fr2) / xsc
    sn_k = np.sin(fr2) / xsc
    def xtile(tab):
        return np.ascontiguousarray(
            tab.reshape(16, 128, dh).transpose(1, 0, 2).reshape(128, 16 * dh))
    ssn_q = np.concatenate([-sn_q[:, 0:32], sn_q[:, 32:64]], -1)
    ssn_k = np.concatenate([-sn_k[:, 0:32], sn_k[:, 32:64]], -1)
    C["l1f_cs"] = xtile(cs_q)
    C["l1f_ssn"] = xtile(ssn_q)
    C["l1f_csi"] = xtile(cs_k)
    C["l1f_ssi"] = xtile(ssn_k)
    C["mask_low"] = np.ascontiguousarray(
        np.tril(np.ones((128, 128), f32), k=-1) * NEG)

    C["l2f_wgw"] = I["L2_ffn_Wg"]
    C["l2f_ws1"] = I["L2_ffn_Ws1"]
    C["l2f_ws3"] = I["L2_ffn_Ws3"]
    C["l2f_ws2"] = I["L2_ffn_Ws2"]
    C["l2f_we1"] = np.ascontiguousarray(I["L2_ffn_We1"].reshape(8 * D, 2048))
    C["l2f_we3"] = np.ascontiguousarray(I["L2_ffn_We3"].reshape(8 * D, 2048))
    C["l2f_we2"] = np.ascontiguousarray(I["L2_ffn_We2"].reshape(8 * 2048, D))

    C["l3f_wu"] = np.ascontiguousarray(I["L3_ffn_w_up"].T)
    C["l3f_wd"] = np.ascontiguousarray(I["L3_ffn_w_down"].T)

    x = I["L0_attn_x"]
    xs = []
    for c in range(NCORES):
        xc = x[2 * c:2 * c + 2].transpose(2, 0, 1)
        xs.append(np.ascontiguousarray(xc).reshape(NCH, 128, TOK))
    return C, xs


def unprep_out(outs):
    res = np.empty((16, T, D), np.float32)
    for c, oc in enumerate(outs):
        res[2 * c:2 * c + 2] = oc.reshape(D, SEQ, T).transpose(1, 2, 0)
    return res



def x_span_ap(x_dram, t0, w=512, nch=NCH):
    return AP(tensor=x_dram.tensor,
              offset=x_dram.offset + t0,
              ap=[[TOK, 128], [128 * TOK, nch], [1, w]])


def x_chunk_span_ap(x_dram, c, t0, w=512):
    return AP(tensor=x_dram.tensor,
              offset=x_dram.offset + c * 128 * TOK + t0,
              ap=[[TOK, 128], [1, w]])


def col_block_ap(w_dram, dout_total, col0, w, nch=NCH):
    return AP(tensor=w_dram.tensor,
              offset=w_dram.offset + col0,
              ap=[[dout_total, 128], [128 * dout_total, nch], [1, w]])


class WStream:

    def __init__(self, ctx, st, tag, cols, stage_cols=None, stage=None, nch=NCH):
        self.ctx = ctx
        self.cols = cols
        self.nch = nch
        sc = stage_cols or cols
        self.stage = stage if stage is not None else \
            ctx.sbuf(f"{tag}_wst", [128, 2 * sc], F32, st)
        self.sc = sc
        self.bf = ctx.sbuf(f"{tag}_wbf", [128, nch * cols], BF16, st)

    def load(self, w_ap, row0=0):
        ctx = self.ctx
        if not hasattr(ctx, "_stage_guard"):
            ctx._stage_guard = {}
        guard = ctx._stage_guard.get(id(self.stage), 0)
        marks = {}
        for c in range(self.nch):
            half = (c % 2) * self.sc
            if c >= 2:
                ctx.q.wait(ctx.gp, marks[c - 2])
            elif guard:
                ctx.q.wait(ctx.gp, guard)
            lv = ctx.dma("q", self.stage[:, half:half + self.cols],
                         w_ap[row0 + c * 128: row0 + (c + 1) * 128, :])
            ctx.g.wait(*lv)
            ctx.g.tensor_copy(self.bf[:, c * self.cols:(c + 1) * self.cols],
                              self.stage[:, half:half + self.cols])
            marks[c] = ctx.g.mark(ctx.gp)
        ctx._stage_guard[id(self.stage)] = marks[self.nch - 1]
        return marks[self.nch - 1]

    def tile(self, c, j0, w=128):
        return self.bf[:, c * self.cols + j0: c * self.cols + j0 + w]


def emit_ln(ctx, cst, w, x_dram, xn_dst, ntok, tok0=0, eps=1e-5,
            fp32_cb=None, ln_width=D, banks=None, xn_ntok=None, scale_cb=None,
            shift=False, seq_start=None, affine=None, fp32_store=None):
    nsp = ntok // 512
    xn_ntok = xn_ntok or ntok
    ps_s = banks[0][0:1, :]
    ps_q = banks[1][0:1, :]
    ps_b1 = banks[2]
    ps_b2 = banks[3]
    r = w["rows"]
    mu = r[0:1, 0:512]
    mu2 = r[0:1, 512:1024]
    var = r[0:1, 1024:1536]
    lnv = r[0:1, 512:1024]
    rstd = r[0:1, 1536:2048]
    nm = r[0:1, 1024:1536]
    rb = w["rows_b"]
    rstd_b, nm_b = rb[0:1, 0:512], rb[0:1, 512:1024]
    rstdb, nmb = w["bc"][:, 0:512], w["bc"][:, 512:1024]
    xfw = w["xf"].shape[1]
    stw = 513 if shift else 512
    fst = {}
    for sp in range(nsp):
        t0 = tok0 + sp * 512
        if shift:
            at_start = (t0 == seq_start)
            if at_start:
                lv = ctx.dma("q", AP(tensor=w["xf"], offset=1,
                                     ap=[[xfw, 128], [stw, NCH], [1, 512]]),
                             x_span_ap(x_dram, t0, w=512))
                ctx.v.wait(*lv)
                ctx.v.memset(AP(tensor=w["xf"], offset=0,
                                ap=[[xfw, 128], [stw, NCH], [1, 1]]), 0.0)
                ctx.v.fence()
                zm = ctx.v.mark(ctx.vec)
                ctx.s.wait(ctx.vec, zm)
            else:
                lv = ctx.dma("q", AP(tensor=w["xf"], offset=0,
                                     ap=[[xfw, 128], [stw, NCH], [1, stw]]),
                             x_span_ap(x_dram, t0 - 1, w=stw))
                ctx.s.wait(*lv)
                ctx.v.wait(*lv)

            def csl(c):
                off = c * stw + (1 if c < 4 else 0)
                return slice(off, off + 512)
        else:
            lv = ctx.dma("q", w["xf"][:, 0:NCH * 512], x_span_ap(x_dram, t0))
            ctx.s.wait(*lv)
            ctx.v.wait(*lv)

            def csl(c):
                return slice(c * 512, (c + 1) * 512)
        cm = {}
        for c in range(NCH):
            hb = slice((c % 2) * 512, (c % 2) * 512 + 512)
            if c >= 2:
                ctx.s.wait(ctx.pe, cm[c - 2])
                ctx.v.wait(ctx.pe, cm[c - 2])
            ctx.s.copy(w["xb"][:, hb], w["xf"][:, csl(c)])
            am = ctx.s.mark(ctx.act)
            ctx.v.tensor_tensor(w["xqb"][:, hb], w["xf"][:, csl(c)],
                                w["xf"][:, csl(c)], ALU.mult)
            vm = ctx.v.mark(ctx.vec)
            ctx.t.wait(ctx.act, am)
            ctx.t.wait(ctx.vec, vm)
            ctx.t.matmul(ps_s, cst["ones_col"][:], w["xb"][:, hb],
                         start=(c == 0), stop=(c == NCH - 1))
            ctx.t.matmul(ps_q, cst["ones_col"][:], w["xqb"][:, hb],
                         start=(c == 0), stop=(c == NCH - 1))
            cm[c] = ctx.t.mark(ctx.pe)
        pm = cm[NCH - 1]
        ctx.s.wait(ctx.pe, pm)
        ctx.s.mul(mu, ps_s, 1.0 / ln_width)
        ctx.s.fence()
        ctx.s.activation(mu2, mu, AF.Square)
        am2 = ctx.s.mark(ctx.act)
        ctx.v.wait(ctx.act, am2)
        ctx.v.scalar_tensor_tensor(var, ps_q, 1.0 / ln_width, mu2,
                                   ALU.mult, ALU.subtract)
        vm2 = ctx.v.mark(ctx.vec)
        ctx.s.wait(ctx.vec, vm2)
        ctx.s.activation(lnv, var, AF.Ln, bias=float(eps))
        ctx.s.fence()
        ctx.s.activation(rstd, lnv, AF.Exp, scale=-0.5)
        am3 = ctx.s.mark(ctx.act)
        if scale_cb is not None:
            scale_cb(ctx, w, mu, var, rstd)
        ctx.v.wait(ctx.act, am3)
        ctx.v.tensor_tensor(nm, mu, rstd, ALU.mult)
        ctx.v.fence()
        vm3 = ctx.v.mark(ctx.vec)
        ctx.s.wait(ctx.vec, vm3)
        ctx.s.fence()
        ctx.s.copy(rstd_b, rstd)
        ctx.s.copy(nm_b, nm)
        am4 = ctx.s.mark(ctx.act)
        ctx.t.wait(ctx.act, am4)
        ctx.t.matmul(ps_b1[:, :], cst["ones_row"][:], rstd_b, start=True, stop=True)
        ctx.t.matmul(ps_b2[:, :], cst["ones_row"][:], nm_b, start=True, stop=True)
        pm2 = ctx.t.mark(ctx.pe)
        ctx.s.wait(ctx.pe, pm2)
        ctx.s.copy(rstdb, ps_b1[:, :])
        ctx.s.copy(nmb, ps_b2[:, :])
        am5 = ctx.s.mark(ctx.act)
        ctx.v.wait(ctx.act, am5)
        for c in range(NCH):
            dsl = slice(c * xn_ntok + sp * 512, c * xn_ntok + sp * 512 + 512)
            ctx.v.tensor_tensor(w["tmp"][:], w["xf"][:, csl(c)], rstdb, ALU.mult)
            ctx.v.fence()
            if affine is not None:
                gfn, bfn = affine
                ctx.v.tensor_tensor(w["tmp"][:], w["tmp"][:], nmb, ALU.subtract)
                ctx.v.fence()
                ctx.v.scalar_tensor_tensor(xn_dst[:, dsl], w["tmp"][:],
                                           gfn(c), bfn(c), ALU.mult, ALU.add)
            else:
                if xn_dst is not None:
                    ctx.v.tensor_tensor(xn_dst[:, dsl], w["tmp"][:], nmb,
                                        ALU.subtract)
                if fp32_cb is not None:
                    ctx.v.tensor_tensor(fp32_cb(c, sp), w["tmp"][:], nmb,
                                        ALU.subtract)
                if fp32_store is not None:
                    ydram, yscr = fp32_store
                    gi = sp * NCH + c
                    if gi >= 2:
                        ctx.v.wait(*fst[gi - 2])
                    ysl = yscr[:, (gi % 2) * 512:(gi % 2) * 512 + 512]
                    ctx.v.tensor_tensor(ysl, w["tmp"][:], nmb, ALU.subtract)
                    ctx.v.fence()
                    fm = ctx.v.mark(ctx.vec)
                    ctx.s.wait(ctx.vec, fm)
                    fst[gi] = ctx.dma("s", x_chunk_span_ap(ydram, c, t0), ysl)
            ctx.v.fence()
        vm4 = ctx.v.mark(ctx.vec)
        ctx.q.wait(ctx.vec, vm4)
        ctx.t.wait(ctx.vec, vm4)


def ln_work(ctx, st, tag):
    return {
        "xf": ctx.sbuf(f"{tag}_xf", [128, NCH * 513], F32, st),
        "xb": ctx.sbuf(f"{tag}_xb", [128, 1024], BF16, st),
        "xqb": ctx.sbuf(f"{tag}_xqb", [128, 1024], BF16, st),
        "rows": ctx.sbuf(f"{tag}_rows", [1, 512 * 4], F32, st),
        "rows_b": ctx.sbuf(f"{tag}_rowsb", [1, 1024], BF16, st),
        "bc": ctx.sbuf(f"{tag}_bc", [128, 1024], F32, st),
        "tmp": ctx.sbuf(f"{tag}_tmp", [128, 512], F32, st),
    }


def emit_residual_out(ctx, cst, lnw, xin, xout, tok0, ntok, proj_cb, bias_sb,
                      banks, extra_add=None, scale=None):
    nsp = ntok // 512
    xnew0 = lnw["bc"][:, 0:512]
    xnew1 = lnw["bc"][:, 512:1024]
    evs = {}
    stores = {}
    for spn in range(nsp):
        t0 = tok0 + spn * 512
        lv = ctx.dma("q", lnw["xf"][:, 0:NCH * 512], x_span_ap(xin, t0))
        for dc in range(NCH):
            i = spn * NCH + dc
            ps = banks[dc % 2]
            if i >= 2:
                ctx.t.wait(ctx.vec, evs[i - 2])
            proj_cb(dc, spn, ps)
            pm = ctx.t.mark(ctx.pe)
            ctx.s.wait(ctx.pe, pm)
            if bias_sb is not None:
                ctx.s.activation(lnw["tmp"][:], ps[:, :], AF.Identity,
                                 bias=bias_sb[:, dc:dc + 1],
                                 scale=(scale or 1.0))
            else:
                ctx.s.activation(lnw["tmp"][:], ps[:, :], AF.Identity,
                                 scale=(scale or 1.0))
            am = ctx.s.mark(ctx.act)
            xnew = xnew0 if i % 2 == 0 else xnew1
            ctx.v.wait(ctx.act, am)
            ctx.v.wait(*lv)
            if i >= 2:
                ctx.v.wait(*stores[i - 2])
            if extra_add is not None:
                ctx.v.tensor_tensor(lnw["tmp"][:], lnw["tmp"][:],
                                    extra_add(dc, spn), ALU.add)
                ctx.v.fence()
            ctx.v.tensor_tensor(xnew, lnw["tmp"][:],
                                lnw["xf"][:, dc * 512:(dc + 1) * 512], ALU.add)
            evs[i] = ctx.v.mark(ctx.vec)
            ctx.s.wait(ctx.vec, evs[i])
            stores[i] = ctx.dma("s", x_chunk_span_ap(xout, dc, t0), xnew)
        ctx.q.wait(ctx.vec, evs[spn * NCH + NCH - 1])


def emit_softmax_attn_head(ctx, cst, qf, kf, vtm, s_sb, p_sb, pt_sb, stat,
                           banks, ps_t, ps_pv, mask, r0, scale, dv, vcol,
                           acat_cb, causal=True, bias_mm=None, width_full=T,
                           normalize=True, dbg_tag=None):
    dh = dv if r0 is None else 64
    bank_ev = {0: 0, 1: 0}
    nslots = 2 if s_sb.shape[1] >= 2 * width_full else 1
    exp_am = {}
    for qs in range(4):
        qts = [qs * 4 + i for i in range(4)]
        pns = {}
        for i, qt in enumerate(qts):
            negmax = stat[:, 4 * i:4 * i + 1]
            negmax_s = stat[:, 4 * i + 1:4 * i + 2]
            sums = stat[:, 4 * i + 2:4 * i + 3]
            recip = stat[:, 4 * i + 3:4 * i + 4]
            slot = qt % nslots
            sofs = slot * width_full
            wk = (qt + 1) * 128 if causal else width_full
            nks = (wk + 511) // 512
            for ks in range(nks):
                kw = min(512, wk - ks * 512)
                b = ks % 2
                ps = banks[b]
                ctx.t.wait(ctx.act, bank_ev[b])
                ctx.t.matmul(ps[:, 0:kw],
                             qf[r0:r0 + dh, qt * 128:(qt + 1) * 128] if r0 is not None
                             else qf[:, qt * 128:(qt + 1) * 128],
                             kf[r0:r0 + dh, ks * 512:ks * 512 + kw] if r0 is not None
                             else kf[:, ks * 512:ks * 512 + kw],
                             start=True, stop=(bias_mm is None))
                if bias_mm is not None:
                    bias_mm(ctx, ps, ks, kw, qt)
                pm = ctx.t.mark(ctx.pe)
                if ks == 0 and slot in exp_am:
                    ctx.s.wait(ctx.act, exp_am[slot])
                ctx.s.wait(ctx.pe, pm)
                ctx.s.copy(s_sb[:, sofs + ks * 512:sofs + ks * 512 + kw],
                           ps[:, 0:kw])
                bank_ev[b] = ctx.s.mark(ctx.act)
            ctx.v.wait(ctx.act, bank_ev[(nks - 1) % 2])
            if causal and mask is not None:
                ctx.v.tensor_tensor(s_sb[:, sofs + qt * 128:sofs + (qt + 1) * 128],
                                    s_sb[:, sofs + qt * 128:sofs + (qt + 1) * 128],
                                    mask[:], ALU.add)
                ctx.v.fence()
            ctx.v.reduce_max(negmax, s_sb[:, sofs:sofs + wk], axis=AX.X,
                             negate=True)
            vm = ctx.v.mark(ctx.vec)
            ctx.s.wait(ctx.vec, vm)
            ctx.s.mul(negmax_s, negmax, float(scale))
            ctx.s.memzero(sums)
            ctx.s.fence()
            ctx.s.activation(p_sb[:, i * T:i * T + wk], s_sb[:, sofs:sofs + wk],
                             AF.Exp, scale=float(scale), bias=negmax_s,
                             accum_out=sums)
            am = ctx.s.mark(ctx.act)
            exp_am[slot] = am
            if dbg_tag and qt == 15:
                ctx.dbg(dbg_tag + "_s15", s_sb[:, sofs:sofs + width_full])
                ctx.dbg(dbg_tag + "_praw15", p_sb[:, i * T:i * T + wk])
                ctx.dbg(dbg_tag + "_stat15", stat[:])
            ctx.v.wait(ctx.act, am)
            if normalize:
                ctx.v.reciprocal(recip, sums)
                ctx.v.fence()
                ctx.v.tensor_scalar(p_sb[:, i * T:i * T + wk],
                                    p_sb[:, i * T:i * T + wk],
                                    recip, None, ALU.mult)
            pns[i] = ctx.v.mark(ctx.vec)
        nkt = qts[-1] + 1 if causal else width_full // 128
        first_pv = True
        ctx.t.wait(ctx.vec, pns[3])
        for kt in range(nkt):
            idx = [i for i, qt in enumerate(qts) if (not causal) or qt >= kt]
            for j, i in enumerate(idx):
                ctx.t.matmul(ps_t[:, i * 128:(i + 1) * 128],
                             p_sb[:, i * T + kt * 128:i * T + (kt + 1) * 128],
                             cst["ident_sb"][:], is_transpose=True,
                             start=(j == 0), stop=(j == len(idx) - 1))
            tm = ctx.t.mark(ctx.pe)
            ctx.v.wait(ctx.pe, tm)
            off = idx[0] * 128
            ctx.v.tensor_copy(pt_sb[:, off:512], ps_t[:, off:512])
            ptm = ctx.v.mark(ctx.vec)
            ctx.t.wait(ctx.vec, ptm)
            col0, w = vcol(kt)
            ctx.t.matmul(ps_pv[0:dv, off:512], vtm[:, col0:col0 + w],
                         pt_sb[:, off:512],
                         start=first_pv, stop=(kt == nkt - 1))
            first_pv = False
        pvm = ctx.t.mark(ctx.pe)
        acat_cb(qs, pvm)


def emit_sparsemax_attn_head(ctx, cst, qf, kf, vtm, sp_all, stat16, pt_sb,
                             banks, ps_t, ps_pv, mask, r0, scale, dv, vcol,
                             acat_cb, scr_bf, niter=8):
    OFF = [128 * qt * (qt + 1) // 2 for qt in range(17)]
    m_all = stat16[:, 0:16]
    negtau = stat16[:, 16:32]
    S_all = stat16[:, 32:48]
    cnt = stat16[:, 48:64]
    rc = stat16[:, 64:80]
    postau = stat16[:, 80:96]
    bank_ev = {0: 0, 1: 0}
    for qt in range(16):
        wk = (qt + 1) * 128
        nks = (wk + 511) // 512
        for ks in range(nks):
            kw = min(512, wk - ks * 512)
            b = ks % 2
            ctx.t.wait(ctx.act, bank_ev[b])
            ctx.t.matmul(banks[b][:, 0:kw],
                         qf[r0:r0 + 64, qt * 128:(qt + 1) * 128],
                         kf[r0:r0 + 64, ks * 512:ks * 512 + kw],
                         start=True, stop=True)
            pm = ctx.t.mark(ctx.pe)
            ctx.s.wait(ctx.pe, pm)
            ctx.s.activation(sp_all[:, OFF[qt] + ks * 512:OFF[qt] + ks * 512 + kw],
                             banks[b][:, 0:kw], AF.Identity, scale=float(scale))
            bank_ev[b] = ctx.s.mark(ctx.act)
        ctx.v.wait(ctx.act, bank_ev[(nks - 1) % 2])
        ctx.v.tensor_tensor(sp_all[:, OFF[qt] + qt * 128:OFF[qt] + (qt + 1) * 128],
                            sp_all[:, OFF[qt] + qt * 128:OFF[qt] + (qt + 1) * 128],
                            cst["maskb_sb"][:], ALU.add)
        ctx.v.fence()
        ctx.v.reduce_max(m_all[:, qt:qt + 1], sp_all[:, OFF[qt]:OFF[qt] + wk],
                         axis=AX.X)
    ctx.v.fence()
    ctx.v.tensor_scalar(negtau, m_all, -1.0, 1.0, ALU.mult, ALU.add)
    ctx.v.tensor_scalar_add(postau, m_all, -1.0)
    ctx.v.fence()
    vm = ctx.v.mark(ctx.vec)
    ctx.s.wait(ctx.vec, vm)
    for it in range(niter):
        ctx.s.memzero(S_all)
        ctx.s.fence()
        for qt in range(16):
            wk = (qt + 1) * 128
            ctx.s.activation(scr_bf[:, 0:wk], sp_all[:, OFF[qt]:OFF[qt] + wk],
                             AF.Relu, bias=negtau[:, qt:qt + 1],
                             accum_out=S_all[:, qt:qt + 1])
            ctx.s.fence()
            ctx.v.tensor_scalar(scr_bf[:, T:T + wk], sp_all[:, OFF[qt]:OFF[qt] + wk],
                                postau[:, qt:qt + 1], None, ALU.is_gt, ALU.add,
                                accum_out=cnt[:, qt:qt + 1])
            ctx.v.fence()
        am = ctx.s.mark(ctx.act)
        ctx.v.fence()
        ctx.v.tensor_scalar_max(cnt, cnt, 1.0)
        ctx.v.fence()
        ctx.v.reciprocal(rc, cnt)
        ctx.v.wait(ctx.act, am)
        ctx.v.tensor_scalar_add(S_all, S_all, -1.0)
        ctx.v.fence()
        ctx.v.tensor_tensor(S_all, S_all, rc, ALU.mult)
        ctx.v.fence()
        ctx.v.tensor_tensor(negtau, negtau, S_all, ALU.subtract)
        ctx.v.tensor_tensor(postau, postau, S_all, ALU.add)
        ctx.v.fence()
        vm = ctx.v.mark(ctx.vec)
        ctx.s.wait(ctx.vec, vm)
    for qs in range(4):
        qts = [qs * 4 + i for i in range(4)]
        for qt in qts:
            wk = (qt + 1) * 128
            ctx.s.activation(sp_all[:, OFF[qt]:OFF[qt] + wk],
                             sp_all[:, OFF[qt]:OFF[qt] + wk],
                             AF.Relu, bias=negtau[:, qt:qt + 1])
        am2 = ctx.s.mark(ctx.act)
        ctx.t.wait(ctx.act, am2)
        ctx.v.wait(ctx.act, am2)
        nkt = qts[-1] + 1
        first_pv = True
        for kt in range(nkt):
            idx = [i for i, qt in enumerate(qts) if qt >= kt]
            for j, i in enumerate(idx):
                qt = qts[i]
                ctx.t.matmul(ps_t[:, i * 128:(i + 1) * 128],
                             sp_all[:, OFF[qt] + kt * 128:OFF[qt] + (kt + 1) * 128],
                             cst["ident_sb"][:], is_transpose=True,
                             start=(j == 0), stop=(j == len(idx) - 1))
            tm = ctx.t.mark(ctx.pe)
            ctx.v.wait(ctx.pe, tm)
            off = idx[0] * 128
            ctx.v.tensor_copy(pt_sb[:, off:512], ps_t[:, off:512])
            ptm = ctx.v.mark(ctx.vec)
            ctx.t.wait(ctx.vec, ptm)
            col0, w = vcol(kt)
            ctx.t.matmul(ps_pv[0:dv, off:512], vtm[:, col0:col0 + w],
                         pt_sb[:, off:512],
                         start=first_pv, stop=(kt == nkt - 1))
            first_pv = False
        pvm = ctx.t.mark(ctx.pe)
        acat_cb(qs, pvm)



def emit_l0a(ctx, cst, xin, xout):
    st = ExitStack()
    xn = ctx.sbuf("l0a_xn", [128, NCH * T], BF16, st)
    lnw = ln_work(ctx, st, "l0a")
    wstage = ctx.sbuf("l0a_wstage", [128, 2 * D], F32, st)
    ws_qk = WStream(ctx, st, "l0a_qk", 128, stage=wstage)
    ws_v = WStream(ctx, st, "l0a_v", D, stage=wstage)
    ws_o = WStream(ctx, st, "l0a_o", D, stage=wstage)
    qb_sb = ctx.sbuf("l0a_qb_s", [128, NCH], F32, st)
    kb_sb = ctx.sbuf("l0a_kb_s", [128, NCH], F32, st)
    ob_sb = ctx.sbuf("l0a_ob_s", [128, NCH], F32, st)
    vb_bc = ctx.sbuf("l0a_vbbc", [128, D], F32, st)
    mask = ctx.sbuf("l0a_mask", [128, 128], F32, st)
    qf = ctx.sbuf("l0a_qf", [128, T], BF16, st)
    kf = ctx.sbuf("l0a_kf", [128, T], BF16, st)
    vtm = ctx.sbuf("l0a_vtm", [128, 16 * D], BF16, st)
    acat = ctx.sbuf("l0a_acat", [128, NCH * T], BF16, st)
    s_sb = lnw["xf"][:, 0:2 * T]
    p_sb = ctx.sbuf("l0a_p", [128, 4 * T], BF16, st)
    pt_sb = ctx.sbuf("l0a_pt", [128, 512], BF16, st)
    stat = ctx.sbuf("l0a_stat", [128, 16], F32, st)
    banks = [ctx.psum(f"l0a_ps{i}", [128, 512], F32, st) for i in range(4)]
    ps_t = ctx.psum("l0a_pst", [128, 1024], BF16, st)
    ps_pv = ctx.psum("l0a_pspv", [128, 512], F32, st)

    W = ctx.wins
    for dst, key in ((qb_sb, "l0a_qb"), (kb_sb, "l0a_kb"), (ob_sb, "l0a_ob"),
                     (mask, "mask_l0a")):
        ctx.dma("q", dst[:], W[key][:, :])
    ctx.dma("q", vb_bc[:], W["l0a_vb"][0:1, :].broadcast_to([128, D]))
    ws_v.load(W["l0a_vw"])
    ws_o.load(W["l0a_ow"])
    ctx.barrier()

    for sq in range(SEQ):
        tok0 = sq * T
        emit_ln(ctx, cst, lnw, xin, xn, T, tok0=tok0, banks=banks)
        ctx.barrier()
        if sq == 0:
            ctx.dbg("xn", xn[:])

        ev = {}
        for kt in range(16):
            for half in range(2):
                i = kt * 2 + half
                ps = banks[i % 4]
                if i >= 4:
                    ctx.t.wait(ctx.vec, ev[i - 4])
                for c in range(NCH):
                    ctx.t.matmul(ps[:, :],
                                 xn[:, c * T + kt * 128:c * T + (kt + 1) * 128],
                                 ws_v.bf[:, c * D + half * 512:c * D + half * 512 + 512],
                                 start=(c == 0), stop=(c == NCH - 1))
                pm = ctx.t.mark(ctx.pe)
                ctx.v.wait(ctx.pe, pm)
                ctx.v.tensor_tensor(
                    vtm[:, kt * D + half * 512:kt * D + half * 512 + 512],
                    ps[:, :], vb_bc[:, half * 512:half * 512 + 512], ALU.add)
                ev[i] = ctx.v.mark(ctx.vec)
        ctx.barrier()
        if sq == 0:
            ctx.dbg("vtm", vtm[:])

        for g in range(8):
            for wb_is_q in (True, False):
                wdram = W["l0a_qw"] if wb_is_q else W["l0a_kw"]
                dst = qf if wb_is_q else kf
                bias = qb_sb if wb_is_q else kb_sb
                lv = ctx.dma("q", ws_qk.stage[:, 0:NCH * 128],
                             col_block_ap(wdram, D, g * 128, 128))
                ctx.g.wait(*lv)
                for c in range(NCH):
                    ctx.g.tensor_copy(ws_qk.bf[:, c * 128:(c + 1) * 128],
                                      ws_qk.stage[:, c * 128:(c + 1) * 128])
                gm = ctx.g.mark(ctx.gp)
                ctx.t.wait(ctx.gp, gm)
                for spn in range(4):
                    ps = banks[spn]
                    for c in range(NCH):
                        ctx.t.matmul(ps[:, :], ws_qk.bf[:, c * 128:(c + 1) * 128],
                                     xn[:, c * T + spn * 512:c * T + spn * 512 + 512],
                                     start=(c == 0), stop=(c == NCH - 1))
                    pm = ctx.t.mark(ctx.pe)
                    ctx.s.wait(ctx.pe, pm)
                    ctx.s.activation(dst[:, spn * 512:spn * 512 + 512], ps[:, :],
                                     AF.Identity, bias=bias[:, g:g + 1])
                    ctx.s.mark(ctx.act)
                ctx.barrier()

            for hh in range(2):
                h = 2 * g + hh

                def acat_cb(qs, pvm, h=h):
                    ctx.s.wait(ctx.pe, pvm)
                    ctx.s.copy(acat[(h % 2) * 64:(h % 2) * 64 + 64,
                                    (h // 2) * T + qs * 512:(h // 2) * T + qs * 512 + 512],
                               ps_pv[0:64, :])
                    sm = ctx.s.mark(ctx.act)
                    ctx.t.wait(ctx.act, sm)

                emit_softmax_attn_head(
                    ctx, cst, qf, kf, vtm, s_sb, p_sb, pt_sb, stat,
                    banks, ps_t, ps_pv, mask, r0=hh * 64, scale=0.125, dv=64,
                    vcol=lambda kt, h=h: (kt * D + h * 64, 64),
                    acat_cb=acat_cb, causal=True,
                    dbg_tag=("h0" if (sq == 0 and g == 0 and hh == 0) else None))
                if sq == 0 and g == 0 and hh == 0:
                    ctx.dbg("p0", p_sb[:])
            ctx.barrier()
            if sq == 0 and g == 0:
                ctx.dbg("qf", qf[:])
                ctx.dbg("kf", kf[:])
                ctx.dbg("acat0", acat[:, 0:T])

        def oproj(dc, spn, ps):
            for c in range(NCH):
                ctx.t.matmul(ps[:, :], ws_o.tile(c, dc * 128),
                             acat[:, c * T + spn * 512:c * T + spn * 512 + 512],
                             start=(c == 0), stop=(c == NCH - 1))

        emit_residual_out(ctx, cst, lnw, xin, xout, tok0, T, oproj, ob_sb,
                          banks)
        ctx.barrier()
    st.close()



def emit_l1a(ctx, cst, xin, xout):
    st = ExitStack()
    xn = ctx.sbuf("l1a_xn", [128, NCH * T], BF16, st)
    lnw = ln_work(ctx, st, "l1a")
    wstage = ctx.sbuf("l1a_wstage", [128, 2 * D], F32, st)
    ws_qk = WStream(ctx, st, "l1a_qk", 128, stage=wstage)
    ws_v = WStream(ctx, st, "l1a_vo", D, stage=wstage)
    ws_o = ws_v
    qb_sb = ctx.sbuf("l1a_qb_s", [128, NCH], F32, st)
    kb_sb = ctx.sbuf("l1a_kb_s", [128, NCH], F32, st)
    pb_sb = ctx.sbuf("l1a_pb_s", [128, NCH], F32, st)
    vb_bc = ctx.sbuf("l1a_vbbc", [128, D], F32, st)
    qf = ctx.sbuf("l1a_qf", [128, T], BF16, st)
    kf = ctx.sbuf("l1a_kf", [128, T], BF16, st)
    vtm = ctx.sbuf("l1a_vtm", [128, 16 * D], BF16, st)
    acat = ctx.sbuf("l1a_acat", [128, NCH * T], BF16, st)
    sp_all = ctx.sbuf("l1a_sp", [128, 17408], BF16, st)
    scr_bf = lnw["xf"][:].bitcast(BF16)
    stat16 = ctx.sbuf("l1a_stat", [128, 96], F32, st)
    pt_sb = ctx.sbuf("l1a_pt", [128, 512], BF16, st)
    banks = [ctx.psum(f"l1a_ps{i}", [128, 512], F32, st) for i in range(4)]
    ps_t = ctx.psum("l1a_pst", [128, 1024], BF16, st)
    ps_pv = ctx.psum("l1a_pspv", [128, 512], F32, st)
    W = ctx.wins

    for dst, key in ((qb_sb, "l1a_qb"), (kb_sb, "l1a_kb"), (pb_sb, "l1a_pb")):
        ctx.dma("q", dst[:], W[key][:, :])
    ctx.dma("q", vb_bc[:], W["l1a_vb"][0:1, :].broadcast_to([128, D]))
    ctx.barrier()

    for sq in range(SEQ):
        tok0 = sq * T
        ws_v.load(W["l1a_w"][:, 2048:3072])
        emit_ln(ctx, cst, lnw, xin, xn, T, tok0=tok0, banks=banks)
        ctx.barrier()

        ev = {}
        for kt in range(16):
            for half in range(2):
                i = kt * 2 + half
                ps = banks[i % 4]
                if i >= 4:
                    ctx.t.wait(ctx.vec, ev[i - 4])
                for c in range(NCH):
                    ctx.t.matmul(ps[:, :],
                                 xn[:, c * T + kt * 128:c * T + (kt + 1) * 128],
                                 ws_v.bf[:, c * D + half * 512:c * D + half * 512 + 512],
                                 start=(c == 0), stop=(c == NCH - 1))
                pm = ctx.t.mark(ctx.pe)
                ctx.v.wait(ctx.pe, pm)
                ctx.v.tensor_tensor(
                    vtm[:, kt * D + half * 512:kt * D + half * 512 + 512],
                    ps[:, :], vb_bc[:, half * 512:half * 512 + 512], ALU.add)
                ev[i] = ctx.v.mark(ctx.vec)
        ctx.barrier()

        for g in range(8):
            for wb_is_q in (True, False):
                col0 = g * 128 if wb_is_q else 1024 + g * 128
                dst = qf if wb_is_q else kf
                bias = qb_sb if wb_is_q else kb_sb
                lv = ctx.dma("q", ws_qk.stage[:, 0:NCH * 128],
                             col_block_ap(W["l1a_w"], 3072, col0, 128))
                ctx.g.wait(*lv)
                for c in range(NCH):
                    ctx.g.tensor_copy(ws_qk.bf[:, c * 128:(c + 1) * 128],
                                      ws_qk.stage[:, c * 128:(c + 1) * 128])
                gm = ctx.g.mark(ctx.gp)
                ctx.t.wait(ctx.gp, gm)
                for spn in range(4):
                    ps = banks[spn]
                    for c in range(NCH):
                        ctx.t.matmul(ps[:, :], ws_qk.bf[:, c * 128:(c + 1) * 128],
                                     xn[:, c * T + spn * 512:c * T + spn * 512 + 512],
                                     start=(c == 0), stop=(c == NCH - 1))
                    pm = ctx.t.mark(ctx.pe)
                    ctx.s.wait(ctx.pe, pm)
                    ctx.s.activation(dst[:, spn * 512:spn * 512 + 512], ps[:, :],
                                     AF.Identity, bias=bias[:, g:g + 1])
                    ctx.s.mark(ctx.act)
                ctx.barrier()

            for hh in range(2):
                h = 2 * g + hh

                def acat_cb(qs, pvm, h=h):
                    ctx.s.wait(ctx.pe, pvm)
                    ctx.s.copy(acat[(h % 2) * 64:(h % 2) * 64 + 64,
                                    (h // 2) * T + qs * 512:(h // 2) * T + qs * 512 + 512],
                               ps_pv[0:64, :])
                    sm = ctx.s.mark(ctx.act)
                    ctx.t.wait(ctx.act, sm)

                emit_sparsemax_attn_head(
                    ctx, cst, qf, kf, vtm, sp_all, stat16, pt_sb,
                    banks, ps_t, ps_pv, None, r0=hh * 64, scale=0.125, dv=64,
                    vcol=lambda kt, h=h: (kt * D + h * 64, 64),
                    acat_cb=acat_cb, scr_bf=scr_bf)
            ctx.barrier()

        ws_o.load(W["l1a_wp"])
        ctx.barrier()

        def oproj(dc, spn, ps):
            for c in range(NCH):
                ctx.t.matmul(ps[:, :], ws_o.tile(c, dc * 128),
                             acat[:, c * T + spn * 512:c * T + spn * 512 + 512],
                             start=(c == 0), stop=(c == NCH - 1))

        emit_residual_out(ctx, cst, lnw, xin, xout, tok0, T, oproj, pb_sb,
                          banks)
        ctx.barrier()
    st.close()



def emit_l3a(ctx, cst, xin, xout):
    st = ExitStack()
    xn = ctx.sbuf("l3a_xn", [128, NCH * T], BF16, st)
    lnw = ln_work(ctx, st, "l3a")
    wstage = ctx.sbuf("l3a_wstage", [128, 2 * D], F32, st)
    ws_qk = WStream(ctx, st, "l3a_qk", 128, stage=wstage)
    ws_v = WStream(ctx, st, "l3a_vo", D, stage=wstage)
    ws_o = ws_v
    qb_sb = ctx.sbuf("l3a_qb_s", [128, NCH], F32, st)
    kb_sb = ctx.sbuf("l3a_kb_s", [128, NCH], F32, st)
    ob_sb = ctx.sbuf("l3a_ob_s", [128, NCH], F32, st)
    vb_bc = ctx.sbuf("l3a_vbbc", [128, D], F32, st)
    qf = ctx.sbuf("l3a_qf", [128, T], BF16, st)
    kf = ctx.sbuf("l3a_kf", [128, T], BF16, st)
    vtm = ctx.sbuf("l3a_vtm", [128, 16 * D], BF16, st)
    acat = ctx.sbuf("l3a_acat", [128, NCH * T], BF16, st)
    s_sb = lnw["xf"][:, 0:2 * T]
    p_sb = ctx.sbuf("l3a_p", [128, 4 * T], BF16, st)
    pt_sb = ctx.sbuf("l3a_pt", [128, 512], BF16, st)
    stat = ctx.sbuf("l3a_stat", [128, 16], F32, st)
    grepb = ctx.sbuf("l3a_grepb", [128, 4096], BF16, st)
    adiag = ctx.sbuf("l3a_adiag", [128, 128], BF16, st)
    banks = [ctx.psum(f"l3a_ps{i}", [128, 512], F32, st) for i in range(4)]
    ps_t = ctx.psum("l3a_pst", [128, 1024], BF16, st)
    ps_pv = ctx.psum("l3a_pspv", [128, 512], F32, st)
    W = ctx.wins

    for dst, key in ((qb_sb, "l3a_qb"), (kb_sb, "l3a_kb"), (ob_sb, "l3a_ob"),
                     (adiag, "antidiag8")):
        ctx.dma("q", dst[:], W[key][:, :])
    ctx.dma("q", vb_bc[:], W["l3a_vb"][0:1, :].broadcast_to([128, D]))
    ctx.barrier()

    for sq in range(SEQ):
        tok0 = sq * T
        ws_v.load(W["l3a_vw"])
        emit_ln(ctx, cst, lnw, xin, xn, T, tok0=tok0, banks=banks)
        ctx.barrier()

        ev = {}
        for kt in range(16):
            for half in range(2):
                i = kt * 2 + half
                ps = banks[i % 4]
                if i >= 4:
                    ctx.t.wait(ctx.vec, ev[i - 4])
                for c in range(NCH):
                    ctx.t.matmul(ps[:, :],
                                 xn[:, c * T + kt * 128:c * T + (kt + 1) * 128],
                                 ws_v.bf[:, c * D + half * 512:c * D + half * 512 + 512],
                                 start=(c == 0), stop=(c == NCH - 1))
                pm = ctx.t.mark(ctx.pe)
                ctx.v.wait(ctx.pe, pm)
                ctx.v.tensor_tensor(
                    vtm[:, kt * D + half * 512:kt * D + half * 512 + 512],
                    ps[:, :], vb_bc[:, half * 512:half * 512 + 512], ALU.add)
                ev[i] = ctx.v.mark(ctx.vec)
        ctx.barrier()

        for g in range(8):
            for wb_is_q in (True, False):
                wdram = W["l3a_qw"] if wb_is_q else W["l3a_kw"]
                dst = qf if wb_is_q else kf
                bias = qb_sb if wb_is_q else kb_sb
                lv = ctx.dma("q", ws_qk.stage[:, 0:NCH * 128],
                             col_block_ap(wdram, D, g * 128, 128))
                ctx.g.wait(*lv)
                for c in range(NCH):
                    ctx.g.tensor_copy(ws_qk.bf[:, c * 128:(c + 1) * 128],
                                      ws_qk.stage[:, c * 128:(c + 1) * 128])
                gm = ctx.g.mark(ctx.gp)
                ctx.t.wait(ctx.gp, gm)
                for spn in range(4):
                    ps = banks[spn]
                    for c in range(NCH):
                        ctx.t.matmul(ps[:, :], ws_qk.bf[:, c * 128:(c + 1) * 128],
                                     xn[:, c * T + spn * 512:c * T + spn * 512 + 512],
                                     start=(c == 0), stop=(c == NCH - 1))
                    pm = ctx.t.mark(ctx.pe)
                    ctx.s.wait(ctx.pe, pm)
                    ctx.s.activation(dst[:, spn * 512:spn * 512 + 512], ps[:, :],
                                     AF.Identity, bias=bias[:, g:g + 1])
                    ctx.s.mark(ctx.act)
                ctx.barrier()

            for hh in range(2):
                h = 2 * g + hh
                for piece in range(3):
                    p0 = piece * 1536
                    pw = min(1536, 4096 - p0)
                    gext_src = AP(tensor=W["l3a_gext"].tensor,
                                  offset=W["l3a_gext"].offset + h * 4224 + p0,
                                  ap=[[1, 128], [1, pw]])
                    lv = ctx.dma("q", wstage[:, 0:pw], gext_src)
                    ctx.g.wait(*lv)
                    ctx.g.tensor_copy(grepb[:, p0:p0 + pw], wstage[:, 0:pw])
                    gm2 = ctx.g.mark(ctx.gp)
                    ctx.q.wait(ctx.gp, gm2)
                ctx.t.wait(ctx.gp, gm2)

                def bias_mm(ctx, ps, ks, kw, qt):
                    col0 = ks * 512 - qt * 128 + 2047
                    ctx.t.matmul(ps[:, 0:kw], adiag[:], grepb[:, col0:col0 + kw],
                                 start=False, stop=True)

                def acat_cb(qs, pvm, h=h):
                    ctx.s.wait(ctx.pe, pvm)
                    ctx.s.copy(acat[(h % 2) * 64:(h % 2) * 64 + 64,
                                    (h // 2) * T + qs * 512:(h // 2) * T + qs * 512 + 512],
                               ps_pv[0:64, :])
                    sm = ctx.s.mark(ctx.act)
                    ctx.t.wait(ctx.act, sm)

                emit_softmax_attn_head(
                    ctx, cst, qf, kf, vtm, s_sb, p_sb, pt_sb, stat,
                    banks, ps_t, ps_pv, None, r0=hh * 64, scale=0.125, dv=64,
                    vcol=lambda kt, h=h: (kt * D + h * 64, 64),
                    acat_cb=acat_cb, causal=False, bias_mm=bias_mm)
                ctx.barrier()

        ws_o.load(W["l3a_ow"])
        ctx.barrier()

        def oproj(dc, spn, ps):
            for c in range(NCH):
                ctx.t.matmul(ps[:, :], ws_o.tile(c, dc * 128),
                             acat[:, c * T + spn * 512:c * T + spn * 512 + 512],
                             start=(c == 0), stop=(c == NCH - 1))

        emit_residual_out(ctx, cst, lnw, xin, xout, tok0, T, oproj, ob_sb,
                          banks)
        ctx.barrier()
    st.close()



def emit_l3f(ctx, cst, xin, xout):
    st = ExitStack()
    QTOK = 1024
    wtup = ctx.sbuf("l3f_wtup", [128, NCH * 4096], BF16, st)
    wtdn = ctx.sbuf("l3f_wtdn", [128, 32 * D], BF16, st)
    h_bf = ctx.sbuf("l3f_h", [128, 32 * 256], BF16, st)
    xn = ctx.sbuf("l3f_xn", [128, NCH * QTOK], BF16, st)
    wstage = h_bf[:].bitcast(F32)
    sgf = xn[:, 0:4096].bitcast(F32)
    msf = xn[:, 4096:8192].bitcast(F32)
    apart = ctx.sbuf("l3f_apart", [128, 40], F32, st)
    arow = ctx.sbuf("l3f_arow", [1, 16], F32, st)
    thr = ctx.sbuf("l3f_thr", [128, 1], F32, st)
    thr_d = ctx.dram("l3f_thr_d", [1, 1], F32)
    hq = ctx.sbuf("l3f_hq", [128, 512], BF16, st)
    r2rows = ctx.sbuf("l3f_r2", [1, 3 * 256], F32, st)
    r2b = ctx.sbuf("l3f_r2b", [1, 256], BF16, st)
    r2bc = ctx.sbuf("l3f_r2bc", [128, 256], BF16, st)
    otmp = ctx.sbuf("l3f_otmp", [128, 2 * 256], F32, st)
    banks = [ctx.psum(f"l3f_ps{i}", [128, 512], F32, st) for i in range(6)]
    W = ctx.wins

    for wi, (wdram, wdst, nch, cols) in enumerate(
            ((W["l3f_wu"], wtup, NCH, 4096), (W["l3f_wd"], wtdn, 32, D))):
        pw = min(cols, 2048)
        npiece = nch * cols // pw
        per_chunk = cols // pw
        for p in range(npiece):
            c, sub = divmod(p, per_chunk)
            half = (p % 2) * 2048
            if p >= 2:
                ctx.q.wait(ctx.vec, vparts[p - 2])
            lv = ctx.dma("q", wstage[:, half:half + pw],
                         wdram[c * 128:(c + 1) * 128, sub * pw:sub * pw + pw])
            ctx.v.wait(*lv)
            ctx.v.tensor_reduce(apart[:, p:p + 1],
                                wstage[:, half:half + pw], AX.X, ALU.add,
                                apply_absolute_value=True)
            if p == 0:
                vparts = {}
            ctx.v.fence()
            vparts[p] = ctx.v.mark(ctx.vec)
        ctx.v.fence()
        ctx.v.reduce_sum(apart[:, 39:40], apart[:, 0:npiece], axis=AX.X)
        ctx.v.fence()
        vm = ctx.v.mark(ctx.vec)
        ctx.t.wait(ctx.vec, vm)
        ctx.t.matmul(banks[0][0:1, 0:128], apart[:, 39:40], cst["ident_f32"][:],
                     is_transpose=True, start=True, stop=True)
        pm = ctx.t.mark(ctx.pe)
        ctx.v.wait(ctx.pe, pm)
        ctx.v.reduce_sum(arow[0:1, 1:2], banks[0][0:1, 0:128], axis=AX.X)
        ctx.v.fence()
        ctx.v.tensor_scalar_mul(arow[0:1, 0:1], arow[0:1, 1:2],
                                0.5 / float(nch * 128 * cols))
        ctx.v.fence()
        vm2 = ctx.v.mark(ctx.vec)
        ctx.s.wait(ctx.vec, vm2)
        sv = ctx.dma("s", thr_d[0:1, :], arow[0:1, 0:1])
        ctx.q.wait(*sv)
        lv = ctx.dma("q", thr[:], thr_d[0:1, :].broadcast_to([128, 1]))
        ctx.v.wait(*lv)
        ctx.s.wait(*lv)
        gmarks = {}
        for p in range(npiece):
            c, sub = divmod(p, per_chunk)
            half = (p % 2) * 2048
            if p >= 2:
                ctx.q.wait(ctx.gp, gmarks[p - 2])
            lv = ctx.dma("q", wstage[:, half:half + pw],
                         wdram[c * 128:(c + 1) * 128, sub * pw:sub * pw + pw])
            ctx.s.wait(*lv)
            ctx.s.sign(sgf[:, 0:pw], wstage[:, half:half + pw])
            ctx.s.activation(msf[:, 0:pw], wstage[:, half:half + pw], AF.Abs)
            am = ctx.s.mark(ctx.act)
            ctx.v.wait(ctx.act, am)
            ctx.v.tensor_scalar(msf[:, 0:pw], msf[:, 0:pw], thr[:], None,
                                ALU.is_gt)
            vm3 = ctx.v.mark(ctx.vec)
            ctx.g.wait(ctx.act, am)
            ctx.g.wait(ctx.vec, vm3)
            ctx.g.tensor_tensor(wdst[:, c * cols + sub * pw:
                                     c * cols + sub * pw + pw],
                                sgf[:, 0:pw], msf[:, 0:pw], ALU.mult)
            gmarks[p] = ctx.g.mark(ctx.gp)
            ctx.s.wait(ctx.gp, gmarks[p])
            ctx.v.wait(ctx.gp, gmarks[p])
        ctx.barrier()

    def l3f_scale(ctx, w, mu, var, rstd):
        r = w["rows"]
        b_row = r[0:1, 512:1024]
        ctx.s.fence()
        ctx.s.activation(b_row, var, AF.Ln)
        ctx.s.fence()
        ctx.s.activation(b_row, b_row, AF.Exp, scale=0.5)
        am = ctx.s.mark(ctx.act)
        ctx.v.wait(ctx.act, am)
        ctx.v.tensor_tensor(b_row, b_row, rstd, ALU.mult)
        ctx.v.fence()
        ctx.v.tensor_scalar_add(b_row, b_row, 1e-8)
        ctx.v.fence()
        ctx.v.reciprocal(b_row, b_row)
        ctx.v.fence()
        ctx.v.tensor_tensor(rstd, rstd, b_row, ALU.mult)
        ctx.v.fence()
        vm = ctx.v.mark(ctx.vec)
        ctx.s.wait(ctx.vec, vm)

    lnw = ln_work(ctx, st, "l3f_lnw")
    for qq in range(4):
        tq0 = qq * QTOK
        emit_ln(ctx, cst, lnw, xin, xn, QTOK, tok0=tq0, banks=banks[0:4],
                scale_cb=l3f_scale)
        ctx.barrier()

        for sp in range(QTOK // 256):
            t0 = tq0 + sp * 256
            hev = {}
            for j in range(32):
                b = j % 2
                if j >= 2:
                    ctx.t.wait(ctx.act, hev[j - 2])
                for c in range(NCH):
                    ctx.t.matmul(banks[b][:, 0:256],
                                 wtup[:, c * 4096 + j * 128:c * 4096 + (j + 1) * 128],
                                 xn[:, c * QTOK + sp * 256:c * QTOK + sp * 256 + 256],
                                 start=(c == 0), stop=(c == NCH - 1))
                pm = ctx.t.mark(ctx.pe)
                ctx.s.wait(ctx.pe, pm)
                ctx.s.activation(h_bf[:, j * 256:(j + 1) * 256], banks[b][:, 0:256],
                                 AF.Gelu)
                hev[j] = ctx.s.mark(ctx.act)
            s2 = r2rows[0:1, 0:256]
            l2r = r2rows[0:1, 256:512]
            rr2 = r2rows[0:1, 512:768]
            ctx.v.wait(ctx.act, hev[31])
            for j in range(32):
                hb2 = slice((j % 2) * 256, (j % 2) * 256 + 256)
                if j >= 2:
                    ctx.v.wait(ctx.pe, c2[j - 2])
                ctx.v.tensor_tensor(hq[:, hb2], h_bf[:, j * 256:(j + 1) * 256],
                                    h_bf[:, j * 256:(j + 1) * 256], ALU.mult)
                vm = ctx.v.mark(ctx.vec)
                ctx.t.wait(ctx.vec, vm)
                ctx.t.matmul(banks[2][0:1, 0:256], cst["ones_col"][:], hq[:, hb2],
                             start=(j == 0), stop=(j == 31))
                if j == 0:
                    c2 = {}
                c2[j] = ctx.t.mark(ctx.pe)
            ctx.s.wait(ctx.pe, c2[31])
            ctx.s.activation(l2r, banks[2][0:1, 0:256], AF.Ln, scale=1.0 / 4096.0)
            ctx.s.fence()
            ctx.s.activation(s2, l2r, AF.Exp, scale=0.5)
            ctx.s.fence()
            ctx.s.add(rr2, s2, 1e-8)
            am = ctx.s.mark(ctx.act)
            ctx.v.wait(ctx.act, am)
            ctx.v.reciprocal(l2r, rr2)
            ctx.v.fence()
            ctx.v.tensor_copy(r2b[0:1, :], l2r)
            vm2 = ctx.v.mark(ctx.vec)
            ctx.t.wait(ctx.vec, vm2)
            ctx.t.matmul(banks[3][:, 0:256], cst["ones_row"][:], r2b[0:1, :],
                         start=True, stop=True)
            pm2 = ctx.t.mark(ctx.pe)
            ctx.v.wait(ctx.pe, pm2)
            ctx.v.tensor_copy(r2bc[:], banks[3][:, 0:256])
            ctx.v.fence()
            for j in range(32):
                ctx.v.tensor_tensor(h_bf[:, j * 256:(j + 1) * 256],
                                    h_bf[:, j * 256:(j + 1) * 256], r2bc[:], ALU.mult)
            vm3 = ctx.v.mark(ctx.vec)
            ctx.t.wait(ctx.vec, vm3)
            lvx = ctx.dma("q", lnw["xf"][:, 0:NCH * 256], x_span_ap(xin, t0, w=256))
            for dc in range(NCH):
                gi = sp * NCH + dc
                bb = banks[4 + dc % 2]
                if gi >= 2:
                    ctx.t.wait(ctx.vec, l3f_oev[gi - 2])
                for j in range(32):
                    ctx.t.matmul(bb[:, 0:256],
                                 wtdn[:, j * D + dc * 128:j * D + dc * 128 + 128],
                                 h_bf[:, j * 256:(j + 1) * 256],
                                 start=(j == 0), stop=(j == 31))
                pm3 = ctx.t.mark(ctx.pe)
                ctx.v.wait(ctx.pe, pm3)
                ctx.v.wait(*lvx)
                if gi >= 2:
                    ctx.v.wait(*l3f_sts[gi - 2])
                ot = otmp[:, (gi % 2) * 256:(gi % 2) * 256 + 256]
                ctx.v.tensor_tensor(ot, bb[:, 0:256],
                                    lnw["xf"][:, dc * 256:dc * 256 + 256], ALU.add)
                if gi == 0:
                    l3f_oev = {}
                    l3f_sts = {}
                l3f_oev[gi] = ctx.v.mark(ctx.vec)
                ctx.s.wait(ctx.vec, l3f_oev[gi])
                l3f_sts[gi] = ctx.dma("s", x_chunk_span_ap(xout, dc, t0, w=256), ot)
            ctx.q.wait(ctx.vec, l3f_oev[sp * NCH + NCH - 1])
        ctx.barrier()
    st.close()



LAMBDA_INIT = 0.8 - 0.6 * math.exp(-0.3 * 12)

def emit_l2a(ctx, cst, xin, xout):
    st = ExitStack()
    xn = ctx.sbuf("l2a_xn", [128, NCH * T], BF16, st)
    lnw = ln_work(ctx, st, "l2a")
    wstage = ctx.sbuf("l2a_wstage", [128, 2 * D], F32, st)
    ws_q = WStream(ctx, st, "l2a_wq", D, stage=wstage)
    ws_k = WStream(ctx, st, "l2a_wk", D, stage=wstage)
    ws_v = WStream(ctx, st, "l2a_wv", D, stage=wstage)
    ws_o = ws_q
    cosb = ctx.sbuf("l2a_cos_s", [128, 1024], F32, st)
    sab = ctx.sbuf("l2a_sa_s", [128, 1024], F32, st)
    sbb = ctx.sbuf("l2a_sb_s", [128, 1024], F32, st)
    lamrow = ctx.sbuf("l2a_lam", [1, 64 * 4 + 16], F32, st)
    lam_bc = ctx.sbuf("l2a_lambc", [128, 1], F32, st)
    lam_d = ctx.dram("l2a_lam_d", [1, 1], F32)
    lnwp = ctx.sbuf("l2a_lnwp", [128, 128], F32, st)
    lnbp = ctx.sbuf("l2a_lnbp", [128, 128], F32, st)
    qtf = ctx.sbuf("l2a_qtf", [128, 1024], F32, st)
    ktf = ctx.sbuf("l2a_ktf", [128, 1024], F32, st)
    vtf = ctx.sbuf("l2a_vtf", [128, 1024], F32, st)
    rsc = ctx.sbuf("l2a_rsc", [128, 1024], F32, st)
    dots = ctx.sbuf("l2a_dots", [128, 128], F32, st)
    dscr = ctx.sbuf("l2a_dscr", [128, 4 * 64], F32, st)
    probs = ctx.sbuf("l2a_probs", [128, 128], F32, st)
    stat8 = ctx.sbuf("l2a_stat8", [128, 64], F32, st)
    mix = ctx.sbuf("l2a_mix", [128, 2 * 128], F32, st)
    a_f = ctx.sbuf("l2a_af", [128, 1024], F32, st)
    a_bf = ctx.sbuf("l2a_abf", [128, 1024], BF16, st)
    acat = ctx.sbuf("l2a_acat", [128, NCH * T], BF16, st)
    banks = [ctx.psum(f"l2a_ps{i}", [128, 512], F32, st) for i in range(4)]
    ps_t = ctx.psum("l2a_pst", [128, 1024], BF16, st)
    W = ctx.wins

    for dst, key in ((cosb, "l2a_cos"), (sab, "l2a_sa"), (sbb, "l2a_sb")):
        ctx.dma("q", dst[:], W[key][:, :])
    for i, key in enumerate(("l2a_lambda_q1", "l2a_lambda_k1",
                             "l2a_lambda_q2", "l2a_lambda_k2")):
        ctx.dma("q", lamrow[0:1, i * 64:(i + 1) * 64], W[key][0:1, :])
    ctx.dma("q", lnwp[:], W["l2a_lnw"][0:1, :].broadcast_to([128, 128]))
    ctx.dma("q", lnbp[:], W["l2a_lnb"][0:1, :].broadcast_to([128, 128]))
    ctx.barrier()
    c0 = lamrow[0:1, 256:257]
    c1 = lamrow[0:1, 257:258]
    ctx.v.tensor_tensor(lamrow[0:1, 0:64], lamrow[0:1, 0:64],
                        lamrow[0:1, 64:128], ALU.mult)
    ctx.v.tensor_tensor(lamrow[0:1, 128:192], lamrow[0:1, 128:192],
                        lamrow[0:1, 192:256], ALU.mult)
    ctx.v.fence()
    ctx.v.reduce_sum(c0, lamrow[0:1, 0:64], axis=AX.X)
    ctx.v.reduce_sum(c1, lamrow[0:1, 128:192], axis=AX.X)
    ctx.v.fence()
    vm = ctx.v.mark(ctx.vec)
    ctx.s.wait(ctx.vec, vm)
    ctx.s.activation(c0, c0, AF.Exp)
    ctx.s.fence()
    ctx.s.activation(c1, c1, AF.Exp)
    am = ctx.s.mark(ctx.act)
    ctx.v.wait(ctx.act, am)
    ctx.v.tensor_tensor(c1, c1, c0, ALU.subtract)
    ctx.v.fence()
    ctx.v.tensor_scalar_add(c1, c1, -LAMBDA_INIT)
    ctx.v.fence()
    vm2 = ctx.v.mark(ctx.vec)
    ctx.s.wait(ctx.vec, vm2)
    sv = ctx.dma("s", lam_d[0:1, :], c1)
    ctx.q.wait(*sv)
    ctx.dma("q", lam_bc[:], lam_d[0:1, :].broadcast_to([128, 1]))
    ctx.v.tensor_scalar_mul(lnwp[:], lnwp[:], 1.0 - LAMBDA_INIT)
    ctx.v.tensor_scalar_mul(lnbp[:], lnbp[:], 1.0 - LAMBDA_INIT)
    ctx.barrier()

    for sq in range(SEQ):
        tok0 = sq * T
        ws_q.load(W["l2a_wq"])
        ws_k.load(W["l2a_wk"])
        ws_v.load(W["l2a_wv"])
        emit_ln(ctx, cst, lnw, xin, xn, T, tok0=tok0, banks=banks)
        ctx.barrier()

        for kt in range(16):
            for which, ws, dstf in ((0, ws_q, qtf), (1, ws_k, ktf), (2, ws_v, vtf)):
                for half in range(2):
                    ps = banks[half]
                    for c in range(NCH):
                        ctx.t.matmul(ps[:, :],
                                     xn[:, c * T + kt * 128:c * T + (kt + 1) * 128],
                                     ws.bf[:, c * D + half * 512:c * D + half * 512 + 512],
                                     start=(c == 0), stop=(c == NCH - 1))
                    pm = ctx.t.mark(ctx.pe)
                    ctx.s.wait(ctx.pe, pm)
                    ctx.s.copy(dstf[:, half * 512:half * 512 + 512], ps[:, :])
                    am = ctx.s.mark(ctx.act)
                    ctx.t.wait(ctx.act, am)
            am_qkv = ctx.s.mark(ctx.act)
            tsl = slice(kt * 64, (kt + 1) * 64)
            hrep = lambda tab: AP(tensor=tab, offset=kt * 64,
                                  ap=[[1024, 128], [0, 16], [1, 64]])
            ctx.v.wait(ctx.act, am_qkv)
            for tf in (qtf, ktf):
                ctx.v.tensor_tensor(rsc[:], tf[:], hrep(cosb), ALU.mult)
                even = AP(tensor=tf, offset=1,
                          ap=[[1024, 128], [64, 16], [2, 32]])
                odd0 = AP(tensor=tf, offset=0,
                          ap=[[1024, 128], [64, 16], [2, 32]])
                sae = AP(tensor=sab, offset=kt * 64,
                         ap=[[1024, 128], [0, 16], [2, 32]])
                sbo = AP(tensor=sbb, offset=kt * 64 + 1,
                         ap=[[1024, 128], [0, 16], [2, 32]])
                dste = AP(tensor=rsc, offset=1024 * 0,
                          ap=[[1024, 128], [64, 16], [2, 32]])
                scr2e = AP(tensor=a_f, offset=0,
                           ap=[[1024, 128], [64, 16], [2, 32]])
                scr2o = AP(tensor=a_f, offset=1,
                           ap=[[1024, 128], [64, 16], [2, 32]])
                ctx.v.tensor_tensor(scr2e, even, sae, ALU.mult)
                ctx.v.tensor_tensor(scr2o, odd0, sbo, ALU.mult)
                ctx.v.fence()
                ctx.v.tensor_tensor(tf[:], rsc[:], a_f[:], ALU.add)
                ctx.v.fence()
            vrot = ctx.v.mark(ctx.vec)
            ctx.v.wait(ctx.vec, vrot)
            nops = 0
            for g2 in range(2):
                for n in range(8):
                    for m in range(8):
                        db = dscr[:, (nops % 4) * 64:(nops % 4) * 64 + 64]
                        if nops >= 4 and nops % 4 == 0:
                            ctx.v.fence()
                        ctx.v.scalar_tensor_tensor(
                            db, qtf[:, n * 128 + g2 * 64:n * 128 + g2 * 64 + 64],
                            0.35355339059327373,
                            ktf[:, m * 128 + g2 * 64:m * 128 + g2 * 64 + 64],
                            ALU.mult, ALU.mult,
                            accum_out=dots[:, g2 * 64 + n * 8 + m:g2 * 64 + n * 8 + m + 1])
                        nops += 1
            ctx.v.fence()
            ctx.v.tensor_reduce(stat8[:, 0:16], dots[:].rearrange("p (s m) -> p s m", m=8),
                                AX.X, ALU.max, negate=True)
            ctx.v.fence()
            seg = AP(tensor=stat8, offset=0,
                     ap=[[64, 128], [1, 16], [0, 8]])
            ctx.v.tensor_tensor(probs[:], dots[:], seg, ALU.add)
            vmx = ctx.v.mark(ctx.vec)
            ctx.s.wait(ctx.vec, vmx)
            ctx.s.activation(probs[:], probs[:], AF.Exp)
            am2 = ctx.s.mark(ctx.act)
            ctx.v.wait(ctx.act, am2)
            ctx.v.tensor_reduce(stat8[:, 16:32], probs[:].rearrange("p (s m) -> p s m", m=8),
                                AX.X, ALU.add)
            ctx.v.fence()
            ctx.v.reciprocal(stat8[:, 32:48], stat8[:, 16:32])
            ctx.v.fence()
            seg2 = AP(tensor=stat8, offset=32,
                      ap=[[64, 128], [1, 16], [0, 8]])
            ctx.v.tensor_tensor(probs[:], probs[:], seg2, ALU.mult)
            ctx.v.fence()
            for g2 in range(2):
                for n in range(8):
                    for m in range(8):
                        pcell = probs[:, g2 * 64 + n * 8 + m:g2 * 64 + n * 8 + m + 1]
                        if m == 0:
                            ctx.v.tensor_scalar(mix[:, 0:128], vtf[:, m * 128:(m + 1) * 128],
                                                pcell, None, ALU.mult)
                        else:
                            ctx.v.scalar_tensor_tensor(
                                mix[:, (m % 2) * 128:(m % 2) * 128 + 128],
                                vtf[:, m * 128:(m + 1) * 128], pcell,
                                mix[:, ((m - 1) % 2) * 128:((m - 1) % 2) * 128 + 128],
                                ALU.mult, ALU.add)
                        ctx.v.fence()
                    mres = mix[:, ((7) % 2) * 128:((7) % 2) * 128 + 128]
                    if g2 == 0:
                        ctx.v.tensor_copy(a_f[:, n * 128:(n + 1) * 128], mres)
                    else:
                        ctx.v.scalar_tensor_tensor(
                            a_f[:, n * 128:(n + 1) * 128], mres, lam_bc[:],
                            a_f[:, n * 128:(n + 1) * 128], ALU.mult, ALU.add)
                    ctx.v.fence()
            ctx.v.tensor_tensor(rsc[:], a_f[:], a_f[:], ALU.mult)
            ctx.v.fence()
            ctx.v.tensor_reduce(stat8[:, 48:56], rsc[:].rearrange("p (s m) -> p s m", m=128),
                                AX.X, ALU.add)
            ctx.v.fence()
            vms = ctx.v.mark(ctx.vec)
            ctx.s.wait(ctx.vec, vms)
            ctx.s.activation(stat8[:, 56:64], stat8[:, 48:56], AF.Ln,
                             scale=1.0 / 128.0, bias=1e-8)
            ctx.s.fence()
            ctx.s.activation(stat8[:, 48:56], stat8[:, 56:64], AF.Exp, scale=-0.5)
            am3 = ctx.s.mark(ctx.act)
            ctx.v.wait(ctx.act, am3)
            segr = AP(tensor=stat8, offset=48,
                      ap=[[64, 128], [1, 8], [0, 128]])
            lnw_rep = AP(tensor=lnwp, offset=0,
                         ap=[[128, 128], [0, 8], [1, 128]])
            lnb_rep = AP(tensor=lnbp, offset=0,
                         ap=[[128, 128], [0, 8], [1, 128]])
            ctx.v.tensor_tensor(a_f[:], a_f[:], segr, ALU.mult)
            ctx.v.fence()
            ctx.v.tensor_tensor(a_f[:], a_f[:], lnw_rep, ALU.mult)
            ctx.v.fence()
            ctx.v.tensor_tensor(a_bf[:], a_f[:], lnb_rep, ALU.add)
            ctx.v.fence()
            vfin = ctx.v.mark(ctx.vec)
            ctx.t.wait(ctx.vec, vfin)
            for c in range(NCH):
                ctx.t.matmul(ps_t[:, c * 128:(c + 1) * 128],
                             a_bf[:, c * 128:(c + 1) * 128], cst["ident_sb"][:],
                             is_transpose=True, start=(c == 0), stop=(c == NCH - 1))
            tm = ctx.t.mark(ctx.pe)
            ctx.v.wait(ctx.pe, tm)
            out_ap = AP(tensor=acat, offset=kt * 128,
                        ap=[[NCH * T, 128], [T, NCH], [1, 128]])
            ctx.v.tensor_copy(out_ap, ps_t[:])
            vtr = ctx.v.mark(ctx.vec)
            ctx.t.wait(ctx.vec, vtr)
            ctx.s.wait(ctx.vec, vtr)
        ctx.barrier()

        ws_o.load(W["l2a_wo"])
        ctx.barrier()

        def oproj(dc, spn, ps):
            for c in range(NCH):
                ctx.t.matmul(ps[:, :], ws_o.tile(c, dc * 128),
                             acat[:, c * T + spn * 512:c * T + spn * 512 + 512],
                             start=(c == 0), stop=(c == NCH - 1))

        emit_residual_out(ctx, cst, lnw, xin, xout, tok0, T, oproj, None, banks)
        ctx.barrier()
    st.close()



def emit_l2f(ctx, cst, xin, xout):
    st = ExitStack()
    QTOK = 1024
    lnw = ln_work(ctx, st, "l2f")
    xn = ctx.sbuf("l2f_xn", [128, NCH * QTOK], BF16, st)
    we1b = ctx.sbuf("l2f_we1b", [128, NCH * 2048], BF16, st)
    we3b = ctx.sbuf("l2f_we3b", [128, NCH * 2048], BF16, st)
    we2b = ctx.sbuf("l2f_we2b", [128, 2 * D], BF16, st)
    wstage = ctx.sbuf("l2f_wst", [128, 2 * 2048], F32, st)
    wgb = ctx.sbuf("l2f_wgb", [128, NCH * 8], BF16, st)
    out_acc = ctx.sbuf("l2f_oacc", [128, NCH * QTOK], F32, st)
    h_bf = ctx.sbuf("l2f_h", [128, 16 * 256], BF16, st)
    s_bf = ctx.sbuf("l2f_sbf", [128, 2 * 256], BF16, st)
    t_bf = ctx.sbuf("l2f_tbf", [128, 2 * 256], BF16, st)
    wb_bc = ctx.sbuf("l2f_wbbc", [128, 256], BF16, st)
    gate = ctx.sbuf("l2f_gate", [128, 64], F32, st)
    wq_tm = ctx.sbuf("l2f_wqtm", [128, 8 * 8], F32, st)
    wsT = ctx.sbuf("l2f_wsT", [8, 1024], F32, st)
    werow = ctx.sbuf("l2f_werow", [1, QTOK], F32, st)
    werow_b = ctx.sbuf("l2f_werowb", [1, QTOK], BF16, st)
    w_dram = ctx.dram("l2f_w_d", [8, QTOK], F32)
    banks = [ctx.psum(f"l2f_ps{i}", [128, 512], F32, st) for i in range(8)]
    W = ctx.wins

    ws_g = WStream(ctx, st, "l2f_wg", 8, stage=wstage)
    ws_g.bf = wgb
    ws_g.load(W["l2f_wgw"])
    ctx.barrier()

    for qq in range(4):
        tq0 = qq * QTOK
        emit_ln(ctx, cst, lnw, xin, xn, QTOK, tok0=tq0, banks=banks[0:4])
        ctx.barrier()

        m1 = gate[:, 8:9]; m2 = gate[:, 9:10]; ssum = gate[:, 10:11]
        l2c = gate[:, 16:24]; sel = gate[:, 24:32]; lg = gate[:, 32:40]
        sig = gate[:, 40:48]
        for kt in range(QTOK // 128):
            ps = banks[0]
            for c in range(NCH):
                ctx.t.matmul(ps[:, 0:8],
                             xn[:, c * QTOK + kt * 128:c * QTOK + (kt + 1) * 128],
                             wgb[:, c * 8:(c + 1) * 8],
                             start=(c == 0), stop=(c == NCH - 1))
            pm = ctx.t.mark(ctx.pe)
            ctx.v.wait(ctx.pe, pm)
            ctx.v.tensor_copy(lg, ps[:, 0:8])
            ctx.v.fence()
            ctx.v.reduce_max(m1, lg, axis=AX.X)
            ctx.v.fence()
            ctx.v.tensor_scalar(l2c, lg, m1, None, ALU.is_ge)
            ctx.v.fence()
            ctx.v.scalar_tensor_tensor(l2c, l2c, NEG, lg, ALU.mult, ALU.add)
            ctx.v.fence()
            ctx.v.reduce_max(m2, l2c, axis=AX.X)
            ctx.v.fence()
            ctx.v.tensor_scalar(sel, lg, m2, None, ALU.is_ge)
            vm = ctx.v.mark(ctx.vec)
            ctx.s.wait(ctx.vec, vm)
            ctx.s.activation(sig, lg, AF.Sigmoid)
            am = ctx.s.mark(ctx.act)
            ctx.v.wait(ctx.act, am)
            ctx.v.tensor_tensor(sel, sel, sig, ALU.mult)
            ctx.v.fence()
            ctx.v.reduce_sum(ssum, sel, axis=AX.X)
            ctx.v.fence()
            ctx.v.reciprocal(m1, ssum)
            ctx.v.fence()
            ctx.v.tensor_scalar(wq_tm[:, kt * 8:(kt + 1) * 8], sel, m1, None,
                                ALU.mult)
            ctx.v.fence()
            vm2 = ctx.v.mark(ctx.vec)
            ctx.t.wait(ctx.vec, vm2)
            ctx.t.matmul(banks[1][0:8, kt * 128 % 512:kt * 128 % 512 + 128],
                         wq_tm[:, kt * 8:(kt + 1) * 8], cst["ident_f32"][:],
                         is_transpose=True, start=True, stop=True)
            tm = ctx.t.mark(ctx.pe)
            ctx.v.wait(ctx.pe, tm)
            ctx.v.tensor_copy(wsT[0:8, kt * 128:(kt + 1) * 128],
                              banks[1][0:8, kt * 128 % 512:kt * 128 % 512 + 128])
            vtr = ctx.v.mark(ctx.vec)
            ctx.t.wait(ctx.vec, vtr)
        vfin = ctx.v.mark(ctx.vec)
        ctx.s.wait(ctx.vec, vfin)
        sv = ctx.dma("s", w_dram[:, :], wsT[0:8, :])
        ctx.q.wait(*sv)
        ctx.barrier()

        l2f_oev = 0
        wsg = 0
        l2f_w2last = 0
        for e in range(9):
            shared = (e == 8)
            if not shared:
                lv = ctx.dma("q", werow[0:1, :], w_dram[e:e + 1, :])
                ctx.v.wait(*lv)
                ctx.v.tensor_copy(werow_b[0:1, :], werow[0:1, :])
                ctx.v.fence()
                wem = ctx.v.mark(ctx.vec)
                ctx.t.wait(ctx.vec, wem)
            if shared:
                w1d, w3d, w2d = W["l2f_ws1"], W["l2f_ws3"], W["l2f_ws2"]
            else:
                w1d = W["l2f_we1"][e * D:(e + 1) * D, :]
                w3d = W["l2f_we3"][e * D:(e + 1) * D, :]
                w2d = W["l2f_we2"][e * 2048:(e + 1) * 2048, :]
            gm1 = {}
            for c in range(NCH):
                half = (c % 2) * 2048
                if c >= 2:
                    ctx.q.wait(ctx.gp, gm1[c - 2])
                else:
                    ctx.q.wait(ctx.gp, wsg)
                lv = ctx.dma("q", wstage[:, half:half + 2048],
                             w1d[c * 128:(c + 1) * 128, :])
                ctx.g.wait(*lv)
                ctx.g.tensor_copy(we1b[:, c * 2048:(c + 1) * 2048],
                                  wstage[:, half:half + 2048])
                gm1[c] = ctx.g.mark(ctx.gp)
            for c in range(NCH):
                half = (c % 2) * 2048
                ctx.q.wait(ctx.gp, gm1[c + 6] if c >= 2 else gm1[NCH - 1])
                lv = ctx.dma("q", wstage[:, half:half + 2048],
                             w3d[c * 128:(c + 1) * 128, :])
                ctx.g.wait(*lv)
                ctx.g.tensor_copy(we3b[:, c * 2048:(c + 1) * 2048],
                                  wstage[:, half:half + 2048])
                gm1[c + 8] = ctx.g.mark(ctx.gp)
            gw = gm1[15]
            wsg = gw
            ctx.t.wait(ctx.gp, gw)

            for sp in range(QTOK // 256):
                t0l = sp * 256
                if not shared:
                    ctx.t.wait(ctx.vec, l2f_oev)
                    ctx.t.matmul(banks[4][:, 0:256], cst["ones_row"][:],
                                 werow_b[0:1, t0l:t0l + 256],
                                 start=True, stop=True)
                    bm = ctx.t.mark(ctx.pe)
                    ctx.s.wait(ctx.pe, bm)
                    ctx.s.copy(wb_bc[:], banks[4][:, 0:256])
                    wbev = ctx.s.mark(ctx.act)
                    ctx.t.wait(ctx.act, wbev)
                hev = {}
                for j in range(16):
                    b1 = banks[j % 2]
                    b3 = banks[2 + j % 2]
                    if j >= 2:
                        ctx.t.wait(ctx.vec, hev[j - 2])
                    for c in range(NCH):
                        ctx.t.matmul(b1[:, 0:256],
                                     we1b[:, c * 2048 + j * 128:c * 2048 + (j + 1) * 128],
                                     xn[:, c * QTOK + t0l:c * QTOK + t0l + 256],
                                     start=(c == 0), stop=(c == NCH - 1))
                    for c in range(NCH):
                        ctx.t.matmul(b3[:, 0:256],
                                     we3b[:, c * 2048 + j * 128:c * 2048 + (j + 1) * 128],
                                     xn[:, c * QTOK + t0l:c * QTOK + t0l + 256],
                                     start=(c == 0), stop=(c == NCH - 1))
                    pm = ctx.t.mark(ctx.pe)
                    ctx.s.wait(ctx.pe, pm)
                    ctx.s.activation(s_bf[:, (j % 2) * 256:(j % 2) * 256 + 256],
                                     b1[:, 0:256], AF.Silu)
                    am = ctx.s.mark(ctx.act)
                    ctx.v.wait(ctx.pe, pm)
                    ctx.v.tensor_copy(t_bf[:, (j % 2) * 256:(j % 2) * 256 + 256],
                                      b3[:, 0:256])
                    ctx.v.wait(ctx.act, am)
                    if shared:
                        ctx.v.fence()
                        ctx.v.tensor_tensor(h_bf[:, j * 256:(j + 1) * 256],
                                            s_bf[:, (j % 2) * 256:(j % 2) * 256 + 256],
                                            t_bf[:, (j % 2) * 256:(j % 2) * 256 + 256],
                                            ALU.mult)
                    else:
                        ctx.v.fence()
                        ctx.v.tensor_tensor(h_bf[:, j * 256:(j + 1) * 256],
                                            s_bf[:, (j % 2) * 256:(j % 2) * 256 + 256],
                                            t_bf[:, (j % 2) * 256:(j % 2) * 256 + 256],
                                            ALU.mult)
                        ctx.v.wait(ctx.act, wbev)
                        ctx.v.fence()
                        ctx.v.tensor_tensor(h_bf[:, j * 256:(j + 1) * 256],
                                            h_bf[:, j * 256:(j + 1) * 256],
                                            wb_bc[:], ALU.mult)
                    hev[j] = ctx.v.mark(ctx.vec)
                ctx.t.wait(ctx.vec, hev[15])
                w2m = {}
                for j in range(16):
                    wb2 = we2b[:, (j % 2) * D:(j % 2) * D + D]
                    if j >= 2:
                        ctx.q.wait(ctx.pe, w2m[j - 2])
                    else:
                        ctx.q.wait(ctx.gp, wsg)
                    lv = ctx.dma("q", wstage[:, 0:D] if j % 2 == 0 else wstage[:, 2048:2048 + D],
                                 w2d[j * 128:(j + 1) * 128, :])
                    ctx.g.wait(*lv)
                    ctx.g.wait(ctx.pe, w2m[j - 2] if j >= 2 else l2f_w2last)
                    ctx.g.tensor_copy(wb2, wstage[:, 0:D] if j % 2 == 0 else wstage[:, 2048:2048 + D])
                    gm2 = ctx.g.mark(ctx.gp)
                    wsg = gm2
                    ctx.t.wait(ctx.gp, gm2)
                    if j == 0:
                        ctx.t.wait(ctx.vec, l2f_oev)
                    for dc in range(NCH):
                        ctx.t.matmul(banks[4 + dc // 2][:, (dc % 2) * 256:(dc % 2) * 256 + 256],
                                     wb2[:, dc * 128:(dc + 1) * 128],
                                     h_bf[:, j * 256:(j + 1) * 256],
                                     start=(j == 0 and dc % 2 == 0),
                                     stop=(j == 15 and dc % 2 == 1))
                    w2m[j] = ctx.t.mark(ctx.pe)
                l2f_w2last = w2m[15]
                pmf = w2m[15]
                ctx.v.wait(ctx.pe, pmf)
                for dc in range(NCH):
                    src_ps = banks[4 + dc // 2][:, (dc % 2) * 256:(dc % 2) * 256 + 256]
                    dsl = slice(dc * QTOK + t0l, dc * QTOK + t0l + 256)
                    if e == 0:
                        ctx.v.tensor_copy(out_acc[:, dsl], src_ps)
                    else:
                        ctx.v.tensor_tensor(out_acc[:, dsl], out_acc[:, dsl],
                                            src_ps, ALU.add)
                    ctx.v.fence()
                l2f_oev = ctx.v.mark(ctx.vec)
        ctx.barrier()

        sts = {}
        for sp in range(QTOK // 512):
            t0 = tq0 + sp * 512
            lv = ctx.dma("q", lnw["xf"][:, 0:NCH * 512], x_span_ap(xin, t0))
            for dc in range(NCH):
                gi = sp * NCH + dc
                ctx.v.wait(*lv)
                if gi >= 2:
                    ctx.v.wait(*sts[gi - 2])
                ot = lnw["bc"][:, (gi % 2) * 512:(gi % 2) * 512 + 512]
                ctx.v.scalar_tensor_tensor(
                    ot, out_acc[:, dc * QTOK + sp * 512:dc * QTOK + sp * 512 + 512],
                    1.0 / 3.0, lnw["xf"][:, dc * 512:(dc + 1) * 512],
                    ALU.mult, ALU.add)
                ctx.v.fence()
                ov = ctx.v.mark(ctx.vec)
                ctx.s.wait(ctx.vec, ov)
                sts[gi] = ctx.dma("s", x_chunk_span_ap(xout, dc, t0), ot)
            ctx.q.wait(ctx.vec, ov)
        ctx.barrier()
    st.close()



def emit_l0f(ctx, cst, xin, xout):
    st = ExitStack()
    xn = ctx.sbuf("l0f_xn", [128, NCH * T], BF16, st)
    lnw = ln_work(ctx, st, "l0f")
    acat_d = ctx.dram("l0f_acat_d", [2, NCH, 128, T], BF16)
    ac_scr = ctx.sbuf("l0f_acs", [128, 2 * 512], BF16, st)
    p_sb = lnw["xf"][:].bitcast(BF16)
    pt_sb = ctx.sbuf("l0f_pt", [128, 512], BF16, st)
    stat = ctx.sbuf("l0f_stat", [128, 16], F32, st)
    wstage = ctx.sbuf("l0f_wstage", [128, 2048], F32, st)
    maskf = ctx.sbuf("l0f_mask", [128, 128], F32, st)
    banks = [ctx.psum(f"l0f_ps{i}", [128, 512], F32, st) for i in range(4)]
    ps_t = ctx.psum("l0f_pst", [128, 1024], BF16, st)
    ps_pv = ctx.psum("l0f_pspv", [128, 512], F32, st)
    W = ctx.wins
    win = W["l0f_win"]
    SC = 1.0 / math.sqrt(128.0)
    acq = [0]
    ctx.dma("q", maskf[:], W["mask_neg"][:, :])

    def acat_store(part, h, qs, pvm, rows=128):
        i = acq[0]
        acq[0] += 1
        sl = (i % 2) * 512
        ctx.s.wait(ctx.pe, pvm)
        if i >= 2:
            ctx.s.wait(*ac_sts[i - 2])
        ctx.s.copy(ac_scr[:rows, sl:sl + 512], ps_pv[0:rows, :])
        sm = ctx.s.mark(ctx.act)
        ctx.t.wait(ctx.act, sm)
        ctx.s.wait(ctx.act, sm)
        dst = AP(tensor=acat_d.tensor,
                 offset=acat_d.offset + (part * NCH + h) * 128 * T + qs * 512,
                 ap=[[T, rows], [1, 512]])
        if i == 0:
            ac_sts.clear()
        ac_sts[i] = ctx.dma("s", dst, ac_scr[:rows, sl:sl + 512])
    ac_sts = {}

    for sq in range(SEQ):
        tok0 = sq * T
        emit_ln(ctx, cst, lnw, xin, xn, T, tok0=tok0, banks=banks)
        ctx.barrier()

        stA = ExitStack()
        vtm = ctx.sbuf(f"l0f_vtm{sq}", [128, 16 * D], BF16, stA)
        stA1 = ExitStack()
        wv_b = ctx.sbuf(f"l0f_wvb{sq}", [128, NCH * D], BF16, stA1)
        gmk = {}
        for c in range(NCH):
            half = (c % 2) * 1024
            if c >= 2:
                ctx.q.wait(ctx.gp, gmk[c - 2])
            lv = ctx.dma("q", wstage[:, half:half + 1024],
                         win[c * 128:(c + 1) * 128, 3072:4096])
            ctx.g.wait(*lv)
            ctx.g.tensor_copy(wv_b[:, c * D:(c + 1) * D], wstage[:, half:half + 1024])
            gmk[c] = ctx.g.mark(ctx.gp)
        ctx.t.wait(ctx.gp, gmk[NCH - 1])
        ev = {}
        for kt in range(16):
            for half in range(2):
                i = kt * 2 + half
                ps = banks[i % 4]
                if i >= 4:
                    ctx.t.wait(ctx.act, ev[i - 4])
                for c in range(NCH):
                    ctx.t.matmul(ps[:, :],
                                 xn[:, c * T + kt * 128:c * T + (kt + 1) * 128],
                                 wv_b[:, c * D + half * 512:c * D + half * 512 + 512],
                                 start=(c == 0), stop=(c == NCH - 1))
                pm = ctx.t.mark(ctx.pe)
                ctx.s.wait(ctx.pe, pm)
                ctx.s.copy(vtm[:, kt * D + half * 512:kt * D + half * 512 + 512],
                           ps[:, :])
                ev[i] = ctx.s.mark(ctx.act)
        ctx.barrier()
        stA1.close()

        wq_h = ctx.sbuf(f"l0f_wqh{sq}", [128, NCH * 128], BF16, stA)
        wk_h = ctx.sbuf(f"l0f_wkh{sq}", [128, NCH * 128], BF16, stA)
        wk_h2 = None
        qa_f = ctx.sbuf(f"l0f_qaf{sq}", [128, 2 * 128], F32, stA)
        m_f = ctx.sbuf(f"l0f_mf{sq}", [128, 2 * 128], F32, stA)
        rp_b = ctx.sbuf(f"l0f_rpb{sq}", [128, 2 * 128], BF16, stA)
        cosb = ctx.sbuf(f"l0f_cosb{sq}", [128, 1024], F32, stA)
        sinb = ctx.sbuf(f"l0f_sinb{sq}", [128, 1024], F32, stA)
        qf_h = ctx.sbuf(f"l0f_qfh{sq}", [128, T], BF16, stA)
        kf_h = ctx.sbuf(f"l0f_kfh{sq}", [128, T], BF16, stA)
        s_sb = ctx.sbuf(f"l0f_ssb{sq}", [128, T], F32, stA)
        ctx.dma("q", cosb[:], W["l0f_cos"][:, :])
        ctx.dma("q", sinb[:], W["l0f_sin"][:, :])
        ctx.barrier()
        for h in range(8):
            for is_q in (True, False):
                col0 = (1024 if is_q else 2048) + h * 128
                wb = wq_h if is_q else wk_h
                dstf = qf_h if is_q else kf_h
                ctx.q.wait(ctx.gp)
                lv = ctx.dma("q", wstage[:, 0:NCH * 128],
                             col_block_ap(win, 4112, col0, 128))
                ctx.g.wait(*lv)
                for c in range(NCH):
                    ctx.g.tensor_copy(wb[:, c * 128:(c + 1) * 128],
                                      wstage[:, c * 128:(c + 1) * 128])
                gm = ctx.g.mark(ctx.gp)
                ctx.t.wait(ctx.gp, gm)
                for kt in range(16):
                    sl = (kt % 2) * 128
                    bb = banks[kt % 2]
                    if kt >= 2:
                        ctx.t.wait(ctx.act, qev[kt - 2])
                    for c in range(NCH):
                        ctx.t.matmul(bb[:, 0:128],
                                     xn[:, c * T + kt * 128:c * T + (kt + 1) * 128],
                                     wb[:, c * 128:(c + 1) * 128],
                                     start=(c == 0), stop=(c == NCH - 1))
                    pm = ctx.t.mark(ctx.pe)
                    ctx.s.wait(ctx.pe, pm)
                    if kt >= 2:
                        ctx.s.wait(ctx.vec, rev[kt - 2])
                    ctx.s.copy(qa_f[:, sl:sl + 128], bb[:, 0:128])
                    if kt == 0:
                        qev = {}
                        rev = {}
                    qev[kt] = ctx.s.mark(ctx.act)
                    t1 = qa_f[:, sl:sl + 64]
                    t2 = qa_f[:, sl + 64:sl + 128]
                    o1 = rp_b[:, sl:sl + 64]
                    o2 = rp_b[:, sl + 64:sl + 128]
                    u1 = m_f[:, sl:sl + 64]
                    u2 = m_f[:, sl + 64:sl + 128]
                    csl = cosb[:, kt * 64:(kt + 1) * 64]
                    ssl = sinb[:, kt * 64:(kt + 1) * 64]
                    ctx.v.wait(ctx.act, qev[kt])
                    ctx.v.tensor_tensor(u1, t1, csl, ALU.mult)
                    ctx.v.tensor_tensor(u2, t2, csl, ALU.mult)
                    ctx.v.fence()
                    ctx.v.tensor_tensor(t1, t1, ssl, ALU.mult)
                    ctx.v.tensor_tensor(t2, t2, ssl, ALU.mult)
                    ctx.v.fence()
                    ctx.v.tensor_tensor(o1, u1, t2, ALU.subtract)
                    ctx.v.tensor_tensor(o2, u2, t1, ALU.add)
                    ctx.v.fence()
                    rev[kt] = ctx.v.mark(ctx.vec)
                    ctx.t.wait(ctx.vec, rev[kt])
                    ctx.t.matmul(ps_t[:, sl + (kt % 8 // 2) * 256:
                                      sl + (kt % 8 // 2) * 256 + 128],
                                 rp_b[:, sl:sl + 128], cst["ident_sb"][:],
                                 is_transpose=True,
                                 start=(kt % 8 == 0), stop=(kt % 8 == 7))
                    tm = ctx.t.mark(ctx.pe)
                    if kt % 8 == 7:
                        ctx.v.wait(ctx.pe, tm)
                        ctx.v.tensor_copy(dstf[:, (kt - 7) * 128:(kt + 1) * 128],
                                          ps_t[:])
                        trm = ctx.v.mark(ctx.vec)
                        ctx.t.wait(ctx.vec, trm)
            ctx.barrier()

            def acat_cb(qs, pvm, h=h):
                acat_store(0, h, qs, pvm)

            emit_softmax_attn_head(
                ctx, cst, qf_h, kf_h, vtm, s_sb, p_sb, pt_sb, stat,
                banks, ps_t, ps_pv, maskf, r0=None, scale=SC, dv=128,
                vcol=lambda kt, h=h: (kt * D + h * 128, 128),
                acat_cb=acat_cb, causal=True)
            ctx.barrier()
        stA.close()
        ctx.barrier()

        stB = ExitStack()
        qf = ctx.sbuf(f"l0f_qf{sq}", [128, NCH * T], BF16, stB)
        ke_b = ctx.sbuf(f"l0f_keb{sq}", [128, 4096], BF16, stB)
        ve_b = ctx.sbuf(f"l0f_veb{sq}", [128, 4096], BF16, stB)
        wbc = lnw["xf"][:].bitcast(BF16)
        aT = ctx.sbuf(f"l0f_aT{sq}", [128, 2 * 512], BF16, stB)
        wrt = ctx.sbuf(f"l0f_wrt{sq}", [128, NCH * 16], BF16, stB)
        gcell = ctx.sbuf(f"l0f_gcell{sq}", [128, 64], F32, stB)
        wsT16 = ctx.sbuf(f"l0f_wsT{sq}", [16, T], F32, stB)
        wsT16b = ctx.sbuf(f"l0f_wsTb{sq}", [16, T], BF16, stB)
        w_dram = ctx.dram(f"l0f_wd{sq}", [16, T], BF16)
        for dc in range(NCH):
            lv = ctx.dma("q", wstage[:, 0:NCH * 128],
                         col_block_ap(win, 4112, dc * 128, 128))
            ctx.g.wait(*lv)
            for c in range(NCH):
                ctx.g.tensor_copy(ke_b[:, c * 128:(c + 1) * 128],
                                  wstage[:, c * 128:(c + 1) * 128])
            gm = ctx.g.mark(ctx.gp)
            ctx.t.wait(ctx.gp, gm)
            for spn in range(4):
                ps = banks[spn]
                for c in range(NCH):
                    ctx.t.matmul(ps[:, :], ke_b[:, c * 128:(c + 1) * 128],
                                 xn[:, c * T + spn * 512:c * T + spn * 512 + 512],
                                 start=(c == 0), stop=(c == NCH - 1))
                pm = ctx.t.mark(ctx.pe)
                ctx.s.wait(ctx.pe, pm)
                ctx.s.copy(qf[:, dc * T + spn * 512:dc * T + spn * 512 + 512],
                           ps[:, :])
                ctx.s.mark(ctx.act)
            ctx.barrier()
        lv = ctx.dma("q", wstage[:, 0:NCH * 16], col_block_ap(win, 4112, 4096, 16))
        ctx.g.wait(*lv)
        for c in range(NCH):
            ctx.g.tensor_copy(wrt[:, c * 16:(c + 1) * 16],
                              wstage[:, c * 16:(c + 1) * 16])
        gm = ctx.g.mark(ctx.gp)
        ctx.t.wait(ctx.gp, gm)
        m1 = gcell[:, 16:17]; m2 = gcell[:, 17:18]
        lg = gcell[:, 32:48]; l2c = gcell[:, 48:64]
        for kt in range(16):
            ps = banks[0]
            for c in range(NCH):
                ctx.t.matmul(ps[:, 0:16],
                             xn[:, c * T + kt * 128:c * T + (kt + 1) * 128],
                             wrt[:, c * 16:(c + 1) * 16],
                             start=(c == 0), stop=(c == NCH - 1))
            pm = ctx.t.mark(ctx.pe)
            ctx.v.wait(ctx.pe, pm)
            ctx.v.tensor_copy(lg, ps[:, 0:16])
            ctx.v.fence()
            ctx.v.reduce_max(m1, lg, axis=AX.X)
            ctx.v.fence()
            ctx.v.tensor_scalar(l2c, lg, m1, None, ALU.is_ge)
            ctx.v.fence()
            ctx.v.scalar_tensor_tensor(l2c, l2c, NEG, lg, ALU.mult, ALU.add)
            ctx.v.fence()
            ctx.v.reduce_max(m2, l2c, axis=AX.X)
            ctx.v.fence()
            ctx.v.tensor_scalar(l2c, lg, m2, None, ALU.is_ge)
            vm = ctx.v.mark(ctx.vec)
            ctx.s.wait(ctx.vec, vm)
            ctx.s.activation(gcell[:, 0:16], lg, AF.Sigmoid)
            am = ctx.s.mark(ctx.act)
            ctx.v.wait(ctx.act, am)
            ctx.v.tensor_tensor(gcell[:, 0:16], gcell[:, 0:16], l2c, ALU.mult)
            ctx.v.fence()
            vm2 = ctx.v.mark(ctx.vec)
            ctx.t.wait(ctx.vec, vm2)
            ctx.t.matmul(banks[1][0:16, (kt % 4) * 128:(kt % 4) * 128 + 128],
                         gcell[:, 0:16], cst["ident_f32"][:],
                         is_transpose=True, start=(kt % 4 == 0), stop=(kt % 4 == 3))
            tm = ctx.t.mark(ctx.pe)
            if kt % 4 == 3:
                ctx.v.wait(ctx.pe, tm)
                ctx.v.tensor_copy(wsT16[0:16, (kt - 3) * 128:(kt + 1) * 128],
                                  banks[1][0:16, :])
                wtm = ctx.v.mark(ctx.vec)
                ctx.t.wait(ctx.vec, wtm)
        ctx.v.fence()
        ctx.v.tensor_copy(wsT16b[0:16, :], wsT16[0:16, :])
        vmb = ctx.v.mark(ctx.vec)
        ctx.s.wait(ctx.vec, vmb)
        sv = ctx.dma("s", w_dram[:, :], wsT16b[0:16, :])
        ctx.q.wait(*sv)
        ctx.barrier()

        last_gml = 0
        last_pvm = 0
        for h in range(8):
            ctx.q.wait(ctx.gp)
            lvk = ctx.dma("q", wstage[:, 0:2048],
                          W["l0f_ke"][h * 128:(h + 1) * 128, 0:2048])
            ctx.g.wait(*lvk)
            ctx.g.wait(ctx.pe, last_pvm)
            ctx.g.tensor_copy(ke_b[:, 0:2048], wstage[:, 0:2048])
            g1 = ctx.g.mark(ctx.gp)
            ctx.q.wait(ctx.gp, g1)
            lvk = ctx.dma("q", wstage[:, 0:2048],
                          W["l0f_ke"][h * 128:(h + 1) * 128, 2048:4096])
            ctx.g.wait(*lvk)
            ctx.g.tensor_copy(ke_b[:, 2048:4096], wstage[:, 0:2048])
            g2 = ctx.g.mark(ctx.gp)
            ctx.q.wait(ctx.gp, g2)
            lvk = ctx.dma("q", wstage[:, 0:2048],
                          W["l0f_ve"][h * 128:(h + 1) * 128, 0:2048])
            ctx.g.wait(*lvk)
            ctx.g.tensor_copy(ve_b[:, 0:2048], wstage[:, 0:2048])
            g3 = ctx.g.mark(ctx.gp)
            ctx.q.wait(ctx.gp, g3)
            lvk = ctx.dma("q", wstage[:, 0:2048],
                          W["l0f_ve"][h * 128:(h + 1) * 128, 2048:4096])
            ctx.g.wait(*lvk)
            ctx.g.tensor_copy(ve_b[:, 2048:4096], wstage[:, 0:2048])
            g4 = ctx.g.mark(ctx.gp)
            ctx.t.wait(ctx.gp, g4)
            for spn in range(4):
                t0l = spn * 512
                ctx.q.wait(ctx.vec, last_gml)
                lvb = ctx.dma("q", wbc[:, 0:8192],
                              AP(tensor=w_dram.tensor, offset=w_dram.offset + t0l,
                                 ap=[[0, 128], [T, 16], [1, 512]]))
                ctx.v.wait(*lvb)
                ctx.t.wait(*lvb)
                first = True
                for e in range(16):
                    for sk in range(2):
                        bb = banks[sk % 2]
                        i2 = e * 2 + sk
                        if i2 >= 2:
                            ctx.t.wait(ctx.act, gel[i2 - 2])
                        ctx.t.matmul(bb[:, :],
                                     ke_b[:, e * 256 + sk * 128:e * 256 + sk * 128 + 128],
                                     qf[:, h * T + t0l:h * T + t0l + 512],
                                     start=True, stop=True)
                        pm = ctx.t.mark(ctx.pe)
                        ctx.s.wait(ctx.pe, pm)
                        if i2 >= 2:
                            ctx.s.wait(ctx.vec, gml[i2 - 2])
                        ctx.s.activation(aT[:, sk * 512:sk * 512 + 512], bb[:, :],
                                         AF.Gelu)
                        if i2 == 0:
                            gel = {}
                            gml = {}
                        gel[i2] = ctx.s.mark(ctx.act)
                        ctx.v.wait(ctx.act, gel[i2])
                        ctx.v.tensor_tensor(aT[:, sk * 512:sk * 512 + 512],
                                            aT[:, sk * 512:sk * 512 + 512],
                                            wbc[:, e * 512:(e + 1) * 512], ALU.mult)
                        ctx.v.fence()
                        gml[i2] = ctx.v.mark(ctx.vec)
                        ctx.t.wait(ctx.vec, gml[i2])
                        ctx.t.matmul(ps_pv[:, :],
                                     ve_b[:, e * 256 + sk * 128:e * 256 + sk * 128 + 128],
                                     aT[:, sk * 512:sk * 512 + 512],
                                     start=first, stop=(e == 15 and sk == 1))
                        first = False
                pvm = ctx.t.mark(ctx.pe)
                last_gml = gml[31]
                last_pvm = pvm
                acat_store(1, h, spn, pvm)
        ctx.barrier()
        stB.close()

        stC = ExitStack()
        ws_out = WStream(ctx, stC, f"l0f_wo{sq}", D, nch=16, stage=wstage)
        ac_in = ctx.sbuf(f"l0f_acin{sq}", [128, 2 * NCH * 512], BF16, stC)
        ws_out.load(W["l0f_wout"])
        ctx.barrier()
        cur_spn = [-1]

        def oproj(dc, spn, ps):
            if spn != cur_spn[0]:
                src_ap = AP(tensor=acat_d.tensor,
                            offset=acat_d.offset + spn * 512,
                            ap=[[T, 128], [128 * T, 2 * NCH], [1, 512]])
                lv = ctx.dma("q", ac_in[:], src_ap)
                ctx.t.wait(*lv)
                cur_spn[0] = spn
            for c in range(NCH):
                ctx.t.matmul(ps[:, :], ws_out.tile(c, dc * 128),
                             ac_in[:, c * 512:(c + 1) * 512],
                             start=(c == 0), stop=False)
            for c in range(NCH):
                ctx.t.matmul(ps[:, :], ws_out.tile(8 + c, dc * 128),
                             ac_in[:, (NCH + c) * 512:(NCH + c + 1) * 512],
                             start=False, stop=(c == NCH - 1))

        emit_residual_out(ctx, cst, lnw, xin, xout, tok0, T, oproj, None, banks)
        ctx.barrier()
        stC.close()
    st.close()



def emit_l1f(ctx, cst, xin, xout):
    st = ExitStack()
    lnw = ln_work(ctx, st, "l1f")
    y_res = ctx.dram("l1f_y", [NCH, 128, TOK], F32)
    acat_d = ctx.dram("l1f_acat", [NCH, 128, T], BF16)
    qr_d = ctx.dram("l1f_qr", [NCH, 128, T], BF16)
    kr_d = ctx.dram("l1f_kr", [NCH, 128, T], BF16)
    wstage = ctx.sbuf("l1f_wst", [128, 2 * 1024], F32, st)
    maskup = ctx.sbuf("l1f_mup", [128, 128], F32, st)
    masklo = ctx.sbuf("l1f_mlo", [128, 128], F32, st)
    banks = [ctx.psum(f"l1f_ps{i}", [128, 512], F32, st) for i in range(6)]
    ps_t = ctx.psum("l1f_pst", [128, 1024], BF16, st)
    ps_pv = ctx.psum("l1f_pspv", [128, 512], F32, st)
    W = ctx.wins
    ctx.dma("q", maskup[:], W["mask_neg"][:, :])
    ctx.dma("q", masklo[:], W["mask_low"][:, :])

    st0 = ExitStack()
    yscr = ctx.sbuf("l1f_yscr", [128, 1024], F32, st0)
    for sq in range(SEQ):
        emit_ln(ctx, cst, lnw, xin, None, T, tok0=sq * T, banks=banks[0:4],
                fp32_store=(y_res, yscr))
        ctx.barrier()
    st0.close()

    for dep in range(6):
        row0 = dep * D
        for sq in range(SEQ):
            tok0 = sq * T
            stA = ExitStack()
            ybs = ctx.sbuf(f"l1f_ybs{dep}{sq}", [128, NCH * T], BF16, stA)
            vtm = ctx.sbuf(f"l1f_vtm{dep}{sq}", [128, 16 * D], BF16, stA)
            wsec = ctx.sbuf(f"l1f_wsec{dep}{sq}", [128, NCH * 1024], BF16, stA)
            qkv_f = ctx.sbuf(f"l1f_qkvf{dep}{sq}", [128, 1024], F32, stA)
            rp_f = ctx.sbuf(f"l1f_rpf{dep}{sq}", [128, 1024], F32, stA)
            rp_b = ctx.sbuf(f"l1f_rpb{dep}{sq}", [128, 1024], BF16, stA)
            seg16 = ctx.sbuf(f"l1f_seg{dep}{sq}", [128, 64], F32, stA)
            csq = ctx.sbuf(f"l1f_csq{dep}{sq}", [128, 1024], F32, stA)
            ssq = ctx.sbuf(f"l1f_ssq{dep}{sq}", [128, 1024], F32, stA)
            csk = ctx.sbuf(f"l1f_csk{dep}{sq}", [128, 1024], F32, stA)
            ssk = ctx.sbuf(f"l1f_ssk{dep}{sq}", [128, 1024], F32, stA)
            sc_bc = ctx.sbuf(f"l1f_scbc{dep}{sq}", [128, 128], F32, stA)
            trs = ctx.sbuf(f"l1f_trs{dep}{sq}", [128, 1024], BF16, stA)
            qkh = ctx.sbuf(f"l1f_qkh{dep}{sq}", [64, 2 * T], BF16, stA)
            s_sb = ctx.sbuf(f"l1f_ssb{dep}{sq}", [128, 768], F32, stA)
            p_b = ctx.sbuf(f"l1f_pb{dep}{sq}", [128, 768], BF16, stA)
            pt_s = ctx.sbuf(f"l1f_pts{dep}{sq}", [128, 384], BF16, stA)
            aout = ctx.sbuf(f"l1f_aout{dep}{sq}", [64, T], BF16, stA)
            stat = ctx.sbuf(f"l1f_stat{dep}{sq}", [128, 8], F32, stA)

            emit_shift_cast(ctx, y_res, ybs, tok0, lnw)
            lvq = ctx.dma("q", sc_bc[:, 0:64],
                          W["l1f_qs"][dep:dep + 1, :].broadcast_to([128, 64]))
            lvk = ctx.dma("q", sc_bc[:, 64:128],
                          W["l1f_ks"][dep:dep + 1, :].broadcast_to([128, 64]))
            ctx.dma("q", csq[:], W["l1f_cs"][:, :])
            ctx.dma("q", ssq[:], W["l1f_ssn"][:, :])
            ctx.dma("q", csk[:], W["l1f_csi"][:, :])
            ctx.dma("q", ssk[:], W["l1f_ssi"][:, :])
            ctx.barrier()

            for sec in range(3):
                gm = {}
                for c in range(NCH):
                    half = (c % 2) * 1024
                    if c >= 2:
                        ctx.q.wait(ctx.gp, gm[c - 2])
                    else:
                        ctx.q.wait(ctx.gp)
                    lv = ctx.dma("q", wstage[:, half:half + 1024],
                                 W["l1f_wqkv"][row0 + c * 128:row0 + (c + 1) * 128,
                                               sec * 1024:(sec + 1) * 1024])
                    ctx.g.wait(*lv)
                    ctx.g.tensor_copy(wsec[:, c * 1024:(c + 1) * 1024],
                                      wstage[:, half:half + 1024])
                    gm[c] = ctx.g.mark(ctx.gp)
                ctx.t.wait(ctx.gp, gm[NCH - 1])
                for kt in range(16):
                    for half in range(2):
                        i = kt * 2 + half
                        ps = banks[i % 4]
                        if i >= 4:
                            ctx.t.wait(ctx.act, qkev[i - 4])
                        for c in range(NCH):
                            ctx.t.matmul(ps[:, :],
                                         ybs[:, c * T + kt * 128:c * T + (kt + 1) * 128],
                                         wsec[:, c * 1024 + half * 512:
                                              c * 1024 + half * 512 + 512],
                                         start=(c == 0), stop=(c == NCH - 1))
                        pm = ctx.t.mark(ctx.pe)
                        ctx.s.wait(ctx.pe, pm)
                        if sec == 2:
                            ctx.s.copy(vtm[:, kt * D + half * 512:
                                           kt * D + half * 512 + 512], ps[:, :])
                        else:
                            if half == 0 and kt >= 1:
                                ctx.s.wait(ctx.vec, rpev[kt - 1])
                            ctx.s.copy(qkv_f[:, half * 512:half * 512 + 512],
                                       ps[:, :])
                        if i == 0:
                            qkev = {}
                            rpev = {}
                        qkev[i] = ctx.s.mark(ctx.act)
                    if sec == 2:
                        continue
                    ctx.v.wait(ctx.act, qkev[kt * 2 + 1])
                    ctx.v.tensor_tensor(rp_f[:], qkv_f[:], qkv_f[:], ALU.mult)
                    ctx.v.fence()
                    ctx.v.tensor_reduce(seg16[:, 0:16],
                                        rp_f[:].rearrange("p (s k) -> p s k", k=64),
                                        AX.X, ALU.add)
                    ctx.v.fence()
                    ctx.v.tensor_scalar_max(seg16[:, 0:16], seg16[:, 0:16], 1e-24)
                    ctx.v.fence()
                    vm = ctx.v.mark(ctx.vec)
                    ctx.s.wait(ctx.vec, vm)
                    ctx.s.activation(seg16[:, 16:32], seg16[:, 0:16], AF.Ln)
                    ctx.s.fence()
                    ctx.s.activation(seg16[:, 0:16], seg16[:, 16:32], AF.Exp,
                                     scale=0.5)
                    am = ctx.s.mark(ctx.act)
                    ctx.v.wait(ctx.act, am)
                    ctx.v.tensor_scalar_max(seg16[:, 0:16], seg16[:, 0:16], 1e-12)
                    ctx.v.fence()
                    ctx.v.reciprocal(seg16[:, 16:32], seg16[:, 0:16])
                    ctx.v.fence()
                    rnrep = AP(tensor=seg16, offset=16,
                               ap=[[64, 128], [1, 16], [0, 64]])
                    ctx.v.tensor_tensor(qkv_f[:], qkv_f[:], rnrep, ALU.mult)
                    ctx.v.fence()
                    screp = AP(tensor=sc_bc, offset=(0 if sec == 0 else 64),
                               ap=[[128, 128], [0, 16], [1, 64]])
                    ctx.v.tensor_tensor(qkv_f[:], qkv_f[:], screp, ALU.mult)
                    ctx.v.fence()
                    cstab = csq if sec == 0 else csk
                    sstab = ssq if sec == 0 else ssk
                    tabsl = slice(kt * 64, (kt + 1) * 64)
                    cs_rep = AP(tensor=cstab, offset=kt * 64,
                                ap=[[1024, 128], [0, 16], [1, 64]])
                    ctx.v.tensor_tensor(rp_f[:], qkv_f[:], cs_rep, ALU.mult)
                    for hswap in range(2):
                        qsw = AP(tensor=qkv_f, offset=(32 if hswap == 0 else 0),
                                 ap=[[1024, 128], [64, 16], [1, 32]])
                        ssl = AP(tensor=sstab, offset=kt * 64 + hswap * 32,
                                 ap=[[1024, 128], [0, 16], [1, 32]])
                        dsw = AP(tensor=rp_b, offset=hswap * 32,
                                 ap=[[1024, 128], [64, 16], [1, 32]])
                        ctx.v.tensor_tensor(dsw, qsw, ssl, ALU.mult)
                    ctx.v.fence()
                    ctx.v.tensor_tensor(rp_b[:], rp_b[:], rp_f[:], ALU.add)
                    ctx.v.fence()
                    rpev[kt] = ctx.v.mark(ctx.vec)
                    ctx.t.wait(ctx.vec, rpev[kt])
                    for hp in range(8):
                        ctx.t.matmul(ps_t[:, hp * 128:(hp + 1) * 128],
                                     rp_b[:, hp * 128:(hp + 1) * 128],
                                     cst["ident_sb"][:], is_transpose=True,
                                     start=(hp == 0), stop=(hp == 7))
                    tm = ctx.t.mark(ctx.pe)
                    ctx.v.wait(ctx.pe, tm)
                    if kt >= 1:
                        ctx.v.wait(*trst[kt - 1])
                    ctx.v.tensor_copy(trs[:], ps_t[:])
                    ctx.v.fence()
                    trm = ctx.v.mark(ctx.vec)
                    ctx.t.wait(ctx.vec, trm)
                    ctx.s.wait(ctx.vec, trm)
                    if kt == 0:
                        trst = {}
                    qdst = qr_d if sec == 0 else kr_d
                    dst = AP(tensor=qdst.tensor,
                             offset=qdst.offset + kt * 128,
                             ap=[[T, 128], [128 * T, NCH], [1, 128]])
                    trst[kt] = ctx.dma("s", dst, trs[:])
                ctx.barrier()

            prev_pvm = 0
            sev = {}
            pnv = {}
            ams = {}
            aev = {}
            for h in range(16):
                hp, r0 = h // 2, (h % 2) * 64
                ctx.q.wait(ctx.pe, prev_pvm)
                lv1 = ctx.dma("q", qkh[0:64, 0:T],
                              AP(tensor=qr_d.tensor,
                                 offset=qr_d.offset + (hp * 128 + r0) * T,
                                 ap=[[T, 64], [1, T]]))
                lv2 = ctx.dma("q", qkh[0:64, T:2 * T],
                              AP(tensor=kr_d.tensor,
                                 offset=kr_d.offset + (hp * 128 + r0) * T,
                                 ap=[[T, 64], [1, T]]))
                ctx.t.wait(*lv1)
                ctx.t.wait(*lv2)
                for qt in range(16):
                    gq = h * 16 + qt
                    kt0 = max(0, qt - 2)
                    nkt = qt - kt0 + 1
                    wk = nkt * 128
                    ofs = (qt % 2) * 384
                    st0 = (qt % 2) * 4
                    bb = banks[qt % 2]
                    if gq >= 2:
                        ctx.t.wait(ctx.act, sev[gq - 2])
                    ctx.t.matmul(bb[:, 0:wk], qkh[0:64, qt * 128:(qt + 1) * 128],
                                 qkh[0:64, T + kt0 * 128:T + kt0 * 128 + wk],
                                 start=True, stop=True)
                    pm = ctx.t.mark(ctx.pe)
                    ctx.s.wait(ctx.pe, pm)
                    if gq >= 2:
                        ctx.s.wait(ctx.act, ams[gq - 2])
                    ctx.s.copy(s_sb[:, ofs:ofs + wk], bb[:, 0:wk])
                    sev[gq] = ctx.s.mark(ctx.act)
                    ctx.v.wait(ctx.act, sev[gq])
                    ctx.v.tensor_tensor(s_sb[:, ofs + wk - 128:ofs + wk],
                                        s_sb[:, ofs + wk - 128:ofs + wk],
                                        maskup[:], ALU.add)
                    if nkt == 3:
                        ctx.v.tensor_tensor(s_sb[:, ofs:ofs + 128],
                                            s_sb[:, ofs:ofs + 128],
                                            masklo[:], ALU.add)
                    ctx.v.fence()
                    ctx.v.reduce_max(stat[:, st0:st0 + 1], s_sb[:, ofs:ofs + wk],
                                     axis=AX.X, negate=True)
                    ctx.v.fence()
                    vm2 = ctx.v.mark(ctx.vec)
                    ctx.s.wait(ctx.vec, vm2)
                    ctx.s.mul(stat[:, st0 + 1:st0 + 2], stat[:, st0:st0 + 1], 8.0)
                    ctx.s.memzero(stat[:, st0 + 2:st0 + 3])
                    ctx.s.fence()
                    ctx.s.activation(p_b[:, ofs:ofs + wk], s_sb[:, ofs:ofs + wk],
                                     AF.Exp, scale=8.0,
                                     bias=stat[:, st0 + 1:st0 + 2],
                                     accum_out=stat[:, st0 + 2:st0 + 3])
                    am2 = ctx.s.mark(ctx.act)
                    ams[gq] = am2
                    ctx.v.wait(ctx.act, am2)
                    ctx.v.reciprocal(stat[:, st0 + 3:st0 + 4],
                                     stat[:, st0 + 2:st0 + 3])
                    ctx.v.fence()
                    ctx.v.tensor_scalar(p_b[:, ofs:ofs + wk], p_b[:, ofs:ofs + wk],
                                        stat[:, st0 + 3:st0 + 4], None, ALU.mult)
                    ctx.v.fence()
                    pnv[gq] = ctx.v.mark(ctx.vec)
                    ctx.t.wait(ctx.vec, pnv[gq])
                    for j in range(nkt):
                        ctx.t.matmul(ps_t[:, j * 128:(j + 1) * 128],
                                     p_b[:, ofs + j * 128:ofs + (j + 1) * 128],
                                     cst["ident_sb"][:], is_transpose=True,
                                     start=(j == 0), stop=(j == nkt - 1))
                    tm2 = ctx.t.mark(ctx.pe)
                    ctx.v.wait(ctx.pe, tm2)
                    ctx.v.tensor_copy(pt_s[:, 0:wk], ps_t[:, 0:wk])
                    ptm = ctx.v.mark(ctx.vec)
                    ctx.t.wait(ctx.vec, ptm)
                    if gq >= 1:
                        ctx.t.wait(ctx.act, aev[gq - 1])
                    for j in range(nkt):
                        ctx.t.matmul(ps_pv[0:64, 0:128],
                                     vtm[:, (kt0 + j) * D + h * 64:
                                         (kt0 + j) * D + h * 64 + 64],
                                     pt_s[:, j * 128:(j + 1) * 128],
                                     start=(j == 0), stop=(j == nkt - 1))
                    pvm = ctx.t.mark(ctx.pe)
                    prev_pvm = pvm
                    ctx.s.wait(ctx.pe, pvm)
                    ctx.s.copy(aout[0:64, qt * 128:(qt + 1) * 128], ps_pv[0:64, 0:128])
                    aev[gq] = ctx.s.mark(ctx.act)
                ctx.s.wait(ctx.act, aev[h * 16 + 15])
                dsta = AP(tensor=acat_d.tensor,
                          offset=acat_d.offset + (hp * 128 + r0) * T,
                          ap=[[T, 64], [1, T]])
                hst = ctx.dma("s", dsta, aout[0:64, :])
                ctx.q.wait(*hst)
            ctx.barrier()
            stA.close()

            stW = ExitStack()
            ws_o = WStream(ctx, stW, f"l1f_wo{dep}{sq}", D, stage=wstage)
            ac_in = ctx.sbuf(f"l1f_acin{dep}{sq}", [128, NCH * 512], BF16, stW)
            ws_o.load(W["l1f_wo"], row0=row0)
            ctx.barrier()
            cur = [-1]

            def oproj(dc, spn, ps):
                if spn != cur[0]:
                    srcp = AP(tensor=acat_d.tensor,
                              offset=acat_d.offset + spn * 512,
                              ap=[[T, 128], [128 * T, NCH], [1, 512]])
                    lv = ctx.dma("q", ac_in[:], srcp)
                    ctx.t.wait(*lv)
                    cur[0] = spn
                for c in range(NCH):
                    ctx.t.matmul(ps[:, :], ws_o.tile(c, dc * 128),
                                 ac_in[:, c * 512:(c + 1) * 512],
                                 start=(c == 0), stop=(c == NCH - 1))

            emit_residual_out(ctx, cst, lnw, y_res, y_res, tok0, T, oproj,
                              None, banks)
            ctx.barrier()
            stW.close()

            stF = ExitStack()
            w1b = ctx.sbuf(f"l1f_w1b{dep}{sq}", [128, NCH * 5632], BF16, stF)
            w2b = ctx.sbuf(f"l1f_w2b{dep}{sq}", [128, 22 * D], BF16, stF)
            xnf = ctx.sbuf(f"l1f_xnf{dep}{sq}", [128, NCH * 1024], BF16, stF)
            h_all = ctx.sbuf(f"l1f_hall{dep}{sq}", [128, 22 * 256], BF16, stF)
            afgl = ctx.sbuf(f"l1f_afgl{dep}{sq}", [128, 2 * 256], BF16, stF)
            cols = ctx.sbuf(f"l1f_cols{dep}{sq}", [128, 4 * 32], F32, stF)
            otmp = lnw["bc"]
            for dst2, key, wdt in ((cols[:, 0:22], "l1f_b1a", 22),
                                   (cols[:, 32:54], "l1f_b1g", 22),
                                   (cols[:, 64:72], "l1f_b2", 8),
                                   (cols[:, 96:104], "l1f_lng", 8),
                                   (cols[:, 104:112], "l1f_lnb", 8)):
                ctx.dma("q", dst2, W[key][dep * 128:(dep + 1) * 128, :])
            gmw = {}
            gi = 0
            rot = ((ctx.v, ctx.vec), (ctx.s, ctx.act), (ctx.g, ctx.gp))

            def stage_copy(dst, half, pw):
                eng, sem = rot[gi % 3]
                eng.wait(*lv)
                if eng is ctx.s:
                    eng.copy(dst, wstage[:, half:half + pw])
                else:
                    eng.tensor_copy(dst, wstage[:, half:half + pw])
                gmw[gi] = (sem, eng.mark(sem))

            for c in range(NCH):
                for piece in range(6):
                    p0 = piece * 1024
                    pw = min(1024, 5632 - p0)
                    half = (gi % 2) * 1024
                    if gi >= 2:
                        ctx.q.wait(*gmw[gi - 2])
                    else:
                        ctx.q.wait(ctx.gp)
                    lv = ctx.dma("q", wstage[:, half:half + pw],
                                 W["l1f_w1"][dep * D + c * 128:
                                             dep * D + (c + 1) * 128, p0:p0 + pw])
                    stage_copy(w1b[:, c * 5632 + p0:c * 5632 + p0 + pw], half, pw)
                    gi += 1
            for j in range(22):
                half = (gi % 2) * 1024
                ctx.q.wait(*gmw[gi - 2])
                lv = ctx.dma("q", wstage[:, half:half + 1024],
                             W["l1f_w2"][dep * 2816 + j * 128:
                                         dep * 2816 + (j + 1) * 128, :])
                stage_copy(w2b[:, j * D:(j + 1) * D], half, 1024)
                gi += 1
            ctx.barrier()

            for qq in range(2):
                tq0 = tok0 + qq * 1024
                emit_ln(ctx, cst, lnw, y_res, xnf, 1024, tok0=tq0,
                        banks=banks[0:4], xn_ntok=1024, shift=True,
                        seq_start=tok0,
                        affine=(lambda c: cols[:, 96 + c:97 + c],
                                lambda c: AP(tensor=cols, offset=104 + c,
                                             ap=[[128, 128], [0, 512]])))
                ctx.barrier()
                for sp in range(4):
                    t0l = sp * 256
                    hev = {}
                    for j in range(22):
                        ba = banks[4]
                        bg = banks[5]
                        if j >= 1:
                            ctx.t.wait(ctx.vec, hev[j - 1])
                        for c in range(NCH):
                            ctx.t.matmul(ba[:, 0:256],
                                         w1b[:, c * 5632 + j * 128:
                                             c * 5632 + (j + 1) * 128],
                                         xnf[:, c * 1024 + t0l:c * 1024 + t0l + 256],
                                         start=(c == 0), stop=(c == NCH - 1))
                        for c in range(NCH):
                            ctx.t.matmul(bg[:, 0:256],
                                         w1b[:, c * 5632 + 2816 + j * 128:
                                             c * 5632 + 2816 + (j + 1) * 128],
                                         xnf[:, c * 1024 + t0l:c * 1024 + t0l + 256],
                                         start=(c == 0), stop=(c == NCH - 1))
                        pm = ctx.t.mark(ctx.pe)
                        ctx.s.wait(ctx.pe, pm)
                        ctx.s.activation(afgl[:, 0:256], ba[:, 0:256], AF.Identity,
                                         bias=cols[:, j:j + 1])
                        ctx.s.fence()
                        ctx.s.activation(afgl[:, 256:512], bg[:, 0:256], AF.Gelu,
                                         bias=cols[:, 32 + j:33 + j])
                        am = ctx.s.mark(ctx.act)
                        ctx.v.wait(ctx.act, am)
                        ctx.v.tensor_tensor(h_all[:, j * 256:(j + 1) * 256],
                                            afgl[:, 0:256], afgl[:, 256:512],
                                            ALU.mult)
                        hev[j] = ctx.v.mark(ctx.vec)
                    ctx.t.wait(ctx.vec, hev[21])
                    cm2 = {}
                    for j in range(22):
                        for dc in range(NCH):
                            ctx.t.matmul(banks[dc // 2][:, (dc % 2) * 256:
                                                        (dc % 2) * 256 + 256],
                                         w2b[:, j * D + dc * 128:
                                             j * D + dc * 128 + 128],
                                         h_all[:, j * 256:(j + 1) * 256],
                                         start=(j == 0 and dc % 2 == 0),
                                         stop=(j == 21 and dc % 2 == 1))
                    pmf = ctx.t.mark(ctx.pe)
                    lvx = ctx.dma("q", lnw["xf"][:, 0:NCH * 256],
                                  x_span_ap(y_res, tq0 + t0l, w=256))
                    for dc in range(NCH):
                        gi2 = sp * NCH + dc
                        ctx.s.wait(ctx.pe, pmf)
                        if gi2 >= 2:
                            ctx.s.wait(*fsts[gi2 - 2])
                        ctx.s.activation(otmp[:, (gi2 % 2) * 256:(gi2 % 2) * 256 + 256],
                                         banks[dc // 2][:, (dc % 2) * 256:
                                                        (dc % 2) * 256 + 256],
                                         AF.Identity, bias=cols[:, 64 + dc:65 + dc])
                        am3 = ctx.s.mark(ctx.act)
                        ctx.v.wait(ctx.act, am3)
                        ctx.v.wait(*lvx)
                        if gi2 >= 2:
                            ctx.v.wait(*fsts[gi2 - 2])
                        ot = otmp[:, (gi2 % 2) * 256:(gi2 % 2) * 256 + 256]
                        ctx.v.tensor_tensor(ot, ot,
                                            lnw["xf"][:, dc * 256:(dc + 1) * 256],
                                            ALU.add)
                        ctx.v.fence()
                        if gi2 == 0:
                            fsts = {}
                        fsv = ctx.v.mark(ctx.vec)
                        ctx.s.wait(ctx.vec, fsv)
                        fsts[gi2] = ctx.dma("s",
                                            x_chunk_span_ap(y_res, dc, tq0 + t0l,
                                                            w=256), ot)
                    ctx.q.wait(ctx.vec, fsv)
                    ctx.t.wait(ctx.vec, fsv)
                ctx.barrier()
            stF.close()
    yscr = ctx.sbuf("l1f_yscr2", [128, 1024], F32, st)
    sts2 = {}
    for spn in range(TOK // 512):
        t0 = spn * 512
        lv1 = ctx.dma("q", lnw["xf"][:, 0:NCH * 512], x_span_ap(xin, t0))
        for dc in range(NCH):
            gi3 = spn * NCH + dc
            if gi3 >= 2:
                ctx.q.wait(ctx.vec, ovs2[gi3 - 2])
            lvy = ctx.dma("q", yscr[:, (gi3 % 2) * 512:(gi3 % 2) * 512 + 512],
                          x_chunk_span_ap(y_res, dc, t0))
            ctx.v.wait(*lv1)
            ctx.v.wait(*lvy)
            if gi3 >= 2:
                ctx.v.wait(*sts2[gi3 - 2])
            ot = lnw["bc"][:, (gi3 % 2) * 512:(gi3 % 2) * 512 + 512]
            ctx.v.tensor_tensor(ot, yscr[:, (gi3 % 2) * 512:(gi3 % 2) * 512 + 512],
                                lnw["xf"][:, dc * 512:(dc + 1) * 512], ALU.add)
            ctx.v.fence()
            if gi3 == 0:
                ovs2 = {}
            ovs2[gi3] = ctx.v.mark(ctx.vec)
            ctx.s.wait(ctx.vec, ovs2[gi3])
            sts2[gi3] = ctx.dma("s", x_chunk_span_ap(xout, dc, t0), ot)
        ctx.q.wait(ctx.vec, ovs2[spn * NCH + NCH - 1])
    ctx.barrier()
    st.close()


def emit_shift_cast(ctx, ydram, ybs, tok0, lnw):
    xf = lnw["xf"]
    xfw = xf.shape[1]
    stw = 513
    for sp in range(T // 512):
        t0 = tok0 + sp * 512
        at_start = (sp == 0)
        if at_start:
            lv = ctx.dma("q", AP(tensor=xf, offset=1,
                                 ap=[[xfw, 128], [stw, NCH], [1, 512]]),
                         x_span_ap(ydram, t0, w=512))
            ctx.v.wait(*lv)
            ctx.v.memset(AP(tensor=xf, offset=0,
                            ap=[[xfw, 128], [stw, NCH], [1, 1]]), 0.0)
            ctx.v.fence()
        else:
            lv = ctx.dma("q", AP(tensor=xf, offset=0,
                                 ap=[[xfw, 128], [stw, NCH], [1, stw]]),
                         x_span_ap(ydram, t0 - 1, w=stw))
            ctx.v.wait(*lv)
        for c in range(NCH):
            off = c * stw + (1 if c < 4 else 0)
            ctx.v.tensor_copy(ybs[:, c * T + sp * 512:c * T + sp * 512 + 512],
                              xf[:, off:off + 512])
        ctx.v.fence()
        vm = ctx.v.mark(ctx.vec)
        ctx.q.wait(ctx.vec, vm)



CONST_SPECS = [
    ("ident", (128, 128), BF16), ("ones_col", (128, 1), BF16),
    ("ones_row", (1, 128), BF16), ("mask_l0a", (128, 128), F32),
    ("mask_neg", (128, 128), F32), ("mask_negb", (128, 128), BF16),
    ("antidiag8", (128, 128), BF16), ("ident_f32", (128, 128), F32),
]

SUBLAYER_WEIGHTS = {
    0: [("l0a_qw", (D, D)), ("l0a_kw", (D, D)), ("l0a_vw", (D, D)),
        ("l0a_ow", (D, D)), ("l0a_qb", (128, NCH)), ("l0a_kb", (128, NCH)),
        ("l0a_vb", (1, D)), ("l0a_ob", (128, NCH))],
    7: [("l3f_wu", (D, 4096)), ("l3f_wd", (4096, D))],
    3: [("l1f_wqkv", (6 * D, 3072)), ("l1f_wo", (6 * D, D)),
        ("l1f_w1", (6 * D, 5632)), ("l1f_w2", (6 * 2816, D)),
        ("l1f_b1a", (6 * 128, 22)), ("l1f_b1g", (6 * 128, 22)),
        ("l1f_b2", (6 * 128, 8)), ("l1f_lng", (6 * 128, 8)),
        ("l1f_lnb", (6 * 128, 8)), ("l1f_qs", (6, 64)), ("l1f_ks", (6, 64)),
        ("l1f_cs", (128, 1024)), ("l1f_ssn", (128, 1024)),
        ("l1f_csi", (128, 1024)), ("l1f_ssi", (128, 1024)),
        ("mask_low", (128, 128))],
    1: [("l0f_win", (D, 4112)), ("l0f_wout", (2048, D)),
        ("l0f_cos", (128, 1024)), ("l0f_sin", (128, 1024)),
        ("l0f_ke", (8 * 128, 4096)), ("l0f_ve", (8 * 128, 4096))],
    5: [("l2f_wgw", (D, 8)), ("l2f_ws1", (D, 2048)), ("l2f_ws3", (D, 2048)),
        ("l2f_ws2", (2048, D)), ("l2f_we1", (8 * D, 2048)),
        ("l2f_we3", (8 * D, 2048)), ("l2f_we2", (8 * 2048, D))],
    4: [("l2a_wq", (D, D)), ("l2a_wk", (D, D)), ("l2a_wv", (D, D)),
        ("l2a_wo", (D, D)), ("l2a_lambda_q1", (1, 64)), ("l2a_lambda_k1", (1, 64)),
        ("l2a_lambda_q2", (1, 64)), ("l2a_lambda_k2", (1, 64)),
        ("l2a_lnw", (1, 128)), ("l2a_lnb", (1, 128)),
        ("l2a_cos", (128, 1024)), ("l2a_sa", (128, 1024)), ("l2a_sb", (128, 1024))],
    2: [("l1a_w", (D, 3072)), ("l1a_wp", (D, D)), ("l1a_qb", (128, NCH)),
        ("l1a_kb", (128, NCH)), ("l1a_vb", (1, D)), ("l1a_pb", (128, NCH))],
    6: [("l3a_qw", (D, D)), ("l3a_kw", (D, D)), ("l3a_vw", (D, D)),
        ("l3a_ow", (D, D)), ("l3a_qb", (128, NCH)), ("l3a_kb", (128, NCH)),
        ("l3a_vb", (1, D)), ("l3a_ob", (128, NCH)),
        ("l3a_gext", (16, 4224))],
}
SUBLAYER_EMIT = {}
SUBLAYER_EMIT[0] = emit_l0a
SUBLAYER_EMIT[2] = emit_l1a
SUBLAYER_EMIT[6] = emit_l3a
SUBLAYER_EMIT[7] = emit_l3f
SUBLAYER_EMIT[4] = emit_l2a
SUBLAYER_EMIT[5] = emit_l2f
SUBLAYER_EMIT[1] = emit_l0f
SUBLAYER_EMIT[3] = emit_l1f


def build_program(sublayers):
    nc = bass.Bass("TRN2", target_bir_lowering=False)
    ctx = Ctx(nc)
    ctx.wins = {}
    for name, shape, dt in CONST_SPECS:
        ctx.wins[name] = ctx.dram(name, list(shape), dt, kind="ExternalInput")
    for sl in sublayers:
        for name, shape in SUBLAYER_WEIGHTS.get(sl, []):
            ctx.wins[name] = ctx.dram(name, list(shape), F32, kind="ExternalInput")

    xa = ctx.dram("x_in", [NCH, 128, TOK], F32, kind="ExternalInput")
    xb = ctx.dram("x_out", [NCH, 128, TOK], F32, kind="ExternalOutput")
    xmid = ctx.dram("x_mid", [NCH, 128, TOK], F32, kind="Internal") \
        if len(sublayers) > 1 else None

    cst = {}
    cst["ones_col"] = ctx.sbuf("c_onescol", [128, 1], BF16)
    cst["ones_row"] = ctx.sbuf("c_onesrow", [1, 128], BF16)
    cst["ident_sb"] = ctx.sbuf("c_ident", [128, 128], BF16)
    cst["maskb_sb"] = ctx.sbuf("c_maskb", [128, 128], BF16)
    cst["ident_f32"] = ctx.sbuf("c_identf", [128, 128], F32)
    cst["cpool"] = ctx.sbuf("c_pool", [128, len(CONST_FLOATS)], F32)
    for sb_t, key in ((cst["ones_col"], "ones_col"), (cst["ones_row"], "ones_row"),
                      (cst["ident_sb"], "ident"), (cst["maskb_sb"], "mask_negb"),
                      (cst["ident_f32"], "ident_f32")):
        ctx.dma("q", sb_t[:], ctx.wins[key][:, :])
    for i, v in enumerate(CONST_FLOATS):
        ctx.g.memset(cst["cpool"][:, i:i + 1], float(v))
        nc.const_aps.register(F32, float(v), cst["cpool"][:, i:i + 1])
    ctx.g.mark(ctx.gp)
    ctx.barrier()

    n = len(sublayers)
    for i, sl in enumerate(sublayers):
        src = xa if i == 0 else xmid
        dst = xb if i == n - 1 else xmid
        SUBLAYER_EMIT[sl](ctx, cst, src, dst)
        ctx.barrier()
    for e in ctx.engs:
        for sem in ctx.lds + ctx.sts:
            e.wait(sem)
    ctx.es.close()
    return nc


def make_in_map(C, x_core, sublayers):
    m = {"x_in": x_core}
    for name, shape, dt in CONST_SPECS:
        m[name] = C[name]
    for sl in sublayers:
        for name, shape in SUBLAYER_WEIGHTS.get(sl, []):
            m[name] = C[name]
    return m


def kernel(**inputs):
    sublayers = sorted(SUBLAYER_EMIT.keys())
    C, xs = prep_host(inputs)
    nc = build_program(sublayers)
    in_maps = [make_in_map(C, xs[c], sublayers) for c in range(NCORES)]
    res = run_bass_kernel_spmd(nc, in_maps, core_ids=list(range(NCORES)))
    return unprep_out([r["x_out"] for r in res.results])


# version 05 (final)
# speedup vs baseline: 1.0200x; 1.0138x over previous
import math
import numpy as np
import ml_dtypes
from contextlib import ExitStack

import concourse.bass as bass
import concourse.mybir as mybir
from concourse.bass_utils import run_bass_kernel_spmd
from concourse.ap import AP

F32 = mybir.dt.float32
BF16 = mybir.dt.bfloat16
AF = mybir.ActivationFunctionType
ALU = mybir.AluOpType
AX = mybir.AxisListType
BFDT = ml_dtypes.bfloat16

NCORES = 8
SEQ = 2
T = 2048
TOK = SEQ * T
D = 1024
NCH = D // 128
NEG = -1e9
DBG = False
DBG_NAMES = None


class Sem:
    def __init__(self, nc, es, name):
        self.h = es.enter_context(nc.semaphore(name))
        self.n = 0
        self.name = name


class Eng:

    def __init__(self, name, eng):
        self.name = name
        self.e = eng
        self.last = None
        self.seen = {}
        self._marked_ins = None
        self._marked_val = 0

    def __getattr__(self, op):
        fn = getattr(self.e, op)

        def wrap(*a, **k):
            ins = fn(*a, **k)
            if op != "dma_start":
                self.last = ins
            return ins
        return wrap

    def wait(self, sem, val=None):
        if val is None:
            val = sem.n
        if val <= 0:
            return
        if self.seen.get(sem.name, -1) >= val:
            return
        self.e.wait_ge(sem.h, val)
        self.seen[sem.name] = val

    def mark(self, sem, by=1):
        assert self.last is not None, f"nothing to mark on {self.name}"
        if self._marked_ins is self.last:
            return self._marked_val
        sem.n += by
        self.last.then_inc(sem.h, by)
        self._marked_ins = self.last
        self._marked_val = sem.n
        return sem.n

    def fence(self):
        if self.last is None:
            return
        v = self.mark(self.own)
        self.wait(self.own, v)


class Ctx:
    def __init__(self, nc):
        self.nc = nc
        self.es = ExitStack()
        self.q = Eng("q", nc.sync)
        self.s = Eng("s", nc.scalar)
        self.v = Eng("v", nc.vector)
        self.t = Eng("t", nc.tensor)
        self.g = Eng("g", nc.gpsimd)
        self.engs = [self.q, self.s, self.v, self.t, self.g]
        self.lds = [Sem(nc, self.es, f"ld{i}") for i in range(4)]
        self.sts = [Sem(nc, self.es, f"st{i}") for i in range(4)]
        self.ld = self.lds[0]
        self.st = self.sts[0]
        self._ldi = 0
        self._sti = 0
        self.pe = Sem(nc, self.es, "pe")
        self.act = Sem(nc, self.es, "act")
        self.vec = Sem(nc, self.es, "vec")
        self.gp = Sem(nc, self.es, "gp")
        self._own = {"q": self.ld, "s": self.act, "v": self.vec,
                     "t": self.pe, "g": self.gp}
        for e in self.engs:
            e.own = self._own[e.name]
        self._consts = {}

    def dram(self, name, shape, dt=F32, kind="Internal"):
        return self.nc.dram_tensor(name, shape, dt, kind=kind).ap()

    def sbuf(self, name, shape, dt=F32, stack=None):
        return (stack or self.es).enter_context(self.nc.sbuf_tensor(name, shape, dt))

    def psum(self, name, shape, dt=F32, stack=None):
        return (stack or self.es).enter_context(self.nc.psum_tensor(name, shape, dt))

    def dma(self, eng, out, in_, **kw):
        e = {"q": self.q, "s": self.s}[eng]
        if eng == "q":
            sem = self.lds[self._ldi % len(self.lds)]
            self._ldi += 1
        else:
            sem = self.sts[self._sti % len(self.sts)]
            self._sti += 1
        e.wait(sem, sem.n)
        ins = e.dma_start(out=out, in_=in_, **kw)
        sem.n += 16
        ins.then_inc(sem.h, 16)
        return (sem, sem.n)

    def dbg(self, name, view):
        if not DBG or (DBG_NAMES and name not in DBG_NAMES):
            return
        self.barrier()
        shape = list(view.shape)
        ap = self.dram("dbg_" + name, shape,
                       view.dtype if view.dtype == F32 else view.dtype,
                       kind="ExternalOutput")
        self.dma("s", ap[tuple(slice(0, s) for s in shape)], view)
        self.barrier()

    def constf(self, val):
        key = float(val)
        assert key in self._consts, f"const {val} not pre-registered"
        return key

    def barrier(self):
        vals = {}
        for e in self.engs:
            sem = self._own[e.name]
            if e.last is not None:
                e.mark(sem)
                e.last = None
            vals[e.name] = sem.n
        for e in self.engs:
            for o in self.engs:
                if o is not e:
                    e.wait(self._own[o.name], vals[o.name])
            for sem in self.lds + self.sts:
                e.wait(sem)


CONST_FLOATS = [1e-5, 1e-8, 1.0, 0.5, -1.0]



def prep_host(I):
    C = {}
    f32 = np.float32
    C["ident"] = np.eye(128, dtype=BFDT)
    C["ident_f32"] = np.eye(128, dtype=f32)
    C["ones_col"] = np.ones((128, 1), dtype=BFDT)
    C["ones_row"] = np.ones((1, 128), dtype=BFDT)
    tri = np.triu(np.ones((128, 128), f32), k=1)
    C["mask_l0a"] = np.ascontiguousarray(tri * -100000.0)
    C["mask_neg"] = np.ascontiguousarray(tri * NEG)
    C["mask_negb"] = (tri * NEG).astype(BFDT)

    C["l0a_qw"] = np.ascontiguousarray(I["L0_attn_Q_w"].transpose(1, 0, 2).reshape(D, D))
    C["l0a_kw"] = np.ascontiguousarray(I["L0_attn_K_w"].transpose(1, 0, 2).reshape(D, D))
    C["l0a_vw"] = np.ascontiguousarray(I["L0_attn_V_w"].transpose(1, 0, 2).reshape(D, D))
    C["l0a_ow"] = np.ascontiguousarray(I["L0_attn_O_w"].reshape(D, D))
    C["l0a_qb"] = np.ascontiguousarray(I["L0_attn_Q_b"].reshape(NCH, 128).T)
    C["l0a_kb"] = np.ascontiguousarray(I["L0_attn_K_b"].reshape(NCH, 128).T)
    C["l0a_vb"] = np.ascontiguousarray(I["L0_attn_V_b"].reshape(1, D))
    C["l0a_ob"] = np.ascontiguousarray(I["L0_attn_O_b"].reshape(NCH, 128).T)

    C["l1a_w"] = I["L1_attn_w_attn"]
    C["l1a_wp"] = I["L1_attn_w_proj"]
    ba = I["L1_attn_b_attn"]
    C["l1a_qb"] = np.ascontiguousarray(ba[0:1024].reshape(NCH, 128).T)
    C["l1a_kb"] = np.ascontiguousarray(ba[1024:2048].reshape(NCH, 128).T)
    C["l1a_vb"] = np.ascontiguousarray(ba[2048:3072].reshape(1, D))
    C["l1a_pb"] = np.ascontiguousarray(I["L1_attn_b_proj"].reshape(NCH, 128).T)

    C["l3a_qw"] = np.ascontiguousarray(I["L3_attn_Wq"].T)
    C["l3a_kw"] = np.ascontiguousarray(I["L3_attn_Wk"].T)
    C["l3a_vw"] = np.ascontiguousarray(I["L3_attn_Wv"].T)
    C["l3a_ow"] = np.ascontiguousarray(I["L3_attn_Wo"].T)
    C["l3a_qb"] = np.ascontiguousarray(I["L3_attn_bq"].reshape(NCH, 128).T)
    C["l3a_kb"] = np.ascontiguousarray(I["L3_attn_bk"].reshape(NCH, 128).T)
    C["l3a_vb"] = np.ascontiguousarray(I["L3_attn_bv"].reshape(1, D))
    C["l3a_ob"] = np.ascontiguousarray(I["L3_attn_bo"].reshape(NCH, 128).T)
    tt = np.arange(4224)
    idx = np.clip(tt - 2174, -15, 15) + 15
    C["l3a_gext"] = np.ascontiguousarray(I["L3_attn_rel_bias"][idx, :].T)
    ad8 = np.zeros((128, 128), np.float32)
    ad8[np.arange(128), 127 - np.arange(128)] = 8.0
    C["antidiag8"] = ad8.astype(BFDT)

    for nm in ("Wq", "Wk", "Wv", "Wo"):
        C[f"l2a_{nm.lower()}"] = I[f"L2_attn_{nm}"]
    for nm in ("lambda_q1", "lambda_k1", "lambda_q2", "lambda_k2"):
        C[f"l2a_{nm}"] = np.ascontiguousarray(I[f"L2_attn_{nm}"].reshape(1, 64))
    C["l2a_lnw"] = np.ascontiguousarray(I["L2_attn_ln_w"].reshape(1, 128))
    C["l2a_lnb"] = np.ascontiguousarray(I["L2_attn_ln_b"].reshape(1, 128))
    dd = 64
    invf = 1.0 / 10000.0 ** (np.arange(0, dd, 2, dtype=f32) / dd)
    pos = np.arange(T, dtype=f32)
    fr = np.repeat(pos[:, None] * invf[None, :], 2, axis=-1)
    cosr = np.cos(fr)
    sinr = np.sin(fr)
    sA = np.zeros((T, dd), f32); sA[:, 0::2] = -sinr[:, 0::2]
    sB = np.zeros((T, dd), f32); sB[:, 1::2] = sinr[:, 1::2]
    def tile_tok(tab):
        return np.ascontiguousarray(
            tab.reshape(16, 128, dd).transpose(1, 0, 2).reshape(128, 16 * dd))
    C["l2a_cos"] = tile_tok(cosr)
    C["l2a_sa"] = tile_tok(sA)
    C["l2a_sb"] = tile_tok(sB)

    C["l0f_win"] = np.ascontiguousarray(I["L0_ffn_w_in"].T)
    C["l0f_wout"] = np.ascontiguousarray(I["L0_ffn_w_out"].T)
    half64 = np.arange(64, dtype=f32)
    invr = 1.0 / 10000.0 ** (half64 / 64.0)
    ang = np.arange(T, dtype=f32)[:, None] * invr[None, :]
    C["l0f_cos"] = np.ascontiguousarray(
        np.cos(ang).reshape(16, 128, 64).transpose(1, 0, 2).reshape(128, 1024))
    C["l0f_sin"] = np.ascontiguousarray(
        np.sin(ang).reshape(16, 128, 64).transpose(1, 0, 2).reshape(128, 1024))
    C["l0f_ke"] = np.ascontiguousarray(
        I["L0_ffn_k_ffwd"].transpose(1, 3, 0, 2).reshape(8 * 128, 16 * 256))
    vf = I["L0_ffn_v_ffwd"].reshape(16, 8, 2, 128, 128)
    C["l0f_ve"] = np.ascontiguousarray(
        vf.transpose(1, 3, 0, 2, 4).reshape(8 * 128, 16 * 2 * 128))

    C["l1f_wqkv"] = np.ascontiguousarray(I["L1_ffn_Wqkv"].reshape(6 * D, 3072))
    C["l1f_wo"] = np.ascontiguousarray(I["L1_ffn_Wo"].reshape(6 * D, D))
    w1p = np.zeros((6, D, 5632), f32)
    w1p[:, :, 0:2730] = I["L1_ffn_W1"][:, :, 0:2730]
    w1p[:, :, 2816:2816 + 2730] = I["L1_ffn_W1"][:, :, 2730:5460]
    C["l1f_w1"] = np.ascontiguousarray(w1p.reshape(6 * D, 5632))
    w2p = np.zeros((6, 2816, D), f32)
    w2p[:, 0:2730, :] = I["L1_ffn_W2"]
    C["l1f_w2"] = np.ascontiguousarray(w2p.reshape(6 * 2816, D))
    b1p = np.zeros((6, 2, 2816), f32)
    b1p[:, 0, 0:2730] = I["L1_ffn_b1"][:, 0:2730]
    b1p[:, 1, 0:2730] = I["L1_ffn_b1"][:, 2730:5460]
    C["l1f_b1a"] = np.ascontiguousarray(
        b1p[:, 0].reshape(6, 22, 128).transpose(0, 2, 1).reshape(6 * 128, 22))
    C["l1f_b1g"] = np.ascontiguousarray(
        b1p[:, 1].reshape(6, 22, 128).transpose(0, 2, 1).reshape(6 * 128, 22))
    C["l1f_b2"] = np.ascontiguousarray(
        I["L1_ffn_b2"].reshape(6, 8, 128).transpose(0, 2, 1).reshape(6 * 128, 8))
    C["l1f_lng"] = np.ascontiguousarray(
        I["L1_ffn_ln_g"].reshape(6, 8, 128).transpose(0, 2, 1).reshape(6 * 128, 8))
    C["l1f_lnb"] = np.ascontiguousarray(
        I["L1_ffn_ln_b"].reshape(6, 8, 128).transpose(0, 2, 1).reshape(6 * 128, 8))
    C["l1f_qs"] = I["L1_ffn_q_scale"]
    C["l1f_ks"] = I["L1_ffn_k_scale"]
    dh = 64
    half32 = np.arange(0, dh, 2, dtype=f32)
    invf2 = 1.0 / 10000.0 ** (half32 / dh)
    tpos = np.arange(T, dtype=f32)
    fr2 = tpos[:, None] * invf2[None, :]
    fr2 = np.concatenate([fr2, fr2], -1)
    base_sc = (half32 + 0.4 * dh) / (1.4 * dh)
    power = (tpos - T // 2) / (256 // 2)
    xsc = base_sc[None, :] ** power[:, None]
    xsc = np.concatenate([xsc, xsc], -1)
    cs_q = np.cos(fr2) * xsc
    sn_q = np.sin(fr2) * xsc
    cs_k = np.cos(fr2) / xsc
    sn_k = np.sin(fr2) / xsc
    def xtile(tab):
        return np.ascontiguousarray(
            tab.reshape(16, 128, dh).transpose(1, 0, 2).reshape(128, 16 * dh))
    ssn_q = np.concatenate([-sn_q[:, 0:32], sn_q[:, 32:64]], -1)
    ssn_k = np.concatenate([-sn_k[:, 0:32], sn_k[:, 32:64]], -1)
    C["l1f_cs"] = xtile(cs_q)
    C["l1f_ssn"] = xtile(ssn_q)
    C["l1f_csi"] = xtile(cs_k)
    C["l1f_ssi"] = xtile(ssn_k)
    C["mask_low"] = np.ascontiguousarray(
        np.tril(np.ones((128, 128), f32), k=-1) * NEG)

    C["l2f_wgw"] = I["L2_ffn_Wg"]
    C["l2f_ws1"] = I["L2_ffn_Ws1"]
    C["l2f_ws3"] = I["L2_ffn_Ws3"]
    C["l2f_ws2"] = I["L2_ffn_Ws2"]
    C["l2f_we1"] = np.ascontiguousarray(I["L2_ffn_We1"].reshape(8 * D, 2048))
    C["l2f_we3"] = np.ascontiguousarray(I["L2_ffn_We3"].reshape(8 * D, 2048))
    C["l2f_we2"] = np.ascontiguousarray(I["L2_ffn_We2"].reshape(8 * 2048, D))

    C["l3f_wu"] = np.ascontiguousarray(I["L3_ffn_w_up"].T)
    C["l3f_wd"] = np.ascontiguousarray(I["L3_ffn_w_down"].T)

    x = I["L0_attn_x"]
    xs = []
    for c in range(NCORES):
        xc = x[2 * c:2 * c + 2].transpose(2, 0, 1)
        xs.append(np.ascontiguousarray(xc).reshape(NCH, 128, TOK))
    return C, xs


def unprep_out(outs):
    res = np.empty((16, T, D), np.float32)
    for c, oc in enumerate(outs):
        res[2 * c:2 * c + 2] = oc.reshape(D, SEQ, T).transpose(1, 2, 0)
    return res



def x_span_ap(x_dram, t0, w=512, nch=NCH):
    return AP(tensor=x_dram.tensor,
              offset=x_dram.offset + t0,
              ap=[[TOK, 128], [128 * TOK, nch], [1, w]])


def x_chunk_span_ap(x_dram, c, t0, w=512):
    return AP(tensor=x_dram.tensor,
              offset=x_dram.offset + c * 128 * TOK + t0,
              ap=[[TOK, 128], [1, w]])


def col_block_ap(w_dram, dout_total, col0, w, nch=NCH):
    return AP(tensor=w_dram.tensor,
              offset=w_dram.offset + col0,
              ap=[[dout_total, 128], [128 * dout_total, nch], [1, w]])


class WStream:

    def __init__(self, ctx, st, tag, cols, stage_cols=None, stage=None, nch=NCH):
        self.ctx = ctx
        self.cols = cols
        self.nch = nch
        sc = stage_cols or cols
        self.stage = stage if stage is not None else \
            ctx.sbuf(f"{tag}_wst", [128, 2 * sc], F32, st)
        self.sc = sc
        self.bf = ctx.sbuf(f"{tag}_wbf", [128, nch * cols], BF16, st)

    def load(self, w_ap, row0=0):
        ctx = self.ctx
        if not hasattr(ctx, "_stage_guard"):
            ctx._stage_guard = {}
        guard = ctx._stage_guard.get(id(self.stage), 0)
        marks = {}
        for c in range(self.nch):
            half = (c % 2) * self.sc
            if c >= 2:
                ctx.q.wait(ctx.gp, marks[c - 2])
            elif guard:
                ctx.q.wait(ctx.gp, guard)
            lv = ctx.dma("q", self.stage[:, half:half + self.cols],
                         w_ap[row0 + c * 128: row0 + (c + 1) * 128, :])
            ctx.g.wait(*lv)
            ctx.g.tensor_copy(self.bf[:, c * self.cols:(c + 1) * self.cols],
                              self.stage[:, half:half + self.cols])
            marks[c] = ctx.g.mark(ctx.gp)
        ctx._stage_guard[id(self.stage)] = marks[self.nch - 1]
        return marks[self.nch - 1]

    def tile(self, c, j0, w=128):
        return self.bf[:, c * self.cols + j0: c * self.cols + j0 + w]


def emit_ln(ctx, cst, w, x_dram, xn_dst, ntok, tok0=0, eps=1e-5,
            fp32_cb=None, ln_width=D, banks=None, xn_ntok=None, scale_cb=None,
            shift=False, seq_start=None, affine=None, fp32_store=None):
    nsp = ntok // 512
    xn_ntok = xn_ntok or ntok
    ps_s = banks[0][0:1, :]
    ps_q = banks[1][0:1, :]
    ps_b1 = banks[2]
    ps_b2 = banks[3]
    r = w["rows"]
    mu = r[0:1, 0:512]
    mu2 = r[0:1, 512:1024]
    var = r[0:1, 1024:1536]
    lnv = r[0:1, 512:1024]
    rstd = r[0:1, 1536:2048]
    nm = r[0:1, 1024:1536]
    rb = w["rows_b"]
    rstd_b, nm_b = rb[0:1, 0:512], rb[0:1, 512:1024]
    rstdb, nmb = w["bc"][:, 0:512], w["bc"][:, 512:1024]
    xfw = w["xf"].shape[1]
    stw = 513 if shift else 512
    fst = {}
    for sp in range(nsp):
        t0 = tok0 + sp * 512
        if shift:
            at_start = (t0 == seq_start)
            if at_start:
                lv = ctx.dma("q", AP(tensor=w["xf"], offset=1,
                                     ap=[[xfw, 128], [stw, NCH], [1, 512]]),
                             x_span_ap(x_dram, t0, w=512))
                ctx.v.wait(*lv)
                ctx.v.memset(AP(tensor=w["xf"], offset=0,
                                ap=[[xfw, 128], [stw, NCH], [1, 1]]), 0.0)
                ctx.v.fence()
                zm = ctx.v.mark(ctx.vec)
                ctx.s.wait(ctx.vec, zm)
            else:
                lv = ctx.dma("q", AP(tensor=w["xf"], offset=0,
                                     ap=[[xfw, 128], [stw, NCH], [1, stw]]),
                             x_span_ap(x_dram, t0 - 1, w=stw))
                ctx.s.wait(*lv)
                ctx.v.wait(*lv)

            def csl(c):
                off = c * stw + (1 if c < 4 else 0)
                return slice(off, off + 512)
        else:
            lv = ctx.dma("q", w["xf"][:, 0:NCH * 512], x_span_ap(x_dram, t0))
            ctx.s.wait(*lv)
            ctx.v.wait(*lv)

            def csl(c):
                return slice(c * 512, (c + 1) * 512)
        cm = {}
        for c in range(NCH):
            hb = slice((c % 2) * 512, (c % 2) * 512 + 512)
            if c >= 2:
                ctx.s.wait(ctx.pe, cm[c - 2])
                ctx.v.wait(ctx.pe, cm[c - 2])
            ctx.s.copy(w["xb"][:, hb], w["xf"][:, csl(c)])
            am = ctx.s.mark(ctx.act)
            ctx.v.tensor_tensor(w["xqb"][:, hb], w["xf"][:, csl(c)],
                                w["xf"][:, csl(c)], ALU.mult)
            vm = ctx.v.mark(ctx.vec)
            ctx.t.wait(ctx.act, am)
            ctx.t.wait(ctx.vec, vm)
            ctx.t.matmul(ps_s, cst["ones_col"][:], w["xb"][:, hb],
                         start=(c == 0), stop=(c == NCH - 1))
            ctx.t.matmul(ps_q, cst["ones_col"][:], w["xqb"][:, hb],
                         start=(c == 0), stop=(c == NCH - 1))
            cm[c] = ctx.t.mark(ctx.pe)
        pm = cm[NCH - 1]
        ctx.s.wait(ctx.pe, pm)
        ctx.s.mul(mu, ps_s, 1.0 / ln_width)
        ctx.s.fence()
        ctx.s.activation(mu2, mu, AF.Square)
        am2 = ctx.s.mark(ctx.act)
        ctx.v.wait(ctx.act, am2)
        ctx.v.scalar_tensor_tensor(var, ps_q, 1.0 / ln_width, mu2,
                                   ALU.mult, ALU.subtract)
        vm2 = ctx.v.mark(ctx.vec)
        ctx.s.wait(ctx.vec, vm2)
        ctx.s.activation(lnv, var, AF.Ln, bias=float(eps))
        ctx.s.fence()
        ctx.s.activation(rstd, lnv, AF.Exp, scale=-0.5)
        am3 = ctx.s.mark(ctx.act)
        if scale_cb is not None:
            scale_cb(ctx, w, mu, var, rstd)
        ctx.v.wait(ctx.act, am3)
        ctx.v.tensor_tensor(nm, mu, rstd, ALU.mult)
        ctx.v.fence()
        vm3 = ctx.v.mark(ctx.vec)
        ctx.s.wait(ctx.vec, vm3)
        ctx.s.fence()
        ctx.s.copy(rstd_b, rstd)
        ctx.s.copy(nm_b, nm)
        am4 = ctx.s.mark(ctx.act)
        ctx.t.wait(ctx.act, am4)
        ctx.t.matmul(ps_b1[:, :], cst["ones_row"][:], rstd_b, start=True, stop=True)
        ctx.t.matmul(ps_b2[:, :], cst["ones_row"][:], nm_b, start=True, stop=True)
        pm2 = ctx.t.mark(ctx.pe)
        ctx.s.wait(ctx.pe, pm2)
        ctx.s.copy(rstdb, ps_b1[:, :])
        ctx.s.copy(nmb, ps_b2[:, :])
        am5 = ctx.s.mark(ctx.act)
        ctx.v.wait(ctx.act, am5)
        for c in range(NCH):
            dsl = slice(c * xn_ntok + sp * 512, c * xn_ntok + sp * 512 + 512)
            ctx.v.tensor_tensor(w["tmp"][:], w["xf"][:, csl(c)], rstdb, ALU.mult)
            ctx.v.fence()
            if affine is not None:
                gfn, bfn = affine
                ctx.v.tensor_tensor(w["tmp"][:], w["tmp"][:], nmb, ALU.subtract)
                ctx.v.fence()
                ctx.v.scalar_tensor_tensor(xn_dst[:, dsl], w["tmp"][:],
                                           gfn(c), bfn(c), ALU.mult, ALU.add)
            else:
                if xn_dst is not None:
                    ctx.v.tensor_tensor(xn_dst[:, dsl], w["tmp"][:], nmb,
                                        ALU.subtract)
                if fp32_cb is not None:
                    ctx.v.tensor_tensor(fp32_cb(c, sp), w["tmp"][:], nmb,
                                        ALU.subtract)
                if fp32_store is not None:
                    ydram, yscr = fp32_store
                    gi = sp * NCH + c
                    if gi >= 2:
                        ctx.v.wait(*fst[gi - 2])
                    ysl = yscr[:, (gi % 2) * 512:(gi % 2) * 512 + 512]
                    ctx.v.tensor_tensor(ysl, w["tmp"][:], nmb, ALU.subtract)
                    ctx.v.fence()
                    fm = ctx.v.mark(ctx.vec)
                    ctx.s.wait(ctx.vec, fm)
                    fst[gi] = ctx.dma("s", x_chunk_span_ap(ydram, c, t0), ysl)
            ctx.v.fence()
        vm4 = ctx.v.mark(ctx.vec)
        ctx.q.wait(ctx.vec, vm4)
        ctx.t.wait(ctx.vec, vm4)


def ln_work(ctx, st, tag):
    return {
        "xf": ctx.sbuf(f"{tag}_xf", [128, NCH * 513], F32, st),
        "xb": ctx.sbuf(f"{tag}_xb", [128, 1024], BF16, st),
        "xqb": ctx.sbuf(f"{tag}_xqb", [128, 1024], BF16, st),
        "rows": ctx.sbuf(f"{tag}_rows", [1, 512 * 4], F32, st),
        "rows_b": ctx.sbuf(f"{tag}_rowsb", [1, 1024], BF16, st),
        "bc": ctx.sbuf(f"{tag}_bc", [128, 1024], F32, st),
        "tmp": ctx.sbuf(f"{tag}_tmp", [128, 512], F32, st),
    }


def emit_residual_out(ctx, cst, lnw, xin, xout, tok0, ntok, proj_cb, bias_sb,
                      banks, extra_add=None, scale=None):
    nsp = ntok // 512
    xnew0 = lnw["bc"][:, 0:512]
    xnew1 = lnw["bc"][:, 512:1024]
    evs = {}
    stores = {}
    for spn in range(nsp):
        t0 = tok0 + spn * 512
        lv = ctx.dma("q", lnw["xf"][:, 0:NCH * 512], x_span_ap(xin, t0))
        for dc in range(NCH):
            i = spn * NCH + dc
            ps = banks[dc % 2]
            if i >= 2:
                ctx.t.wait(ctx.vec, evs[i - 2])
            proj_cb(dc, spn, ps)
            pm = ctx.t.mark(ctx.pe)
            ctx.s.wait(ctx.pe, pm)
            if bias_sb is not None:
                ctx.s.activation(lnw["tmp"][:], ps[:, :], AF.Identity,
                                 bias=bias_sb[:, dc:dc + 1],
                                 scale=(scale or 1.0))
            else:
                ctx.s.activation(lnw["tmp"][:], ps[:, :], AF.Identity,
                                 scale=(scale or 1.0))
            am = ctx.s.mark(ctx.act)
            xnew = xnew0 if i % 2 == 0 else xnew1
            ctx.v.wait(ctx.act, am)
            ctx.v.wait(*lv)
            if i >= 2:
                ctx.v.wait(*stores[i - 2])
            if extra_add is not None:
                ctx.v.tensor_tensor(lnw["tmp"][:], lnw["tmp"][:],
                                    extra_add(dc, spn), ALU.add)
                ctx.v.fence()
            ctx.v.tensor_tensor(xnew, lnw["tmp"][:],
                                lnw["xf"][:, dc * 512:(dc + 1) * 512], ALU.add)
            evs[i] = ctx.v.mark(ctx.vec)
            ctx.s.wait(ctx.vec, evs[i])
            stores[i] = ctx.dma("s", x_chunk_span_ap(xout, dc, t0), xnew)
        ctx.q.wait(ctx.vec, evs[spn * NCH + NCH - 1])


def emit_softmax_attn_head(ctx, cst, qf, kf, vtm, s_sb, p_sb, pt_sb, stat,
                           banks, ps_t, ps_pv, mask, r0, scale, dv, vcol,
                           acat_cb, causal=True, bias_mm=None, width_full=T,
                           normalize=True, dbg_tag=None):
    dh = dv if r0 is None else 64
    bank_ev = {0: 0, 1: 0}
    nslots = 2 if s_sb.shape[1] >= 2 * width_full else 1
    exp_am = {}
    for qs in range(4):
        qts = [qs * 4 + i for i in range(4)]
        pns = {}
        for i, qt in enumerate(qts):
            negmax = stat[:, 4 * i:4 * i + 1]
            negmax_s = stat[:, 4 * i + 1:4 * i + 2]
            sums = stat[:, 4 * i + 2:4 * i + 3]
            recip = stat[:, 4 * i + 3:4 * i + 4]
            slot = qt % nslots
            sofs = slot * width_full
            wk = (qt + 1) * 128 if causal else width_full
            nks = (wk + 511) // 512
            for ks in range(nks):
                kw = min(512, wk - ks * 512)
                b = ks % 2
                ps = banks[b]
                ctx.t.wait(ctx.act, bank_ev[b])
                ctx.t.matmul(ps[:, 0:kw],
                             qf[r0:r0 + dh, qt * 128:(qt + 1) * 128] if r0 is not None
                             else qf[:, qt * 128:(qt + 1) * 128],
                             kf[r0:r0 + dh, ks * 512:ks * 512 + kw] if r0 is not None
                             else kf[:, ks * 512:ks * 512 + kw],
                             start=True, stop=(bias_mm is None))
                if bias_mm is not None:
                    bias_mm(ctx, ps, ks, kw, qt)
                pm = ctx.t.mark(ctx.pe)
                if ks == 0 and slot in exp_am:
                    ctx.s.wait(ctx.act, exp_am[slot])
                ctx.s.wait(ctx.pe, pm)
                ctx.s.copy(s_sb[:, sofs + ks * 512:sofs + ks * 512 + kw],
                           ps[:, 0:kw])
                bank_ev[b] = ctx.s.mark(ctx.act)
            ctx.v.wait(ctx.act, bank_ev[(nks - 1) % 2])
            if causal and mask is not None:
                ctx.v.tensor_tensor(s_sb[:, sofs + qt * 128:sofs + (qt + 1) * 128],
                                    s_sb[:, sofs + qt * 128:sofs + (qt + 1) * 128],
                                    mask[:], ALU.add)
                ctx.v.fence()
            ctx.v.reduce_max(negmax, s_sb[:, sofs:sofs + wk], axis=AX.X,
                             negate=True)
            vm = ctx.v.mark(ctx.vec)
            ctx.s.wait(ctx.vec, vm)
            ctx.s.mul(negmax_s, negmax, float(scale))
            ctx.s.memzero(sums)
            ctx.s.fence()
            ctx.s.activation(p_sb[:, i * T:i * T + wk], s_sb[:, sofs:sofs + wk],
                             AF.Exp, scale=float(scale), bias=negmax_s,
                             accum_out=sums)
            am = ctx.s.mark(ctx.act)
            exp_am[slot] = am
            if dbg_tag and qt == 15:
                ctx.dbg(dbg_tag + "_s15", s_sb[:, sofs:sofs + width_full])
                ctx.dbg(dbg_tag + "_praw15", p_sb[:, i * T:i * T + wk])
                ctx.dbg(dbg_tag + "_stat15", stat[:])
            ctx.v.wait(ctx.act, am)
            if normalize:
                ctx.v.reciprocal(recip, sums)
                ctx.v.fence()
                ctx.v.tensor_scalar(p_sb[:, i * T:i * T + wk],
                                    p_sb[:, i * T:i * T + wk],
                                    recip, None, ALU.mult)
            pns[i] = ctx.v.mark(ctx.vec)
        nkt = qts[-1] + 1 if causal else width_full // 128
        first_pv = True
        ctx.t.wait(ctx.vec, pns[3])
        for kt in range(nkt):
            idx = [i for i, qt in enumerate(qts) if (not causal) or qt >= kt]
            for j, i in enumerate(idx):
                ctx.t.matmul(ps_t[:, i * 128:(i + 1) * 128],
                             p_sb[:, i * T + kt * 128:i * T + (kt + 1) * 128],
                             cst["ident_sb"][:], is_transpose=True,
                             start=(j == 0), stop=(j == len(idx) - 1))
            tm = ctx.t.mark(ctx.pe)
            ctx.v.wait(ctx.pe, tm)
            off = idx[0] * 128
            ctx.v.tensor_copy(pt_sb[:, off:512], ps_t[:, off:512])
            ptm = ctx.v.mark(ctx.vec)
            ctx.t.wait(ctx.vec, ptm)
            col0, w = vcol(kt)
            ctx.t.matmul(ps_pv[0:dv, off:512], vtm[:, col0:col0 + w],
                         pt_sb[:, off:512],
                         start=first_pv, stop=(kt == nkt - 1))
            first_pv = False
        pvm = ctx.t.mark(ctx.pe)
        acat_cb(qs, pvm)


def emit_sparsemax_attn_head(ctx, cst, qf, kf, vtm, sp_all, stat16, pt_sb,
                             banks, ps_t, ps_pv, mask, r0, scale, dv, vcol,
                             acat_cb, scr_bf, niter=6):
    OFF = [128 * qt * (qt + 1) // 2 for qt in range(17)]
    m_all = stat16[:, 0:16]
    negtau = stat16[:, 16:32]
    S_all = stat16[:, 32:48]
    cnt = stat16[:, 48:64]
    rc = stat16[:, 64:80]
    postau = stat16[:, 80:96]
    bank_ev = {0: 0, 1: 0}
    for qt in range(16):
        wk = (qt + 1) * 128
        nks = (wk + 511) // 512
        for ks in range(nks):
            kw = min(512, wk - ks * 512)
            b = ks % 2
            ctx.t.wait(ctx.act, bank_ev[b])
            ctx.t.matmul(banks[b][:, 0:kw],
                         qf[r0:r0 + 64, qt * 128:(qt + 1) * 128],
                         kf[r0:r0 + 64, ks * 512:ks * 512 + kw],
                         start=True, stop=True)
            pm = ctx.t.mark(ctx.pe)
            ctx.s.wait(ctx.pe, pm)
            ctx.s.activation(sp_all[:, OFF[qt] + ks * 512:OFF[qt] + ks * 512 + kw],
                             banks[b][:, 0:kw], AF.Identity, scale=float(scale))
            bank_ev[b] = ctx.s.mark(ctx.act)
        ctx.v.wait(ctx.act, bank_ev[(nks - 1) % 2])
        ctx.v.tensor_tensor(sp_all[:, OFF[qt] + qt * 128:OFF[qt] + (qt + 1) * 128],
                            sp_all[:, OFF[qt] + qt * 128:OFF[qt] + (qt + 1) * 128],
                            cst["maskb_sb"][:], ALU.add)
        ctx.v.fence()
        ctx.v.reduce_max(m_all[:, qt:qt + 1], sp_all[:, OFF[qt]:OFF[qt] + wk],
                         axis=AX.X)
    ctx.v.fence()
    ctx.v.tensor_scalar(negtau, m_all, -1.0, 1.0, ALU.mult, ALU.add)
    ctx.v.tensor_scalar_add(postau, m_all, -1.0)
    ctx.v.fence()
    vm = ctx.v.mark(ctx.vec)
    ctx.s.wait(ctx.vec, vm)
    for it in range(niter):
        ctx.s.memzero(S_all)
        ctx.s.fence()
        for qt in range(16):
            wk = (qt + 1) * 128
            ctx.s.activation(scr_bf[:, 0:wk], sp_all[:, OFF[qt]:OFF[qt] + wk],
                             AF.Relu, bias=negtau[:, qt:qt + 1],
                             accum_out=S_all[:, qt:qt + 1])
            ctx.s.fence()
            ctx.v.tensor_scalar(scr_bf[:, T:T + wk], sp_all[:, OFF[qt]:OFF[qt] + wk],
                                postau[:, qt:qt + 1], None, ALU.is_gt, ALU.add,
                                accum_out=cnt[:, qt:qt + 1])
            ctx.v.fence()
        am = ctx.s.mark(ctx.act)
        ctx.v.fence()
        ctx.v.tensor_scalar_max(cnt, cnt, 1.0)
        ctx.v.fence()
        ctx.v.reciprocal(rc, cnt)
        ctx.v.wait(ctx.act, am)
        ctx.v.tensor_scalar_add(S_all, S_all, -1.0)
        ctx.v.fence()
        ctx.v.tensor_tensor(S_all, S_all, rc, ALU.mult)
        ctx.v.fence()
        ctx.v.tensor_tensor(negtau, negtau, S_all, ALU.subtract)
        ctx.v.tensor_tensor(postau, postau, S_all, ALU.add)
        ctx.v.fence()
        vm = ctx.v.mark(ctx.vec)
        ctx.s.wait(ctx.vec, vm)
    for qs in range(4):
        qts = [qs * 4 + i for i in range(4)]
        for qt in qts:
            wk = (qt + 1) * 128
            ctx.s.activation(sp_all[:, OFF[qt]:OFF[qt] + wk],
                             sp_all[:, OFF[qt]:OFF[qt] + wk],
                             AF.Relu, bias=negtau[:, qt:qt + 1])
        am2 = ctx.s.mark(ctx.act)
        ctx.t.wait(ctx.act, am2)
        ctx.v.wait(ctx.act, am2)
        nkt = qts[-1] + 1
        first_pv = True
        for kt in range(nkt):
            idx = [i for i, qt in enumerate(qts) if qt >= kt]
            for j, i in enumerate(idx):
                qt = qts[i]
                ctx.t.matmul(ps_t[:, i * 128:(i + 1) * 128],
                             sp_all[:, OFF[qt] + kt * 128:OFF[qt] + (kt + 1) * 128],
                             cst["ident_sb"][:], is_transpose=True,
                             start=(j == 0), stop=(j == len(idx) - 1))
            tm = ctx.t.mark(ctx.pe)
            ctx.v.wait(ctx.pe, tm)
            off = idx[0] * 128
            ctx.v.tensor_copy(pt_sb[:, off:512], ps_t[:, off:512])
            ptm = ctx.v.mark(ctx.vec)
            ctx.t.wait(ctx.vec, ptm)
            col0, w = vcol(kt)
            ctx.t.matmul(ps_pv[0:dv, off:512], vtm[:, col0:col0 + w],
                         pt_sb[:, off:512],
                         start=first_pv, stop=(kt == nkt - 1))
            first_pv = False
        pvm = ctx.t.mark(ctx.pe)
        acat_cb(qs, pvm)



def emit_l0a(ctx, cst, xin, xout):
    st = ExitStack()
    xn = ctx.sbuf("l0a_xn", [128, NCH * T], BF16, st)
    lnw = ln_work(ctx, st, "l0a")
    wstage = ctx.sbuf("l0a_wstage", [128, 2 * D], F32, st)
    ws_qk = WStream(ctx, st, "l0a_qk", 128, stage=wstage)
    ws_v = WStream(ctx, st, "l0a_v", D, stage=wstage)
    ws_o = WStream(ctx, st, "l0a_o", D, stage=wstage)
    qb_sb = ctx.sbuf("l0a_qb_s", [128, NCH], F32, st)
    kb_sb = ctx.sbuf("l0a_kb_s", [128, NCH], F32, st)
    ob_sb = ctx.sbuf("l0a_ob_s", [128, NCH], F32, st)
    vb_bc = ctx.sbuf("l0a_vbbc", [128, D], F32, st)
    mask = ctx.sbuf("l0a_mask", [128, 128], F32, st)
    qf = ctx.sbuf("l0a_qf", [128, T], BF16, st)
    kf = ctx.sbuf("l0a_kf", [128, T], BF16, st)
    vtm = ctx.sbuf("l0a_vtm", [128, 16 * D], BF16, st)
    acat = ctx.sbuf("l0a_acat", [128, NCH * T], BF16, st)
    s_sb = lnw["xf"][:, 0:2 * T]
    p_sb = ctx.sbuf("l0a_p", [128, 4 * T], BF16, st)
    pt_sb = ctx.sbuf("l0a_pt", [128, 512], BF16, st)
    stat = ctx.sbuf("l0a_stat", [128, 16], F32, st)
    banks = [ctx.psum(f"l0a_ps{i}", [128, 512], F32, st) for i in range(4)]
    ps_t = ctx.psum("l0a_pst", [128, 1024], BF16, st)
    ps_pv = ctx.psum("l0a_pspv", [128, 512], F32, st)

    W = ctx.wins
    for dst, key in ((qb_sb, "l0a_qb"), (kb_sb, "l0a_kb"), (ob_sb, "l0a_ob"),
                     (mask, "mask_l0a")):
        ctx.dma("q", dst[:], W[key][:, :])
    ctx.dma("q", vb_bc[:], W["l0a_vb"][0:1, :].broadcast_to([128, D]))
    ws_v.load(W["l0a_vw"])
    ws_o.load(W["l0a_ow"])
    ctx.barrier()

    for sq in range(SEQ):
        tok0 = sq * T
        emit_ln(ctx, cst, lnw, xin, xn, T, tok0=tok0, banks=banks)
        ctx.barrier()
        if sq == 0:
            ctx.dbg("xn", xn[:])

        ev = {}
        for kt in range(16):
            for half in range(2):
                i = kt * 2 + half
                ps = banks[i % 4]
                if i >= 4:
                    ctx.t.wait(ctx.vec, ev[i - 4])
                for c in range(NCH):
                    ctx.t.matmul(ps[:, :],
                                 xn[:, c * T + kt * 128:c * T + (kt + 1) * 128],
                                 ws_v.bf[:, c * D + half * 512:c * D + half * 512 + 512],
                                 start=(c == 0), stop=(c == NCH - 1))
                pm = ctx.t.mark(ctx.pe)
                ctx.v.wait(ctx.pe, pm)
                ctx.v.tensor_tensor(
                    vtm[:, kt * D + half * 512:kt * D + half * 512 + 512],
                    ps[:, :], vb_bc[:, half * 512:half * 512 + 512], ALU.add)
                ev[i] = ctx.v.mark(ctx.vec)
        ctx.barrier()
        if sq == 0:
            ctx.dbg("vtm", vtm[:])

        for g in range(8):
            for wb_is_q in (True, False):
                wdram = W["l0a_qw"] if wb_is_q else W["l0a_kw"]
                dst = qf if wb_is_q else kf
                bias = qb_sb if wb_is_q else kb_sb
                lv = ctx.dma("q", ws_qk.stage[:, 0:NCH * 128],
                             col_block_ap(wdram, D, g * 128, 128))
                ctx.g.wait(*lv)
                for c in range(NCH):
                    ctx.g.tensor_copy(ws_qk.bf[:, c * 128:(c + 1) * 128],
                                      ws_qk.stage[:, c * 128:(c + 1) * 128])
                gm = ctx.g.mark(ctx.gp)
                ctx.t.wait(ctx.gp, gm)
                for spn in range(4):
                    ps = banks[spn]
                    for c in range(NCH):
                        ctx.t.matmul(ps[:, :], ws_qk.bf[:, c * 128:(c + 1) * 128],
                                     xn[:, c * T + spn * 512:c * T + spn * 512 + 512],
                                     start=(c == 0), stop=(c == NCH - 1))
                    pm = ctx.t.mark(ctx.pe)
                    ctx.s.wait(ctx.pe, pm)
                    ctx.s.activation(dst[:, spn * 512:spn * 512 + 512], ps[:, :],
                                     AF.Identity, bias=bias[:, g:g + 1])
                    ctx.s.mark(ctx.act)
                ctx.barrier()

            for hh in range(2):
                h = 2 * g + hh

                def acat_cb(qs, pvm, h=h):
                    ctx.s.wait(ctx.pe, pvm)
                    ctx.s.copy(acat[(h % 2) * 64:(h % 2) * 64 + 64,
                                    (h // 2) * T + qs * 512:(h // 2) * T + qs * 512 + 512],
                               ps_pv[0:64, :])
                    sm = ctx.s.mark(ctx.act)
                    ctx.t.wait(ctx.act, sm)

                emit_softmax_attn_head(
                    ctx, cst, qf, kf, vtm, s_sb, p_sb, pt_sb, stat,
                    banks, ps_t, ps_pv, mask, r0=hh * 64, scale=0.125, dv=64,
                    vcol=lambda kt, h=h: (kt * D + h * 64, 64),
                    acat_cb=acat_cb, causal=True,
                    dbg_tag=("h0" if (sq == 0 and g == 0 and hh == 0) else None))
                if sq == 0 and g == 0 and hh == 0:
                    ctx.dbg("p0", p_sb[:])
            ctx.barrier()
            if sq == 0 and g == 0:
                ctx.dbg("qf", qf[:])
                ctx.dbg("kf", kf[:])
                ctx.dbg("acat0", acat[:, 0:T])

        def oproj(dc, spn, ps):
            for c in range(NCH):
                ctx.t.matmul(ps[:, :], ws_o.tile(c, dc * 128),
                             acat[:, c * T + spn * 512:c * T + spn * 512 + 512],
                             start=(c == 0), stop=(c == NCH - 1))

        emit_residual_out(ctx, cst, lnw, xin, xout, tok0, T, oproj, ob_sb,
                          banks)
        ctx.barrier()
    st.close()



def emit_l1a(ctx, cst, xin, xout):
    st = ExitStack()
    xn = ctx.sbuf("l1a_xn", [128, NCH * T], BF16, st)
    lnw = ln_work(ctx, st, "l1a")
    wstage = ctx.sbuf("l1a_wstage", [128, 2 * D], F32, st)
    ws_qk = WStream(ctx, st, "l1a_qk", 128, stage=wstage)
    ws_v = WStream(ctx, st, "l1a_vo", D, stage=wstage)
    ws_o = ws_v
    qb_sb = ctx.sbuf("l1a_qb_s", [128, NCH], F32, st)
    kb_sb = ctx.sbuf("l1a_kb_s", [128, NCH], F32, st)
    pb_sb = ctx.sbuf("l1a_pb_s", [128, NCH], F32, st)
    vb_bc = ctx.sbuf("l1a_vbbc", [128, D], F32, st)
    qf = ctx.sbuf("l1a_qf", [128, T], BF16, st)
    kf = ctx.sbuf("l1a_kf", [128, T], BF16, st)
    vtm = ctx.sbuf("l1a_vtm", [128, 16 * D], BF16, st)
    acat = ctx.sbuf("l1a_acat", [128, NCH * T], BF16, st)
    sp_all = ctx.sbuf("l1a_sp", [128, 17408], BF16, st)
    scr_bf = lnw["xf"][:].bitcast(BF16)
    stat16 = ctx.sbuf("l1a_stat", [128, 96], F32, st)
    pt_sb = ctx.sbuf("l1a_pt", [128, 512], BF16, st)
    banks = [ctx.psum(f"l1a_ps{i}", [128, 512], F32, st) for i in range(4)]
    ps_t = ctx.psum("l1a_pst", [128, 1024], BF16, st)
    ps_pv = ctx.psum("l1a_pspv", [128, 512], F32, st)
    W = ctx.wins

    for dst, key in ((qb_sb, "l1a_qb"), (kb_sb, "l1a_kb"), (pb_sb, "l1a_pb")):
        ctx.dma("q", dst[:], W[key][:, :])
    ctx.dma("q", vb_bc[:], W["l1a_vb"][0:1, :].broadcast_to([128, D]))
    ctx.barrier()

    for sq in range(SEQ):
        tok0 = sq * T
        ws_v.load(W["l1a_w"][:, 2048:3072])
        emit_ln(ctx, cst, lnw, xin, xn, T, tok0=tok0, banks=banks)
        ctx.barrier()

        ev = {}
        for kt in range(16):
            for half in range(2):
                i = kt * 2 + half
                ps = banks[i % 4]
                if i >= 4:
                    ctx.t.wait(ctx.vec, ev[i - 4])
                for c in range(NCH):
                    ctx.t.matmul(ps[:, :],
                                 xn[:, c * T + kt * 128:c * T + (kt + 1) * 128],
                                 ws_v.bf[:, c * D + half * 512:c * D + half * 512 + 512],
                                 start=(c == 0), stop=(c == NCH - 1))
                pm = ctx.t.mark(ctx.pe)
                ctx.v.wait(ctx.pe, pm)
                ctx.v.tensor_tensor(
                    vtm[:, kt * D + half * 512:kt * D + half * 512 + 512],
                    ps[:, :], vb_bc[:, half * 512:half * 512 + 512], ALU.add)
                ev[i] = ctx.v.mark(ctx.vec)
        ctx.barrier()

        for g in range(8):
            for wb_is_q in (True, False):
                col0 = g * 128 if wb_is_q else 1024 + g * 128
                dst = qf if wb_is_q else kf
                bias = qb_sb if wb_is_q else kb_sb
                lv = ctx.dma("q", ws_qk.stage[:, 0:NCH * 128],
                             col_block_ap(W["l1a_w"], 3072, col0, 128))
                ctx.g.wait(*lv)
                for c in range(NCH):
                    ctx.g.tensor_copy(ws_qk.bf[:, c * 128:(c + 1) * 128],
                                      ws_qk.stage[:, c * 128:(c + 1) * 128])
                gm = ctx.g.mark(ctx.gp)
                ctx.t.wait(ctx.gp, gm)
                for spn in range(4):
                    ps = banks[spn]
                    for c in range(NCH):
                        ctx.t.matmul(ps[:, :], ws_qk.bf[:, c * 128:(c + 1) * 128],
                                     xn[:, c * T + spn * 512:c * T + spn * 512 + 512],
                                     start=(c == 0), stop=(c == NCH - 1))
                    pm = ctx.t.mark(ctx.pe)
                    ctx.s.wait(ctx.pe, pm)
                    ctx.s.activation(dst[:, spn * 512:spn * 512 + 512], ps[:, :],
                                     AF.Identity, bias=bias[:, g:g + 1])
                    ctx.s.mark(ctx.act)
                ctx.barrier()

            for hh in range(2):
                h = 2 * g + hh

                def acat_cb(qs, pvm, h=h):
                    ctx.s.wait(ctx.pe, pvm)
                    ctx.s.copy(acat[(h % 2) * 64:(h % 2) * 64 + 64,
                                    (h // 2) * T + qs * 512:(h // 2) * T + qs * 512 + 512],
                               ps_pv[0:64, :])
                    sm = ctx.s.mark(ctx.act)
                    ctx.t.wait(ctx.act, sm)

                emit_sparsemax_attn_head(
                    ctx, cst, qf, kf, vtm, sp_all, stat16, pt_sb,
                    banks, ps_t, ps_pv, None, r0=hh * 64, scale=0.125, dv=64,
                    vcol=lambda kt, h=h: (kt * D + h * 64, 64),
                    acat_cb=acat_cb, scr_bf=scr_bf)
            ctx.barrier()

        ws_o.load(W["l1a_wp"])
        ctx.barrier()

        def oproj(dc, spn, ps):
            for c in range(NCH):
                ctx.t.matmul(ps[:, :], ws_o.tile(c, dc * 128),
                             acat[:, c * T + spn * 512:c * T + spn * 512 + 512],
                             start=(c == 0), stop=(c == NCH - 1))

        emit_residual_out(ctx, cst, lnw, xin, xout, tok0, T, oproj, pb_sb,
                          banks)
        ctx.barrier()
    st.close()



def emit_l3a(ctx, cst, xin, xout):
    st = ExitStack()
    xn = ctx.sbuf("l3a_xn", [128, NCH * T], BF16, st)
    lnw = ln_work(ctx, st, "l3a")
    wstage = ctx.sbuf("l3a_wstage", [128, 2 * D], F32, st)
    ws_qk = WStream(ctx, st, "l3a_qk", 128, stage=wstage)
    ws_v = WStream(ctx, st, "l3a_vo", D, stage=wstage)
    ws_o = ws_v
    qb_sb = ctx.sbuf("l3a_qb_s", [128, NCH], F32, st)
    kb_sb = ctx.sbuf("l3a_kb_s", [128, NCH], F32, st)
    ob_sb = ctx.sbuf("l3a_ob_s", [128, NCH], F32, st)
    vb_bc = ctx.sbuf("l3a_vbbc", [128, D], F32, st)
    qf = ctx.sbuf("l3a_qf", [128, T], BF16, st)
    kf = ctx.sbuf("l3a_kf", [128, T], BF16, st)
    vtm = ctx.sbuf("l3a_vtm", [128, 16 * D], BF16, st)
    acat = ctx.sbuf("l3a_acat", [128, NCH * T], BF16, st)
    s_sb = lnw["xf"][:, 0:2 * T]
    p_sb = ctx.sbuf("l3a_p", [128, 4 * T], BF16, st)
    pt_sb = ctx.sbuf("l3a_pt", [128, 512], BF16, st)
    stat = ctx.sbuf("l3a_stat", [128, 16], F32, st)
    grepb = ctx.sbuf("l3a_grepb", [128, 4096], BF16, st)
    adiag = ctx.sbuf("l3a_adiag", [128, 128], BF16, st)
    banks = [ctx.psum(f"l3a_ps{i}", [128, 512], F32, st) for i in range(4)]
    ps_t = ctx.psum("l3a_pst", [128, 1024], BF16, st)
    ps_pv = ctx.psum("l3a_pspv", [128, 512], F32, st)
    W = ctx.wins

    for dst, key in ((qb_sb, "l3a_qb"), (kb_sb, "l3a_kb"), (ob_sb, "l3a_ob"),
                     (adiag, "antidiag8")):
        ctx.dma("q", dst[:], W[key][:, :])
    ctx.dma("q", vb_bc[:], W["l3a_vb"][0:1, :].broadcast_to([128, D]))
    ctx.barrier()

    for sq in range(SEQ):
        tok0 = sq * T
        ws_v.load(W["l3a_vw"])
        emit_ln(ctx, cst, lnw, xin, xn, T, tok0=tok0, banks=banks)
        ctx.barrier()

        ev = {}
        for kt in range(16):
            for half in range(2):
                i = kt * 2 + half
                ps = banks[i % 4]
                if i >= 4:
                    ctx.t.wait(ctx.vec, ev[i - 4])
                for c in range(NCH):
                    ctx.t.matmul(ps[:, :],
                                 xn[:, c * T + kt * 128:c * T + (kt + 1) * 128],
                                 ws_v.bf[:, c * D + half * 512:c * D + half * 512 + 512],
                                 start=(c == 0), stop=(c == NCH - 1))
                pm = ctx.t.mark(ctx.pe)
                ctx.v.wait(ctx.pe, pm)
                ctx.v.tensor_tensor(
                    vtm[:, kt * D + half * 512:kt * D + half * 512 + 512],
                    ps[:, :], vb_bc[:, half * 512:half * 512 + 512], ALU.add)
                ev[i] = ctx.v.mark(ctx.vec)
        ctx.barrier()

        for g in range(8):
            for wb_is_q in (True, False):
                wdram = W["l3a_qw"] if wb_is_q else W["l3a_kw"]
                dst = qf if wb_is_q else kf
                bias = qb_sb if wb_is_q else kb_sb
                lv = ctx.dma("q", ws_qk.stage[:, 0:NCH * 128],
                             col_block_ap(wdram, D, g * 128, 128))
                ctx.g.wait(*lv)
                for c in range(NCH):
                    ctx.g.tensor_copy(ws_qk.bf[:, c * 128:(c + 1) * 128],
                                      ws_qk.stage[:, c * 128:(c + 1) * 128])
                gm = ctx.g.mark(ctx.gp)
                ctx.t.wait(ctx.gp, gm)
                for spn in range(4):
                    ps = banks[spn]
                    for c in range(NCH):
                        ctx.t.matmul(ps[:, :], ws_qk.bf[:, c * 128:(c + 1) * 128],
                                     xn[:, c * T + spn * 512:c * T + spn * 512 + 512],
                                     start=(c == 0), stop=(c == NCH - 1))
                    pm = ctx.t.mark(ctx.pe)
                    ctx.s.wait(ctx.pe, pm)
                    ctx.s.activation(dst[:, spn * 512:spn * 512 + 512], ps[:, :],
                                     AF.Identity, bias=bias[:, g:g + 1])
                    ctx.s.mark(ctx.act)
                ctx.barrier()

            for hh in range(2):
                h = 2 * g + hh
                for piece in range(3):
                    p0 = piece * 1536
                    pw = min(1536, 4096 - p0)
                    gext_src = AP(tensor=W["l3a_gext"].tensor,
                                  offset=W["l3a_gext"].offset + h * 4224 + p0,
                                  ap=[[1, 128], [1, pw]])
                    lv = ctx.dma("q", wstage[:, 0:pw], gext_src)
                    ctx.g.wait(*lv)
                    ctx.g.tensor_copy(grepb[:, p0:p0 + pw], wstage[:, 0:pw])
                    gm2 = ctx.g.mark(ctx.gp)
                    ctx.q.wait(ctx.gp, gm2)
                ctx.t.wait(ctx.gp, gm2)

                def bias_mm(ctx, ps, ks, kw, qt):
                    col0 = ks * 512 - qt * 128 + 2047
                    ctx.t.matmul(ps[:, 0:kw], adiag[:], grepb[:, col0:col0 + kw],
                                 start=False, stop=True)

                def acat_cb(qs, pvm, h=h):
                    ctx.s.wait(ctx.pe, pvm)
                    ctx.s.copy(acat[(h % 2) * 64:(h % 2) * 64 + 64,
                                    (h // 2) * T + qs * 512:(h // 2) * T + qs * 512 + 512],
                               ps_pv[0:64, :])
                    sm = ctx.s.mark(ctx.act)
                    ctx.t.wait(ctx.act, sm)

                emit_softmax_attn_head(
                    ctx, cst, qf, kf, vtm, s_sb, p_sb, pt_sb, stat,
                    banks, ps_t, ps_pv, None, r0=hh * 64, scale=0.125, dv=64,
                    vcol=lambda kt, h=h: (kt * D + h * 64, 64),
                    acat_cb=acat_cb, causal=False, bias_mm=bias_mm)
                ctx.barrier()

        ws_o.load(W["l3a_ow"])
        ctx.barrier()

        def oproj(dc, spn, ps):
            for c in range(NCH):
                ctx.t.matmul(ps[:, :], ws_o.tile(c, dc * 128),
                             acat[:, c * T + spn * 512:c * T + spn * 512 + 512],
                             start=(c == 0), stop=(c == NCH - 1))

        emit_residual_out(ctx, cst, lnw, xin, xout, tok0, T, oproj, ob_sb,
                          banks)
        ctx.barrier()
    st.close()



def emit_l3f(ctx, cst, xin, xout):
    st = ExitStack()
    QTOK = 1024
    wtup = ctx.sbuf("l3f_wtup", [128, NCH * 4096], BF16, st)
    wtdn = ctx.sbuf("l3f_wtdn", [128, 32 * D], BF16, st)
    h_bf = ctx.sbuf("l3f_h", [128, 32 * 256], BF16, st)
    xn = ctx.sbuf("l3f_xn", [128, NCH * QTOK], BF16, st)
    wstage = h_bf[:].bitcast(F32)
    sgf = xn[:, 0:4096].bitcast(F32)
    msf = xn[:, 4096:8192].bitcast(F32)
    apart = ctx.sbuf("l3f_apart", [128, 40], F32, st)
    arow = ctx.sbuf("l3f_arow", [1, 16], F32, st)
    thr = ctx.sbuf("l3f_thr", [128, 1], F32, st)
    thr_d = ctx.dram("l3f_thr_d", [1, 1], F32)
    hq = ctx.sbuf("l3f_hq", [128, 512], BF16, st)
    r2rows = ctx.sbuf("l3f_r2", [1, 3 * 256], F32, st)
    r2b = ctx.sbuf("l3f_r2b", [1, 256], BF16, st)
    r2bc = ctx.sbuf("l3f_r2bc", [128, 256], BF16, st)
    otmp = ctx.sbuf("l3f_otmp", [128, 2 * 256], F32, st)
    banks = [ctx.psum(f"l3f_ps{i}", [128, 512], F32, st) for i in range(6)]
    W = ctx.wins

    for wi, (wdram, wdst, nch, cols) in enumerate(
            ((W["l3f_wu"], wtup, NCH, 4096), (W["l3f_wd"], wtdn, 32, D))):
        pw = min(cols, 2048)
        npiece = nch * cols // pw
        per_chunk = cols // pw
        for p in range(npiece):
            c, sub = divmod(p, per_chunk)
            half = (p % 2) * 2048
            if p >= 2:
                ctx.q.wait(ctx.vec, vparts[p - 2])
            lv = ctx.dma("q", wstage[:, half:half + pw],
                         wdram[c * 128:(c + 1) * 128, sub * pw:sub * pw + pw])
            ctx.v.wait(*lv)
            ctx.v.tensor_reduce(apart[:, p:p + 1],
                                wstage[:, half:half + pw], AX.X, ALU.add,
                                apply_absolute_value=True)
            if p == 0:
                vparts = {}
            ctx.v.fence()
            vparts[p] = ctx.v.mark(ctx.vec)
        ctx.v.fence()
        ctx.v.reduce_sum(apart[:, 39:40], apart[:, 0:npiece], axis=AX.X)
        ctx.v.fence()
        vm = ctx.v.mark(ctx.vec)
        ctx.t.wait(ctx.vec, vm)
        ctx.t.matmul(banks[0][0:1, 0:128], apart[:, 39:40], cst["ident_f32"][:],
                     is_transpose=True, start=True, stop=True)
        pm = ctx.t.mark(ctx.pe)
        ctx.v.wait(ctx.pe, pm)
        ctx.v.reduce_sum(arow[0:1, 1:2], banks[0][0:1, 0:128], axis=AX.X)
        ctx.v.fence()
        ctx.v.tensor_scalar_mul(arow[0:1, 0:1], arow[0:1, 1:2],
                                0.5 / float(nch * 128 * cols))
        ctx.v.fence()
        vm2 = ctx.v.mark(ctx.vec)
        ctx.s.wait(ctx.vec, vm2)
        sv = ctx.dma("s", thr_d[0:1, :], arow[0:1, 0:1])
        ctx.q.wait(*sv)
        lv = ctx.dma("q", thr[:], thr_d[0:1, :].broadcast_to([128, 1]))
        ctx.v.wait(*lv)
        ctx.s.wait(*lv)
        gmarks = {}
        for p in range(npiece):
            c, sub = divmod(p, per_chunk)
            half = (p % 2) * 2048
            if p >= 2:
                ctx.q.wait(ctx.gp, gmarks[p - 2])
            lv = ctx.dma("q", wstage[:, half:half + pw],
                         wdram[c * 128:(c + 1) * 128, sub * pw:sub * pw + pw])
            ctx.s.wait(*lv)
            ctx.s.sign(sgf[:, 0:pw], wstage[:, half:half + pw])
            ctx.s.activation(msf[:, 0:pw], wstage[:, half:half + pw], AF.Abs)
            am = ctx.s.mark(ctx.act)
            ctx.v.wait(ctx.act, am)
            ctx.v.tensor_scalar(msf[:, 0:pw], msf[:, 0:pw], thr[:], None,
                                ALU.is_gt)
            vm3 = ctx.v.mark(ctx.vec)
            ctx.g.wait(ctx.act, am)
            ctx.g.wait(ctx.vec, vm3)
            ctx.g.tensor_tensor(wdst[:, c * cols + sub * pw:
                                     c * cols + sub * pw + pw],
                                sgf[:, 0:pw], msf[:, 0:pw], ALU.mult)
            gmarks[p] = ctx.g.mark(ctx.gp)
            ctx.s.wait(ctx.gp, gmarks[p])
            ctx.v.wait(ctx.gp, gmarks[p])
        ctx.barrier()

    def l3f_scale(ctx, w, mu, var, rstd):
        r = w["rows"]
        b_row = r[0:1, 512:1024]
        ctx.s.fence()
        ctx.s.activation(b_row, var, AF.Ln)
        ctx.s.fence()
        ctx.s.activation(b_row, b_row, AF.Exp, scale=0.5)
        am = ctx.s.mark(ctx.act)
        ctx.v.wait(ctx.act, am)
        ctx.v.tensor_tensor(b_row, b_row, rstd, ALU.mult)
        ctx.v.fence()
        ctx.v.tensor_scalar_add(b_row, b_row, 1e-8)
        ctx.v.fence()
        ctx.v.reciprocal(b_row, b_row)
        ctx.v.fence()
        ctx.v.tensor_tensor(rstd, rstd, b_row, ALU.mult)
        ctx.v.fence()
        vm = ctx.v.mark(ctx.vec)
        ctx.s.wait(ctx.vec, vm)

    lnw = ln_work(ctx, st, "l3f_lnw")
    for qq in range(4):
        tq0 = qq * QTOK
        emit_ln(ctx, cst, lnw, xin, xn, QTOK, tok0=tq0, banks=banks[0:4],
                scale_cb=l3f_scale)
        ctx.barrier()

        for sp in range(QTOK // 256):
            t0 = tq0 + sp * 256
            hev = {}
            for j in range(32):
                b = j % 2
                if j >= 2:
                    ctx.t.wait(ctx.act, hev[j - 2])
                for c in range(NCH):
                    ctx.t.matmul(banks[b][:, 0:256],
                                 wtup[:, c * 4096 + j * 128:c * 4096 + (j + 1) * 128],
                                 xn[:, c * QTOK + sp * 256:c * QTOK + sp * 256 + 256],
                                 start=(c == 0), stop=(c == NCH - 1))
                pm = ctx.t.mark(ctx.pe)
                ctx.s.wait(ctx.pe, pm)
                ctx.s.activation(h_bf[:, j * 256:(j + 1) * 256], banks[b][:, 0:256],
                                 AF.Gelu)
                hev[j] = ctx.s.mark(ctx.act)
            s2 = r2rows[0:1, 0:256]
            l2r = r2rows[0:1, 256:512]
            rr2 = r2rows[0:1, 512:768]
            ctx.v.wait(ctx.act, hev[31])
            for j in range(32):
                hb2 = slice((j % 2) * 256, (j % 2) * 256 + 256)
                if j >= 2:
                    ctx.v.wait(ctx.pe, c2[j - 2])
                ctx.v.tensor_tensor(hq[:, hb2], h_bf[:, j * 256:(j + 1) * 256],
                                    h_bf[:, j * 256:(j + 1) * 256], ALU.mult)
                vm = ctx.v.mark(ctx.vec)
                ctx.t.wait(ctx.vec, vm)
                ctx.t.matmul(banks[2][0:1, 0:256], cst["ones_col"][:], hq[:, hb2],
                             start=(j == 0), stop=(j == 31))
                if j == 0:
                    c2 = {}
                c2[j] = ctx.t.mark(ctx.pe)
            ctx.s.wait(ctx.pe, c2[31])
            ctx.s.activation(l2r, banks[2][0:1, 0:256], AF.Ln, scale=1.0 / 4096.0)
            ctx.s.fence()
            ctx.s.activation(s2, l2r, AF.Exp, scale=0.5)
            ctx.s.fence()
            ctx.s.add(rr2, s2, 1e-8)
            am = ctx.s.mark(ctx.act)
            ctx.v.wait(ctx.act, am)
            ctx.v.reciprocal(l2r, rr2)
            ctx.v.fence()
            ctx.v.tensor_copy(r2b[0:1, :], l2r)
            vm2 = ctx.v.mark(ctx.vec)
            ctx.t.wait(ctx.vec, vm2)
            ctx.t.matmul(banks[3][:, 0:256], cst["ones_row"][:], r2b[0:1, :],
                         start=True, stop=True)
            pm2 = ctx.t.mark(ctx.pe)
            ctx.v.wait(ctx.pe, pm2)
            ctx.v.tensor_copy(r2bc[:], banks[3][:, 0:256])
            ctx.v.fence()
            for j in range(32):
                ctx.v.tensor_tensor(h_bf[:, j * 256:(j + 1) * 256],
                                    h_bf[:, j * 256:(j + 1) * 256], r2bc[:], ALU.mult)
            vm3 = ctx.v.mark(ctx.vec)
            ctx.t.wait(ctx.vec, vm3)
            lvx = ctx.dma("q", lnw["xf"][:, 0:NCH * 256], x_span_ap(xin, t0, w=256))
            for dc in range(NCH):
                gi = sp * NCH + dc
                bb = banks[4 + dc % 2]
                if gi >= 2:
                    ctx.t.wait(ctx.vec, l3f_oev[gi - 2])
                for j in range(32):
                    ctx.t.matmul(bb[:, 0:256],
                                 wtdn[:, j * D + dc * 128:j * D + dc * 128 + 128],
                                 h_bf[:, j * 256:(j + 1) * 256],
                                 start=(j == 0), stop=(j == 31))
                pm3 = ctx.t.mark(ctx.pe)
                ctx.v.wait(ctx.pe, pm3)
                ctx.v.wait(*lvx)
                if gi >= 2:
                    ctx.v.wait(*l3f_sts[gi - 2])
                ot = otmp[:, (gi % 2) * 256:(gi % 2) * 256 + 256]
                ctx.v.tensor_tensor(ot, bb[:, 0:256],
                                    lnw["xf"][:, dc * 256:dc * 256 + 256], ALU.add)
                if gi == 0:
                    l3f_oev = {}
                    l3f_sts = {}
                l3f_oev[gi] = ctx.v.mark(ctx.vec)
                ctx.s.wait(ctx.vec, l3f_oev[gi])
                l3f_sts[gi] = ctx.dma("s", x_chunk_span_ap(xout, dc, t0, w=256), ot)
            ctx.q.wait(ctx.vec, l3f_oev[sp * NCH + NCH - 1])
        ctx.barrier()
    st.close()



LAMBDA_INIT = 0.8 - 0.6 * math.exp(-0.3 * 12)

def emit_l2a(ctx, cst, xin, xout):
    st = ExitStack()
    xn = ctx.sbuf("l2a_xn", [128, NCH * T], BF16, st)
    lnw = ln_work(ctx, st, "l2a")
    wstage = ctx.sbuf("l2a_wstage", [128, 2 * D], F32, st)
    ws_q = WStream(ctx, st, "l2a_wq", D, stage=wstage)
    ws_k = WStream(ctx, st, "l2a_wk", D, stage=wstage)
    ws_v = WStream(ctx, st, "l2a_wv", D, stage=wstage)
    ws_o = ws_q
    cosb = ctx.sbuf("l2a_cos_s", [128, 1024], F32, st)
    sab = ctx.sbuf("l2a_sa_s", [128, 1024], F32, st)
    sbb = ctx.sbuf("l2a_sb_s", [128, 1024], F32, st)
    lamrow = ctx.sbuf("l2a_lam", [1, 64 * 4 + 16], F32, st)
    lam_bc = ctx.sbuf("l2a_lambc", [128, 1], F32, st)
    lam_d = ctx.dram("l2a_lam_d", [1, 1], F32)
    lnwp = ctx.sbuf("l2a_lnwp", [128, 128], F32, st)
    lnbp = ctx.sbuf("l2a_lnbp", [128, 128], F32, st)
    qtf = ctx.sbuf("l2a_qtf", [128, 1024], F32, st)
    ktf = ctx.sbuf("l2a_ktf", [128, 1024], F32, st)
    vtf = ctx.sbuf("l2a_vtf", [128, 1024], F32, st)
    rsc = ctx.sbuf("l2a_rsc", [128, 1024], F32, st)
    dots = ctx.sbuf("l2a_dots", [128, 128], F32, st)
    dscr = ctx.sbuf("l2a_dscr", [128, 4 * 64], F32, st)
    probs = ctx.sbuf("l2a_probs", [128, 128], F32, st)
    stat8 = ctx.sbuf("l2a_stat8", [128, 64], F32, st)
    mix = ctx.sbuf("l2a_mix", [128, 2 * 128], F32, st)
    a_f = ctx.sbuf("l2a_af", [128, 1024], F32, st)
    a_bf = ctx.sbuf("l2a_abf", [128, 1024], BF16, st)
    acat = ctx.sbuf("l2a_acat", [128, NCH * T], BF16, st)
    banks = [ctx.psum(f"l2a_ps{i}", [128, 512], F32, st) for i in range(4)]
    ps_t = ctx.psum("l2a_pst", [128, 1024], BF16, st)
    W = ctx.wins

    for dst, key in ((cosb, "l2a_cos"), (sab, "l2a_sa"), (sbb, "l2a_sb")):
        ctx.dma("q", dst[:], W[key][:, :])
    for i, key in enumerate(("l2a_lambda_q1", "l2a_lambda_k1",
                             "l2a_lambda_q2", "l2a_lambda_k2")):
        ctx.dma("q", lamrow[0:1, i * 64:(i + 1) * 64], W[key][0:1, :])
    ctx.dma("q", lnwp[:], W["l2a_lnw"][0:1, :].broadcast_to([128, 128]))
    ctx.dma("q", lnbp[:], W["l2a_lnb"][0:1, :].broadcast_to([128, 128]))
    ctx.barrier()
    c0 = lamrow[0:1, 256:257]
    c1 = lamrow[0:1, 257:258]
    ctx.v.tensor_tensor(lamrow[0:1, 0:64], lamrow[0:1, 0:64],
                        lamrow[0:1, 64:128], ALU.mult)
    ctx.v.tensor_tensor(lamrow[0:1, 128:192], lamrow[0:1, 128:192],
                        lamrow[0:1, 192:256], ALU.mult)
    ctx.v.fence()
    ctx.v.reduce_sum(c0, lamrow[0:1, 0:64], axis=AX.X)
    ctx.v.reduce_sum(c1, lamrow[0:1, 128:192], axis=AX.X)
    ctx.v.fence()
    vm = ctx.v.mark(ctx.vec)
    ctx.s.wait(ctx.vec, vm)
    ctx.s.activation(c0, c0, AF.Exp)
    ctx.s.fence()
    ctx.s.activation(c1, c1, AF.Exp)
    am = ctx.s.mark(ctx.act)
    ctx.v.wait(ctx.act, am)
    ctx.v.tensor_tensor(c1, c1, c0, ALU.subtract)
    ctx.v.fence()
    ctx.v.tensor_scalar_add(c1, c1, -LAMBDA_INIT)
    ctx.v.fence()
    vm2 = ctx.v.mark(ctx.vec)
    ctx.s.wait(ctx.vec, vm2)
    sv = ctx.dma("s", lam_d[0:1, :], c1)
    ctx.q.wait(*sv)
    ctx.dma("q", lam_bc[:], lam_d[0:1, :].broadcast_to([128, 1]))
    ctx.v.tensor_scalar_mul(lnwp[:], lnwp[:], 1.0 - LAMBDA_INIT)
    ctx.v.tensor_scalar_mul(lnbp[:], lnbp[:], 1.0 - LAMBDA_INIT)
    ctx.barrier()

    for sq in range(SEQ):
        tok0 = sq * T
        ws_q.load(W["l2a_wq"])
        ws_k.load(W["l2a_wk"])
        ws_v.load(W["l2a_wv"])
        emit_ln(ctx, cst, lnw, xin, xn, T, tok0=tok0, banks=banks)
        ctx.barrier()

        for kt in range(16):
            for which, ws, dstf in ((0, ws_q, qtf), (1, ws_k, ktf), (2, ws_v, vtf)):
                for half in range(2):
                    ps = banks[half]
                    for c in range(NCH):
                        ctx.t.matmul(ps[:, :],
                                     xn[:, c * T + kt * 128:c * T + (kt + 1) * 128],
                                     ws.bf[:, c * D + half * 512:c * D + half * 512 + 512],
                                     start=(c == 0), stop=(c == NCH - 1))
                    pm = ctx.t.mark(ctx.pe)
                    ctx.s.wait(ctx.pe, pm)
                    ctx.s.copy(dstf[:, half * 512:half * 512 + 512], ps[:, :])
                    am = ctx.s.mark(ctx.act)
                    ctx.t.wait(ctx.act, am)
            am_qkv = ctx.s.mark(ctx.act)
            tsl = slice(kt * 64, (kt + 1) * 64)
            hrep = lambda tab: AP(tensor=tab, offset=kt * 64,
                                  ap=[[1024, 128], [0, 16], [1, 64]])
            ctx.v.wait(ctx.act, am_qkv)
            for tf in (qtf, ktf):
                ctx.v.tensor_tensor(rsc[:], tf[:], hrep(cosb), ALU.mult)
                even = AP(tensor=tf, offset=1,
                          ap=[[1024, 128], [64, 16], [2, 32]])
                odd0 = AP(tensor=tf, offset=0,
                          ap=[[1024, 128], [64, 16], [2, 32]])
                sae = AP(tensor=sab, offset=kt * 64,
                         ap=[[1024, 128], [0, 16], [2, 32]])
                sbo = AP(tensor=sbb, offset=kt * 64 + 1,
                         ap=[[1024, 128], [0, 16], [2, 32]])
                dste = AP(tensor=rsc, offset=1024 * 0,
                          ap=[[1024, 128], [64, 16], [2, 32]])
                scr2e = AP(tensor=a_f, offset=0,
                           ap=[[1024, 128], [64, 16], [2, 32]])
                scr2o = AP(tensor=a_f, offset=1,
                           ap=[[1024, 128], [64, 16], [2, 32]])
                ctx.v.tensor_tensor(scr2e, even, sae, ALU.mult)
                ctx.v.tensor_tensor(scr2o, odd0, sbo, ALU.mult)
                ctx.v.fence()
                ctx.v.tensor_tensor(tf[:], rsc[:], a_f[:], ALU.add)
                ctx.v.fence()
            vrot = ctx.v.mark(ctx.vec)
            ctx.v.wait(ctx.vec, vrot)
            nops = 0
            for g2 in range(2):
                for n in range(8):
                    for m in range(8):
                        db = dscr[:, (nops % 4) * 64:(nops % 4) * 64 + 64]
                        if nops >= 4 and nops % 4 == 0:
                            ctx.v.fence()
                        ctx.v.scalar_tensor_tensor(
                            db, qtf[:, n * 128 + g2 * 64:n * 128 + g2 * 64 + 64],
                            0.35355339059327373,
                            ktf[:, m * 128 + g2 * 64:m * 128 + g2 * 64 + 64],
                            ALU.mult, ALU.mult,
                            accum_out=dots[:, g2 * 64 + n * 8 + m:g2 * 64 + n * 8 + m + 1])
                        nops += 1
            ctx.v.fence()
            ctx.v.tensor_reduce(stat8[:, 0:16], dots[:].rearrange("p (s m) -> p s m", m=8),
                                AX.X, ALU.max, negate=True)
            ctx.v.fence()
            seg = AP(tensor=stat8, offset=0,
                     ap=[[64, 128], [1, 16], [0, 8]])
            ctx.v.tensor_tensor(probs[:], dots[:], seg, ALU.add)
            vmx = ctx.v.mark(ctx.vec)
            ctx.s.wait(ctx.vec, vmx)
            ctx.s.activation(probs[:], probs[:], AF.Exp)
            am2 = ctx.s.mark(ctx.act)
            ctx.v.wait(ctx.act, am2)
            ctx.v.tensor_reduce(stat8[:, 16:32], probs[:].rearrange("p (s m) -> p s m", m=8),
                                AX.X, ALU.add)
            ctx.v.fence()
            ctx.v.reciprocal(stat8[:, 32:48], stat8[:, 16:32])
            ctx.v.fence()
            seg2 = AP(tensor=stat8, offset=32,
                      ap=[[64, 128], [1, 16], [0, 8]])
            ctx.v.tensor_tensor(probs[:], probs[:], seg2, ALU.mult)
            ctx.v.fence()
            for g2 in range(2):
                for n in range(8):
                    for m in range(8):
                        pcell = probs[:, g2 * 64 + n * 8 + m:g2 * 64 + n * 8 + m + 1]
                        if m == 0:
                            ctx.v.tensor_scalar(mix[:, 0:128], vtf[:, m * 128:(m + 1) * 128],
                                                pcell, None, ALU.mult)
                        else:
                            ctx.v.scalar_tensor_tensor(
                                mix[:, (m % 2) * 128:(m % 2) * 128 + 128],
                                vtf[:, m * 128:(m + 1) * 128], pcell,
                                mix[:, ((m - 1) % 2) * 128:((m - 1) % 2) * 128 + 128],
                                ALU.mult, ALU.add)
                        ctx.v.fence()
                    mres = mix[:, ((7) % 2) * 128:((7) % 2) * 128 + 128]
                    if g2 == 0:
                        ctx.v.tensor_copy(a_f[:, n * 128:(n + 1) * 128], mres)
                    else:
                        ctx.v.scalar_tensor_tensor(
                            a_f[:, n * 128:(n + 1) * 128], mres, lam_bc[:],
                            a_f[:, n * 128:(n + 1) * 128], ALU.mult, ALU.add)
                    ctx.v.fence()
            ctx.v.tensor_tensor(rsc[:], a_f[:], a_f[:], ALU.mult)
            ctx.v.fence()
            ctx.v.tensor_reduce(stat8[:, 48:56], rsc[:].rearrange("p (s m) -> p s m", m=128),
                                AX.X, ALU.add)
            ctx.v.fence()
            vms = ctx.v.mark(ctx.vec)
            ctx.s.wait(ctx.vec, vms)
            ctx.s.activation(stat8[:, 56:64], stat8[:, 48:56], AF.Ln,
                             scale=1.0 / 128.0, bias=1e-8)
            ctx.s.fence()
            ctx.s.activation(stat8[:, 48:56], stat8[:, 56:64], AF.Exp, scale=-0.5)
            am3 = ctx.s.mark(ctx.act)
            ctx.v.wait(ctx.act, am3)
            segr = AP(tensor=stat8, offset=48,
                      ap=[[64, 128], [1, 8], [0, 128]])
            lnw_rep = AP(tensor=lnwp, offset=0,
                         ap=[[128, 128], [0, 8], [1, 128]])
            lnb_rep = AP(tensor=lnbp, offset=0,
                         ap=[[128, 128], [0, 8], [1, 128]])
            ctx.v.tensor_tensor(a_f[:], a_f[:], segr, ALU.mult)
            ctx.v.fence()
            ctx.v.tensor_tensor(a_f[:], a_f[:], lnw_rep, ALU.mult)
            ctx.v.fence()
            ctx.v.tensor_tensor(a_bf[:], a_f[:], lnb_rep, ALU.add)
            ctx.v.fence()
            vfin = ctx.v.mark(ctx.vec)
            ctx.t.wait(ctx.vec, vfin)
            for c in range(NCH):
                ctx.t.matmul(ps_t[:, c * 128:(c + 1) * 128],
                             a_bf[:, c * 128:(c + 1) * 128], cst["ident_sb"][:],
                             is_transpose=True, start=(c == 0), stop=(c == NCH - 1))
            tm = ctx.t.mark(ctx.pe)
            ctx.v.wait(ctx.pe, tm)
            out_ap = AP(tensor=acat, offset=kt * 128,
                        ap=[[NCH * T, 128], [T, NCH], [1, 128]])
            ctx.v.tensor_copy(out_ap, ps_t[:])
            vtr = ctx.v.mark(ctx.vec)
            ctx.t.wait(ctx.vec, vtr)
            ctx.s.wait(ctx.vec, vtr)
        ctx.barrier()

        ws_o.load(W["l2a_wo"])
        ctx.barrier()

        def oproj(dc, spn, ps):
            for c in range(NCH):
                ctx.t.matmul(ps[:, :], ws_o.tile(c, dc * 128),
                             acat[:, c * T + spn * 512:c * T + spn * 512 + 512],
                             start=(c == 0), stop=(c == NCH - 1))

        emit_residual_out(ctx, cst, lnw, xin, xout, tok0, T, oproj, None, banks)
        ctx.barrier()
    st.close()



def emit_l2f(ctx, cst, xin, xout):
    st = ExitStack()
    QTOK = 1024
    lnw = ln_work(ctx, st, "l2f")
    xn = ctx.sbuf("l2f_xn", [128, NCH * QTOK], BF16, st)
    we1b = ctx.sbuf("l2f_we1b", [128, NCH * 2048], BF16, st)
    we3b = ctx.sbuf("l2f_we3b", [128, NCH * 2048], BF16, st)
    we2b = ctx.sbuf("l2f_we2b", [128, 2 * D], BF16, st)
    wstage = ctx.sbuf("l2f_wst", [128, 2 * 2048], F32, st)
    wgb = ctx.sbuf("l2f_wgb", [128, NCH * 8], BF16, st)
    out_acc = ctx.sbuf("l2f_oacc", [128, NCH * QTOK], F32, st)
    h_bf = ctx.sbuf("l2f_h", [128, 16 * 256], BF16, st)
    s_bf = ctx.sbuf("l2f_sbf", [128, 2 * 256], BF16, st)
    t_bf = ctx.sbuf("l2f_tbf", [128, 2 * 256], BF16, st)
    wb_bc = ctx.sbuf("l2f_wbbc", [128, 256], BF16, st)
    gate = ctx.sbuf("l2f_gate", [128, 64], F32, st)
    wq_tm = ctx.sbuf("l2f_wqtm", [128, 8 * 8], F32, st)
    wsT = ctx.sbuf("l2f_wsT", [8, 1024], F32, st)
    werow = ctx.sbuf("l2f_werow", [1, QTOK], F32, st)
    werow_b = ctx.sbuf("l2f_werowb", [1, QTOK], BF16, st)
    w_dram = ctx.dram("l2f_w_d", [8, QTOK], F32)
    banks = [ctx.psum(f"l2f_ps{i}", [128, 512], F32, st) for i in range(8)]
    W = ctx.wins

    ws_g = WStream(ctx, st, "l2f_wg", 8, stage=wstage)
    ws_g.bf = wgb
    ws_g.load(W["l2f_wgw"])
    ctx.barrier()

    for qq in range(4):
        tq0 = qq * QTOK
        emit_ln(ctx, cst, lnw, xin, xn, QTOK, tok0=tq0, banks=banks[0:4])
        ctx.barrier()

        m1 = gate[:, 8:9]; m2 = gate[:, 9:10]; ssum = gate[:, 10:11]
        l2c = gate[:, 16:24]; sel = gate[:, 24:32]; lg = gate[:, 32:40]
        sig = gate[:, 40:48]
        for kt in range(QTOK // 128):
            ps = banks[0]
            for c in range(NCH):
                ctx.t.matmul(ps[:, 0:8],
                             xn[:, c * QTOK + kt * 128:c * QTOK + (kt + 1) * 128],
                             wgb[:, c * 8:(c + 1) * 8],
                             start=(c == 0), stop=(c == NCH - 1))
            pm = ctx.t.mark(ctx.pe)
            ctx.v.wait(ctx.pe, pm)
            ctx.v.tensor_copy(lg, ps[:, 0:8])
            ctx.v.fence()
            ctx.v.reduce_max(m1, lg, axis=AX.X)
            ctx.v.fence()
            ctx.v.tensor_scalar(l2c, lg, m1, None, ALU.is_ge)
            ctx.v.fence()
            ctx.v.scalar_tensor_tensor(l2c, l2c, NEG, lg, ALU.mult, ALU.add)
            ctx.v.fence()
            ctx.v.reduce_max(m2, l2c, axis=AX.X)
            ctx.v.fence()
            ctx.v.tensor_scalar(sel, lg, m2, None, ALU.is_ge)
            vm = ctx.v.mark(ctx.vec)
            ctx.s.wait(ctx.vec, vm)
            ctx.s.activation(sig, lg, AF.Sigmoid)
            am = ctx.s.mark(ctx.act)
            ctx.v.wait(ctx.act, am)
            ctx.v.tensor_tensor(sel, sel, sig, ALU.mult)
            ctx.v.fence()
            ctx.v.reduce_sum(ssum, sel, axis=AX.X)
            ctx.v.fence()
            ctx.v.reciprocal(m1, ssum)
            ctx.v.fence()
            ctx.v.tensor_scalar(wq_tm[:, kt * 8:(kt + 1) * 8], sel, m1, None,
                                ALU.mult)
            ctx.v.fence()
            vm2 = ctx.v.mark(ctx.vec)
            ctx.t.wait(ctx.vec, vm2)
            ctx.t.matmul(banks[1][0:8, kt * 128 % 512:kt * 128 % 512 + 128],
                         wq_tm[:, kt * 8:(kt + 1) * 8], cst["ident_f32"][:],
                         is_transpose=True, start=True, stop=True)
            tm = ctx.t.mark(ctx.pe)
            ctx.v.wait(ctx.pe, tm)
            ctx.v.tensor_copy(wsT[0:8, kt * 128:(kt + 1) * 128],
                              banks[1][0:8, kt * 128 % 512:kt * 128 % 512 + 128])
            vtr = ctx.v.mark(ctx.vec)
            ctx.t.wait(ctx.vec, vtr)
        vfin = ctx.v.mark(ctx.vec)
        ctx.s.wait(ctx.vec, vfin)
        sv = ctx.dma("s", w_dram[:, :], wsT[0:8, :])
        ctx.q.wait(*sv)
        ctx.barrier()

        l2f_oev = 0
        wsg = 0
        l2f_w2last = 0
        for e in range(9):
            shared = (e == 8)
            if not shared:
                lv = ctx.dma("q", werow[0:1, :], w_dram[e:e + 1, :])
                ctx.v.wait(*lv)
                ctx.v.tensor_copy(werow_b[0:1, :], werow[0:1, :])
                ctx.v.fence()
                wem = ctx.v.mark(ctx.vec)
                ctx.t.wait(ctx.vec, wem)
            if shared:
                w1d, w3d, w2d = W["l2f_ws1"], W["l2f_ws3"], W["l2f_ws2"]
            else:
                w1d = W["l2f_we1"][e * D:(e + 1) * D, :]
                w3d = W["l2f_we3"][e * D:(e + 1) * D, :]
                w2d = W["l2f_we2"][e * 2048:(e + 1) * 2048, :]
            gm1 = {}
            for c in range(NCH):
                half = (c % 2) * 2048
                if c >= 2:
                    ctx.q.wait(ctx.gp, gm1[c - 2])
                else:
                    ctx.q.wait(ctx.gp, wsg)
                lv = ctx.dma("q", wstage[:, half:half + 2048],
                             w1d[c * 128:(c + 1) * 128, :])
                ctx.g.wait(*lv)
                ctx.g.tensor_copy(we1b[:, c * 2048:(c + 1) * 2048],
                                  wstage[:, half:half + 2048])
                gm1[c] = ctx.g.mark(ctx.gp)
            for c in range(NCH):
                half = (c % 2) * 2048
                ctx.q.wait(ctx.gp, gm1[c + 6] if c >= 2 else gm1[NCH - 1])
                lv = ctx.dma("q", wstage[:, half:half + 2048],
                             w3d[c * 128:(c + 1) * 128, :])
                ctx.g.wait(*lv)
                ctx.g.tensor_copy(we3b[:, c * 2048:(c + 1) * 2048],
                                  wstage[:, half:half + 2048])
                gm1[c + 8] = ctx.g.mark(ctx.gp)
            gw = gm1[15]
            wsg = gw
            ctx.t.wait(ctx.gp, gw)

            for sp in range(QTOK // 256):
                t0l = sp * 256
                if not shared:
                    ctx.t.wait(ctx.vec, l2f_oev)
                    ctx.t.matmul(banks[4][:, 0:256], cst["ones_row"][:],
                                 werow_b[0:1, t0l:t0l + 256],
                                 start=True, stop=True)
                    bm = ctx.t.mark(ctx.pe)
                    ctx.s.wait(ctx.pe, bm)
                    ctx.s.copy(wb_bc[:], banks[4][:, 0:256])
                    wbev = ctx.s.mark(ctx.act)
                    ctx.t.wait(ctx.act, wbev)
                hev = {}
                for j in range(16):
                    b1 = banks[j % 2]
                    b3 = banks[2 + j % 2]
                    if j >= 2:
                        ctx.t.wait(ctx.vec, hev[j - 2])
                    for c in range(NCH):
                        ctx.t.matmul(b1[:, 0:256],
                                     we1b[:, c * 2048 + j * 128:c * 2048 + (j + 1) * 128],
                                     xn[:, c * QTOK + t0l:c * QTOK + t0l + 256],
                                     start=(c == 0), stop=(c == NCH - 1))
                    for c in range(NCH):
                        ctx.t.matmul(b3[:, 0:256],
                                     we3b[:, c * 2048 + j * 128:c * 2048 + (j + 1) * 128],
                                     xn[:, c * QTOK + t0l:c * QTOK + t0l + 256],
                                     start=(c == 0), stop=(c == NCH - 1))
                    pm = ctx.t.mark(ctx.pe)
                    ctx.s.wait(ctx.pe, pm)
                    ctx.s.activation(s_bf[:, (j % 2) * 256:(j % 2) * 256 + 256],
                                     b1[:, 0:256], AF.Silu)
                    am = ctx.s.mark(ctx.act)
                    ctx.v.wait(ctx.pe, pm)
                    ctx.v.tensor_copy(t_bf[:, (j % 2) * 256:(j % 2) * 256 + 256],
                                      b3[:, 0:256])
                    ctx.v.wait(ctx.act, am)
                    if shared:
                        ctx.v.fence()
                        ctx.v.tensor_tensor(h_bf[:, j * 256:(j + 1) * 256],
                                            s_bf[:, (j % 2) * 256:(j % 2) * 256 + 256],
                                            t_bf[:, (j % 2) * 256:(j % 2) * 256 + 256],
                                            ALU.mult)
                    else:
                        ctx.v.fence()
                        ctx.v.tensor_tensor(h_bf[:, j * 256:(j + 1) * 256],
                                            s_bf[:, (j % 2) * 256:(j % 2) * 256 + 256],
                                            t_bf[:, (j % 2) * 256:(j % 2) * 256 + 256],
                                            ALU.mult)
                        ctx.v.wait(ctx.act, wbev)
                        ctx.v.fence()
                        ctx.v.tensor_tensor(h_bf[:, j * 256:(j + 1) * 256],
                                            h_bf[:, j * 256:(j + 1) * 256],
                                            wb_bc[:], ALU.mult)
                    hev[j] = ctx.v.mark(ctx.vec)
                ctx.t.wait(ctx.vec, hev[15])
                w2m = {}
                for j in range(16):
                    wb2 = we2b[:, (j % 2) * D:(j % 2) * D + D]
                    if j >= 2:
                        ctx.q.wait(ctx.pe, w2m[j - 2])
                    else:
                        ctx.q.wait(ctx.gp, wsg)
                    lv = ctx.dma("q", wstage[:, 0:D] if j % 2 == 0 else wstage[:, 2048:2048 + D],
                                 w2d[j * 128:(j + 1) * 128, :])
                    ctx.g.wait(*lv)
                    ctx.g.wait(ctx.pe, w2m[j - 2] if j >= 2 else l2f_w2last)
                    ctx.g.tensor_copy(wb2, wstage[:, 0:D] if j % 2 == 0 else wstage[:, 2048:2048 + D])
                    gm2 = ctx.g.mark(ctx.gp)
                    wsg = gm2
                    ctx.t.wait(ctx.gp, gm2)
                    if j == 0:
                        ctx.t.wait(ctx.vec, l2f_oev)
                    for dc in range(NCH):
                        ctx.t.matmul(banks[4 + dc // 2][:, (dc % 2) * 256:(dc % 2) * 256 + 256],
                                     wb2[:, dc * 128:(dc + 1) * 128],
                                     h_bf[:, j * 256:(j + 1) * 256],
                                     start=(j == 0 and dc % 2 == 0),
                                     stop=(j == 15 and dc % 2 == 1))
                    w2m[j] = ctx.t.mark(ctx.pe)
                l2f_w2last = w2m[15]
                pmf = w2m[15]
                ctx.v.wait(ctx.pe, pmf)
                for dc in range(NCH):
                    src_ps = banks[4 + dc // 2][:, (dc % 2) * 256:(dc % 2) * 256 + 256]
                    dsl = slice(dc * QTOK + t0l, dc * QTOK + t0l + 256)
                    if e == 0:
                        ctx.v.tensor_copy(out_acc[:, dsl], src_ps)
                    else:
                        ctx.v.tensor_tensor(out_acc[:, dsl], out_acc[:, dsl],
                                            src_ps, ALU.add)
                    ctx.v.fence()
                l2f_oev = ctx.v.mark(ctx.vec)
        ctx.barrier()

        sts = {}
        for sp in range(QTOK // 512):
            t0 = tq0 + sp * 512
            lv = ctx.dma("q", lnw["xf"][:, 0:NCH * 512], x_span_ap(xin, t0))
            for dc in range(NCH):
                gi = sp * NCH + dc
                ctx.v.wait(*lv)
                if gi >= 2:
                    ctx.v.wait(*sts[gi - 2])
                ot = lnw["bc"][:, (gi % 2) * 512:(gi % 2) * 512 + 512]
                ctx.v.scalar_tensor_tensor(
                    ot, out_acc[:, dc * QTOK + sp * 512:dc * QTOK + sp * 512 + 512],
                    1.0 / 3.0, lnw["xf"][:, dc * 512:(dc + 1) * 512],
                    ALU.mult, ALU.add)
                ctx.v.fence()
                ov = ctx.v.mark(ctx.vec)
                ctx.s.wait(ctx.vec, ov)
                sts[gi] = ctx.dma("s", x_chunk_span_ap(xout, dc, t0), ot)
            ctx.q.wait(ctx.vec, ov)
        ctx.barrier()
    st.close()



def emit_l0f(ctx, cst, xin, xout):
    st = ExitStack()
    xn = ctx.sbuf("l0f_xn", [128, NCH * T], BF16, st)
    lnw = ln_work(ctx, st, "l0f")
    acat_d = ctx.dram("l0f_acat_d", [2, NCH, 128, T], BF16)
    ac_scr = ctx.sbuf("l0f_acs", [128, 2 * 512], BF16, st)
    p_sb = lnw["xf"][:].bitcast(BF16)
    pt_sb = ctx.sbuf("l0f_pt", [128, 512], BF16, st)
    stat = ctx.sbuf("l0f_stat", [128, 16], F32, st)
    wstage = ctx.sbuf("l0f_wstage", [128, 2048], F32, st)
    maskf = ctx.sbuf("l0f_mask", [128, 128], F32, st)
    banks = [ctx.psum(f"l0f_ps{i}", [128, 512], F32, st) for i in range(4)]
    ps_t = ctx.psum("l0f_pst", [128, 1024], BF16, st)
    ps_pv = ctx.psum("l0f_pspv", [128, 512], F32, st)
    W = ctx.wins
    win = W["l0f_win"]
    SC = 1.0 / math.sqrt(128.0)
    acq = [0]
    ctx.dma("q", maskf[:], W["mask_neg"][:, :])

    def acat_store(part, h, qs, pvm, rows=128):
        i = acq[0]
        acq[0] += 1
        sl = (i % 2) * 512
        ctx.s.wait(ctx.pe, pvm)
        if i >= 2:
            ctx.s.wait(*ac_sts[i - 2])
        ctx.s.copy(ac_scr[:rows, sl:sl + 512], ps_pv[0:rows, :])
        sm = ctx.s.mark(ctx.act)
        ctx.t.wait(ctx.act, sm)
        ctx.s.wait(ctx.act, sm)
        dst = AP(tensor=acat_d.tensor,
                 offset=acat_d.offset + (part * NCH + h) * 128 * T + qs * 512,
                 ap=[[T, rows], [1, 512]])
        if i == 0:
            ac_sts.clear()
        ac_sts[i] = ctx.dma("s", dst, ac_scr[:rows, sl:sl + 512])
    ac_sts = {}

    for sq in range(SEQ):
        tok0 = sq * T
        emit_ln(ctx, cst, lnw, xin, xn, T, tok0=tok0, banks=banks)
        ctx.barrier()

        stA = ExitStack()
        vtm = ctx.sbuf(f"l0f_vtm{sq}", [128, 16 * D], BF16, stA)
        stA1 = ExitStack()
        wv_b = ctx.sbuf(f"l0f_wvb{sq}", [128, NCH * D], BF16, stA1)
        gmk = {}
        for c in range(NCH):
            half = (c % 2) * 1024
            if c >= 2:
                ctx.q.wait(ctx.gp, gmk[c - 2])
            lv = ctx.dma("q", wstage[:, half:half + 1024],
                         win[c * 128:(c + 1) * 128, 3072:4096])
            ctx.g.wait(*lv)
            ctx.g.tensor_copy(wv_b[:, c * D:(c + 1) * D], wstage[:, half:half + 1024])
            gmk[c] = ctx.g.mark(ctx.gp)
        ctx.t.wait(ctx.gp, gmk[NCH - 1])
        ev = {}
        for kt in range(16):
            for half in range(2):
                i = kt * 2 + half
                ps = banks[i % 4]
                if i >= 4:
                    ctx.t.wait(ctx.act, ev[i - 4])
                for c in range(NCH):
                    ctx.t.matmul(ps[:, :],
                                 xn[:, c * T + kt * 128:c * T + (kt + 1) * 128],
                                 wv_b[:, c * D + half * 512:c * D + half * 512 + 512],
                                 start=(c == 0), stop=(c == NCH - 1))
                pm = ctx.t.mark(ctx.pe)
                ctx.s.wait(ctx.pe, pm)
                ctx.s.copy(vtm[:, kt * D + half * 512:kt * D + half * 512 + 512],
                           ps[:, :])
                ev[i] = ctx.s.mark(ctx.act)
        ctx.barrier()
        stA1.close()

        wq_h = ctx.sbuf(f"l0f_wqh{sq}", [128, NCH * 128], BF16, stA)
        wk_h = ctx.sbuf(f"l0f_wkh{sq}", [128, NCH * 128], BF16, stA)
        wk_h2 = None
        qa_f = ctx.sbuf(f"l0f_qaf{sq}", [128, 2 * 128], F32, stA)
        m_f = ctx.sbuf(f"l0f_mf{sq}", [128, 2 * 128], F32, stA)
        rp_b = ctx.sbuf(f"l0f_rpb{sq}", [128, 2 * 128], BF16, stA)
        cosb = ctx.sbuf(f"l0f_cosb{sq}", [128, 1024], F32, stA)
        sinb = ctx.sbuf(f"l0f_sinb{sq}", [128, 1024], F32, stA)
        qf_h = ctx.sbuf(f"l0f_qfh{sq}", [128, T], BF16, stA)
        kf_h = ctx.sbuf(f"l0f_kfh{sq}", [128, T], BF16, stA)
        s_sb = ctx.sbuf(f"l0f_ssb{sq}", [128, T], F32, stA)
        ctx.dma("q", cosb[:], W["l0f_cos"][:, :])
        ctx.dma("q", sinb[:], W["l0f_sin"][:, :])
        ctx.barrier()
        for h in range(8):
            for is_q in (True, False):
                col0 = (1024 if is_q else 2048) + h * 128
                wb = wq_h if is_q else wk_h
                dstf = qf_h if is_q else kf_h
                ctx.q.wait(ctx.gp)
                lv = ctx.dma("q", wstage[:, 0:NCH * 128],
                             col_block_ap(win, 4112, col0, 128))
                ctx.g.wait(*lv)
                for c in range(NCH):
                    ctx.g.tensor_copy(wb[:, c * 128:(c + 1) * 128],
                                      wstage[:, c * 128:(c + 1) * 128])
                gm = ctx.g.mark(ctx.gp)
                ctx.t.wait(ctx.gp, gm)
                for kt in range(16):
                    sl = (kt % 2) * 128
                    bb = banks[kt % 2]
                    if kt >= 2:
                        ctx.t.wait(ctx.act, qev[kt - 2])
                    for c in range(NCH):
                        ctx.t.matmul(bb[:, 0:128],
                                     xn[:, c * T + kt * 128:c * T + (kt + 1) * 128],
                                     wb[:, c * 128:(c + 1) * 128],
                                     start=(c == 0), stop=(c == NCH - 1))
                    pm = ctx.t.mark(ctx.pe)
                    ctx.s.wait(ctx.pe, pm)
                    if kt >= 2:
                        ctx.s.wait(ctx.vec, rev[kt - 2])
                    ctx.s.copy(qa_f[:, sl:sl + 128], bb[:, 0:128])
                    if kt == 0:
                        qev = {}
                        rev = {}
                    qev[kt] = ctx.s.mark(ctx.act)
                    t1 = qa_f[:, sl:sl + 64]
                    t2 = qa_f[:, sl + 64:sl + 128]
                    o1 = rp_b[:, sl:sl + 64]
                    o2 = rp_b[:, sl + 64:sl + 128]
                    u1 = m_f[:, sl:sl + 64]
                    u2 = m_f[:, sl + 64:sl + 128]
                    csl = cosb[:, kt * 64:(kt + 1) * 64]
                    ssl = sinb[:, kt * 64:(kt + 1) * 64]
                    ctx.v.wait(ctx.act, qev[kt])
                    ctx.v.tensor_tensor(u1, t1, csl, ALU.mult)
                    ctx.v.tensor_tensor(u2, t2, csl, ALU.mult)
                    ctx.v.fence()
                    ctx.v.tensor_tensor(t1, t1, ssl, ALU.mult)
                    ctx.v.tensor_tensor(t2, t2, ssl, ALU.mult)
                    ctx.v.fence()
                    ctx.v.tensor_tensor(o1, u1, t2, ALU.subtract)
                    ctx.v.tensor_tensor(o2, u2, t1, ALU.add)
                    ctx.v.fence()
                    rev[kt] = ctx.v.mark(ctx.vec)
                    ctx.t.wait(ctx.vec, rev[kt])
                    ctx.t.matmul(ps_t[:, sl + (kt % 8 // 2) * 256:
                                      sl + (kt % 8 // 2) * 256 + 128],
                                 rp_b[:, sl:sl + 128], cst["ident_sb"][:],
                                 is_transpose=True,
                                 start=(kt % 8 == 0), stop=(kt % 8 == 7))
                    tm = ctx.t.mark(ctx.pe)
                    if kt % 8 == 7:
                        ctx.v.wait(ctx.pe, tm)
                        ctx.v.tensor_copy(dstf[:, (kt - 7) * 128:(kt + 1) * 128],
                                          ps_t[:])
                        trm = ctx.v.mark(ctx.vec)
                        ctx.t.wait(ctx.vec, trm)
            ctx.barrier()

            def acat_cb(qs, pvm, h=h):
                acat_store(0, h, qs, pvm)

            emit_softmax_attn_head(
                ctx, cst, qf_h, kf_h, vtm, s_sb, p_sb, pt_sb, stat,
                banks, ps_t, ps_pv, maskf, r0=None, scale=SC, dv=128,
                vcol=lambda kt, h=h: (kt * D + h * 128, 128),
                acat_cb=acat_cb, causal=True)
            ctx.barrier()
        stA.close()
        ctx.barrier()

        stB = ExitStack()
        qf = ctx.sbuf(f"l0f_qf{sq}", [128, NCH * T], BF16, stB)
        ke_b = ctx.sbuf(f"l0f_keb{sq}", [128, 4096], BF16, stB)
        ve_b = ctx.sbuf(f"l0f_veb{sq}", [128, 4096], BF16, stB)
        wbc = lnw["xf"][:].bitcast(BF16)
        aT = ctx.sbuf(f"l0f_aT{sq}", [128, 2 * 512], BF16, stB)
        wrt = ctx.sbuf(f"l0f_wrt{sq}", [128, NCH * 16], BF16, stB)
        gcell = ctx.sbuf(f"l0f_gcell{sq}", [128, 64], F32, stB)
        wsT16 = ctx.sbuf(f"l0f_wsT{sq}", [16, T], F32, stB)
        wsT16b = ctx.sbuf(f"l0f_wsTb{sq}", [16, T], BF16, stB)
        w_dram = ctx.dram(f"l0f_wd{sq}", [16, T], BF16)
        for dc in range(NCH):
            lv = ctx.dma("q", wstage[:, 0:NCH * 128],
                         col_block_ap(win, 4112, dc * 128, 128))
            ctx.g.wait(*lv)
            for c in range(NCH):
                ctx.g.tensor_copy(ke_b[:, c * 128:(c + 1) * 128],
                                  wstage[:, c * 128:(c + 1) * 128])
            gm = ctx.g.mark(ctx.gp)
            ctx.t.wait(ctx.gp, gm)
            for spn in range(4):
                ps = banks[spn]
                for c in range(NCH):
                    ctx.t.matmul(ps[:, :], ke_b[:, c * 128:(c + 1) * 128],
                                 xn[:, c * T + spn * 512:c * T + spn * 512 + 512],
                                 start=(c == 0), stop=(c == NCH - 1))
                pm = ctx.t.mark(ctx.pe)
                ctx.s.wait(ctx.pe, pm)
                ctx.s.copy(qf[:, dc * T + spn * 512:dc * T + spn * 512 + 512],
                           ps[:, :])
                ctx.s.mark(ctx.act)
            ctx.barrier()
        lv = ctx.dma("q", wstage[:, 0:NCH * 16], col_block_ap(win, 4112, 4096, 16))
        ctx.g.wait(*lv)
        for c in range(NCH):
            ctx.g.tensor_copy(wrt[:, c * 16:(c + 1) * 16],
                              wstage[:, c * 16:(c + 1) * 16])
        gm = ctx.g.mark(ctx.gp)
        ctx.t.wait(ctx.gp, gm)
        m1 = gcell[:, 16:17]; m2 = gcell[:, 17:18]
        lg = gcell[:, 32:48]; l2c = gcell[:, 48:64]
        for kt in range(16):
            ps = banks[0]
            for c in range(NCH):
                ctx.t.matmul(ps[:, 0:16],
                             xn[:, c * T + kt * 128:c * T + (kt + 1) * 128],
                             wrt[:, c * 16:(c + 1) * 16],
                             start=(c == 0), stop=(c == NCH - 1))
            pm = ctx.t.mark(ctx.pe)
            ctx.v.wait(ctx.pe, pm)
            ctx.v.tensor_copy(lg, ps[:, 0:16])
            ctx.v.fence()
            ctx.v.reduce_max(m1, lg, axis=AX.X)
            ctx.v.fence()
            ctx.v.tensor_scalar(l2c, lg, m1, None, ALU.is_ge)
            ctx.v.fence()
            ctx.v.scalar_tensor_tensor(l2c, l2c, NEG, lg, ALU.mult, ALU.add)
            ctx.v.fence()
            ctx.v.reduce_max(m2, l2c, axis=AX.X)
            ctx.v.fence()
            ctx.v.tensor_scalar(l2c, lg, m2, None, ALU.is_ge)
            vm = ctx.v.mark(ctx.vec)
            ctx.s.wait(ctx.vec, vm)
            ctx.s.activation(gcell[:, 0:16], lg, AF.Sigmoid)
            am = ctx.s.mark(ctx.act)
            ctx.v.wait(ctx.act, am)
            ctx.v.tensor_tensor(gcell[:, 0:16], gcell[:, 0:16], l2c, ALU.mult)
            ctx.v.fence()
            vm2 = ctx.v.mark(ctx.vec)
            ctx.t.wait(ctx.vec, vm2)
            ctx.t.matmul(banks[1][0:16, (kt % 4) * 128:(kt % 4) * 128 + 128],
                         gcell[:, 0:16], cst["ident_f32"][:],
                         is_transpose=True, start=(kt % 4 == 0), stop=(kt % 4 == 3))
            tm = ctx.t.mark(ctx.pe)
            if kt % 4 == 3:
                ctx.v.wait(ctx.pe, tm)
                ctx.v.tensor_copy(wsT16[0:16, (kt - 3) * 128:(kt + 1) * 128],
                                  banks[1][0:16, :])
                wtm = ctx.v.mark(ctx.vec)
                ctx.t.wait(ctx.vec, wtm)
        ctx.v.fence()
        ctx.v.tensor_copy(wsT16b[0:16, :], wsT16[0:16, :])
        vmb = ctx.v.mark(ctx.vec)
        ctx.s.wait(ctx.vec, vmb)
        sv = ctx.dma("s", w_dram[:, :], wsT16b[0:16, :])
        ctx.q.wait(*sv)
        ctx.barrier()

        last_gml = 0
        last_pvm = 0
        for h in range(8):
            ctx.q.wait(ctx.gp)
            lvk = ctx.dma("q", wstage[:, 0:2048],
                          W["l0f_ke"][h * 128:(h + 1) * 128, 0:2048])
            ctx.g.wait(*lvk)
            ctx.g.wait(ctx.pe, last_pvm)
            ctx.g.tensor_copy(ke_b[:, 0:2048], wstage[:, 0:2048])
            g1 = ctx.g.mark(ctx.gp)
            ctx.q.wait(ctx.gp, g1)
            lvk = ctx.dma("q", wstage[:, 0:2048],
                          W["l0f_ke"][h * 128:(h + 1) * 128, 2048:4096])
            ctx.g.wait(*lvk)
            ctx.g.tensor_copy(ke_b[:, 2048:4096], wstage[:, 0:2048])
            g2 = ctx.g.mark(ctx.gp)
            ctx.q.wait(ctx.gp, g2)
            lvk = ctx.dma("q", wstage[:, 0:2048],
                          W["l0f_ve"][h * 128:(h + 1) * 128, 0:2048])
            ctx.g.wait(*lvk)
            ctx.g.tensor_copy(ve_b[:, 0:2048], wstage[:, 0:2048])
            g3 = ctx.g.mark(ctx.gp)
            ctx.q.wait(ctx.gp, g3)
            lvk = ctx.dma("q", wstage[:, 0:2048],
                          W["l0f_ve"][h * 128:(h + 1) * 128, 2048:4096])
            ctx.g.wait(*lvk)
            ctx.g.tensor_copy(ve_b[:, 2048:4096], wstage[:, 0:2048])
            g4 = ctx.g.mark(ctx.gp)
            ctx.t.wait(ctx.gp, g4)
            for spn in range(4):
                t0l = spn * 512
                ctx.q.wait(ctx.vec, last_gml)
                lvb = ctx.dma("q", wbc[:, 0:8192],
                              AP(tensor=w_dram.tensor, offset=w_dram.offset + t0l,
                                 ap=[[0, 128], [T, 16], [1, 512]]))
                ctx.v.wait(*lvb)
                ctx.t.wait(*lvb)
                first = True
                for e in range(16):
                    for sk in range(2):
                        bb = banks[sk % 2]
                        i2 = e * 2 + sk
                        if i2 >= 2:
                            ctx.t.wait(ctx.act, gel[i2 - 2])
                        ctx.t.matmul(bb[:, :],
                                     ke_b[:, e * 256 + sk * 128:e * 256 + sk * 128 + 128],
                                     qf[:, h * T + t0l:h * T + t0l + 512],
                                     start=True, stop=True)
                        pm = ctx.t.mark(ctx.pe)
                        ctx.s.wait(ctx.pe, pm)
                        if i2 >= 2:
                            ctx.s.wait(ctx.vec, gml[i2 - 2])
                        ctx.s.activation(aT[:, sk * 512:sk * 512 + 512], bb[:, :],
                                         AF.Gelu)
                        if i2 == 0:
                            gel = {}
                            gml = {}
                        gel[i2] = ctx.s.mark(ctx.act)
                        ctx.v.wait(ctx.act, gel[i2])
                        ctx.v.tensor_tensor(aT[:, sk * 512:sk * 512 + 512],
                                            aT[:, sk * 512:sk * 512 + 512],
                                            wbc[:, e * 512:(e + 1) * 512], ALU.mult)
                        ctx.v.fence()
                        gml[i2] = ctx.v.mark(ctx.vec)
                        ctx.t.wait(ctx.vec, gml[i2])
                        ctx.t.matmul(ps_pv[:, :],
                                     ve_b[:, e * 256 + sk * 128:e * 256 + sk * 128 + 128],
                                     aT[:, sk * 512:sk * 512 + 512],
                                     start=first, stop=(e == 15 and sk == 1))
                        first = False
                pvm = ctx.t.mark(ctx.pe)
                last_gml = gml[31]
                last_pvm = pvm
                acat_store(1, h, spn, pvm)
        ctx.barrier()
        stB.close()

        stC = ExitStack()
        ws_out = WStream(ctx, stC, f"l0f_wo{sq}", D, nch=16, stage=wstage)
        ac_in = ctx.sbuf(f"l0f_acin{sq}", [128, 2 * NCH * 512], BF16, stC)
        ws_out.load(W["l0f_wout"])
        ctx.barrier()
        cur_spn = [-1]

        def oproj(dc, spn, ps):
            if spn != cur_spn[0]:
                src_ap = AP(tensor=acat_d.tensor,
                            offset=acat_d.offset + spn * 512,
                            ap=[[T, 128], [128 * T, 2 * NCH], [1, 512]])
                lv = ctx.dma("q", ac_in[:], src_ap)
                ctx.t.wait(*lv)
                cur_spn[0] = spn
            for c in range(NCH):
                ctx.t.matmul(ps[:, :], ws_out.tile(c, dc * 128),
                             ac_in[:, c * 512:(c + 1) * 512],
                             start=(c == 0), stop=False)
            for c in range(NCH):
                ctx.t.matmul(ps[:, :], ws_out.tile(8 + c, dc * 128),
                             ac_in[:, (NCH + c) * 512:(NCH + c + 1) * 512],
                             start=False, stop=(c == NCH - 1))

        emit_residual_out(ctx, cst, lnw, xin, xout, tok0, T, oproj, None, banks)
        ctx.barrier()
        stC.close()
    st.close()



def emit_l1f(ctx, cst, xin, xout):
    st = ExitStack()
    lnw = ln_work(ctx, st, "l1f")
    y_res = ctx.dram("l1f_y", [NCH, 128, TOK], F32)
    acat_d = ctx.dram("l1f_acat", [NCH, 128, T], BF16)
    qr_d = ctx.dram("l1f_qr", [NCH, 128, T], BF16)
    kr_d = ctx.dram("l1f_kr", [NCH, 128, T], BF16)
    wstage = ctx.sbuf("l1f_wst", [128, 2 * 1024], F32, st)
    maskup = ctx.sbuf("l1f_mup", [128, 128], F32, st)
    masklo = ctx.sbuf("l1f_mlo", [128, 128], F32, st)
    banks = [ctx.psum(f"l1f_ps{i}", [128, 512], F32, st) for i in range(6)]
    ps_t = ctx.psum("l1f_pst", [128, 1024], BF16, st)
    ps_pv = ctx.psum("l1f_pspv", [128, 512], F32, st)
    W = ctx.wins
    ctx.dma("q", maskup[:], W["mask_neg"][:, :])
    ctx.dma("q", masklo[:], W["mask_low"][:, :])

    st0 = ExitStack()
    yscr = ctx.sbuf("l1f_yscr", [128, 1024], F32, st0)
    for sq in range(SEQ):
        emit_ln(ctx, cst, lnw, xin, None, T, tok0=sq * T, banks=banks[0:4],
                fp32_store=(y_res, yscr))
        ctx.barrier()
    st0.close()

    for dep in range(6):
        row0 = dep * D
        for sq in range(SEQ):
            tok0 = sq * T
            stA = ExitStack()
            ybs = ctx.sbuf(f"l1f_ybs{dep}{sq}", [128, NCH * T], BF16, stA)
            vtm = ctx.sbuf(f"l1f_vtm{dep}{sq}", [128, 16 * D], BF16, stA)
            wsec = ctx.sbuf(f"l1f_wsec{dep}{sq}", [128, NCH * 1024], BF16, stA)
            qkv_f = ctx.sbuf(f"l1f_qkvf{dep}{sq}", [128, 1024], F32, stA)
            rp_f = ctx.sbuf(f"l1f_rpf{dep}{sq}", [128, 1024], F32, stA)
            rp_b = ctx.sbuf(f"l1f_rpb{dep}{sq}", [128, 1024], BF16, stA)
            seg16 = ctx.sbuf(f"l1f_seg{dep}{sq}", [128, 64], F32, stA)
            csq = ctx.sbuf(f"l1f_csq{dep}{sq}", [128, 1024], F32, stA)
            ssq = ctx.sbuf(f"l1f_ssq{dep}{sq}", [128, 1024], F32, stA)
            csk = ctx.sbuf(f"l1f_csk{dep}{sq}", [128, 1024], F32, stA)
            ssk = ctx.sbuf(f"l1f_ssk{dep}{sq}", [128, 1024], F32, stA)
            sc_bc = ctx.sbuf(f"l1f_scbc{dep}{sq}", [128, 128], F32, stA)
            trs = ctx.sbuf(f"l1f_trs{dep}{sq}", [128, 1024], BF16, stA)
            qkh = ctx.sbuf(f"l1f_qkh{dep}{sq}", [64, 2 * T], BF16, stA)
            s_sb = ctx.sbuf(f"l1f_ssb{dep}{sq}", [128, 768], F32, stA)
            p_b = ctx.sbuf(f"l1f_pb{dep}{sq}", [128, 768], BF16, stA)
            pt_s = ctx.sbuf(f"l1f_pts{dep}{sq}", [128, 384], BF16, stA)
            aout = ctx.sbuf(f"l1f_aout{dep}{sq}", [64, T], BF16, stA)
            stat = ctx.sbuf(f"l1f_stat{dep}{sq}", [128, 8], F32, stA)

            emit_shift_cast(ctx, y_res, ybs, tok0, lnw)
            lvq = ctx.dma("q", sc_bc[:, 0:64],
                          W["l1f_qs"][dep:dep + 1, :].broadcast_to([128, 64]))
            lvk = ctx.dma("q", sc_bc[:, 64:128],
                          W["l1f_ks"][dep:dep + 1, :].broadcast_to([128, 64]))
            ctx.dma("q", csq[:], W["l1f_cs"][:, :])
            ctx.dma("q", ssq[:], W["l1f_ssn"][:, :])
            ctx.dma("q", csk[:], W["l1f_csi"][:, :])
            ctx.dma("q", ssk[:], W["l1f_ssi"][:, :])
            ctx.barrier()

            for sec in range(3):
                gm = {}
                for c in range(NCH):
                    half = (c % 2) * 1024
                    if c >= 2:
                        ctx.q.wait(ctx.gp, gm[c - 2])
                    else:
                        ctx.q.wait(ctx.gp)
                    lv = ctx.dma("q", wstage[:, half:half + 1024],
                                 W["l1f_wqkv"][row0 + c * 128:row0 + (c + 1) * 128,
                                               sec * 1024:(sec + 1) * 1024])
                    ctx.g.wait(*lv)
                    ctx.g.tensor_copy(wsec[:, c * 1024:(c + 1) * 1024],
                                      wstage[:, half:half + 1024])
                    gm[c] = ctx.g.mark(ctx.gp)
                ctx.t.wait(ctx.gp, gm[NCH - 1])
                for kt in range(16):
                    for half in range(2):
                        i = kt * 2 + half
                        ps = banks[i % 4]
                        if i >= 4:
                            ctx.t.wait(ctx.act, qkev[i - 4])
                        for c in range(NCH):
                            ctx.t.matmul(ps[:, :],
                                         ybs[:, c * T + kt * 128:c * T + (kt + 1) * 128],
                                         wsec[:, c * 1024 + half * 512:
                                              c * 1024 + half * 512 + 512],
                                         start=(c == 0), stop=(c == NCH - 1))
                        pm = ctx.t.mark(ctx.pe)
                        ctx.s.wait(ctx.pe, pm)
                        if sec == 2:
                            ctx.s.copy(vtm[:, kt * D + half * 512:
                                           kt * D + half * 512 + 512], ps[:, :])
                        else:
                            if half == 0 and kt >= 1:
                                ctx.s.wait(ctx.vec, rpev[kt - 1])
                            ctx.s.copy(qkv_f[:, half * 512:half * 512 + 512],
                                       ps[:, :])
                        if i == 0:
                            qkev = {}
                            rpev = {}
                        qkev[i] = ctx.s.mark(ctx.act)
                    if sec == 2:
                        continue
                    ctx.v.wait(ctx.act, qkev[kt * 2 + 1])
                    ctx.v.tensor_tensor(rp_f[:], qkv_f[:], qkv_f[:], ALU.mult)
                    ctx.v.fence()
                    ctx.v.tensor_reduce(seg16[:, 0:16],
                                        rp_f[:].rearrange("p (s k) -> p s k", k=64),
                                        AX.X, ALU.add)
                    ctx.v.fence()
                    ctx.v.tensor_scalar_max(seg16[:, 0:16], seg16[:, 0:16], 1e-24)
                    ctx.v.fence()
                    vm = ctx.v.mark(ctx.vec)
                    ctx.s.wait(ctx.vec, vm)
                    ctx.s.activation(seg16[:, 16:32], seg16[:, 0:16], AF.Ln)
                    ctx.s.fence()
                    ctx.s.activation(seg16[:, 0:16], seg16[:, 16:32], AF.Exp,
                                     scale=0.5)
                    am = ctx.s.mark(ctx.act)
                    ctx.v.wait(ctx.act, am)
                    ctx.v.tensor_scalar_max(seg16[:, 0:16], seg16[:, 0:16], 1e-12)
                    ctx.v.fence()
                    ctx.v.reciprocal(seg16[:, 16:32], seg16[:, 0:16])
                    ctx.v.fence()
                    rnrep = AP(tensor=seg16, offset=16,
                               ap=[[64, 128], [1, 16], [0, 64]])
                    ctx.v.tensor_tensor(qkv_f[:], qkv_f[:], rnrep, ALU.mult)
                    ctx.v.fence()
                    screp = AP(tensor=sc_bc, offset=(0 if sec == 0 else 64),
                               ap=[[128, 128], [0, 16], [1, 64]])
                    ctx.v.tensor_tensor(qkv_f[:], qkv_f[:], screp, ALU.mult)
                    ctx.v.fence()
                    cstab = csq if sec == 0 else csk
                    sstab = ssq if sec == 0 else ssk
                    tabsl = slice(kt * 64, (kt + 1) * 64)
                    cs_rep = AP(tensor=cstab, offset=kt * 64,
                                ap=[[1024, 128], [0, 16], [1, 64]])
                    ctx.v.tensor_tensor(rp_f[:], qkv_f[:], cs_rep, ALU.mult)
                    for hswap in range(2):
                        qsw = AP(tensor=qkv_f, offset=(32 if hswap == 0 else 0),
                                 ap=[[1024, 128], [64, 16], [1, 32]])
                        ssl = AP(tensor=sstab, offset=kt * 64 + hswap * 32,
                                 ap=[[1024, 128], [0, 16], [1, 32]])
                        dsw = AP(tensor=rp_b, offset=hswap * 32,
                                 ap=[[1024, 128], [64, 16], [1, 32]])
                        ctx.v.tensor_tensor(dsw, qsw, ssl, ALU.mult)
                    ctx.v.fence()
                    ctx.v.tensor_tensor(rp_b[:], rp_b[:], rp_f[:], ALU.add)
                    ctx.v.fence()
                    rpev[kt] = ctx.v.mark(ctx.vec)
                    ctx.t.wait(ctx.vec, rpev[kt])
                    for hp in range(8):
                        ctx.t.matmul(ps_t[:, hp * 128:(hp + 1) * 128],
                                     rp_b[:, hp * 128:(hp + 1) * 128],
                                     cst["ident_sb"][:], is_transpose=True,
                                     start=(hp == 0), stop=(hp == 7))
                    tm = ctx.t.mark(ctx.pe)
                    ctx.v.wait(ctx.pe, tm)
                    if kt >= 1:
                        ctx.v.wait(*trst[kt - 1])
                    ctx.v.tensor_copy(trs[:], ps_t[:])
                    ctx.v.fence()
                    trm = ctx.v.mark(ctx.vec)
                    ctx.t.wait(ctx.vec, trm)
                    ctx.s.wait(ctx.vec, trm)
                    if kt == 0:
                        trst = {}
                    qdst = qr_d if sec == 0 else kr_d
                    dst = AP(tensor=qdst.tensor,
                             offset=qdst.offset + kt * 128,
                             ap=[[T, 128], [128 * T, NCH], [1, 128]])
                    trst[kt] = ctx.dma("s", dst, trs[:])
                ctx.barrier()

            prev_pvm = 0
            sev = {}
            pnv = {}
            ams = {}
            aev = {}
            for h in range(16):
                hp, r0 = h // 2, (h % 2) * 64
                ctx.q.wait(ctx.pe, prev_pvm)
                lv1 = ctx.dma("q", qkh[0:64, 0:T],
                              AP(tensor=qr_d.tensor,
                                 offset=qr_d.offset + (hp * 128 + r0) * T,
                                 ap=[[T, 64], [1, T]]))
                lv2 = ctx.dma("q", qkh[0:64, T:2 * T],
                              AP(tensor=kr_d.tensor,
                                 offset=kr_d.offset + (hp * 128 + r0) * T,
                                 ap=[[T, 64], [1, T]]))
                ctx.t.wait(*lv1)
                ctx.t.wait(*lv2)
                for qt in range(16):
                    gq = h * 16 + qt
                    kt0 = max(0, qt - 2)
                    nkt = qt - kt0 + 1
                    wk = nkt * 128
                    ofs = (qt % 2) * 384
                    st0 = (qt % 2) * 4
                    bb = banks[qt % 2]
                    if gq >= 2:
                        ctx.t.wait(ctx.act, sev[gq - 2])
                    ctx.t.matmul(bb[:, 0:wk], qkh[0:64, qt * 128:(qt + 1) * 128],
                                 qkh[0:64, T + kt0 * 128:T + kt0 * 128 + wk],
                                 start=True, stop=True)
                    pm = ctx.t.mark(ctx.pe)
                    ctx.s.wait(ctx.pe, pm)
                    if gq >= 2:
                        ctx.s.wait(ctx.act, ams[gq - 2])
                    ctx.s.copy(s_sb[:, ofs:ofs + wk], bb[:, 0:wk])
                    sev[gq] = ctx.s.mark(ctx.act)
                    ctx.v.wait(ctx.act, sev[gq])
                    ctx.v.tensor_tensor(s_sb[:, ofs + wk - 128:ofs + wk],
                                        s_sb[:, ofs + wk - 128:ofs + wk],
                                        maskup[:], ALU.add)
                    if nkt == 3:
                        ctx.v.tensor_tensor(s_sb[:, ofs:ofs + 128],
                                            s_sb[:, ofs:ofs + 128],
                                            masklo[:], ALU.add)
                    ctx.v.fence()
                    ctx.v.reduce_max(stat[:, st0:st0 + 1], s_sb[:, ofs:ofs + wk],
                                     axis=AX.X, negate=True)
                    ctx.v.fence()
                    vm2 = ctx.v.mark(ctx.vec)
                    ctx.s.wait(ctx.vec, vm2)
                    ctx.s.mul(stat[:, st0 + 1:st0 + 2], stat[:, st0:st0 + 1], 8.0)
                    ctx.s.memzero(stat[:, st0 + 2:st0 + 3])
                    ctx.s.fence()
                    ctx.s.activation(p_b[:, ofs:ofs + wk], s_sb[:, ofs:ofs + wk],
                                     AF.Exp, scale=8.0,
                                     bias=stat[:, st0 + 1:st0 + 2],
                                     accum_out=stat[:, st0 + 2:st0 + 3])
                    am2 = ctx.s.mark(ctx.act)
                    ams[gq] = am2
                    ctx.v.wait(ctx.act, am2)
                    ctx.v.reciprocal(stat[:, st0 + 3:st0 + 4],
                                     stat[:, st0 + 2:st0 + 3])
                    ctx.v.fence()
                    ctx.v.tensor_scalar(p_b[:, ofs:ofs + wk], p_b[:, ofs:ofs + wk],
                                        stat[:, st0 + 3:st0 + 4], None, ALU.mult)
                    ctx.v.fence()
                    pnv[gq] = ctx.v.mark(ctx.vec)
                    ctx.t.wait(ctx.vec, pnv[gq])
                    for j in range(nkt):
                        ctx.t.matmul(ps_t[:, j * 128:(j + 1) * 128],
                                     p_b[:, ofs + j * 128:ofs + (j + 1) * 128],
                                     cst["ident_sb"][:], is_transpose=True,
                                     start=(j == 0), stop=(j == nkt - 1))
                    tm2 = ctx.t.mark(ctx.pe)
                    ctx.v.wait(ctx.pe, tm2)
                    ctx.v.tensor_copy(pt_s[:, 0:wk], ps_t[:, 0:wk])
                    ptm = ctx.v.mark(ctx.vec)
                    ctx.t.wait(ctx.vec, ptm)
                    if gq >= 1:
                        ctx.t.wait(ctx.act, aev[gq - 1])
                    for j in range(nkt):
                        ctx.t.matmul(ps_pv[0:64, 0:128],
                                     vtm[:, (kt0 + j) * D + h * 64:
                                         (kt0 + j) * D + h * 64 + 64],
                                     pt_s[:, j * 128:(j + 1) * 128],
                                     start=(j == 0), stop=(j == nkt - 1))
                    pvm = ctx.t.mark(ctx.pe)
                    prev_pvm = pvm
                    ctx.s.wait(ctx.pe, pvm)
                    ctx.s.copy(aout[0:64, qt * 128:(qt + 1) * 128], ps_pv[0:64, 0:128])
                    aev[gq] = ctx.s.mark(ctx.act)
                ctx.s.wait(ctx.act, aev[h * 16 + 15])
                dsta = AP(tensor=acat_d.tensor,
                          offset=acat_d.offset + (hp * 128 + r0) * T,
                          ap=[[T, 64], [1, T]])
                hst = ctx.dma("s", dsta, aout[0:64, :])
                ctx.q.wait(*hst)
            ctx.barrier()
            stA.close()

            stW = ExitStack()
            ws_o = WStream(ctx, stW, f"l1f_wo{dep}{sq}", D, stage=wstage)
            ac_in = ctx.sbuf(f"l1f_acin{dep}{sq}", [128, NCH * 512], BF16, stW)
            ws_o.load(W["l1f_wo"], row0=row0)
            ctx.barrier()
            cur = [-1]

            def oproj(dc, spn, ps):
                if spn != cur[0]:
                    srcp = AP(tensor=acat_d.tensor,
                              offset=acat_d.offset + spn * 512,
                              ap=[[T, 128], [128 * T, NCH], [1, 512]])
                    lv = ctx.dma("q", ac_in[:], srcp)
                    ctx.t.wait(*lv)
                    cur[0] = spn
                for c in range(NCH):
                    ctx.t.matmul(ps[:, :], ws_o.tile(c, dc * 128),
                                 ac_in[:, c * 512:(c + 1) * 512],
                                 start=(c == 0), stop=(c == NCH - 1))

            emit_residual_out(ctx, cst, lnw, y_res, y_res, tok0, T, oproj,
                              None, banks)
            ctx.barrier()
            stW.close()

            stF = ExitStack()
            w1b = ctx.sbuf(f"l1f_w1b{dep}{sq}", [128, NCH * 5632], BF16, stF)
            w2b = ctx.sbuf(f"l1f_w2b{dep}{sq}", [128, 22 * D], BF16, stF)
            xnf = ctx.sbuf(f"l1f_xnf{dep}{sq}", [128, NCH * 1024], BF16, stF)
            h_all = ctx.sbuf(f"l1f_hall{dep}{sq}", [128, 22 * 256], BF16, stF)
            afgl = ctx.sbuf(f"l1f_afgl{dep}{sq}", [128, 2 * 256], BF16, stF)
            cols = ctx.sbuf(f"l1f_cols{dep}{sq}", [128, 4 * 32], F32, stF)
            otmp = lnw["bc"]
            for dst2, key, wdt in ((cols[:, 0:22], "l1f_b1a", 22),
                                   (cols[:, 32:54], "l1f_b1g", 22),
                                   (cols[:, 64:72], "l1f_b2", 8),
                                   (cols[:, 96:104], "l1f_lng", 8),
                                   (cols[:, 104:112], "l1f_lnb", 8)):
                ctx.dma("q", dst2, W[key][dep * 128:(dep + 1) * 128, :])
            gmw = {}
            gi = 0
            rot = ((ctx.v, ctx.vec), (ctx.s, ctx.act), (ctx.g, ctx.gp))

            def stage_copy(dst, half, pw):
                eng, sem = rot[gi % 3]
                eng.wait(*lv)
                if eng is ctx.s:
                    eng.copy(dst, wstage[:, half:half + pw])
                else:
                    eng.tensor_copy(dst, wstage[:, half:half + pw])
                gmw[gi] = (sem, eng.mark(sem))

            for c in range(NCH):
                for piece in range(6):
                    p0 = piece * 1024
                    pw = min(1024, 5632 - p0)
                    half = (gi % 2) * 1024
                    if gi >= 2:
                        ctx.q.wait(*gmw[gi - 2])
                    else:
                        ctx.q.wait(ctx.gp)
                    lv = ctx.dma("q", wstage[:, half:half + pw],
                                 W["l1f_w1"][dep * D + c * 128:
                                             dep * D + (c + 1) * 128, p0:p0 + pw])
                    stage_copy(w1b[:, c * 5632 + p0:c * 5632 + p0 + pw], half, pw)
                    gi += 1
            for j in range(22):
                half = (gi % 2) * 1024
                ctx.q.wait(*gmw[gi - 2])
                lv = ctx.dma("q", wstage[:, half:half + 1024],
                             W["l1f_w2"][dep * 2816 + j * 128:
                                         dep * 2816 + (j + 1) * 128, :])
                stage_copy(w2b[:, j * D:(j + 1) * D], half, 1024)
                gi += 1
            ctx.barrier()

            for qq in range(2):
                tq0 = tok0 + qq * 1024
                emit_ln(ctx, cst, lnw, y_res, xnf, 1024, tok0=tq0,
                        banks=banks[0:4], xn_ntok=1024, shift=True,
                        seq_start=tok0,
                        affine=(lambda c: cols[:, 96 + c:97 + c],
                                lambda c: AP(tensor=cols, offset=104 + c,
                                             ap=[[128, 128], [0, 512]])))
                ctx.barrier()
                for sp in range(4):
                    t0l = sp * 256
                    hev = {}
                    for j in range(22):
                        ba = banks[4]
                        bg = banks[5]
                        if j >= 1:
                            ctx.t.wait(ctx.vec, hev[j - 1])
                        for c in range(NCH):
                            ctx.t.matmul(ba[:, 0:256],
                                         w1b[:, c * 5632 + j * 128:
                                             c * 5632 + (j + 1) * 128],
                                         xnf[:, c * 1024 + t0l:c * 1024 + t0l + 256],
                                         start=(c == 0), stop=(c == NCH - 1))
                        for c in range(NCH):
                            ctx.t.matmul(bg[:, 0:256],
                                         w1b[:, c * 5632 + 2816 + j * 128:
                                             c * 5632 + 2816 + (j + 1) * 128],
                                         xnf[:, c * 1024 + t0l:c * 1024 + t0l + 256],
                                         start=(c == 0), stop=(c == NCH - 1))
                        pm = ctx.t.mark(ctx.pe)
                        ctx.s.wait(ctx.pe, pm)
                        ctx.s.activation(afgl[:, 0:256], ba[:, 0:256], AF.Identity,
                                         bias=cols[:, j:j + 1])
                        ctx.s.fence()
                        ctx.s.activation(afgl[:, 256:512], bg[:, 0:256], AF.Gelu,
                                         bias=cols[:, 32 + j:33 + j])
                        am = ctx.s.mark(ctx.act)
                        ctx.v.wait(ctx.act, am)
                        ctx.v.tensor_tensor(h_all[:, j * 256:(j + 1) * 256],
                                            afgl[:, 0:256], afgl[:, 256:512],
                                            ALU.mult)
                        hev[j] = ctx.v.mark(ctx.vec)
                    ctx.t.wait(ctx.vec, hev[21])
                    cm2 = {}
                    for j in range(22):
                        for dc in range(NCH):
                            ctx.t.matmul(banks[dc // 2][:, (dc % 2) * 256:
                                                        (dc % 2) * 256 + 256],
                                         w2b[:, j * D + dc * 128:
                                             j * D + dc * 128 + 128],
                                         h_all[:, j * 256:(j + 1) * 256],
                                         start=(j == 0 and dc % 2 == 0),
                                         stop=(j == 21 and dc % 2 == 1))
                    pmf = ctx.t.mark(ctx.pe)
                    lvx = ctx.dma("q", lnw["xf"][:, 0:NCH * 256],
                                  x_span_ap(y_res, tq0 + t0l, w=256))
                    for dc in range(NCH):
                        gi2 = sp * NCH + dc
                        ctx.s.wait(ctx.pe, pmf)
                        if gi2 >= 2:
                            ctx.s.wait(*fsts[gi2 - 2])
                        ctx.s.activation(otmp[:, (gi2 % 2) * 256:(gi2 % 2) * 256 + 256],
                                         banks[dc // 2][:, (dc % 2) * 256:
                                                        (dc % 2) * 256 + 256],
                                         AF.Identity, bias=cols[:, 64 + dc:65 + dc])
                        am3 = ctx.s.mark(ctx.act)
                        ctx.v.wait(ctx.act, am3)
                        ctx.v.wait(*lvx)
                        if gi2 >= 2:
                            ctx.v.wait(*fsts[gi2 - 2])
                        ot = otmp[:, (gi2 % 2) * 256:(gi2 % 2) * 256 + 256]
                        ctx.v.tensor_tensor(ot, ot,
                                            lnw["xf"][:, dc * 256:(dc + 1) * 256],
                                            ALU.add)
                        ctx.v.fence()
                        if gi2 == 0:
                            fsts = {}
                        fsv = ctx.v.mark(ctx.vec)
                        ctx.s.wait(ctx.vec, fsv)
                        fsts[gi2] = ctx.dma("s",
                                            x_chunk_span_ap(y_res, dc, tq0 + t0l,
                                                            w=256), ot)
                    ctx.q.wait(ctx.vec, fsv)
                    ctx.t.wait(ctx.vec, fsv)
                ctx.barrier()
            stF.close()
    yscr = ctx.sbuf("l1f_yscr2", [128, 1024], F32, st)
    sts2 = {}
    for spn in range(TOK // 512):
        t0 = spn * 512
        lv1 = ctx.dma("q", lnw["xf"][:, 0:NCH * 512], x_span_ap(xin, t0))
        for dc in range(NCH):
            gi3 = spn * NCH + dc
            if gi3 >= 2:
                ctx.q.wait(ctx.vec, ovs2[gi3 - 2])
            lvy = ctx.dma("q", yscr[:, (gi3 % 2) * 512:(gi3 % 2) * 512 + 512],
                          x_chunk_span_ap(y_res, dc, t0))
            ctx.v.wait(*lv1)
            ctx.v.wait(*lvy)
            if gi3 >= 2:
                ctx.v.wait(*sts2[gi3 - 2])
            ot = lnw["bc"][:, (gi3 % 2) * 512:(gi3 % 2) * 512 + 512]
            ctx.v.tensor_tensor(ot, yscr[:, (gi3 % 2) * 512:(gi3 % 2) * 512 + 512],
                                lnw["xf"][:, dc * 512:(dc + 1) * 512], ALU.add)
            ctx.v.fence()
            if gi3 == 0:
                ovs2 = {}
            ovs2[gi3] = ctx.v.mark(ctx.vec)
            ctx.s.wait(ctx.vec, ovs2[gi3])
            sts2[gi3] = ctx.dma("s", x_chunk_span_ap(xout, dc, t0), ot)
        ctx.q.wait(ctx.vec, ovs2[spn * NCH + NCH - 1])
    ctx.barrier()
    st.close()


def emit_shift_cast(ctx, ydram, ybs, tok0, lnw):
    xf = lnw["xf"]
    xfw = xf.shape[1]
    stw = 513
    for sp in range(T // 512):
        t0 = tok0 + sp * 512
        at_start = (sp == 0)
        if at_start:
            lv = ctx.dma("q", AP(tensor=xf, offset=1,
                                 ap=[[xfw, 128], [stw, NCH], [1, 512]]),
                         x_span_ap(ydram, t0, w=512))
            ctx.v.wait(*lv)
            ctx.v.memset(AP(tensor=xf, offset=0,
                            ap=[[xfw, 128], [stw, NCH], [1, 1]]), 0.0)
            ctx.v.fence()
        else:
            lv = ctx.dma("q", AP(tensor=xf, offset=0,
                                 ap=[[xfw, 128], [stw, NCH], [1, stw]]),
                         x_span_ap(ydram, t0 - 1, w=stw))
            ctx.v.wait(*lv)
        for c in range(NCH):
            off = c * stw + (1 if c < 4 else 0)
            ctx.v.tensor_copy(ybs[:, c * T + sp * 512:c * T + sp * 512 + 512],
                              xf[:, off:off + 512])
        ctx.v.fence()
        vm = ctx.v.mark(ctx.vec)
        ctx.q.wait(ctx.vec, vm)



CONST_SPECS = [
    ("ident", (128, 128), BF16), ("ones_col", (128, 1), BF16),
    ("ones_row", (1, 128), BF16), ("mask_l0a", (128, 128), F32),
    ("mask_neg", (128, 128), F32), ("mask_negb", (128, 128), BF16),
    ("antidiag8", (128, 128), BF16), ("ident_f32", (128, 128), F32),
]

SUBLAYER_WEIGHTS = {
    0: [("l0a_qw", (D, D)), ("l0a_kw", (D, D)), ("l0a_vw", (D, D)),
        ("l0a_ow", (D, D)), ("l0a_qb", (128, NCH)), ("l0a_kb", (128, NCH)),
        ("l0a_vb", (1, D)), ("l0a_ob", (128, NCH))],
    7: [("l3f_wu", (D, 4096)), ("l3f_wd", (4096, D))],
    3: [("l1f_wqkv", (6 * D, 3072)), ("l1f_wo", (6 * D, D)),
        ("l1f_w1", (6 * D, 5632)), ("l1f_w2", (6 * 2816, D)),
        ("l1f_b1a", (6 * 128, 22)), ("l1f_b1g", (6 * 128, 22)),
        ("l1f_b2", (6 * 128, 8)), ("l1f_lng", (6 * 128, 8)),
        ("l1f_lnb", (6 * 128, 8)), ("l1f_qs", (6, 64)), ("l1f_ks", (6, 64)),
        ("l1f_cs", (128, 1024)), ("l1f_ssn", (128, 1024)),
        ("l1f_csi", (128, 1024)), ("l1f_ssi", (128, 1024)),
        ("mask_low", (128, 128))],
    1: [("l0f_win", (D, 4112)), ("l0f_wout", (2048, D)),
        ("l0f_cos", (128, 1024)), ("l0f_sin", (128, 1024)),
        ("l0f_ke", (8 * 128, 4096)), ("l0f_ve", (8 * 128, 4096))],
    5: [("l2f_wgw", (D, 8)), ("l2f_ws1", (D, 2048)), ("l2f_ws3", (D, 2048)),
        ("l2f_ws2", (2048, D)), ("l2f_we1", (8 * D, 2048)),
        ("l2f_we3", (8 * D, 2048)), ("l2f_we2", (8 * 2048, D))],
    4: [("l2a_wq", (D, D)), ("l2a_wk", (D, D)), ("l2a_wv", (D, D)),
        ("l2a_wo", (D, D)), ("l2a_lambda_q1", (1, 64)), ("l2a_lambda_k1", (1, 64)),
        ("l2a_lambda_q2", (1, 64)), ("l2a_lambda_k2", (1, 64)),
        ("l2a_lnw", (1, 128)), ("l2a_lnb", (1, 128)),
        ("l2a_cos", (128, 1024)), ("l2a_sa", (128, 1024)), ("l2a_sb", (128, 1024))],
    2: [("l1a_w", (D, 3072)), ("l1a_wp", (D, D)), ("l1a_qb", (128, NCH)),
        ("l1a_kb", (128, NCH)), ("l1a_vb", (1, D)), ("l1a_pb", (128, NCH))],
    6: [("l3a_qw", (D, D)), ("l3a_kw", (D, D)), ("l3a_vw", (D, D)),
        ("l3a_ow", (D, D)), ("l3a_qb", (128, NCH)), ("l3a_kb", (128, NCH)),
        ("l3a_vb", (1, D)), ("l3a_ob", (128, NCH)),
        ("l3a_gext", (16, 4224))],
}
SUBLAYER_EMIT = {}
SUBLAYER_EMIT[0] = emit_l0a
SUBLAYER_EMIT[2] = emit_l1a
SUBLAYER_EMIT[6] = emit_l3a
SUBLAYER_EMIT[7] = emit_l3f
SUBLAYER_EMIT[4] = emit_l2a
SUBLAYER_EMIT[5] = emit_l2f
SUBLAYER_EMIT[1] = emit_l0f
SUBLAYER_EMIT[3] = emit_l1f


def build_program(sublayers):
    nc = bass.Bass("TRN2", target_bir_lowering=False)
    ctx = Ctx(nc)
    ctx.wins = {}
    for name, shape, dt in CONST_SPECS:
        ctx.wins[name] = ctx.dram(name, list(shape), dt, kind="ExternalInput")
    for sl in sublayers:
        for name, shape in SUBLAYER_WEIGHTS.get(sl, []):
            ctx.wins[name] = ctx.dram(name, list(shape), F32, kind="ExternalInput")

    xa = ctx.dram("x_in", [NCH, 128, TOK], F32, kind="ExternalInput")
    xb = ctx.dram("x_out", [NCH, 128, TOK], F32, kind="ExternalOutput")
    xmid = ctx.dram("x_mid", [NCH, 128, TOK], F32, kind="Internal") \
        if len(sublayers) > 1 else None

    cst = {}
    cst["ones_col"] = ctx.sbuf("c_onescol", [128, 1], BF16)
    cst["ones_row"] = ctx.sbuf("c_onesrow", [1, 128], BF16)
    cst["ident_sb"] = ctx.sbuf("c_ident", [128, 128], BF16)
    cst["maskb_sb"] = ctx.sbuf("c_maskb", [128, 128], BF16)
    cst["ident_f32"] = ctx.sbuf("c_identf", [128, 128], F32)
    cst["cpool"] = ctx.sbuf("c_pool", [128, len(CONST_FLOATS)], F32)
    for sb_t, key in ((cst["ones_col"], "ones_col"), (cst["ones_row"], "ones_row"),
                      (cst["ident_sb"], "ident"), (cst["maskb_sb"], "mask_negb"),
                      (cst["ident_f32"], "ident_f32")):
        ctx.dma("q", sb_t[:], ctx.wins[key][:, :])
    for i, v in enumerate(CONST_FLOATS):
        ctx.g.memset(cst["cpool"][:, i:i + 1], float(v))
        nc.const_aps.register(F32, float(v), cst["cpool"][:, i:i + 1])
    ctx.g.mark(ctx.gp)
    ctx.barrier()

    n = len(sublayers)
    for i, sl in enumerate(sublayers):
        src = xa if i == 0 else xmid
        dst = xb if i == n - 1 else xmid
        SUBLAYER_EMIT[sl](ctx, cst, src, dst)
        ctx.barrier()
    for e in ctx.engs:
        for sem in ctx.lds + ctx.sts:
            e.wait(sem)
    ctx.es.close()
    return nc


def make_in_map(C, x_core, sublayers):
    m = {"x_in": x_core}
    for name, shape, dt in CONST_SPECS:
        m[name] = C[name]
    for sl in sublayers:
        for name, shape in SUBLAYER_WEIGHTS.get(sl, []):
            m[name] = C[name]
    return m


def kernel(**inputs):
    sublayers = sorted(SUBLAYER_EMIT.keys())
    C, xs = prep_host(inputs)
    nc = build_program(sublayers)
    in_maps = [make_in_map(C, xs[c], sublayers) for c in range(NCORES)]
    res = run_bass_kernel_spmd(nc, in_maps, core_ids=list(range(NCORES)))
    return unprep_out([r["x_out"] for r in res.results])
